# version 09
# baseline (speedup 1.0000x reference)
import math
import types
import numpy as np
import ml_dtypes
import concourse.bass as bass
import concourse.mybir as mybir
from concourse.bass_utils import run_bass_kernel_spmd

F32 = mybir.dt.float32
BF16 = mybir.dt.bfloat16
AF = mybir.ActivationFunctionType
ALU = mybir.AluOpType
AX = mybir.AxisListType
BF = ml_dtypes.bfloat16

NCORES = 8
B, S, D = 4, 2048, 4096
HEADS, HD, LBLK = 32, 128, 256
HL = 16
TOK = 1024
NB_SH = 4
DEBUG_DUMP = False
FUSED_DEBUG = False


def _decays():
    base = 1.0 / 2.0 ** (8.0 / HEADS)
    exponent = np.arange(HEADS, dtype=np.float64) + 1.0
    factor = 1.0 - 1 / (32 - 1 + 1e-05) + 1e-05
    slope = (base ** exponent * factor).astype(np.float32)[:, None, None].astype(np.float32)
    r = np.arange(LBLK, dtype=np.float32) + 1.0
    q_decay = np.exp(-slope * r[:, None]).astype(np.float32)[:, :, 0]
    k_decay = np.exp(-slope * (LBLK - r)[:, None]).astype(np.float32)[:, :, 0]
    diff = r[:, None] - r[None, :]
    diag = slope * diff[None]
    diag = np.where(diag >= 0, -diag, -np.inf)
    diag_decay = np.exp(diag).astype(np.float32)
    block_decay = np.exp(-slope * LBLK).astype(np.float32)[:, 0, 0]
    return q_decay, k_decay, diag_decay, block_decay


C_EPS, C_INV_D, C_ONE, C_ZERO = 0, 1, 2, 3
NCONST = 8

def _consts_host():
    c = np.zeros(NCONST, np.float32)
    c[C_EPS] = 1e-5
    c[C_INV_D] = 1.0 / D
    c[C_ONE] = 1.0
    c[C_ZERO] = 0.0
    return np.tile(c[None, :], (128, 1))



class Ch:
    __slots__ = ("sem", "n")

    def __init__(self, sem):
        self.sem = sem
        self.n = 0


class Prog:

    ENGINES = ("gpsimd", "sync", "tensor", "vector", "scalar")

    def __init__(self, nc, ctxstack):
        self.nc = nc
        self.events = []
        self.chans = []
        self._stack = ctxstack

    def ch(self, name):
        sem = self._stack.enter_context(self.nc.semaphore(name))
        c = Ch(sem)
        self.chans.append(c)
        return c

    def on(self, eng_name, fn):
        assert eng_name in self.ENGINES
        self.events.append((eng_name, fn))

    def emit(self, block):
        for name in self.ENGINES:
            deco = getattr(block, name)

            def body(eng, name=name):
                for c in self.chans:
                    c.n = 0
                for (e, fn) in self.events:
                    fn(eng if e == name else None)

            deco(body)


def sig(t, inst, ch, k=1):
    ch.n += k
    if t is not None and inst is not None:
        inst.then_inc(ch.sem, k)


def sigi(t, ch, k=1):
    ch.n += k
    if t is not None:
        t.sem_inc(ch.sem, k)


def sigd(t, ch, k=1):
    ch.n += k
    if t is not None:
        t.drain()
        t.sem_inc(ch.sem, k)


def wt(t, ch, lag=0):
    if t is not None:
        v = ch.n - lag
        if v > 0:
            t.wait_ge(ch.sem, v)


def _sb(stack, nc, name, shape, dtype):
    return stack.enter_context(nc.sbuf_tensor(name, shape, dtype))


def _ps(stack, nc, name, shape, dtype):
    return stack.enter_context(nc.psum_tensor(name, shape, dtype))



def emit_a(P, c):

    def ev_init(t):
        i = t and t.dma_start(out=c.cst[:], in_=c.consts)
        sig(t, i, c.cInit, 16)
        i = t and t.dma_start(out=c.identb[:], in_=c.identf)
        sig(t, i, c.cInit, 16)
        i = t and t.dma_start(out=c.kdS[:], in_=c.kdall)
        sig(t, i, c.cInit, 16)
        i = t and t.dma_start(out=c.bdS[:], in_=c.bdall)
        sig(t, i, c.cInit, 16)
        if getattr(c, "fused", False):
            i = t and t.dma_start(out=c.gmS[:], in_=c.gmask)
            sig(t, i, c.cInit, 16)
    P.on("gpsimd", ev_init)

    def ev_init_v(t):
        wt(t, c.cInit)
        t and t.memset(c.kvs[:], 0.0)
    P.on("vector", ev_init_v)

    def ev_init_s(t):
        wt(t, c.cInit)
    P.on("scalar", ev_init_s)

    def ev_init_t(t):
        wt(t, c.cInit)
    P.on("tensor", ev_init_t)

    for sh in range(2):
        _emit_a_stage1(P, c, sh)
        if DEBUG_DUMP and sh == 0:
            def ev_dbg(t):
                wt(t, c.cH0)
                for j in range(8):
                    i = t and t.dma_start(out=c.dbg[3, :, j * 128:(j + 1) * 128],
                                          in_=c.h0[:, j * 1024: j * 1024 + 128])
                    sig(t, i, c.cOut, 16)
            P.on("sync", ev_dbg)
        _emit_a_heads(P, c, sh)
        if DEBUG_DUMP and sh == 0:
            def ev_dbg2(t):
                wt(t, c.cQKV)
                for j, buf in enumerate((c.qf, c.kf, c.vf)):
                    i = t and t.dma_start(out=c.dbg[j], in_=buf[:])
                    sig(t, i, c.cOut, 16)
            P.on("sync", ev_dbg2)

    def ev_done(t):
        wt(t, c.cOut)
    P.on("sync", ev_done)




def build_phase_a():
    import contextlib
    nc = bass.Bass("TRN2", target_bir_lowering=False)
    c = types.SimpleNamespace()
    c.xb = nc.dram_tensor("xb", [S, D], F32, kind="ExternalInput").ap()
    c.qkvT = nc.dram_tensor("qkvT", [D, HL * 384], F32, kind="ExternalInput").ap()
    c.dcb = nc.dram_tensor("dcb", [HL, 128, 768], F32, kind="ExternalInput").ap()
    c.kdall = nc.dram_tensor("kdall", [128, 2 * HL], F32, kind="ExternalInput").ap()
    c.bdall = nc.dram_tensor("bdall", [128, HL], F32, kind="ExternalInput").ap()
    c.identf = nc.dram_tensor("identf", [128, 128], F32, kind="ExternalInput").ap()
    c.consts = nc.dram_tensor("consts", [128, NCONST], F32, kind="ExternalInput").ap()
    c.attn_out = nc.dram_tensor("attn_fm", [HL, 128, S], BF16, kind="ExternalOutput").ap()
    if DEBUG_DUMP:
        c.dbg = nc.dram_tensor("dbg", [4, 128, 1024], BF16, kind="ExternalOutput").ap()

    with contextlib.ExitStack() as st:
        sb, ps = (lambda *a: _sb(st, nc, *a)), (lambda *a: _ps(st, nc, *a))
        c.cst = sb("cst", [128, NCONST], F32)
        c.identb = sb("identb", [128, 128], BF16)
        c.kdS = sb("kdS", [128, 2 * HL], F32)
        c.bdS = sb("bdS", [128, HL], F32)
        c.h0 = sb("h0", [128, 32 * 1024], BF16)
        c.xt = sb("xt", [128, D], F32)
        c.scr4k = sb("scr4k", [128, D], F32)
        c.lnst = sb("lnst", [128, 8], F32)
        c.ht = sb("ht", [128, D], BF16)
        c.wbuf = sb("wbuf", [128, 2 * 12288], BF16)
        c.cbuf = sb("cbuf", [128, 2 * 768], BF16)
        c.qf = sb("qf", [128, 1024], BF16)
        c.kf = sb("kf", [128, 1024], BF16)
        c.vf = sb("vf", [128, 1024], BF16)
        c.stg = sb("stg", [128, 2 * 1024], BF16)
        c.STb = sb("STb", [128, 2 * 256], BF16)
        c.ckkd = sb("ckkd", [128, 2 * 128], BF16)
        c.cvt = sb("cvt", [128, 2 * 128], BF16)
        c.cqd = sb("cqd", [128, 256], BF16)
        c.kvs = sb("kvs", [128, HL * 128], F32)
        c.kvb = sb("kvb", [128, HL * 128], BF16)
        c.pS0 = ps("pS0", [128, 512], F32)
        c.pS1 = ps("pS1", [128, 512], F32)
        c.pA = ps("pA", [128, 512], F32)
        c.pU = ps("pU", [128, 512], F32)
        c.pT0 = ps("pT0", [128, 1024], BF16)
        c.pT1 = ps("pT1", [128, 1024], BF16)
        c.pV0 = ps("pV0", [128, 1024], BF16)
        c.pV1 = ps("pV1", [128, 1024], BF16)
        block = st.enter_context(nc.Block())

        P = Prog(nc, st)
        for nm in ("cInit cX cXf cSq cStat cSqrt cHt pT0p pT0f pT1p pT1f pV0p pV0f "
                   "pV1p pV1f cH0 cH0f cW cWf cC cCf pS0p pS0f pS1p pS1f pAp pAf "
                   "pUp pUf cQKV cST cCK cCV cCQD cKV cStgV cOut").split():
            setattr(c, nm, P.ch(nm))
        emit_a(P, c)
        P.emit(block)
    return nc


def _emit_a_stage1(P, c, sh):
    for tile in range(8):
        trow = sh * 1024 + tile * 128

        def ev_ldx(t, trow=trow):
            wt(t, c.cXf)
            i = t and t.dma_start(out=c.xt[:], in_=c.xb[trow:trow + 128, :])
            sig(t, i, c.cX, 16)
        P.on("sync", ev_ldx)

        def ev_sq(t):
            wt(t, c.cX)
            i = t and t.activation(c.scr4k[:], c.xt[:], AF.Square,
                                   accum_out=c.lnst[:, 1:2])
            sig(t, i, c.cSq)
        P.on("scalar", ev_sq)

        def ev_stat(t):
            wt(t, c.cSq)
            if t:
                t.reduce_sum(c.lnst[:, 0:1], c.xt[:], axis=AX.X)
                t.drain()
                t.tensor_scalar(c.lnst[:, 2:3], c.lnst[:, 0:1],
                                c.cst[:, C_INV_D:C_INV_D + 1], None, ALU.mult)
                t.tensor_scalar(c.lnst[:, 3:4], c.lnst[:, 1:2],
                                c.cst[:, C_INV_D:C_INV_D + 1], None, ALU.mult)
                t.drain()
                t.tensor_tensor(c.lnst[:, 4:5], c.lnst[:, 2:3], c.lnst[:, 2:3], ALU.mult)
                t.drain()
                i = t.tensor_tensor(c.lnst[:, 4:5], c.lnst[:, 3:4], c.lnst[:, 4:5], ALU.subtract)
            else:
                i = None
            sig(t, i, c.cStat)
        P.on("vector", ev_stat)

        def ev_sqrt(t):
            wt(t, c.cStat)
            i = t and t.activation(c.lnst[:, 5:6], c.lnst[:, 4:5], AF.Sqrt,
                                   bias=c.cst[:, C_EPS:C_EPS + 1])
            sig(t, i, c.cSqrt)
        P.on("scalar", ev_sqrt)

        def ev_apply(t):
            wt(t, c.cSqrt)
            wt(t, c.pT0p)
            wt(t, c.pT1p)
            if t:
                t.reciprocal(c.lnst[:, 6:7], c.lnst[:, 5:6])
                t.drain()
                i = t.tensor_scalar(c.ht[:], c.xt[:], c.lnst[:, 2:3], c.lnst[:, 6:7],
                                    ALU.subtract, ALU.mult)
            else:
                i = None
            sig(t, i, c.cHt)
            sigd(t, c.cXf)
        P.on("vector", ev_apply)

        for cp in range(16):
            def ev_tr(t, cp=cp, sh=sh):
                wt(t, c.cHt)
                wt(t, c.cH0)
                if sh == 1 and cp == 0:
                    wt(t, c.cH0f)
                if t:
                    t.transpose(c.pT0[:, 0:128],
                                c.ht[:, (2 * cp) * 128:(2 * cp + 1) * 128], c.identb[:])
                    t.transpose(c.pT1[:, 0:128],
                                c.ht[:, (2 * cp + 1) * 128:(2 * cp + 2) * 128], c.identb[:])
                    t.drain()
                sigi(t, c.pT0p)
                sigi(t, c.pT1p)
            P.on("tensor", ev_tr)

            eng = "vector" if cp % 2 == 0 else "scalar"

            def ev_ev(t, cp=cp, tile=tile, eng=eng):
                wt(t, c.pT0p)
                wt(t, c.pT1p)
                i = None
                if t:
                    for j, bank in ((2 * cp, c.pT0), (2 * cp + 1, c.pT1)):
                        dst = c.h0[:, j * 1024 + tile * 128: j * 1024 + tile * 128 + 128]
                        i = (t.tensor_copy(dst, bank[:, 0:128]) if eng == "vector"
                             else t.copy(dst, bank[:, 0:128]))
                sig(t, i, c.cH0, 2)
            P.on(eng, ev_ev)


def _emit_a_heads(P, c, sh):
    for h in range(HL):
        par = h % 2
        H = sh * HL + h

        def ev_w(t, h=h, par=par):
            wt(t, c.cWf, lag=1)
            for kg in range(8):
                src = c.qkvT[kg * 512:(kg + 1) * 512, h * 384:(h + 1) * 384]
                src = src.rearrange("(cc p) m -> p cc m", p=128)
                dst = c.wbuf[:, par * 12288 + kg * 1536:par * 12288 + (kg + 1) * 1536]
                dst = dst.rearrange("p (cc m) -> p cc m", cc=4)
                i = t and t.dma_start(out=dst, in_=src)
                sig(t, i, c.cW, 16)
            wt(t, c.cCf, lag=1)
            i = t and t.dma_start(out=c.cbuf[:, par * 768:(par + 1) * 768], in_=c.dcb[h])
            sig(t, i, c.cC, 16)
        P.on("gpsimd", ev_w)

        for m in range(3):
            for n in range(2):
                bank, bp, bf = ((c.pS0, c.pS0p, c.pS0f), (c.pS1, c.pS1p, c.pS1f),
                                (c.pA, c.pAp, c.pAf))[(m * 2 + n) % 3]

                def ev_mm(t, h=h, m=m, n=n, par=par, bank=bank, bp=bp, bf=bf):
                    wt(t, c.cW)
                    wt(t, c.cH0)
                    wt(t, c.cST)
                    wt(t, c.cStgV)
                    wt(t, bf)
                    i = None
                    if t:
                        for kc in range(32):
                            lhs = c.wbuf[:, par * 12288 + (kc // 4) * 1536 + (kc % 4) * 384
                                         + m * 128: par * 12288 + (kc // 4) * 1536
                                         + (kc % 4) * 384 + m * 128 + 128]
                            rhs = c.h0[:, kc * 1024 + n * 512: kc * 1024 + n * 512 + 512]
                            i = t.matmul(bank[:, 0:512], lhs, rhs,
                                         start=(kc == 0), stop=(kc == 31))
                    sig(t, i, bp)
                P.on("tensor", ev_mm)

                def ev_evac(t, m=m, n=n, bank=bank, bp=bp, bf=bf):
                    wt(t, bp)
                    dstf = (c.qf, c.kf, c.vf)[m]
                    i = t and t.activation(dstf[:, n * 512:(n + 1) * 512],
                                           bank[:, 0:512], AF.Silu)
                    sig(t, i, c.cQKV)
                    sigd(t, bf)
                    if m == 2 and n == 1:
                        sigi(t, c.cH0f)
                        sigi(t, c.cWf)
                P.on("scalar", ev_evac)

        for blk in range(NB_SH):
            glob = sh * NB_SH + blk
            ts0 = blk * 256

            def ev_st_mm(t, ts0=ts0):
                wt(t, c.cQKV)
                wt(t, c.cST)
                i0 = t and t.matmul(c.pS0[:, 0:256], c.kf[:, ts0:ts0 + 128],
                                    c.qf[:, ts0:ts0 + 256], start=True, stop=True)
                sig(t, i0, c.pS0p)
                i1 = t and t.matmul(c.pS1[:, 0:256], c.kf[:, ts0 + 128:ts0 + 256],
                                    c.qf[:, ts0:ts0 + 256], start=True, stop=True)
                sig(t, i1, c.pS1p)
            P.on("tensor", ev_st_mm)

            def ev_st_ev(t, par=par):
                wt(t, c.pS0p)
                wt(t, c.pS1p)
                wt(t, c.cC)
                wt(t, c.pAp)
                i0 = t and t.tensor_tensor(c.STb[:, 0:256], c.pS0[:, 0:256],
                                           c.cbuf[:, par * 768:par * 768 + 256], ALU.mult)
                i1 = t and t.tensor_tensor(c.STb[:, 256:512], c.pS1[:, 0:256],
                                           c.cbuf[:, par * 768 + 256:par * 768 + 512], ALU.mult)
                sig(t, i1, c.cST)
            P.on("vector", ev_st_ev)

            def ev_tp(t, ts0=ts0):
                wt(t, c.cCV)
                wt(t, c.cCK)
                if t:
                    t.transpose(c.pT0[:, 0:128], c.kf[:, ts0:ts0 + 128], c.identb[:])
                    t.transpose(c.pT1[:, 0:128], c.kf[:, ts0 + 128:ts0 + 256], c.identb[:])
                    t.transpose(c.pV0[:, 0:128], c.vf[:, ts0:ts0 + 128], c.identb[:])
                    t.transpose(c.pV1[:, 0:128], c.vf[:, ts0 + 128:ts0 + 256], c.identb[:])
                    t.drain()
                sigi(t, c.pT0p)
                sigi(t, c.pT1p)
                sigi(t, c.pV0p)
                sigi(t, c.pV1p)
            P.on("tensor", ev_tp)

            def ev_tp_ev(t, h=h):
                wt(t, c.pT0p)
                wt(t, c.pT1p)
                wt(t, c.pV0p)
                wt(t, c.pV1p)
                wt(t, c.pUp)
                wt(t, c.pAp)
                i0 = t and t.activation(c.ckkd[:, 0:128], c.pT0[:, 0:128], AF.Copy,
                                        scale=c.kdS[:, 2 * h:2 * h + 1])
                i1 = t and t.activation(c.ckkd[:, 128:256], c.pT1[:, 0:128], AF.Copy,
                                        scale=c.kdS[:, 2 * h + 1:2 * h + 2])
                sig(t, i1, c.cCK)
                i2 = t and t.activation(c.cvt[:, 0:128], c.pV0[:, 0:128], AF.Copy)
                i3 = t and t.activation(c.cvt[:, 128:256], c.pV1[:, 0:128], AF.Copy)
                sig(t, i3, c.cCV)
            P.on("scalar", ev_tp_ev)

            if glob > 0:
                def ev_cqd(t, ts0=ts0, par=par, last=(blk == NB_SH - 1)):
                    wt(t, c.cQKV)
                    wt(t, c.cC)
                    wt(t, c.pAp)
                    i = t and t.tensor_tensor(c.cqd[:], c.qf[:, ts0:ts0 + 256],
                                              c.cbuf[:, par * 768 + 512:par * 768 + 768],
                                              ALU.mult)
                    sig(t, i, c.cCQD)
                    if last:
                        sigi(t, c.cCf)
                P.on("vector", ev_cqd)
            elif blk == NB_SH - 1:
                def ev_cf(t):
                    wt(t, c.cST)
                    sig(t, None, c.cCf)
                P.on("vector", ev_cf)

            def ev_attn(t, h=h, glob=glob):
                wt(t, c.cST)
                wt(t, c.cCV)
                if glob > 0:
                    wt(t, c.cCQD)
                    wt(t, c.cKV)
                wt(t, c.pAf)
                i = None
                if t:
                    t.matmul(c.pA[:, 0:256], c.cvt[:, 0:128], c.STb[:, 0:256],
                             start=True, stop=False)
                    i = t.matmul(c.pA[:, 0:256], c.cvt[:, 128:256], c.STb[:, 256:512],
                                 start=False, stop=(glob == 0))
                    if glob > 0:
                        i = t.matmul(c.pA[:, 0:256], c.kvb[:, h * 128:(h + 1) * 128],
                                     c.cqd[:], start=False, stop=True)
                sig(t, i, c.pAp)
            P.on("tensor", ev_attn)

            def ev_attn_ev(t, blk=blk, par=par, H=H):
                wt(t, c.pAp)
                if blk == 0 and H >= 2:
                    wt(t, c.cOut, lag=16)
                i = t and t.tensor_copy(
                    c.stg[:, par * 1024 + blk * 256: par * 1024 + blk * 256 + 256],
                    c.pA[:, 0:256])
                sig(t, i, c.cStgV)
                sigd(t, c.pAf)
            P.on("vector", ev_attn_ev)

            if glob < 7:
                def ev_upd(t):
                    wt(t, c.cCK)
                    wt(t, c.cCV)
                    wt(t, c.pUf)
                    i = None
                    if t:
                        t.matmul(c.pU[:, 0:128], c.ckkd[:, 0:128], c.cvt[:, 0:128],
                                 start=True, stop=False)
                        i = t.matmul(c.pU[:, 0:128], c.ckkd[:, 128:256],
                                     c.cvt[:, 128:256], start=False, stop=True)
                    sig(t, i, c.pUp)
                P.on("tensor", ev_upd)

                def ev_upd_ev(t, h=h):
                    wt(t, c.pUp)
                    sl = c.kvs[:, h * 128:(h + 1) * 128]
                    i = None
                    if t:
                        t.scalar_tensor_tensor(sl, sl, c.bdS[:, h:h + 1],
                                               c.pU[:, 0:128], ALU.mult, ALU.add)
                        i = t.tensor_copy(c.kvb[:, h * 128:(h + 1) * 128], sl)
                    sig(t, i, c.cKV)
                    sigd(t, c.pUf)
                P.on("vector", ev_upd_ev)

        if getattr(c, "fused", False):
            def ev_mask(t, h=h, sh=sh, par=par):
                wt(t, c.cStgV)
                wt(t, c.cOut)
                if t:
                    t.tensor_scalar(c.stgM[:, 0:1024], c.stg[:, par * 1024:(par + 1) * 1024],
                                    c.gmS[:, 0:1], None, ALU.mult)
                    i = t.tensor_scalar(c.stgM[:, 1024:2048],
                                        c.stg[:, par * 1024:(par + 1) * 1024],
                                        c.gmS[:, 1:2], None, ALU.mult)
                else:
                    i = None
                sig(t, i, c.cStgM)
            P.on("vector", ev_mask)

            def ev_stg_out(t, h=h, sh=sh):
                wt(t, c.cStgM)
                base = sh * 4096 + h * 128
                i = t and t.dma_start(out=c.rsin[base:base + 128, :],
                                      in_=c.stgM[:, 0:1024])
                sig(t, i, c.cOut, 16)
                i = t and t.dma_start(out=c.rsin[base + 2048:base + 2048 + 128, :],
                                      in_=c.stgM[:, 1024:2048])
                sig(t, i, c.cOut, 16)
            P.on("sync", ev_stg_out)

            if FUSED_DEBUG and sh == 0 and h == 0:
                def ev_dbg_h0(t):
                    wt(t, c.cOut)
                    i = t and t.dma_start(out=c.dbg4[0:128, :], in_=c.rsin[0:128, :])
                    sig(t, i, c.cOut, 16)
                P.on("sync", ev_dbg_h0)
        else:
            def ev_stg_out(t, h=h, sh=sh, par=par):
                wt(t, c.cStgV)
                dst = c.attn_out[h, :, sh * 1024:(sh + 1) * 1024]
                i = t and t.dma_start(out=dst, in_=c.stg[:, par * 1024:(par + 1) * 1024])
                sig(t, i, c.cOut, 16)
            P.on("sync", ev_stg_out)



def _prep_a(inputs):
    x = inputs["L0_attn_hidden_states"]
    qkv_w = inputs["L0_attn_qkv_w"]
    qd, kd, dd, bd = _decays()
    consts = _consts_host()
    ident = np.eye(128, dtype=np.float32)
    qkvT_full = np.ascontiguousarray(qkv_w.T)
    in_maps = []
    for core in range(NCORES):
        b, g = core // 2, core % 2
        heads = range(16 * g, 16 * g + 16)
        qkvT = np.ascontiguousarray(qkvT_full[:, 384 * 16 * g: 384 * 16 * (g + 1)])
        dcb = np.zeros((HL, 128, 768), np.float32)
        kdall = np.zeros((128, 2 * HL), np.float32)
        bdall = np.zeros((128, HL), np.float32)
        for i, hh in enumerate(heads):
            dcb[i, :, 0:256] = dd[hh][:, 0:128].T
            dcb[i, :, 256:512] = dd[hh][:, 128:256].T
            dcb[i, :, 512:768] = np.tile(qd[hh][None, :], (128, 1))
            kdall[:, 2 * i] = kd[hh][0:128]
            kdall[:, 2 * i + 1] = kd[hh][128:256]
            bdall[:, i] = bd[hh]
        in_maps.append({
            "xb": np.ascontiguousarray(x[b]),
            "qkvT": qkvT,
            "dcb": dcb,
            "kdall": kdall,
            "bdall": bdall,
            "identf": ident,
            "consts": consts,
        })
    return in_maps


def run_phase_a(inputs):
    nc = build_phase_a()
    in_maps = _prep_a(inputs)
    res = run_bass_kernel_spmd(nc, in_maps, core_ids=list(range(NCORES)))
    return [r["attn_fm"] for r in res.results]


def kernel(**inputs):
    return kernel_fused(inputs)


def kernel_unfused(**inputs):
    a = run_phase_a(inputs)
    xb1 = run_phase_b(inputs, a)
    parts = run_phase_ca(xb1)
    xc = run_phase_cb(inputs, xb1, parts)
    xd = run_phase_d(inputs, xc)
    xe = run_phase_e(inputs, xd)
    out = np.zeros((B, S, D), np.float32)
    for core in range(NCORES):
        bb, g = core // 2, core % 2
        out[bb, g * TOK:(g + 1) * TOK] = np.asarray(xe[core])
    return out



def emit_ln_fm(P, c, src_rows_ap, ntiles, dst_h, extras=None):
    TT = ntiles * 128
    xts = (c.xt, getattr(c, "xt2", c.xt))
    dbuf = xts[0] is not xts[1]
    for tile in range(ntiles):
        xb_ = xts[tile % 2]

        def ev_ldx(t, tile=tile, xb_=xb_):
            wt(t, c.cXf, lag=(1 if dbuf else 0))
            i = t and t.dma_start(out=xb_[:, 0:D], in_=src_rows_ap[tile * 128:(tile + 1) * 128, :])
            sig(t, i, c.cX, 16)
        P.on("sync", ev_ldx)

        def ev_sq(t, xb_=xb_):
            wt(t, c.cX)
            i = t and t.activation(c.scr4k[:], xb_[:, 0:D], AF.Square,
                                   accum_out=c.lnst[:, 1:2])
            sig(t, i, c.cSq)
        P.on("scalar", ev_sq)

        def ev_stat(t, xb_=xb_):
            wt(t, c.cSq)
            if t:
                t.reduce_sum(c.lnst[:, 0:1], xb_[:, 0:D], axis=AX.X)
                t.drain()
                t.tensor_scalar(c.lnst[:, 2:3], c.lnst[:, 0:1],
                                c.cst[:, C_INV_D:C_INV_D + 1], None, ALU.mult)
                t.tensor_scalar(c.lnst[:, 3:4], c.lnst[:, 1:2],
                                c.cst[:, C_INV_D:C_INV_D + 1], None, ALU.mult)
                t.drain()
                t.tensor_tensor(c.lnst[:, 4:5], c.lnst[:, 2:3], c.lnst[:, 2:3], ALU.mult)
                t.drain()
                i = t.tensor_tensor(c.lnst[:, 4:5], c.lnst[:, 3:4], c.lnst[:, 4:5], ALU.subtract)
            else:
                i = None
            sig(t, i, c.cStat)
        P.on("vector", ev_stat)

        def ev_sqrt(t):
            wt(t, c.cStat)
            i = t and t.activation(c.lnst[:, 5:6], c.lnst[:, 4:5], AF.Sqrt,
                                   bias=c.cst[:, C_EPS:C_EPS + 1])
            sig(t, i, c.cSqrt)
        P.on("scalar", ev_sqrt)

        def ev_apply(t, xb_=xb_):
            wt(t, c.cSqrt)
            wt(t, c.pT0p)
            wt(t, c.pT1p)
            if t:
                t.reciprocal(c.lnst[:, 6:7], c.lnst[:, 5:6])
                t.drain()
                i = t.tensor_scalar(c.ht[:], xb_[:, 0:D], c.lnst[:, 2:3], c.lnst[:, 6:7],
                                    ALU.subtract, ALU.mult)
            else:
                i = None
            sig(t, i, c.cHt)
            sigd(t, c.cXf)
        P.on("vector", ev_apply)

        if extras is not None:
            extras(tile)

        for cp in range(16):
            def ev_tr(t, cp=cp):
                wt(t, c.cHt)
                wt(t, c.cH0)
                if t:
                    t.transpose(c.pT0[:, 0:128],
                                c.ht[:, (2 * cp) * 128:(2 * cp + 1) * 128], c.identb[:])
                    t.transpose(c.pT1[:, 0:128],
                                c.ht[:, (2 * cp + 1) * 128:(2 * cp + 2) * 128], c.identb[:])
                    t.drain()
                sigi(t, c.pT0p)
                sigi(t, c.pT1p)
            P.on("tensor", ev_tr)

            eng = "vector" if cp % 2 == 0 else "scalar"

            def ev_ev(t, cp=cp, tile=tile, eng=eng):
                wt(t, c.pT0p)
                wt(t, c.pT1p)
                i = None
                if t:
                    for j, bank in ((2 * cp, c.pT0), (2 * cp + 1, c.pT1)):
                        dst = dst_h[:, j * TT + tile * 128: j * TT + tile * 128 + 128]
                        i = (t.tensor_copy(dst, bank[:, 0:128]) if eng == "vector"
                             else t.copy(dst, bank[:, 0:128]))
                sig(t, i, c.cH0, 2)
            P.on(eng, ev_ev)



def emit_b(P, c):

    ht_alias = c.regB
    c.ht = ht_alias[:, 0:4096]

    def ev_init(t):
        i = t and t.dma_start(out=c.cst[:], in_=c.consts)
        sig(t, i, c.cInit, 16)
        i = t and t.dma_start(out=c.identb[:], in_=c.identf)
        sig(t, i, c.cInit, 16)
        i = t and t.dma_start(out=c.normwS[:], in_=c.normw)
        sig(t, i, c.cInit, 16)
    P.on("gpsimd", ev_init)

    def ev_init_v(t):
        wt(t, c.cInit)
        t and t.memset(c.onesB[:], 1.0)
    P.on("vector", ev_init_v)
    for e in ("scalar", "tensor", "sync"):
        def ev_i(t):
            wt(t, c.cInit)
        P.on(e, ev_i)

    emit_ln_fm(P, c, c.xown, 8, c.regA)

    for j in range(32):
        par = j % 2

        def ev_gw(t, j=j, par=par):
            wt(t, c.cWf, lag=1)
            if j < 2 and getattr(c, "fused", False):
                wt(t, c.cH0)
            src = c.gateTt[j].rearrange("(kc p) m -> p kc m", p=128)
            dst = c.wb[:, par * 4096:(par + 1) * 4096].rearrange("p (kc m) -> p kc m", m=128)
            i = t and t.dma_start(out=dst, in_=src)
            sig(t, i, c.cW, 16)
        P.on("gpsimd", ev_gw)

        for n in range(2):
            bank, bp, bf = ((c.pS0, c.pS0p, c.pS0f), (c.pS1, c.pS1p, c.pS1f),
                            (c.pA, c.pAp, c.pAf))[(j * 2 + n) % 3]

            def ev_gmm(t, j=j, n=n, par=par, bank=bank, bp=bp, bf=bf):
                wt(t, c.cW)
                wt(t, c.cH0)
                wt(t, bf)
                i = None
                if t:
                    for kc in range(32):
                        lhs = c.wb[:, par * 4096 + kc * 128:(par * 4096 + kc * 128) + 128]
                        rhs = c.regA[:, kc * TOK + n * 512: kc * TOK + n * 512 + 512]
                        i = t.matmul(bank[:, 0:512], lhs, rhs,
                                     start=(kc == 0), stop=(kc == 31))
                sig(t, i, bp)
            P.on("tensor", ev_gmm)

            def ev_gev(t, j=j, n=n, bank=bank, bp=bp, bf=bf):
                wt(t, bp)
                i = t and t.activation(c.regB[:, j * TOK + n * 512: j * TOK + n * 512 + 512],
                                       bank[:, 0:512], AF.Sigmoid)
                sig(t, i, c.cGate)
                sigd(t, bf)
                if n == 1:
                    sigi(t, c.cWf)
            P.on("scalar", ev_gev)

    for ch_ in range(32):
        par = ch_ % 2

        def ev_la(t, ch_=ch_, par=par):
            wt(t, c.cAf, lag=1)
            if getattr(c, "fused", False):
                wt(t, c.cAG)
                wt(t, c.cH0)
            i = t and t.dma_start(out=c.abuf[:, par * TOK:(par + 1) * TOK],
                                  in_=c.afm[ch_ * 128:(ch_ + 1) * 128, :])
            sig(t, i, c.cA, 16)
        P.on("sync", ev_la)

        def ev_sqa(t, ch_=ch_, par=par):
            wt(t, c.cA)
            wt(t, c.pUp)
            i = t and t.activation(c.sqb[:], c.abuf[:, par * TOK:(par + 1) * TOK],
                                   AF.Square)
            sig(t, i, c.cSqB)
            sigd(t, c.cAf)
        P.on("scalar", ev_sqa)

        def ev_smm(t, ch_=ch_):
            wt(t, c.cSqB)
            if ch_ == 0:
                wt(t, c.pUf)
                wt(t, c.pRf)
            i = None
            if t:
                t.matmul(c.pU[0:1, 0:512], c.onesB[:, 0:1], c.sqb[:, 0:512],
                         start=(ch_ == 0), stop=(ch_ == 31))
                i = t.matmul(c.pR[0:1, 0:512], c.onesB[:, 0:1], c.sqb[:, 512:1024],
                             start=(ch_ == 0), stop=(ch_ == 31))
            sig(t, i, c.pUp)
        P.on("tensor", ev_smm)

    def ev_rstat(t):
        wt(t, c.pUp)
        if t:
            t.tensor_copy(c.rrow[0:1, 0:512], c.pU[0:1, 0:512])
            i = t.tensor_copy(c.rrow[0:1, 512:1024], c.pR[0:1, 0:512])
        else:
            i = None
        sig(t, i, c.cRr)
        sigd(t, c.pUf)
        sigi(t, c.pRf)
    P.on("vector", ev_rstat)

    def ev_rsqrt(t):
        wt(t, c.cRr)
        i = t and t.activation(c.rrow2[:, :], c.rrow[:, :], AF.Sqrt,
                               bias=c.cst[0:1, C_EPS:C_EPS + 1],
                               scale=c.cst[0:1, C_INV_D:C_INV_D + 1])
        sig(t, i, c.cRstd)
    P.on("scalar", ev_rsqrt)

    def ev_rrec(t):
        wt(t, c.cRstd)
        if t:
            t.reciprocal(c.rrow[:, :], c.rrow2[:, :])
            t.drain()
            i = t.tensor_copy(c.rrowb[:, :], c.rrow[:, :])
        else:
            i = None
        sig(t, i, c.cRb)
    P.on("vector", ev_rrec)

    def ev_rbc(t):
        wt(t, c.cRb)
        wt(t, c.pS0f)
        wt(t, c.pS1f)
        i = None
        if t:
            t.matmul(c.pS0[:, 0:512], c.onesB[0:1, 0:128], c.rrowb[:, 0:512],
                     start=True, stop=True)
            i = t.matmul(c.pS1[:, 0:512], c.onesB[0:1, 0:128], c.rrowb[:, 512:1024],
                         start=True, stop=True)
        sig(t, i, c.pS0p)
        sigi(t, c.pS1p)
    P.on("tensor", ev_rbc)

    def ev_rbev(t):
        wt(t, c.pS0p)
        wt(t, c.pS1p)
        if t:
            t.tensor_copy(c.rstdb[:, 0:512], c.pS0[:, 0:512])
            i = t.tensor_copy(c.rstdb[:, 512:1024], c.pS1[:, 0:512])
        else:
            i = None
        sig(t, i, c.cRbb)
        sigd(t, c.pS0f)
        sigi(t, c.pS1f)
    P.on("vector", ev_rbev)

    for ch_ in range(32):
        par = ch_ % 2

        def ev_la2(t, ch_=ch_, par=par):
            wt(t, c.cAf, lag=1)
            if getattr(c, "fused", False):
                wt(t, c.cAG)
            i = t and t.dma_start(out=c.abuf[:, par * TOK:(par + 1) * TOK],
                                  in_=c.afm[ch_ * 128:(ch_ + 1) * 128, :])
            sig(t, i, c.cA, 16)
        P.on("sync", ev_la2)

        def ev_prod(t, ch_=ch_, par=par):
            wt(t, c.cA)
            wt(t, c.cGate)
            wt(t, c.cRbb)
            if t:
                t.scalar_tensor_tensor(c.sqb[:], c.abuf[:, par * TOK:(par + 1) * TOK],
                                       c.normwS[:, ch_:ch_ + 1],
                                       c.regB[:, ch_ * TOK:(ch_ + 1) * TOK],
                                       ALU.mult, ALU.mult)
                i = t.tensor_tensor(c.regA[:, ch_ * TOK:(ch_ + 1) * TOK],
                                    c.sqb[:], c.rstdb[:], ALU.mult)
            else:
                i = None
            sig(t, i, c.cProd)
            sigd(t, c.cAf)
        P.on("vector", ev_prod)

    for j in range(32):
        par = j % 2

        def ev_ow(t, j=j, par=par):
            wt(t, c.cWf, lag=1)
            src = c.outTt[j].rearrange("(kc p) m -> p kc m", p=128)
            dst = c.wb[:, par * 4096:(par + 1) * 4096].rearrange("p (kc m) -> p kc m", m=128)
            i = t and t.dma_start(out=dst, in_=src)
            sig(t, i, c.cW, 16)
        P.on("gpsimd", ev_ow)

        for n in range(2):
            bank, bp, bf = ((c.pS0, c.pS0p, c.pS0f), (c.pS1, c.pS1p, c.pS1f),
                            (c.pA, c.pAp, c.pAf))[(j * 2 + n) % 3]

            def ev_omm(t, j=j, n=n, par=par, bank=bank, bp=bp, bf=bf):
                wt(t, c.cW)
                wt(t, c.cProd)
                wt(t, bf)
                i = None
                if t:
                    for kc in range(32):
                        lhs = c.wb[:, par * 4096 + kc * 128: par * 4096 + kc * 128 + 128]
                        rhs = c.regA[:, kc * TOK + n * 512: kc * TOK + n * 512 + 512]
                        i = t.matmul(bank[:, 0:512], lhs, rhs,
                                     start=(kc == 0), stop=(kc == 31))
                sig(t, i, bp)
            P.on("tensor", ev_omm)

            def ev_oev(t, j=j, n=n, bank=bank, bp=bp, bf=bf):
                wt(t, bp)
                wt(t, c.cProd)
                i = t and t.tensor_copy(
                    c.regB[:, j * TOK + n * 512: j * TOK + n * 512 + 512],
                    bank[:, 0:512])
                sig(t, i, c.cOT)
                sigd(t, bf)
                if n == 1:
                    sigi(t, c.cWf)
            P.on("vector", ev_oev)

    _emit_b5(P, c)

    def ev_done(t):
        wt(t, c.cOut)
    P.on("sync", ev_done)




def build_phase_b():
    import contextlib
    nc = bass.Bass("TRN2", target_bir_lowering=False)
    c = types.SimpleNamespace()
    c.xown = nc.dram_tensor("xown", [TOK, D], F32, kind="ExternalInput").ap()
    c.afm = nc.dram_tensor("afm", [D, TOK], BF16, kind="ExternalInput").ap()
    c.gateTt = nc.dram_tensor("gateTt", [32, D, 128], F32, kind="ExternalInput").ap()
    c.outTt = nc.dram_tensor("outTt", [32, D, 128], F32, kind="ExternalInput").ap()
    c.normw = nc.dram_tensor("normw", [128, 32], F32, kind="ExternalInput").ap()
    c.consts = nc.dram_tensor("consts", [128, NCONST], F32, kind="ExternalInput").ap()
    c.identf = nc.dram_tensor("identf", [128, 128], F32, kind="ExternalInput").ap()
    c.x1 = nc.dram_tensor("x1", [TOK, D], F32, kind="ExternalOutput").ap()

    with contextlib.ExitStack() as st:
        sb, ps = (lambda *a: _sb(st, nc, *a)), (lambda *a: _ps(st, nc, *a))
        c.cst = sb("cst", [128, NCONST], F32)
        c.identb = sb("identb", [128, 128], BF16)
        c.normwS = sb("normwS", [128, 32], F32)
        c.onesB = sb("onesB", [128, 128], BF16)
        c.regA = sb("regA", [128, 32 * TOK], BF16)
        c.regB = sb("regB", [128, 32 * TOK], BF16)
        c.xt = sb("xt", [128, D], F32)
        c.scr4k = sb("scr4k", [128, D], F32)
        c.lnst = sb("lnst", [128, 8], F32)
        c.wb = sb("wb", [128, 2 * 4096], BF16)
        c.abuf = sb("abuf", [128, 2 * TOK], BF16)
        c.sqb = sb("sqb", [128, TOK], BF16)
        c.rrow = sb("rrow", [1, TOK], F32)
        c.rrow2 = sb("rrow2", [1, TOK], F32)
        c.rrowb = sb("rrowb", [1, TOK], BF16)
        c.ft = sb("ft", [128, D], BF16)
        c.rstdb = sb("rstdb", [128, TOK], F32)
        c.pS0 = ps("pS0", [128, 512], F32)
        c.pS1 = ps("pS1", [128, 512], F32)
        c.pA = ps("pA", [128, 512], F32)
        c.pU = ps("pU", [128, 512], F32)
        c.pT0 = ps("pT0", [128, 1024], BF16)
        c.pT1 = ps("pT1", [128, 1024], BF16)
        c.pR = ps("pR", [128, 512], F32)
        block = st.enter_context(nc.Block())

        P = Prog(nc, st)
        for nm in ("cInit cX cXf cSq cStat cSqrt cHt pT0p pT0f pT1p pT1f cH0 "
                   "cW cWf pS0p pS0f pS1p pS1f pAp pAf pUp pUf pRp pRf cGate cA cAf "
                   "cSqB cRr cRstd cRb cRbb cProd cOT cFt cOut").split():
            setattr(c, nm, P.ch(nm))
        emit_b(P, c)
        P.emit(block)
    return nc


def _emit_b5(P, c):
    for tile in range(8):
        for cp in range(16):
            def ev_tr(t, cp=cp, tile=tile):
                wt(t, c.cOT)
                wt(t, c.cFt)
                if cp == 0:
                    wt(t, c.cXf)
                if t:
                    t.transpose(c.pT0[:, 0:128],
                                c.regB[:, (2 * cp) * TOK + tile * 128:
                                       (2 * cp) * TOK + tile * 128 + 128], c.identb[:])
                    t.transpose(c.pT1[:, 0:128],
                                c.regB[:, (2 * cp + 1) * TOK + tile * 128:
                                       (2 * cp + 1) * TOK + tile * 128 + 128], c.identb[:])
                    t.drain()
                sigi(t, c.pT0p)
                sigi(t, c.pT1p)
            P.on("tensor", ev_tr)

            eng = "vector" if cp % 2 == 0 else "scalar"

            def ev_ev(t, cp=cp, eng=eng):
                wt(t, c.pT0p)
                wt(t, c.pT1p)
                i = None
                if t:
                    for j, bank in ((2 * cp, c.pT0), (2 * cp + 1, c.pT1)):
                        dst = c.ft[:, j * 128:(j + 1) * 128]
                        i = (t.tensor_copy(dst, bank[:, 0:128]) if eng == "vector"
                             else t.copy(dst, bank[:, 0:128]))
                sig(t, i, c.cFt)
            P.on(eng, ev_ev)

        xb_ = (c.xt, getattr(c, "xt2", c.xt))[tile % 2]

        def ev_ldx(t, tile=tile, xb_=xb_):
            wt(t, c.cXf, lag=1)
            wt(t, c.cOut, lag=16)
            wt(t, c.cOT)
            i = t and t.dma_start(out=xb_[:, 0:D], in_=c.xown[tile * 128:(tile + 1) * 128, :])
            sig(t, i, c.cX, 16)
        P.on("sync", ev_ldx)

        def ev_sq(t):
            wt(t, c.cFt)
            wt(t, c.cHt)
            i = t and t.activation(c.scr4k[:], c.ft[:], AF.Square,
                                   accum_out=c.lnst[:, 1:2])
            sig(t, i, c.cSq)
        P.on("scalar", ev_sq)

        def ev_stat(t):
            wt(t, c.cSq)
            if t:
                t.reduce_sum(c.lnst[:, 0:1], c.ft[:], axis=AX.X)
                t.drain()
                t.tensor_scalar(c.lnst[:, 2:3], c.lnst[:, 0:1],
                                c.cst[:, C_INV_D:C_INV_D + 1], None, ALU.mult)
                t.tensor_scalar(c.lnst[:, 3:4], c.lnst[:, 1:2],
                                c.cst[:, C_INV_D:C_INV_D + 1], None, ALU.mult)
                t.drain()
                t.tensor_tensor(c.lnst[:, 4:5], c.lnst[:, 2:3], c.lnst[:, 2:3], ALU.mult)
                t.drain()
                i = t.tensor_tensor(c.lnst[:, 4:5], c.lnst[:, 3:4], c.lnst[:, 4:5], ALU.subtract)
            else:
                i = None
            sig(t, i, c.cStat)
        P.on("vector", ev_stat)

        def ev_sqrt(t):
            wt(t, c.cStat)
            i = t and t.activation(c.lnst[:, 5:6], c.lnst[:, 4:5], AF.Sqrt,
                                   bias=c.cst[:, C_EPS:C_EPS + 1])
            sig(t, i, c.cSqrt)
        P.on("scalar", ev_sqrt)

        def ev_apply(t, tile=tile, xb_=xb_):
            wt(t, c.cSqrt)
            wt(t, c.cX)
            if t:
                t.reciprocal(c.lnst[:, 6:7], c.lnst[:, 5:6])
                t.drain()
                t.tensor_scalar(c.scr4k[:], c.ft[:], c.lnst[:, 2:3], c.lnst[:, 6:7],
                                ALU.subtract, ALU.mult)
                i = t.tensor_tensor(xb_[:, 0:D], c.scr4k[:], xb_[:, 0:D], ALU.add)
            else:
                i = None
            sig(t, i, c.cHt)
            sigd(t, c.cXf)
        P.on("vector", ev_apply)

        def ev_st(t, tile=tile, xb_=xb_):
            wt(t, c.cHt)
            i = t and t.dma_start(out=c.x1[tile * 128:(tile + 1) * 128, :], in_=xb_[:, 0:D])
            sig(t, i, c.cOut, 16)
        P.on("sync", ev_st)


def _prep_b(inputs, attn_outs):
    x = inputs["L0_attn_hidden_states"]
    gT = np.ascontiguousarray(inputs["L0_attn_gate_w"].T)
    oT = np.ascontiguousarray(inputs["L0_attn_out_w"].T)
    gateTt = np.ascontiguousarray(gT.reshape(D, 32, 128).transpose(1, 0, 2))
    outTt = np.ascontiguousarray(oT.reshape(D, 32, 128).transpose(1, 0, 2))
    normw = np.ascontiguousarray(inputs["L0_attn_norm_w"].reshape(32, 128).T)
    consts = _consts_host()
    ident = np.eye(128, dtype=np.float32)
    in_maps = []
    for core in range(NCORES):
        b, g = core // 2, core % 2
        att = np.concatenate([np.asarray(attn_outs[2 * b]), np.asarray(attn_outs[2 * b + 1])],
                             axis=0).reshape(D, S)
        afm = np.ascontiguousarray(att[:, g * TOK:(g + 1) * TOK])
        in_maps.append({
            "xown": np.ascontiguousarray(x[b, g * TOK:(g + 1) * TOK, :]),
            "afm": afm,
            "gateTt": gateTt,
            "outTt": outTt,
            "normw": normw,
            "consts": consts,
            "identf": ident,
        })
    return in_maps


def run_phase_b(inputs, attn_outs):
    nc = build_phase_b()
    in_maps = _prep_b(inputs, attn_outs)
    res = run_bass_kernel_spmd(nc, in_maps, core_ids=list(range(NCORES)))
    return [r["x1"] for r in res.results]


C_INVSQ48, C_INV8, C_GELC, C_INVNTOT = 4, 5, 6, 7


def _consts_host_c():
    c = _consts_host()
    c[:, C_INVSQ48] = 1.0 / math.sqrt(48.0)
    c[:, C_INV8] = 1.0 / math.sqrt(64.0)
    c[:, C_GELC] = 0.7978845608028654
    c[:, C_INVNTOT] = 1.0 / 8192.0
    return c


def emit_ca(P, c):

    def ev_init(t):
        i = t and t.dma_start(out=c.cst[:], in_=c.consts)
        sig(t, i, c.cInit, 16)
        i = t and t.dma_start(out=c.ones1[:], in_=c.onesr)
        sig(t, i, c.cInit, 16)
    P.on("gpsimd", ev_init)
    for e in ("scalar", "tensor", "sync", "vector"):
        def ev_i(t):
            wt(t, c.cInit)
        P.on(e, ev_i)

    for tile in range(8):
        def ev_ldx(t, tile=tile):
            wt(t, c.cXf)
            i = t and t.dma_start(out=c.xt[:], in_=c.x1[tile * 128:(tile + 1) * 128, :])
            sig(t, i, c.cX, 16)
        P.on("sync", ev_ldx)

        def ev_sq(t):
            wt(t, c.cX)
            i = t and t.activation(c.scr4k[:], c.xt[:], AF.Square,
                                   accum_out=c.lnst[:, 1:2])
            sig(t, i, c.cSq)
        P.on("scalar", ev_sq)

        def ev_stat(t):
            wt(t, c.cSq)
            if t:
                t.reduce_sum(c.lnst[:, 0:1], c.xt[:], axis=AX.X)
                t.drain()
                t.tensor_scalar(c.lnst[:, 2:3], c.lnst[:, 0:1],
                                c.cst[:, C_INV_D:C_INV_D + 1], None, ALU.mult)
                t.tensor_scalar(c.lnst[:, 3:4], c.lnst[:, 1:2],
                                c.cst[:, C_INV_D:C_INV_D + 1], None, ALU.mult)
                t.drain()
                t.tensor_tensor(c.lnst[:, 4:5], c.lnst[:, 2:3], c.lnst[:, 2:3], ALU.mult)
                t.drain()
                i = t.tensor_tensor(c.lnst[:, 4:5], c.lnst[:, 3:4], c.lnst[:, 4:5], ALU.subtract)
            else:
                i = None
            sig(t, i, c.cStat)
        P.on("vector", ev_stat)

        def ev_sqrt(t):
            wt(t, c.cStat)
            i = t and t.activation(c.lnst[:, 5:6], c.lnst[:, 4:5], AF.Sqrt,
                                   bias=c.cst[:, C_EPS:C_EPS + 1])
            sig(t, i, c.cSqrt)
        P.on("scalar", ev_sqrt)

        def ev_apply(t, tile=tile):
            wt(t, c.cSqrt)
            wt(t, c.cSqH)
            if t:
                t.reciprocal(c.lnst[:, 6:7], c.lnst[:, 5:6])
                t.drain()
                i = t.tensor_scalar(c.hts[:], c.xt[:], c.lnst[:, 2:3], c.lnst[:, 6:7],
                                    ALU.subtract, ALU.mult)
            else:
                i = None
            sig(t, i, c.cHt)
            sigd(t, c.cXf)
        P.on("vector", ev_apply)

        def ev_sqh(t, tile=tile):
            wt(t, c.cHt)
            i = t and t.activation(c.scr4k[:], c.hts[:], AF.Square,
                                   accum_out=c.lnst[:, 7:8])
            sig(t, i, c.cSqH)
        P.on("scalar", ev_sqh)

        def ev_lv(t, tile=tile):
            wt(t, c.cSqH)
            if t:
                t.reduce_sum(c.lnst[:, 0:1], c.hts[:], axis=AX.X)
                t.drain()
                t.tensor_scalar(c.lnst[:, 2:3], c.lnst[:, 0:1],
                                c.cst[:, C_INV_D:C_INV_D + 1], None, ALU.mult)
                t.tensor_scalar(c.lnst[:, 3:4], c.lnst[:, 7:8],
                                c.cst[:, C_INV_D:C_INV_D + 1], None, ALU.mult)
                t.drain()
                t.tensor_tensor(c.lnst[:, 4:5], c.lnst[:, 2:3], c.lnst[:, 2:3], ALU.mult)
                t.drain()
                i = t.tensor_tensor(c.lnst[:, 7:8], c.lnst[:, 3:4],
                                    c.lnst[:, 4:5], ALU.subtract)
            else:
                i = None
            sig(t, i, c.cLv)
        P.on("vector", ev_lv)

        def ev_ln1p(t, tile=tile):
            wt(t, c.cLv)
            i = t and t.activation(c.lv[:, tile:tile + 1], c.lnst[:, 7:8],
                                   AF.Ln, bias=c.cst[:, C_ONE:C_ONE + 1])
            sig(t, i, c.cLv)
        P.on("scalar", ev_ln1p)

    def ev_red(t):
        wt(t, c.cLv)
        if t:
            i = t.reduce_sum(c.lnst[:, 0:1], c.lv[:], axis=AX.X)
        else:
            i = None
        sig(t, i, c.cLv)
    P.on("vector", ev_red)

    def ev_mm(t):
        wt(t, c.cLv)
        i = t and t.matmul(c.pU[0:1, 0:1], c.lnst[:, 0:1], c.cst[:, C_ONE:C_ONE + 1],
                           start=True, stop=True)
        sig(t, i, c.pUp)
    P.on("tensor", ev_mm)

    def ev_out(t):
        wt(t, c.pUp)
        i = t and t.tensor_copy(c.red[0:1, 0:1], c.pU[0:1, 0:1])
        sig(t, i, c.cHt)
    P.on("vector", ev_out)

    def ev_st(t):
        wt(t, c.cHt)
        i = t and t.dma_start(out=c.partial, in_=c.red[0:1, 0:1])
        sig(t, i, c.cOut, 16)
        wt(t, c.cOut)
    P.on("sync", ev_st)




def build_phase_ca():
    import contextlib
    nc = bass.Bass("TRN2", target_bir_lowering=False)
    c = types.SimpleNamespace()
    c.x1 = nc.dram_tensor("x1", [TOK, D], F32, kind="ExternalInput").ap()
    c.consts = nc.dram_tensor("consts", [128, NCONST], F32, kind="ExternalInput").ap()
    c.onesr = nc.dram_tensor("onesr", [1, 128], F32, kind="ExternalInput").ap()
    c.partial = nc.dram_tensor("partial", [1, 1], F32, kind="ExternalOutput").ap()

    with contextlib.ExitStack() as st:
        sb, ps = (lambda *a: _sb(st, nc, *a)), (lambda *a: _ps(st, nc, *a))
        c.cst = sb("cst", [128, NCONST], F32)
        c.ones1 = sb("ones1", [1, 128], F32)
        c.xt = sb("xt", [128, D], F32)
        c.scr4k = sb("scr4k", [128, D], F32)
        c.hts = sb("hts", [128, D], F32)
        c.lnst = sb("lnst", [128, 8], F32)
        c.lv = sb("lv", [128, 8], F32)
        c.red = sb("red", [1, 8], F32)
        c.pU = ps("pU", [128, 512], F32)
        block = st.enter_context(nc.Block())
        P = Prog(nc, st)
        for nm in "cInit cX cXf cSq cStat cSqrt cHt cSqH cLv pUp cOut".split():
            setattr(c, nm, P.ch(nm))
        emit_ca(P, c)
        P.emit(block)
    return nc


def run_phase_ca(x1_outs):
    nc = build_phase_ca()
    consts = _consts_host_c()
    onesr = np.ones((1, 128), np.float32)
    in_maps = [{"x1": np.ascontiguousarray(np.asarray(x)), "consts": consts,
                "onesr": onesr} for x in x1_outs]
    res = run_bass_kernel_spmd(nc, in_maps, core_ids=list(range(NCORES)))
    return np.array([float(r["partial"][0, 0]) for r in res.results], np.float32)


BIGNEG = 30000.0


def emit_cb(P, c):

    def ev_init(t):
        pairs = [(c.cst[:], c.consts), (c.ones1[:], c.onesr),
                 (c.identb[:], c.identf), (c.logrelS[:], c.logrel),
                 (c.scalsS[:], c.scals), (c.bu1S[:], c.bu1c),
                 (c.wu2S[:], c.wu2c), (c.wr_sb[:], c.wrT),
                 (c.ak_sb[:], c.akT)]
        if not getattr(c, "fused", False):
            pairs.append((c.prow[:], c.partials))
        for dst, src in pairs:
            i = t and t.dma_start(out=dst, in_=src)
            sig(t, i, c.cInit, 16)
        src = c.wqT.rearrange("(kc p) m -> p kc m", p=128)
        dst = c.wq_sb[:].rearrange("p (kc m) -> p kc m", m=64)
        i = t and t.dma_start(out=dst, in_=src)
        sig(t, i, c.cInit, 16)
        src = c.wvuT.rearrange("(kc p) m -> p kc m", p=128)
        dst = c.wvuS[:].rearrange("p (kc m) -> p kc m", m=D)
        i = t and t.dma_start(out=dst, in_=src)
        sig(t, i, c.cInit, 16)
    P.on("gpsimd", ev_init)

    def _emit_meanchain():
        def ev_mean_v(t):
            wt(t, c.cInit)
            if getattr(c, "fused", False):
                wt(t, c.cPt)
            if t:
                if not getattr(c, "fused", False):
                    t.reduce_sum(c.prow[0:1, 0:1], c.prow[0:1, 0:8], axis=AX.X)
                t.drain()
                t.tensor_scalar(c.prow[0:1, 1:2], c.prow[0:1, 0:1], 1.0 / 8192.0, 1e-6,
                                ALU.mult, ALU.add)
                t.drain()
                i = t.reciprocal(c.scalsS[0:1, 3:4], c.prow[0:1, 1:2])
            else:
                i = None
            sig(t, i, c.cBc)
        P.on("vector", ev_mean_v)

        def ev_bc_mm(t):
            wt(t, c.cBc)
            i = t and t.matmul(c.pS0[:, 0:4], c.ones1[0:1, :], c.scalsS[0:1, 0:4],
                               start=True, stop=True)
            sig(t, i, c.pS0p)
        P.on("tensor", ev_bc_mm)

        def ev_bc_ev(t):
            wt(t, c.pS0p)
            i = t and t.tensor_copy(c.scget[:, 0:4], c.pS0[:, 0:4])
            sig(t, i, c.cBc)
            sigd(t, c.pS0f)
        P.on("vector", ev_bc_ev)

    if not getattr(c, "fused", False):
        _emit_meanchain()

    def extras(tile):
        def ev_ms(t, tile=tile):
            wt(t, c.cHt)
            if t:
                t.tensor_copy(c.mstore[:, tile:tile + 1], c.lnst[:, 2:3])
                i = t.tensor_copy(c.rstore[:, tile:tile + 1], c.lnst[:, 6:7])
            else:
                i = None
            sig(t, i, c.cLv)
        P.on("vector", ev_ms)

        def ev_sqh(t, tile=tile):
            wt(t, c.cHt)
            i = t and t.activation(c.scr4k[:], c.ht[:], AF.Square,
                                   accum_out=c.lnst[:, 7:8])
            sig(t, i, c.cSqH)
        P.on("scalar", ev_sqh)

        def ev_lv(t, tile=tile):
            wt(t, c.cSqH)
            if t:
                t.reduce_sum(c.lnst[:, 0:1], c.ht[:], axis=AX.X)
                t.drain()
                t.tensor_scalar(c.lnst[:, 2:3], c.lnst[:, 0:1],
                                c.cst[:, C_INV_D:C_INV_D + 1], None, ALU.mult)
                t.tensor_scalar(c.lnst[:, 3:4], c.lnst[:, 7:8],
                                c.cst[:, C_INV_D:C_INV_D + 1], None, ALU.mult)
                t.drain()
                t.tensor_tensor(c.lnst[:, 4:5], c.lnst[:, 2:3], c.lnst[:, 2:3], ALU.mult)
                t.drain()
                i = t.tensor_tensor(c.lnst[:, 7:8], c.lnst[:, 3:4],
                                    c.lnst[:, 4:5], ALU.subtract)
            else:
                i = None
            sig(t, i, c.cLv)
        P.on("vector", ev_lv)

        def ev_ln1p(t, tile=tile):
            wt(t, c.cLv)
            i = t and t.activation(c.lv[:, tile:tile + 1], c.lnst[:, 7:8],
                                   AF.Ln, bias=c.cst[:, C_ONE:C_ONE + 1])
            sig(t, i, c.cLv)
        P.on("scalar", ev_ln1p)

    emit_ln_fm(P, c, c.x1, 8, c.regA, extras=extras)

    if getattr(c, "fused", False):
        def ev_pred(t):
            wt(t, c.cLv)
            i = t and t.reduce_sum(c.lnst[:, 0:1], c.lv[:], axis=AX.X)
            sig(t, i, c.cPt)
        P.on("vector", ev_pred)

        def ev_pmm(t):
            wt(t, c.cPt)
            wt(t, c.pUf)
            i = t and t.matmul(c.pU[0:1, 0:1], c.lnst[:, 0:1],
                               c.cst[:, C_ONE:C_ONE + 1], start=True, stop=True)
            sig(t, i, c.pUp)
        P.on("tensor", ev_pmm)

        def ev_pout(t):
            wt(t, c.pUp)
            i = t and t.tensor_copy(c.prow[0:1, 0:1], c.pU[0:1, 0:1])
            sig(t, i, c.cPt)
            sigd(t, c.pUf)
        P.on("vector", ev_pout)

        def ev_pst(t):
            wt(t, c.cPt)
            i = t and t.dma_start(out=c.partA[0:1, 0:1], in_=c.prow[0:1, 0:1])
            sig(t, i, c.cPt, 16)
        P.on("sync", ev_pst)

        def ev_ar(t):
            wt(t, c.cPt)
            i = t and t.collective_compute(
                "AllReduce", mybir.AluOpType.add,
                replica_groups=[list(range(8))],
                ins=[c.partA], outs=[c.partR])
            sig(t, i, c.cAR)
        P.on("gpsimd", ev_ar)

        def ev_pld(t):
            wt(t, c.cAR)
            i = t and t.dma_start(out=c.prow[0:1, 0:1], in_=c.partR[0:1, 0:1])
            sig(t, i, c.cPt, 16)
        P.on("sync", ev_pld)

        _emit_meanchain()


    for n in range(2):
        bank, bp, bf = ((c.pS0, c.pS0p, c.pS0f), (c.pS1, c.pS1p, c.pS1f))[n]

        def ev_qmm(t, n=n, bank=bank, bp=bp, bf=bf):
            wt(t, c.cInit)
            wt(t, c.cH0)
            wt(t, bf)
            i = None
            if t:
                for kc in range(32):
                    i = t.matmul(bank[0:64, 0:512], c.wq_sb[:, kc * 64:(kc + 1) * 64],
                                 c.regA[:, kc * TOK + n * 512: kc * TOK + n * 512 + 512],
                                 start=(kc == 0), stop=(kc == 31))
            sig(t, i, bp)
        P.on("tensor", ev_qmm)

        def ev_qev(t, n=n, bank=bank, bp=bp, bf=bf):
            wt(t, bp)
            i = t and t.copy(c.query[:, n * 512:(n + 1) * 512], bank[0:64, 0:512])
            sig(t, i, c.cQev)
            sigd(t, bf)
        P.on("scalar", ev_qev)

    def ev_rkmm(t):
        wt(t, c.cInit)
        wt(t, c.pAf)
        wt(t, c.pUf)
        i = None
        if t:
            t.matmul(c.pA[0:48, 0:512], c.wr_sb[:], c.ak_sb[:, 0:512],
                     start=True, stop=True)
            i = t.matmul(c.pU[0:48, 0:512], c.wr_sb[:], c.ak_sb[:, 512:1024],
                         start=True, stop=True)
        sig(t, i, c.pAp)
        sigi(t, c.pUp)
    P.on("tensor", ev_rkmm)

    def ev_rkev(t):
        wt(t, c.pAp)
        wt(t, c.pUp)
        if t:
            t.copy(c.rk[:, 0:512], c.pA[0:48, 0:512])
            i = t.copy(c.rk[:, 512:1024], c.pU[0:48, 0:512])
        else:
            i = None
        sig(t, i, c.cRKev)
        sigd(t, c.pAf)
        sigi(t, c.pUf)
    P.on("scalar", ev_rkev)

    def ev_rqmm(t):
        wt(t, c.cQev)
        wt(t, c.pAf)
        wt(t, c.pUf)
        i = None
        if t:
            t.matmul(c.pA[0:48, 0:512], c.wr_sb[:], c.query[:, 0:512],
                     start=True, stop=True)
            i = t.matmul(c.pU[0:48, 0:512], c.wr_sb[:], c.query[:, 512:1024],
                         start=True, stop=True)
        sig(t, i, c.pAp)
        sigi(t, c.pUp)
    P.on("tensor", ev_rqmm)

    def ev_rqev(t):
        wt(t, c.pAp)
        wt(t, c.pUp)
        if t:
            t.copy(c.rq[:, 0:512], c.pA[0:48, 0:512])
            i = t.copy(c.rq[:, 512:1024], c.pU[0:48, 0:512])
        else:
            i = None
        sig(t, i, c.cRQev)
        sigd(t, c.pAf)
        sigi(t, c.pUf)
    P.on("scalar", ev_rqev)

    def _emit_router(tt):
        def ev_smm(t, tt=tt):
            wt(t, c.cRQev)
            wt(t, c.cRKev)
            wt(t, c.pS0f)
            wt(t, c.pS1f)
            wt(t, c.pAf)
            wt(t, c.pUf)
            i = None
            if t:
                t.matmul(c.pS0[:, 0:512], c.rq[:, tt * 128:(tt + 1) * 128],
                         c.rk[:, 0:512], start=True, stop=True)
                t.matmul(c.pS1[:, 0:512], c.rq[:, tt * 128:(tt + 1) * 128],
                         c.rk[:, 512:1024], start=True, stop=True)
                t.matmul(c.pA[:, 0:512], c.query[:, tt * 128:(tt + 1) * 128],
                         c.ak_sb[:, 0:512], start=True, stop=True)
                i = t.matmul(c.pU[:, 0:512], c.query[:, tt * 128:(tt + 1) * 128],
                             c.ak_sb[:, 512:1024], start=True, stop=True)
            sig(t, i, c.pS0p)
            sigi(t, c.pS1p)
            sigi(t, c.pAp)
            sigi(t, c.pUp)
        P.on("tensor", ev_smm)

        def ev_sev(t, tt=tt):
            wt(t, c.pS0p)
            wt(t, c.pS1p)
            wt(t, c.pAp)
            wt(t, c.pUp)
            wt(t, c.cWt)
            if t:
                t.scalar_tensor_tensor(c.scores[:, 0:512], c.pS0[:, 0:512],
                                       1.0 / math.sqrt(48.0), c.logrelS[:, 0:512],
                                       ALU.mult, ALU.add)
                t.scalar_tensor_tensor(c.scores[:, 512:1024], c.pS1[:, 0:512],
                                       1.0 / math.sqrt(48.0), c.logrelS[:, 512:1024],
                                       ALU.mult, ALU.add)
                t.scalar_tensor_tensor(c.attnb[:, 0:512], c.pA[:, 0:512],
                                       0.125, c.logrelS[:, 0:512],
                                       ALU.mult, ALU.add)
                i = t.scalar_tensor_tensor(c.attnb[:, 512:1024], c.pU[:, 0:512],
                                           0.125, c.logrelS[:, 512:1024],
                                           ALU.mult, ALU.add)
            else:
                i = None
            sig(t, i, c.cSc)
            sigd(t, c.pS0f)
            sigi(t, c.pS1f)
            sigi(t, c.pAf)
            sigi(t, c.pUf)
        P.on("vector", ev_sev)

        def ev_topk(t, tt=tt):
            wt(t, c.cSc)
            if t:
                t.max(out=c.t8[:], in_=c.scores[:])
                t.drain()
                t.tensor_single_scalar(c.msk[:], c.scores[:], c.t8[:, 7:8], ALU.is_ge)
                t.drain()
                t.scalar_tensor_tensor(c.pbuf[:], c.msk[:], BIGNEG, c.attnb[:],
                                       ALU.mult, ALU.add)
                t.drain()
                t.reduce_max(c.lnst[:, 0:1], c.pbuf[:], axis=AX.X, negate=True)
                t.drain()
                i = t.tensor_scalar(c.lnst[:, 1:2], c.lnst[:, 0:1], BIGNEG, None,
                                    ALU.add)
            else:
                i = None
            sig(t, i, c.cNegm)
        P.on("vector", ev_topk)

        def ev_exp(t, tt=tt):
            wt(t, c.cNegm)
            i = t and t.activation(c.pbuf[:], c.attnb[:], AF.Exp,
                                   bias=c.lnst[:, 1:2])
            sig(t, i, c.cP)
        P.on("scalar", ev_exp)

        def ev_wnorm(t, tt=tt):
            wt(t, c.cP)
            if t:
                t.tensor_tensor(c.scores[:], c.pbuf[:], c.msk[:], ALU.mult)
                t.drain()
                t.reduce_sum(c.lnst[:, 2:3], c.scores[:], axis=AX.X)
                t.drain()
                t.reciprocal(c.lnst[:, 3:4], c.lnst[:, 2:3])
                t.drain()
                i = t.tensor_scalar(c.wtile[:], c.scores[:], c.lnst[:, 3:4], None,
                                    ALU.mult)
            else:
                i = None
            sig(t, i, c.cWt)
        P.on("vector", ev_wnorm)

        for cp in range(4):
            def ev_wtr(t, cp=cp, tt=tt):
                wt(t, c.cWt)
                wt(t, c.cWfm)
                if t:
                    t.transpose(c.pT0[:, 0:128],
                                c.wtile[:, (2 * cp) * 128:(2 * cp + 1) * 128],
                                c.identb[:])
                    t.transpose(c.pT1[:, 0:128],
                                c.wtile[:, (2 * cp + 1) * 128:(2 * cp + 2) * 128],
                                c.identb[:])
                    t.drain()
                sigi(t, c.pT0p)
                sigi(t, c.pT1p)
            P.on("tensor", ev_wtr)

            eng = "vector" if cp % 2 == 0 else "scalar"

            def ev_wev(t, cp=cp, tt=tt, eng=eng):
                wt(t, c.pT0p)
                wt(t, c.pT1p)
                i = None
                if t:
                    for j, bank in ((2 * cp, c.pT0), (2 * cp + 1, c.pT1)):
                        dst = c.w_fm[:, j * TOK + tt * 128: j * TOK + tt * 128 + 128]
                        i = (t.tensor_copy(dst, bank[:, 0:128]) if eng == "vector"
                             else t.copy(dst, bank[:, 0:128]))
                sig(t, i, c.cWfm, 2)
            P.on(eng, ev_wev)

    def _emit_uj(j):
        par = j % 2

        def ev_uw(t, j=j, par=par):
            wt(t, c.cWf, lag=1)
            if j == 0:
                wt(t, c.cQev)
            src = c.wu1t[j].rearrange("(kc p) m -> p kc m", p=128)
            dst = c.wbufC[:, par * 4096:(par + 1) * 4096].rearrange(
                "p (kc m) -> p kc m", m=128)
            i = t and t.dma_start(out=dst, in_=src)
            sig(t, i, c.cW, 16)
        P.on("gpsimd", ev_uw)

        for n in range(2):
            bank, bp, bf = ((c.pS0, c.pS0p, c.pS0f), (c.pS1, c.pS1p, c.pS1f),
                            (c.pA, c.pAp, c.pAf))[(j * 2 + n) % 3]

            def ev_umm(t, j=j, n=n, par=par, bank=bank, bp=bp, bf=bf):
                wt(t, c.cW)
                wt(t, c.cH0)
                wt(t, bf)
                i = None
                if t:
                    for kc in range(32):
                        lhs = c.wbufC[:, par * 4096 + kc * 128: par * 4096 + kc * 128 + 128]
                        rhs = c.regA[:, kc * TOK + n * 512: kc * TOK + n * 512 + 512]
                        i = t.matmul(bank[:, 0:512], lhs, rhs,
                                     start=(kc == 0), stop=(kc == 31))
                sig(t, i, bp)
            P.on("tensor", ev_umm)

            def ev_uev(t, j=j, n=n, bank=bank, bp=bp, bf=bf):
                wt(t, bp)
                i = t and t.activation(c.u1g[:, j * TOK + n * 512: j * TOK + n * 512 + 512],
                                       bank[:, 0:512], AF.Identity,
                                       bias=c.bu1S[:, j:j + 1])
                sig(t, i, c.cU1)
                sigd(t, bf)
                if n == 1:
                    sigi(t, c.cWf)
            P.on("scalar", ev_uev)

        def ev_ge1(t, j=j):
            wt(t, c.cU1)
            a = c.u1g[:, j * TOK:(j + 1) * TOK]
            s = c.gtmp[:, 0:TOK]
            s3 = c.gtmp[:, TOK:2 * TOK]
            if t:
                t.tensor_tensor(s, a, a, ALU.mult)
                t.tensor_tensor(s3, s, a, ALU.mult)
                i = t.scalar_tensor_tensor(s, s3, 0.044715, a, ALU.mult, ALU.add)
            else:
                i = None
            sig(t, i, c.cGe1)
        P.on("vector", ev_ge1)

        def ev_ge2(t, j=j):
            wt(t, c.cGe1)
            i = t and t.activation(c.gtmp[:, TOK:2 * TOK], c.gtmp[:, 0:TOK],
                                   AF.Tanh, scale=c.cst[:, C_GELC:C_GELC + 1])
            sig(t, i, c.cGe2)
        P.on("scalar", ev_ge2)

        def ev_ge3(t, j=j):
            wt(t, c.cGe2)
            a = c.u1g[:, j * TOK:(j + 1) * TOK]
            if t:
                t.scalar_tensor_tensor(c.gtmp[:, 0:TOK], c.gtmp[:, TOK:2 * TOK],
                                       1.0, a, ALU.add, ALU.mult)
                t.drain()
                i = t.tensor_scalar(a, c.gtmp[:, 0:TOK], 0.5, None, ALU.mult)
            else:
                i = None
            sig(t, i, c.cGe3)
        P.on("vector", ev_ge3)

    for _k in range(8):
        _emit_router(_k)
        _emit_uj(_k)

    for tt in range(8):
        def ev_u2mm(t, tt=tt):
            wt(t, c.cGe3)
            wt(t, c.pUf)
            i = None
            if t:
                for kc in range(8):
                    i = t.matmul(c.pU[:, 0:1],
                                 c.u1g[:, kc * TOK + tt * 128: kc * TOK + tt * 128 + 128],
                                 c.wu2S[:, kc:kc + 1],
                                 start=(kc == 0), stop=(kc == 7))
            sig(t, i, c.pUp)
        P.on("tensor", ev_u2mm)

        def ev_u2ev(t, tt=tt):
            wt(t, c.pUp)
            i = t and t.activation(c.u2S[:, tt:tt + 1], c.pU[:, 0:1], AF.Sigmoid,
                                   bias=c.scget[:, 2:3])
            sig(t, i, c.cU2)
            sigd(t, c.pUf)
        P.on("scalar", ev_u2ev)

    def ev_gate1(t):
        wt(t, c.cU2)
        wt(t, c.cLv)
        wt(t, c.cBc)
        if t:
            t.tensor_scalar(c.gateS[:], c.lv[:], c.scget[:, 3:4], None, ALU.mult)
            t.tensor_scalar(c.u2S[:], c.u2S[:], 2.5, None, ALU.mult)
            t.drain()
            t.scalar_tensor_tensor(c.gateS[:], c.gateS[:], 0.5, c.u2S[:],
                                   ALU.mult, ALU.add)
            t.drain()
            t.tensor_scalar(c.gateS[:], c.gateS[:], 0.0, 5.0, ALU.max, ALU.min)
            t.drain()
            t.tensor_scalar(c.gateS[:], c.gateS[:], 0.5, 1.0 / 1.5,
                            ALU.subtract, ALU.mult)
            t.drain()
            t.tensor_scalar(c.gateS[:], c.gateS[:], 0.0, 1.0, ALU.max, ALU.min)
            t.drain()
            i = t.tensor_scalar(c.gateS[:], c.gateS[:], c.scget[:, 0:1],
                                c.scget[:, 1:2], ALU.mult, ALU.add)
        else:
            i = None
        sig(t, i, c.cGate)
    P.on("vector", ev_gate1)

    def ev_gate2(t):
        wt(t, c.cGate)
        i = t and t.activation(c.u2S[:], c.gateS[:], AF.Sigmoid)
        sig(t, i, c.cGate)
    P.on("scalar", ev_gate2)

    def ev_gate3(t):
        wt(t, c.cGate)
        if t:
            t.tensor_single_scalar(c.gateS[:], c.u2S[:], 0.05, ALU.is_ge)
            t.drain()
            i = t.tensor_tensor(c.gateS[:], c.gateS[:], c.u2S[:], ALU.mult)
        else:
            i = None
        sig(t, i, c.cGate)
    P.on("vector", ev_gate3)

    for m in range(32):
        par = m % 2

        def ev_avw(t, m=m, par=par):
            wt(t, c.cWf, lag=1)
            src = c.avt[m].rearrange("(sc p) m2 -> p sc m2", p=128)
            dst = c.wbufC[:, par * 4096:par * 4096 + 1024].rearrange(
                "p (sc m2) -> p sc m2", m2=128)
            i = t and t.dma_start(out=dst, in_=src)
            sig(t, i, c.cW, 16)
        P.on("gpsimd", ev_avw)

        for n in range(2):
            bank, bp, bf = ((c.pS0, c.pS0p, c.pS0f), (c.pS1, c.pS1p, c.pS1f),
                            (c.pA, c.pAp, c.pAf))[(m * 2 + n) % 3]

            def ev_a1mm(t, m=m, n=n, par=par, bank=bank, bp=bp, bf=bf):
                wt(t, c.cW)
                wt(t, c.cWfm)
                if m == 0 and n == 0:
                    wt(t, c.cGe3)
                    wt(t, c.cU2)
                wt(t, bf)
                i = None
                if t:
                    for sc in range(8):
                        lhs = c.wbufC[:, par * 4096 + sc * 128: par * 4096 + sc * 128 + 128]
                        rhs = c.w_fm[:, sc * TOK + n * 512: sc * TOK + n * 512 + 512]
                        i = t.matmul(bank[:, 0:512], lhs, rhs,
                                     start=(sc == 0), stop=(sc == 7))
                sig(t, i, bp)
            P.on("tensor", ev_a1mm)

            def ev_a1ev(t, m=m, n=n, bank=bank, bp=bp, bf=bf):
                wt(t, bp)
                i = t and t.tensor_copy(
                    c.regA[:, m * TOK + n * 512: m * TOK + n * 512 + 512],
                    bank[:, 0:512])
                sig(t, i, c.cA1)
                sigd(t, bf)
                if n == 1:
                    sigi(t, c.cWf)
            P.on("vector", ev_a1ev)

    for kc in range(32):
        par = kc % 2

        def ev_vdw(t, kc=kc, par=par):
            wt(t, c.cVd, lag=1)
            i = t and t.dma_start(out=c.wvdb[:, par * 256:(par + 1) * 256],
                                  in_=c.wvdT[kc * 128:(kc + 1) * 128, :])
            sig(t, i, c.cVdW, 16)
        P.on("gpsimd", ev_vdw)

        def ev_a2mm(t, kc=kc, par=par):
            wt(t, c.cVdW)
            wt(t, c.cA1)
            if kc == 0:
                wt(t, c.pS0f)
                wt(t, c.pS1f)
                wt(t, c.pAf)
                wt(t, c.pUf)
            i = None
            if t:
                for m2 in range(2):
                    for n in range(2):
                        bank = ((c.pS0, c.pS1), (c.pA, c.pU))[m2][n]
                        i = t.matmul(
                            bank[:, 0:512],
                            c.wvdb[:, par * 256 + m2 * 128: par * 256 + m2 * 128 + 128],
                            c.regA[:, kc * TOK + n * 512: kc * TOK + n * 512 + 512],
                            start=(kc == 0), stop=(kc == 31))
            sig(t, i, c.cVd)
        P.on("tensor", ev_a2mm)

    def ev_a2ev(t):
        wt(t, c.cVd)
        if t:
            t.tensor_copy(c.aux2S[:, 0:512], c.pS0[:, 0:512])
            t.tensor_copy(c.aux2S[:, 512:1024], c.pS1[:, 0:512])
            t.tensor_copy(c.aux2S[:, 1024:1536], c.pA[:, 0:512])
            i = t.tensor_copy(c.aux2S[:, 1536:2048], c.pU[:, 0:512])
        else:
            i = None
        sig(t, i, c.cA2)
        sigd(t, c.pS0f)
        sigi(t, c.pS1f)
        sigi(t, c.pAf)
        sigi(t, c.pUf)
    P.on("vector", ev_a2ev)

    for tt in range(8):
        xb_ = (c.xt, getattr(c, "xt2", c.xt))[tt % 2]

        def ev_ldx2(t, tt=tt, xb_=xb_):
            wt(t, c.cXf, lag=1)
            wt(t, c.cOut, lag=16)
            wt(t, c.cA1)
            i = t and t.dma_start(out=xb_[:, 0:D], in_=c.x1[tt * 128:(tt + 1) * 128, :])
            sig(t, i, c.cX, 16)
        P.on("sync", ev_ldx2)

        for dg in range(8):
            bank, bp, bf = ((c.pS0, c.pS0p, c.pS0f), (c.pS1, c.pS1p, c.pS1f),
                            (c.pA, c.pAp, c.pAf))[dg % 3]

            def ev_a3mm(t, tt=tt, dg=dg, bank=bank, bp=bp, bf=bf):
                wt(t, c.cA2)
                wt(t, bf)
                i = None
                if t:
                    for kc2 in range(2):
                        i = t.matmul(
                            bank[:, 0:512],
                            c.aux2S[:, kc2 * TOK + tt * 128: kc2 * TOK + tt * 128 + 128],
                            c.wvuS[:, kc2 * D + dg * 512: kc2 * D + dg * 512 + 512],
                            start=(kc2 == 0), stop=(kc2 == 1))
                sig(t, i, bp)
            P.on("tensor", ev_a3mm)

            def ev_a3ev(t, tt=tt, dg=dg, bank=bank, bp=bp, bf=bf, xb_=xb_):
                wt(t, bp)
                wt(t, c.cX)
                wt(t, c.cGate)
                wt(t, c.cHt)
                i = None
                if t:
                    t.tensor_scalar(c.hsl[:], xb_[:, dg * 512:(dg + 1) * 512],
                                    c.mstore[:, tt:tt + 1], c.rstore[:, tt:tt + 1],
                                    ALU.subtract, ALU.mult)
                    t.drain()
                    i = t.scalar_tensor_tensor(c.fb[:, dg * 512:(dg + 1) * 512],
                                               bank[:, 0:512], c.gateS[:, tt:tt + 1],
                                               c.hsl[:], ALU.mult, ALU.add)
                sig(t, i, c.cFb)
                sigd(t, bf)
            P.on("vector", ev_a3ev)

        def ev_sqf(t, tt=tt):
            wt(t, c.cFb)
            wt(t, c.cHt)
            i = t and t.activation(c.scr4k[:], c.fb[:], AF.Square,
                                   accum_out=c.lnst[:, 1:2])
            sig(t, i, c.cSq)
        P.on("scalar", ev_sqf)

        def ev_statf(t):
            wt(t, c.cSq)
            if t:
                t.reduce_sum(c.lnst[:, 0:1], c.fb[:], axis=AX.X)
                t.drain()
                t.tensor_scalar(c.lnst[:, 2:3], c.lnst[:, 0:1],
                                c.cst[:, C_INV_D:C_INV_D + 1], None, ALU.mult)
                t.tensor_scalar(c.lnst[:, 3:4], c.lnst[:, 1:2],
                                c.cst[:, C_INV_D:C_INV_D + 1], None, ALU.mult)
                t.drain()
                t.tensor_tensor(c.lnst[:, 4:5], c.lnst[:, 2:3], c.lnst[:, 2:3], ALU.mult)
                t.drain()
                i = t.tensor_tensor(c.lnst[:, 4:5], c.lnst[:, 3:4], c.lnst[:, 4:5],
                                    ALU.subtract)
            else:
                i = None
            sig(t, i, c.cStat)
        P.on("vector", ev_statf)

        def ev_sqrtf(t):
            wt(t, c.cStat)
            i = t and t.activation(c.lnst[:, 5:6], c.lnst[:, 4:5], AF.Sqrt,
                                   bias=c.cst[:, C_EPS:C_EPS + 1])
            sig(t, i, c.cSqrt)
        P.on("scalar", ev_sqrtf)

        def ev_applyf(t, tt=tt, xb_=xb_):
            wt(t, c.cSqrt)
            if t:
                t.reciprocal(c.lnst[:, 6:7], c.lnst[:, 5:6])
                t.drain()
                t.tensor_scalar(c.scr4k[:], c.fb[:], c.lnst[:, 2:3], c.lnst[:, 6:7],
                                ALU.subtract, ALU.mult)
                i = t.tensor_tensor(xb_[:, 0:D], c.scr4k[:], xb_[:, 0:D], ALU.add)
            else:
                i = None
            sig(t, i, c.cHt)
            sigd(t, c.cXf)
        P.on("vector", ev_applyf)

        def ev_stx2(t, tt=tt, xb_=xb_):
            wt(t, c.cHt)
            i = t and t.dma_start(out=c.x2[tt * 128:(tt + 1) * 128, :], in_=xb_[:, 0:D])
            sig(t, i, c.cOut, 16)
        P.on("sync", ev_stx2)

    def ev_done(t):
        wt(t, c.cOut)
    P.on("sync", ev_done)




def build_phase_cb():
    import contextlib
    nc = bass.Bass("TRN2", target_bir_lowering=False)
    c = types.SimpleNamespace()
    c.x1 = nc.dram_tensor("x1", [TOK, D], F32, kind="ExternalInput").ap()
    c.partials = nc.dram_tensor("partials", [1, 8], F32, kind="ExternalInput").ap()
    c.consts = nc.dram_tensor("consts", [128, NCONST], F32, kind="ExternalInput").ap()
    c.onesr = nc.dram_tensor("onesr", [1, 128], F32, kind="ExternalInput").ap()
    c.identf = nc.dram_tensor("identf", [128, 128], F32, kind="ExternalInput").ap()
    c.wqT = nc.dram_tensor("wqT", [D, 64], F32, kind="ExternalInput").ap()
    c.wrT = nc.dram_tensor("wrT", [64, 48], F32, kind="ExternalInput").ap()
    c.akT = nc.dram_tensor("akT", [64, 1024], F32, kind="ExternalInput").ap()
    c.avt = nc.dram_tensor("avt", [32, 1024, 128], F32, kind="ExternalInput").ap()
    c.wvdT = nc.dram_tensor("wvdT", [D, 256], F32, kind="ExternalInput").ap()
    c.wvuT = nc.dram_tensor("wvuT", [256, D], F32, kind="ExternalInput").ap()
    c.wu1t = nc.dram_tensor("wu1t", [8, D, 128], F32, kind="ExternalInput").ap()
    c.bu1c = nc.dram_tensor("bu1c", [128, 8], F32, kind="ExternalInput").ap()
    c.wu2c = nc.dram_tensor("wu2c", [128, 8], F32, kind="ExternalInput").ap()
    c.scals = nc.dram_tensor("scals", [1, 4], F32, kind="ExternalInput").ap()
    c.logrel = nc.dram_tensor("logrel", [128, 1024], F32, kind="ExternalInput").ap()
    c.x2 = nc.dram_tensor("x2", [TOK, D], F32, kind="ExternalOutput").ap()

    with contextlib.ExitStack() as st:
        sb, ps = (lambda *a: _sb(st, nc, *a)), (lambda *a: _ps(st, nc, *a))
        c.cst = sb("cst", [128, NCONST], F32)
        c.ones1 = sb("ones1", [1, 128], F32)
        c.identb = sb("identb", [128, 128], BF16)
        c.regA = sb("regA", [128, 32 * TOK], BF16)
        c.u1g = sb("u1g", [128, 8 * TOK], BF16)
        c.w_fm = sb("w_fm", [128, 8 * TOK], BF16)
        c.wvuS = sb("wvuS", [128, 2 * D], BF16)
        c.aux2S = sb("aux2S", [128, 2 * TOK], BF16)
        c.qr1 = sb("qr1", [128, TOK], BF16)
        c.query = c.qr1[0:64, :]
        c.rq = c.qr1[64:112, :]
        c.akr = sb("akr", [128, TOK], BF16)
        c.ak_sb = c.akr[0:64, :]
        c.rk = c.akr[64:112, :]
        c.wr_sb = sb("wr_sb", [64, 48], BF16)
        c.xt = sb("xt", [128, D], F32)
        c.scr4k = sb("scr4k", [128, D], F32)
        c.fb = sb("fb", [128, D], F32)
        c.lnst = sb("lnst", [128, 8], F32)
        c.scores = sb("scores", [128, 1024], F32)
        c.attnb = sb("attnb", [128, 1024], F32)
        c.msk = sb("msk", [128, 1024], BF16)
        c.pbuf = sb("pbuf", [128, 1024], F32)
        c.t8 = sb("t8", [128, 8], F32)
        c.wbufC = sb("wbufC", [128, 2 * 4096], BF16)
        c.wvdb = sb("wvdb", [128, 2 * 256], BF16)
        c.logrelS = sb("logrelS", [128, 1024], BF16)
        c.lv = sb("lv", [128, 8], F32)
        c.mstore = sb("mstore", [128, 8], F32)
        c.rstore = sb("rstore", [128, 8], F32)
        c.u2S = sb("u2S", [128, 8], F32)
        c.gateS = sb("gateS", [128, 8], F32)
        c.scget = sb("scget", [128, 4], F32)
        c.scalsS = sb("scalsS", [1, 4], F32)
        c.prow = sb("prow", [1, 8], F32)
        c.gtmp = sb("gtmp", [128, 2 * TOK], BF16)
        c.bu1S = sb("bu1S", [128, 8], F32)
        c.wu2S = sb("wu2S", [128, 8], BF16)
        c.ht = c.u1g[:, 0:4096]
        c.wq_sb = c.wbufC[:, 0:2048]
        c.wtile = c.pbuf[:].bitcast(BF16)[:, 0:1024]
        c.hsl = c.scr4k[:, 0:512]
        c.pS0 = ps("pS0", [128, 512], F32)
        c.pS1 = ps("pS1", [128, 512], F32)
        c.pA = ps("pA", [128, 512], F32)
        c.pU = ps("pU", [128, 512], F32)
        c.pT0 = ps("pT0", [128, 1024], BF16)
        c.pT1 = ps("pT1", [128, 1024], BF16)
        block = st.enter_context(nc.Block())

        P = Prog(nc, st)
        for nm in ("cInit cBc cX cXf cSq cStat cSqrt cHt pT0p pT0f pT1p pT1f cH0 "
                   "cSqH cLv pS0p pS0f pS1p pS1f pAp pAf pUp pUf cW cWf "
                   "cQev cRKev cRQev cSc cMsk cNegm cP cWt cWfm cU1 cGe1 cGe2 cGe3 "
                   "cU2 cGate cA1 cA2 cA3 cVd cVdW cFb cOut").split():
            setattr(c, nm, P.ch(nm))
        emit_cb(P, c)
        P.emit(block)
    return nc


def _prep_cb(inputs, x1_outs, partials):
    consts = _consts_host_c()
    onesr = np.ones((1, 128), np.float32)
    ident = np.eye(128, dtype=np.float32)
    wqT = np.ascontiguousarray(inputs["L0_ffn_Wq"].T)
    wrT = np.ascontiguousarray(inputs["L0_ffn_Wr"].T)
    akT = np.ascontiguousarray(inputs["L0_ffn_aux_keys"].T)
    av = inputs["L0_ffn_aux_values"]
    avt = np.ascontiguousarray(av.reshape(1024, 32, 128).transpose(1, 0, 2))
    wvdT = np.ascontiguousarray(inputs["L0_ffn_Wvd"].T)
    wvuT = np.ascontiguousarray(inputs["L0_ffn_Wvu"].T)
    wu1T = np.ascontiguousarray(inputs["L0_ffn_Wu1"].T)
    wu1t = np.ascontiguousarray(wu1T.reshape(D, 8, 128).transpose(1, 0, 2))
    bu1c = np.ascontiguousarray(inputs["L0_ffn_bu1"].reshape(8, 128).T)
    wu2c = np.ascontiguousarray(inputs["L0_ffn_Wu2"].reshape(8, 128).T)
    scals = np.array([[inputs["L0_ffn_gate_w1"], inputs["L0_ffn_gate_bias"],
                       inputs["L0_ffn_bu2"].reshape(()), 0.0]], np.float32)
    logrel = np.ascontiguousarray(np.tile(inputs["L0_ffn_log_rel"][None, :], (128, 1)))
    parts_row = np.asarray(partials, np.float32).reshape(1, 8)
    in_maps = []
    for core in range(NCORES):
        in_maps.append({
            "x1": np.ascontiguousarray(np.asarray(x1_outs[core])),
            "partials": parts_row,
            "consts": consts, "onesr": onesr, "identf": ident,
            "wqT": wqT, "wrT": wrT, "akT": akT, "avt": avt,
            "wvdT": wvdT, "wvuT": wvuT, "wu1t": wu1t, "bu1c": bu1c,
            "wu2c": wu2c, "scals": scals, "logrel": logrel,
        })
    return in_maps


def run_phase_cb(inputs, x1_outs, partials):
    nc = build_phase_cb()
    in_maps = _prep_cb(inputs, x1_outs, partials)
    res = run_bass_kernel_spmd(nc, in_maps, core_ids=list(range(NCORES)))
    return [r["x2"] for r in res.results]


C_INVSQ128 = 5


def emit_d(P, c):

    KTs = c.regA[0:8, 0:16384]
    QTs = c.regA[0:8, 16384:32768]

    def ev_init(t):
        for dst, src in ((c.cst[:], c.consts), (c.identb[:], c.identf),
                         (c.bkS[:], c.bkc), (c.bvS[:], c.bvc)):
            i = t and t.dma_start(out=dst, in_=src)
            sig(t, i, c.cInit, 16)
        for j in range(4):
            i = t and t.dma_start(out=c.bqS[:], in_=c.bqj[j],
                                  accum_op=(ALU.bypass if j == 0 else ALU.add))
            sig(t, i, c.cInit, 16)
            if t:
                t.wait_ge(c.cInit.sem, c.cInit.n)
    P.on("gpsimd", ev_init)

    def ev_init_v(t):
        wt(t, c.cInit)
        t and t.memset(c.vS[:, 8 * TOK:9 * TOK], 1.0)
    P.on("vector", ev_init_v)
    for e in ("scalar", "tensor", "sync"):
        def ev_i(t):
            wt(t, c.cInit)
        P.on(e, ev_i)

    emit_ln_fm(P, c, c.x2, 8, c.regA)

    for proj in range(3):
        for m in range(8):
            def ev_pw(t, proj=proj, m=m):
                wt(t, c.cWf)
                if proj == 0 and m == 0 and getattr(c, "fused", False):
                    wt(t, c.cH0)
                if proj == 0:
                    for j in range(4):
                        src = c.wqt[m, j].rearrange("(kc p) mm -> p kc mm", p=128)
                        dst = c.wbufD[:, j * D:(j + 1) * D].rearrange(
                            "p (kc mm) -> p kc mm", mm=128)
                        i = t and t.dma_start(out=dst, in_=src)
                        sig(t, i, c.cW, 16)
                else:
                    srcT = (c.wkt if proj == 1 else c.wvt)[m]
                    src = srcT.rearrange("(kc p) mm -> p kc mm", p=128)
                    dst = c.wbufD[:, 0:D].rearrange("p (kc mm) -> p kc mm", mm=128)
                    i = t and t.dma_start(out=dst, in_=src)
                    sig(t, i, c.cW, 16)
            P.on("gpsimd", ev_pw)

            nj = 4 if proj == 0 else 1
            for n in range(2):
                bank, bp, bf = ((c.pS0, c.pS0p, c.pS0f), (c.pS1, c.pS1p, c.pS1f),
                                (c.pA, c.pAp, c.pAf))[(proj * 16 + m * 2 + n) % 3]

                def ev_pmm(t, proj=proj, m=m, n=n, nj=nj, bank=bank, bp=bp, bf=bf):
                    wt(t, c.cW)
                    wt(t, c.cH0)
                    wt(t, bf)
                    i = None
                    if t:
                        first = True
                        for j in range(nj):
                            for kc in range(32):
                                lhs = c.wbufD[:, j * D + kc * 128: j * D + kc * 128 + 128]
                                rhs = c.regA[:, kc * TOK + n * 512: kc * TOK + n * 512 + 512]
                                i = t.matmul(bank[:, 0:512], lhs, rhs,
                                             start=first,
                                             stop=(j == nj - 1 and kc == 31))
                                first = False
                    sig(t, i, bp)
                P.on("tensor", ev_pmm)

                def ev_pev(t, proj=proj, m=m, n=n, bank=bank, bp=bp, bf=bf):
                    wt(t, bp)
                    if proj == 1:
                        wt(t, c.cBnc)
                    bias = (c.bqS, c.bkS, c.bvS)[proj][:, m:m + 1]
                    if proj == 0:
                        dst = c.qgS[:, m * TOK + n * 512: m * TOK + n * 512 + 512]
                    elif proj == 2:
                        dst = c.vS[:, m * TOK + n * 512: m * TOK + n * 512 + 512]
                    else:
                        dst = c.ht[:, n * 512:(n + 1) * 512]
                    i = t and t.activation(dst, bank[:, 0:512], AF.Identity, bias=bias)
                    sig(t, i, c.cPj)
                    sigd(t, bf)
                    if n == 1:
                        sigi(t, c.cWf)
                P.on("scalar", ev_pev)

            if proj == 1:
                def ev_kst(t, m=m):
                    wt(t, c.cPj)
                    src = c.ht[:, 0:1024].rearrange("p (tc dt) -> p tc dt", dt=128)
                    dst = c.KT[m].rearrange("tc e t -> e tc t")
                    i = t and t.dma_start(out=dst, in_=src)
                    sig(t, i, c.cBnc, 16)
                P.on("sync", ev_kst)
            if proj == 0:
                def ev_qst(t, m=m):
                    wt(t, c.cPj)
                    src = c.qgS[:, m * TOK:(m + 1) * TOK].rearrange(
                        "p (tc dt) -> p tc dt", dt=128)
                    dst = c.QT[m].rearrange("tc dd t -> dd tc t")
                    i = t and t.dma_start(out=dst, in_=src)
                    sig(t, i, c.cBnc, 16)
                P.on("sync", ev_qst)

    for tc in range(8):
        par = tc % 2

        def ev_ldkq(t, tc=tc):
            wt(t, c.cBnc)
            if tc == 0:
                wt(t, c.cPj)
            else:
                wt(t, c.pS0f)
                wt(t, c.pS1f)
            i = t and t.dma_start(out=KTs.rearrange(
                "g (e t) -> g e t", t=128), in_=c.KT[:, tc])
            sig(t, i, c.cKQ, 16)
            i = t and t.dma_start(out=QTs.rearrange(
                "g (dd t) -> g dd t", t=128), in_=c.QT[:, tc])
            sig(t, i, c.cKQ, 16)
        P.on("sync", ev_ldkq)

        for pr in range(64):
            dtA, dtB = 2 * pr, 2 * pr + 1
            sb_, sp_, sf_ = ((c.pS0, c.pS0p, c.pS0f), (c.pS1, c.pS1p, c.pS1f),
                            (c.pF0, c.pT0p, c.pT0f), (c.pF1, c.pT1p, c.pT1f))[pr % 4]
            g3 = pr // 16
            ub_, up_, uf_ = ((c.pA, c.pAp, c.pAf), (c.pU, c.pUp, c.pUf))[g3 % 2]
            epar = pr % 4

            def ev_t1(t, dtA=dtA, dtB=dtB, pr=pr, sb_=sb_, sp_=sp_, sf_=sf_):
                wt(t, c.cKQ)
                if pr < 4:
                    wt(t, c.cOtt)
                wt(t, c.cET, lag=3)
                i = None
                if t:
                    t.matmul(sb_[:, 0:128], KTs[:, dtA:16384:128],
                             QTs[:, dtA:16384:128], start=True, stop=True)
                    i = t.matmul(sb_[:, 128:256], KTs[:, dtB:16384:128],
                                 QTs[:, dtB:16384:128], start=True, stop=True)
                sig(t, i, sp_)
            P.on("tensor", ev_t1)

            def ev_exp(t, epar=epar, sb_=sb_, sp_=sp_, sf_=sf_):
                wt(t, sp_)
                wt(t, c.cET2, lag=3)
                i = t and t.activation(c.ET[:, epar * 256:(epar + 1) * 256],
                                       sb_[:, 0:256], AF.Exp,
                                       scale=c.cst[:, C_INVSQ128:C_INVSQ128 + 1])
                sig(t, i, c.cET)
            P.on("scalar", ev_exp)

            def ev_t2(t, pr=pr, tc=tc, epar=epar, ub_=ub_, up_=up_, uf_=uf_):
                wt(t, c.cET)
                if pr % 16 == 0:
                    wt(t, c.cOBv, lag=1)
                i = None
                if t:
                    for q in range(2):
                        dt = 2 * pr + q
                        tglob = tc * 128 + dt
                        tt_ = dt % 32
                        rhs = c.vS[:, tglob:9 * TOK:TOK]
                        i = t.matmul(ub_[:, 9 * tt_:9 * tt_ + 9],
                                     c.ET[:, epar * 256 + q * 128: epar * 256 + q * 128 + 128],
                                     rhs, start=True, stop=True)
                sig(t, i, up_)
                sigi(t, c.cET2)
            P.on("tensor", ev_t2)

            if pr % 16 == 15:
                def ev_uev(t, g3=g3, ub_=ub_, up_=up_, uf_=uf_):
                    wt(t, up_)
                    if g3 == 0:
                        wt(t, c.cOtt)
                    i = None
                    if t:
                        uview = ub_[:, 0:288].rearrange("p (tt n) -> p tt n", n=9)
                        oview = c.OB[:, 0:1024].rearrange("p (g tt) -> p tt g", tt=128)
                        t.tensor_copy(oview[:, 32 * g3:32 * g3 + 32, :], uview[:, :, 0:8])
                        i = t.tensor_copy(c.ZB[:, 32 * g3:32 * g3 + 32], uview[:, :, 8:9])
                    sig(t, i, c.cOBv)
                P.on("vector", ev_uev)

        def ev_div(t, tc=tc):
            wt(t, c.cOBv)
            i = None
            if t:
                t.reciprocal(c.rzB[:], c.ZB[:])
                t.drain()
                for gp in range(8):
                    i = t.tensor_tensor(c.OB[:, gp * 128:(gp + 1) * 128],
                                        c.OB[:, gp * 128:(gp + 1) * 128],
                                        c.rzB[:], ALU.mult)
            sig(t, i, c.cOBd)
        P.on("vector", ev_div)

        for gp in range(8):
            bank, bp2, bf2 = ((c.pT0, c.pT0p, c.pT0f), (c.pT1, c.pT1p, c.pT1f))[gp % 2]

            def ev_otr(t, gp=gp, bank=bank, bp2=bp2, bf2=bf2):
                wt(t, c.cOBd)
                wt(t, c.cOtt, lag=1)
                if t:
                    t.transpose(bank[:, 0:128], c.OB[:, gp * 128:(gp + 1) * 128],
                                c.identb[:])
                    t.drain()
                sigi(t, bp2)
            P.on("tensor", ev_otr)

            eng = "vector" if gp % 2 == 0 else "scalar"

            def ev_oev(t, gp=gp, eng=eng, bank=bank, bp2=bp2, bf2=bf2):
                wt(t, bp2)
                wt(t, c.cXf)
                i = None
                if t:
                    for j in range(4):
                        out = c.ott[:, 4 * gp + j:4096:32]
                        i = (t.tensor_copy(out, bank[:, 0:128]) if eng == "vector"
                             else t.copy(out, bank[:, 0:128]))
                sig(t, i, c.cOtt)
            P.on(eng, ev_oev)

        if DEBUG_DUMP and tc == 0:
            def ev_dbgd(t):
                wt(t, c.cOtt)
                i = t and t.dma_start(out=c.dbgD[0, :, 0:1024], in_=c.qgS[:, 0:1024])
                sig(t, i, c.cOut, 16)
                i = t and t.dma_start(out=c.dbgD[1, :, 0:1024], in_=c.vS[:, 0:1024])
                sig(t, i, c.cOut, 16)
                i = t and t.dma_start(out=c.dbgD[2], in_=c.ott[:])
                sig(t, i, c.cOut, 16)
                i = t and t.dma_start(out=c.dbgD[3, :, 0:1024], in_=c.OB[:])
                sig(t, i, c.cOut, 16)
            P.on("sync", ev_dbgd)

        xb_ = (c.xt, getattr(c, "xt2", c.xt))[tc % 2]

        def ev_ldx(t, tc=tc, xb_=xb_):
            wt(t, c.cXf, lag=1)
            wt(t, c.cOut, lag=16)
            wt(t, c.cPj)
            i = t and t.dma_start(out=xb_[:, 0:D], in_=c.x2[tc * 128:(tc + 1) * 128, :])
            sig(t, i, c.cX, 16)
        P.on("sync", ev_ldx)

        def ev_sqf(t):
            wt(t, c.cOtt)
            wt(t, c.cHt)
            i = t and t.activation(c.scr4k[:], c.ott[:], AF.Square,
                                   accum_out=c.lnst[:, 1:2])
            sig(t, i, c.cSq)
        P.on("scalar", ev_sqf)

        def ev_statf(t):
            wt(t, c.cSq)
            if t:
                t.reduce_sum(c.lnst[:, 0:1], c.ott[:], axis=AX.X)
                t.drain()
                t.tensor_scalar(c.lnst[:, 2:3], c.lnst[:, 0:1],
                                c.cst[:, C_INV_D:C_INV_D + 1], None, ALU.mult)
                t.tensor_scalar(c.lnst[:, 3:4], c.lnst[:, 1:2],
                                c.cst[:, C_INV_D:C_INV_D + 1], None, ALU.mult)
                t.drain()
                t.tensor_tensor(c.lnst[:, 4:5], c.lnst[:, 2:3], c.lnst[:, 2:3], ALU.mult)
                t.drain()
                i = t.tensor_tensor(c.lnst[:, 4:5], c.lnst[:, 3:4], c.lnst[:, 4:5],
                                    ALU.subtract)
            else:
                i = None
            sig(t, i, c.cStat)
        P.on("vector", ev_statf)

        def ev_sqrtf(t):
            wt(t, c.cStat)
            i = t and t.activation(c.lnst[:, 5:6], c.lnst[:, 4:5], AF.Sqrt,
                                   bias=c.cst[:, C_EPS:C_EPS + 1])
            sig(t, i, c.cSqrt)
        P.on("scalar", ev_sqrtf)

        def ev_applyf(t, tc=tc, xb_=xb_):
            wt(t, c.cSqrt)
            wt(t, c.cX)
            if t:
                t.reciprocal(c.lnst[:, 6:7], c.lnst[:, 5:6])
                t.drain()
                t.tensor_scalar(c.scr4k[:], c.ott[:], c.lnst[:, 2:3], c.lnst[:, 6:7],
                                ALU.subtract, ALU.mult)
                i = t.tensor_tensor(xb_[:, 0:D], c.scr4k[:], xb_[:, 0:D], ALU.add)
            else:
                i = None
            sig(t, i, c.cHt)
            sigd(t, c.cXf)
        P.on("vector", ev_applyf)

        def ev_stx3(t, tc=tc, xb_=xb_):
            wt(t, c.cHt)
            i = t and t.dma_start(out=c.x3[tc * 128:(tc + 1) * 128, :], in_=xb_[:, 0:D])
            sig(t, i, c.cOut, 16)
        P.on("sync", ev_stx3)

    def ev_done(t):
        wt(t, c.cOut)
    P.on("sync", ev_done)




def build_phase_d():
    import contextlib
    nc = bass.Bass("TRN2", target_bir_lowering=False)
    c = types.SimpleNamespace()
    c.x2 = nc.dram_tensor("x2", [TOK, D], F32, kind="ExternalInput").ap()
    c.wqt = nc.dram_tensor("wqt", [8, 4, D, 128], F32, kind="ExternalInput").ap()
    c.wkt = nc.dram_tensor("wkt", [8, D, 128], F32, kind="ExternalInput").ap()
    c.wvt = nc.dram_tensor("wvt", [8, D, 128], F32, kind="ExternalInput").ap()
    c.bqj = nc.dram_tensor("bqj", [4, 128, 8], F32, kind="ExternalInput").ap()
    c.bkc = nc.dram_tensor("bkc", [128, 8], F32, kind="ExternalInput").ap()
    c.bvc = nc.dram_tensor("bvc", [128, 8], F32, kind="ExternalInput").ap()
    c.consts = nc.dram_tensor("consts", [128, NCONST], F32, kind="ExternalInput").ap()
    c.identf = nc.dram_tensor("identf", [128, 128], F32, kind="ExternalInput").ap()
    c.x3 = nc.dram_tensor("x3", [TOK, D], F32, kind="ExternalOutput").ap()
    if DEBUG_DUMP:
        c.dbgD = nc.dram_tensor("dbgD", [4, 128, 4096], BF16, kind="ExternalOutput").ap()
    c.KT = nc.dram_tensor("KT", [8, 8, 128, 128], BF16).ap()
    c.QT = nc.dram_tensor("QT", [8, 8, 128, 128], BF16).ap()

    with contextlib.ExitStack() as st:
        sb, ps = (lambda *a: _sb(st, nc, *a)), (lambda *a: _ps(st, nc, *a))
        c.cst = sb("cst", [128, NCONST], F32)
        c.identb = sb("identb", [128, 128], BF16)
        c.regA = sb("regA", [128, 32 * TOK], BF16)
        c.qgS = sb("qgS", [128, 8 * TOK], BF16)
        c.vS = sb("vS", [128, 9 * TOK], BF16)
        c.wbufD = sb("wbufD", [128, 4 * D], BF16)
        c.bqS = sb("bqS", [128, 8], F32)
        c.bkS = sb("bkS", [128, 8], F32)
        c.bvS = sb("bvS", [128, 8], F32)
        c.xt = sb("xt", [128, D], F32)
        c.scr4k = sb("scr4k", [128, D], F32)
        c.lnst = sb("lnst", [128, 8], F32)
        c.ht = sb("ht", [128, D], BF16)
        c.ET = sb("ET", [128, 4 * 256], BF16)
        c.OB = sb("OB", [128, 1024], BF16)
        c.ZB = sb("ZB", [128, 128], F32)
        c.rzB = sb("rzB", [128, 128], F32)
        c.ott = sb("ott", [128, D], BF16)
        c.rz = sb("rz", [128, 4], F32)
        c.pS0 = ps("pS0", [128, 512], F32)
        c.pS1 = ps("pS1", [128, 512], F32)
        c.pA = ps("pA", [128, 512], F32)
        c.pU = ps("pU", [128, 512], F32)
        c.pT0 = ps("pT0", [128, 1024], BF16)
        c.pT1 = ps("pT1", [128, 1024], BF16)
        c.pF0 = c.pT0[:].bitcast(F32)
        c.pF1 = c.pT1[:].bitcast(F32)
        block = st.enter_context(nc.Block())

        P = Prog(nc, st)
        for nm in ("cInit cX cXf cSq cStat cSqrt cHt pT0p pT0f pT1p pT1f cH0 "
                   "cW cWf pS0p pS0f pS1p pS1f pAp pAf pUp pUf cPj cBnc cKQ cKQf "
                   "cET cET2 cOBv cOBd cOtt cOut").split():
            setattr(c, nm, P.ch(nm))
        emit_d(P, c)
        P.emit(block)
    return nc


def _prep_d(inputs, x2_outs):
    consts = _consts_host_c()
    consts[:, C_INVSQ128] = 1.0 / math.sqrt(128.0)
    ident = np.eye(128, dtype=np.float32)
    Wq = inputs["L1_attn_Wq"]
    Wk = inputs["L1_attn_Wk"]
    Wv = inputs["L1_attn_Wv"]
    wqt = np.ascontiguousarray(
        Wq.reshape(D, 128, 8, 4).transpose(2, 3, 0, 1))
    wkt = np.ascontiguousarray(Wk.transpose(2, 0, 1))
    wvt = np.ascontiguousarray(Wv.transpose(2, 0, 1))
    bq = inputs["L1_attn_bq"]
    bqj = np.ascontiguousarray(bq.reshape(128, 8, 4).transpose(2, 0, 1))
    bkc = np.ascontiguousarray(inputs["L1_attn_bk"])
    bvc = np.ascontiguousarray(inputs["L1_attn_bv"])
    in_maps = []
    for core in range(NCORES):
        in_maps.append({
            "x2": np.ascontiguousarray(np.asarray(x2_outs[core])),
            "wqt": wqt, "wkt": wkt, "wvt": wvt,
            "bqj": bqj, "bkc": bkc, "bvc": bvc,
            "consts": consts, "identf": ident,
        })
    return in_maps


def run_phase_d(inputs, x2_outs):
    nc = build_phase_d()
    in_maps = _prep_d(inputs, x2_outs)
    res = run_bass_kernel_spmd(nc, in_maps, core_ids=list(range(NCORES)))
    return [r["x3"] for r in res.results]


WPAD = 2816


def emit_e(P, c):

    def ev_init(t):
        for dst, src in ((c.cst[:], c.consts), (c.identb[:], c.identf),
                         (c.b2S[:], c.b2T)):
            i = t and t.dma_start(out=dst, in_=src)
            sig(t, i, c.cInit, 16)
        for dst, src in ((c.a1S, c.a1T), (c.a3S, c.a3T)):
            i = t and t.dma_start(out=dst[:].rearrange("p (kc m) -> p kc m", m=2),
                                  in_=src.rearrange("(kc p) m -> p kc m", p=128))
            sig(t, i, c.cInit, 16)
    P.on("gpsimd", ev_init)
    for e in ("scalar", "tensor", "sync", "vector"):
        def ev_i(t):
            wt(t, c.cInit)
        P.on(e, ev_i)

    emit_ln_fm(P, c, c.x3, 8, c.regA)

    for which in range(2):
        aS = (c.a1S, c.a3S)[which]
        cx = (c.c1x, c.c3x)[which]
        banks = (((c.pS0, c.pS0p, c.pS0f), (c.pS1, c.pS1p, c.pS1f)),
                 ((c.pA, c.pAp, c.pAf), (c.pU, c.pUp, c.pUf)))[which]

        def ev_cmm(t, aS=aS, banks=banks):
            wt(t, c.cH0)
            for _, _, bf in banks:
                wt(t, bf)
            i = None
            if t:
                for n in range(2):
                    bank = banks[n][0]
                    for kc in range(32):
                        i = t.matmul(bank[0:2, 0:512],
                                     aS[:, kc * 2:(kc + 1) * 2],
                                     c.regA[:, kc * TOK + n * 512: kc * TOK + n * 512 + 512],
                                     start=(kc == 0), stop=(kc == 31))
            sig(t, i, banks[1][1])
        P.on("tensor", ev_cmm)

        def ev_cev(t, cx=cx, banks=banks):
            wt(t, banks[1][1])
            i = None
            if t:
                t.tensor_scalar(cx[:, 0:512], banks[0][0][0:2, 0:512], 0.5, None, ALU.mult)
                i = t.tensor_scalar(cx[:, 512:1024], banks[1][0][0:2, 0:512], 0.5, None, ALU.mult)
            sig(t, i, c.cC13)
            for _, _, bf in banks:
                sigd(t, bf)
        P.on("vector", ev_cev)

    for i_ in range(8):
        def ev_xf(t, i_=i_):
            wt(t, c.cH0)
            if t:
                t.tensor_tensor(c.xftmp[:], c.regA[:, i_ * TOK:(i_ + 1) * TOK],
                                c.regA[:, (8 + i_) * TOK:(9 + i_) * TOK], ALU.add)
                t.tensor_tensor(c.xftmp[:], c.xftmp[:],
                                c.regA[:, (16 + i_) * TOK:(17 + i_) * TOK], ALU.add)
                i = t.tensor_tensor(c.xf[:, i_ * TOK:(i_ + 1) * TOK], c.xftmp[:],
                                    c.regA[:, (24 + i_) * TOK:(25 + i_) * TOK], ALU.add)
            else:
                i = None
            sig(t, i, c.cXff)
        P.on("vector", ev_xf)

    for k in range(22):
        for tj in range(4):
            par = (k * 4 + tj) % 2

            def ev_ew(t, k=k, tj=tj, par=par):
                wt(t, c.cYf, lag=2)
                for wi, (wt_, dst0) in enumerate(((c.w1pt, 0), (c.w3pt, 2048))):
                    src = wt_[k, tj].rearrange("(kc p) m -> p kc m", p=128)
                    dst = c.wbufE[:, par * 1024 + dst0: par * 1024 + dst0 + 1024]
                    dst = dst.rearrange("p (kc m) -> p kc m", m=128)
                    i = t and t.dma_start(out=dst, in_=src)
                    sig(t, i, c.cW, 16)
                i = t and t.dma_start(out=c.b1S[:, par * 128:(par + 1) * 128],
                                      in_=c.b1t[k, tj])
                sig(t, i, c.cW, 16)
                i = t and t.dma_start(out=c.b3S[:, par * 128:(par + 1) * 128],
                                      in_=c.b3t[k, tj])
                sig(t, i, c.cW, 16)
                i = t and t.dma_start(out=c.a2S[:, par * 2:(par + 1) * 2],
                                      in_=c.a2t[k, tj])
                sig(t, i, c.cW, 16)
            P.on("gpsimd", ev_ew)

            for n in range(2):
                def ev_bmm(t, k=k, tj=tj, n=n, par=par):
                    wt(t, c.cW)
                    wt(t, c.cXff)
                    wt(t, c.cC13)
                    wt(t, c.pS0f)
                    wt(t, c.pS1f)
                    i = None
                    if t:
                        for kc in range(8):
                            t.matmul(c.pS0[:, 0:512],
                                     c.wbufE[:, par * 1024 + kc * 128: par * 1024 + kc * 128 + 128],
                                     c.xf[:, kc * TOK + n * 512: kc * TOK + n * 512 + 512],
                                     start=(kc == 0), stop=False)
                        t.matmul(c.pS0[:, 0:512], c.b1S[:, par * 128:(par + 1) * 128],
                                 c.c1x[:, n * 512:(n + 1) * 512], start=False, stop=True)
                        for kc in range(8):
                            t.matmul(c.pS1[:, 0:512],
                                     c.wbufE[:, par * 1024 + 2048 + kc * 128:
                                             par * 1024 + 2048 + kc * 128 + 128],
                                     c.xf[:, kc * TOK + n * 512: kc * TOK + n * 512 + 512],
                                     start=(kc == 0), stop=False)
                        i = t.matmul(c.pS1[:, 0:512], c.b3S[:, par * 128:(par + 1) * 128],
                                     c.c3x[:, n * 512:(n + 1) * 512], start=False, stop=True)
                    sig(t, i, c.pS0p)
                    sigi(t, c.pS1p)
                P.on("tensor", ev_bmm)

                def ev_gu(t, n=n):
                    wt(t, c.pS0p)
                    wt(t, c.pS1p)
                    wt(t, c.cY)
                    if t:
                        t.activation(c.gub[:, 0:512], c.pS0[:, 0:512], AF.Silu)
                        i = t.activation(c.gub[:, 512:1024], c.pS1[:, 0:512], AF.Copy)
                    else:
                        i = None
                    sig(t, i, c.cGU)
                    sigd(t, c.pS0f)
                    sigi(t, c.pS1f)
                P.on("scalar", ev_gu)

                def ev_y(t, k=k, tj=tj, n=n):
                    wt(t, c.cGU)
                    wt(t, c.cYf)
                    i = t and t.tensor_tensor(c.ybuf[:], c.gub[:, 0:512],
                                              c.gub[:, 512:1024], ALU.mult)
                    sig(t, i, c.cY)
                P.on("vector", ev_y)

                def ev_fold(t, k=k, tj=tj, n=n, par=par, pF=(c.pF0, c.pF1)[n]):
                    wt(t, c.cY)
                    if tj == 0:
                        wt(t, (c.pT0f, c.pT1f)[n])
                    i = None
                    if t:
                        t.matmul(pF[:, 0:512], c.identb[:], c.ybuf[:],
                                 start=(tj == 0), stop=(tj == 3))
                        i = t.matmul((c.pA, c.pU)[n][0:2, 0:512],
                                     c.a2S[:, par * 2:(par + 1) * 2], c.ybuf[:],
                                     start=(k == 0 and tj == 0),
                                     stop=(k == 21 and tj == 3))
                    sig(t, i, c.cYf)
                P.on("tensor", ev_fold)

                if tj == 3:
                    def ev_yfev(t, k=k, n=n, pF=(c.pF0, c.pF1)[n]):
                        wt(t, c.cYf)
                        i = t and t.tensor_copy(
                            c.regA[:, k * TOK + n * 512: k * TOK + n * 512 + 512],
                            pF[:, 0:512])
                        sig(t, i, c.cOT)
                        sigd(t, (c.pT0f, c.pT1f)[n])
                    P.on("vector", ev_yfev)

    def ev_c2(t):
        wt(t, c.cYf)
        if t:
            t.tensor_scalar(c.c2x[:, 0:512], c.pA[0:2, 0:512], 0.5, None, ALU.mult)
            i = t.tensor_scalar(c.c2x[:, 512:1024], c.pU[0:2, 0:512], 0.5, None, ALU.mult)
        else:
            i = None
        sig(t, i, c.cC2)
        sigd(t, c.pAf)
        sigi(t, c.pUf)
    P.on("vector", ev_c2)

    if DEBUG_DUMP:
        def ev_dbge(t):
            wt(t, c.cOT)
            wt(t, c.cC2)
            i = t and t.dma_start(out=c.dbgE[0], in_=c.xf[:, 0:1024])
            sig(t, i, c.cOut, 16)
            i = t and t.dma_start(out=c.dbgE[1], in_=c.regA[:, 0:1024])
            sig(t, i, c.cOut, 16)
            i = t and t.dma_start(out=c.dbgE[2, 0:2, :], in_=c.c1x[:, :])
            sig(t, i, c.cOut, 16)
            i = t and t.dma_start(out=c.dbgE[3, 0:2, :], in_=c.c3x[:, :])
            sig(t, i, c.cOut, 16)
            i = t and t.dma_start(out=c.dbgE[4, 0:2, :], in_=c.c2x[:, :])
            sig(t, i, c.cOut, 16)
            i = t and t.dma_start(out=c.dbgE[5], in_=c.regA[:, 21 * TOK:22 * TOK])
            sig(t, i, c.cOut, 16)
        P.on("sync", ev_dbge)

    for dgrp in range(8):
        def ev_w2(t, dgrp=dgrp):
            wt(t, c.cW2f)
            if dgrp == 0 and getattr(c, "fused", False):
                wt(t, c.cXff)
            src = c.w2qTp[:, dgrp * 512:(dgrp + 1) * 512]
            src = src.rearrange("(kc p) m -> p kc m", p=128)
            dst = c.w2buf[:].rearrange("p (kc m) -> p kc m", m=512)
            i = t and t.dma_start(out=dst, in_=src)
            sig(t, i, c.cW2, 16)
        P.on("gpsimd", ev_w2)

        for tile in range(8):
            bank, bp, bf = ((c.pS0, c.pS0p, c.pS0f), (c.pS1, c.pS1p, c.pS1f),
                            (c.pA, c.pAp, c.pAf))[(dgrp * 8 + tile) % 3]

            def ev_omm(t, dgrp=dgrp, tile=tile, bank=bank, bp=bp, bf=bf):
                wt(t, c.cW2)
                wt(t, c.cOT)
                wt(t, c.cC2)
                wt(t, bf)
                i = None
                if t:
                    for kc in range(22):
                        t.matmul(bank[:, 0:512],
                                 c.regA[:, kc * TOK + tile * 128: kc * TOK + tile * 128 + 128],
                                 c.w2buf[:, kc * 512:(kc + 1) * 512],
                                 start=(kc == 0), stop=False)
                    i = t.matmul(bank[:, 0:512], c.c2x[:, tile * 128:(tile + 1) * 128],
                                 c.b2S[:, dgrp * 512:(dgrp + 1) * 512],
                                 start=False, stop=True)
                sig(t, i, bp)
            P.on("tensor", ev_omm)

            def ev_oev(t, dgrp=dgrp, tile=tile, bank=bank, bp=bp, bf=bf):
                wt(t, bp)
                if dgrp == 0 and tile == 0:
                    wt(t, c.cYf)
                i = t and t.tensor_copy(
                    c.regB[:, tile * 4096 + dgrp * 512: tile * 4096 + dgrp * 512 + 512],
                    bank[:, 0:512])
                sig(t, i, c.cGU)
                sigd(t, bf)
                if tile == 7:
                    sigi(t, c.cW2f)
            P.on("vector", ev_oev)

    for tile in range(8):
        xb_ = (c.xt, getattr(c, "xt2", c.xt))[tile % 2]

        def ev_ldx(t, tile=tile, xb_=xb_):
            wt(t, c.cXf, lag=1)
            wt(t, c.cOut, lag=16)
            wt(t, c.cGU)
            i = t and t.dma_start(out=xb_[:, 0:D], in_=c.x3[tile * 128:(tile + 1) * 128, :])
            sig(t, i, c.cX, 16)
        P.on("sync", ev_ldx)

        def ev_sqf(t, tile=tile):
            wt(t, c.cGU)
            wt(t, c.cHt)
            ot = c.regB[:, tile * 4096:(tile + 1) * 4096]
            i = t and t.activation(c.scr4k[:], ot, AF.Square,
                                   accum_out=c.lnst[:, 1:2])
            sig(t, i, c.cSq)
        P.on("scalar", ev_sqf)

        def ev_statf(t, tile=tile):
            wt(t, c.cSq)
            ot = c.regB[:, tile * 4096:(tile + 1) * 4096]
            if t:
                t.reduce_sum(c.lnst[:, 0:1], ot, axis=AX.X)
                t.drain()
                t.tensor_scalar(c.lnst[:, 2:3], c.lnst[:, 0:1],
                                c.cst[:, C_INV_D:C_INV_D + 1], None, ALU.mult)
                t.tensor_scalar(c.lnst[:, 3:4], c.lnst[:, 1:2],
                                c.cst[:, C_INV_D:C_INV_D + 1], None, ALU.mult)
                t.drain()
                t.tensor_tensor(c.lnst[:, 4:5], c.lnst[:, 2:3], c.lnst[:, 2:3], ALU.mult)
                t.drain()
                i = t.tensor_tensor(c.lnst[:, 4:5], c.lnst[:, 3:4], c.lnst[:, 4:5],
                                    ALU.subtract)
            else:
                i = None
            sig(t, i, c.cStat)
        P.on("vector", ev_statf)

        def ev_sqrtf(t):
            wt(t, c.cStat)
            i = t and t.activation(c.lnst[:, 5:6], c.lnst[:, 4:5], AF.Sqrt,
                                   bias=c.cst[:, C_EPS:C_EPS + 1])
            sig(t, i, c.cSqrt)
        P.on("scalar", ev_sqrtf)

        def ev_applyf(t, tile=tile, xb_=xb_):
            wt(t, c.cSqrt)
            wt(t, c.cX)
            ot = c.regB[:, tile * 4096:(tile + 1) * 4096]
            if t:
                t.reciprocal(c.lnst[:, 6:7], c.lnst[:, 5:6])
                t.drain()
                t.tensor_scalar(c.scr4k[:], ot, c.lnst[:, 2:3], c.lnst[:, 6:7],
                                ALU.subtract, ALU.mult)
                i = t.tensor_tensor(xb_[:, 0:D], c.scr4k[:], xb_[:, 0:D], ALU.add)
            else:
                i = None
            sig(t, i, c.cHt)
            sigd(t, c.cXf)
        P.on("vector", ev_applyf)

        def ev_st(t, tile=tile, xb_=xb_):
            wt(t, c.cHt)
            i = t and t.dma_start(out=c.xout[tile * 128:(tile + 1) * 128, :],
                                  in_=xb_[:, 0:D])
            sig(t, i, c.cOut, 16)
        P.on("sync", ev_st)

    def ev_done(t):
        wt(t, c.cOut)
    P.on("sync", ev_done)




def build_phase_e():
    import contextlib
    nc = bass.Bass("TRN2", target_bir_lowering=False)
    c = types.SimpleNamespace()
    c.x3 = nc.dram_tensor("x3", [TOK, D], F32, kind="ExternalInput").ap()
    c.w1pt = nc.dram_tensor("w1pt", [22, 4, 1024, 128], F32, kind="ExternalInput").ap()
    c.w3pt = nc.dram_tensor("w3pt", [22, 4, 1024, 128], F32, kind="ExternalInput").ap()
    c.b1t = nc.dram_tensor("b1t", [22, 4, 2, 128], F32, kind="ExternalInput").ap()
    c.b3t = nc.dram_tensor("b3t", [22, 4, 2, 128], F32, kind="ExternalInput").ap()
    c.a2t = nc.dram_tensor("a2t", [22, 4, 128, 2], F32, kind="ExternalInput").ap()
    c.a1T = nc.dram_tensor("a1T", [D, 2], F32, kind="ExternalInput").ap()
    c.a3T = nc.dram_tensor("a3T", [D, 2], F32, kind="ExternalInput").ap()
    c.w2qTp = nc.dram_tensor("w2qTp", [WPAD, D], F32, kind="ExternalInput").ap()
    c.b2T = nc.dram_tensor("b2T", [2, D], F32, kind="ExternalInput").ap()
    c.consts = nc.dram_tensor("consts", [128, NCONST], F32, kind="ExternalInput").ap()
    c.identf = nc.dram_tensor("identf", [128, 128], F32, kind="ExternalInput").ap()
    c.xout = nc.dram_tensor("xout", [TOK, D], F32, kind="ExternalOutput").ap()
    if DEBUG_DUMP:
        c.dbgE = nc.dram_tensor("dbgE", [6, 128, 1024], BF16, kind="ExternalOutput").ap()

    with contextlib.ExitStack() as st:
        sb, ps = (lambda *a: _sb(st, nc, *a)), (lambda *a: _ps(st, nc, *a))
        c.cst = sb("cst", [128, NCONST], F32)
        c.identb = sb("identb", [128, 128], BF16)
        c.regA = sb("regA", [128, 32 * TOK], BF16)
        c.regB = sb("regB", [128, 32 * TOK], BF16)
        c.xt = sb("xt", [128, D], F32)
        c.scr4k = sb("scr4k", [128, D], F32)
        c.lnst = sb("lnst", [128, 8], F32)
        c.wbufE = sb("wbufE", [128, 4 * 1024], BF16)
        c.w2buf = sb("w2buf", [128, 22 * 512], BF16)
        c.a1S = sb("a1S", [128, 64], BF16)
        c.a3S = sb("a3S", [128, 64], BF16)
        c.b1S = sb("b1S", [2, 2 * 128], BF16)
        c.b3S = sb("b3S", [2, 2 * 128], BF16)
        c.a2S = sb("a2S", [128, 2 * 2], BF16)
        c.b2S = sb("b2S", [2, D], BF16)
        c.c1x = sb("c1x", [2, TOK], BF16)
        c.c3x = sb("c3x", [2, TOK], BF16)
        c.c2x = sb("c2x", [2, TOK], BF16)
        c.pS0 = ps("pS0", [128, 512], F32)
        c.pS1 = ps("pS1", [128, 512], F32)
        c.pA = ps("pA", [128, 512], F32)
        c.pU = ps("pU", [128, 512], F32)
        c.pT0 = ps("pT0", [128, 1024], BF16)
        c.pT1 = ps("pT1", [128, 1024], BF16)
        c.xf = c.regB[:, 0:8192]
        c.ht = c.regB[:, 8192:12288]
        c.xftmp = c.scr4k[:, 0:1024]
        c.ybuf = c.scr4k[:, 1024:1280].bitcast(BF16)
        c.gub = c.scr4k[:, 1280:1792].bitcast(BF16)
        c.pF0 = c.pT0.ap().bitcast(F32) if False else None
        block = st.enter_context(nc.Block())
        c.pF0 = c.pT0[:].bitcast(F32)
        c.pF1 = c.pT1[:].bitcast(F32)

        P = Prog(nc, st)
        for nm in ("cInit cX cXf cSq cStat cSqrt cHt pT0p pT0f pT1p pT1f cH0 "
                   "cW cWf cW2 cW2f pS0p pS0f pS1p pS1f pAp pAf pUp pUf "
                   "cC13 cXff cGU cY cYf cC2 cOT cOut").split():
            setattr(c, nm, P.ch(nm))
        emit_e(P, c)
        P.emit(block)
    return nc


def _prep_e(inputs, x3_outs):
    consts = _consts_host_c()
    ident = np.eye(128, dtype=np.float32)

    def padw(w):
        w4 = w.reshape(4, 2752, -1)
        out = np.zeros((4, WPAD, w4.shape[2]), np.float32)
        out[:, :2752] = w4
        return out

    w1 = padw(inputs["L1_ffn_w1q"])
    w3 = padw(inputs["L1_ffn_w3q"])
    w1pt = np.ascontiguousarray(
        w1.reshape(4, 22, 128, 1024).transpose(1, 0, 3, 2))
    w3pt = np.ascontiguousarray(
        w3.reshape(4, 22, 128, 1024).transpose(1, 0, 3, 2))
    b1 = padw(inputs["L1_ffn_b1"])
    b3 = padw(inputs["L1_ffn_b3"])
    b1t = np.ascontiguousarray(b1.reshape(4, 22, 128, 2).transpose(1, 0, 3, 2))
    b3t = np.ascontiguousarray(b3.reshape(4, 22, 128, 2).transpose(1, 0, 3, 2))
    a2 = padw(np.ascontiguousarray(inputs["L1_ffn_a2"].T))
    a2t = np.ascontiguousarray(a2.reshape(4, 22, 128, 2).transpose(1, 0, 2, 3))
    a1T = np.ascontiguousarray(inputs["L1_ffn_a1"].T)
    a3T = np.ascontiguousarray(inputs["L1_ffn_a3"].T)
    w2qT = np.ascontiguousarray(inputs["L1_ffn_w2q"].T)
    w2qTp = np.zeros((WPAD, D), np.float32)
    w2qTp[:2752] = w2qT
    b2T = np.ascontiguousarray(inputs["L1_ffn_b2"].T)
    in_maps = []
    for core in range(NCORES):
        in_maps.append({
            "x3": np.ascontiguousarray(np.asarray(x3_outs[core])),
            "w1pt": w1pt, "w3pt": w3pt, "b1t": b1t, "b3t": b3t, "a2t": a2t,
            "a1T": a1T, "a3T": a3T, "w2qTp": w2qTp, "b2T": b2T,
            "consts": consts, "identf": ident,
        })
    return in_maps


def run_phase_e(inputs, x3_outs):
    nc = build_phase_e()
    in_maps = _prep_e(inputs, x3_outs)
    res = run_bass_kernel_spmd(nc, in_maps, core_ids=list(range(NCORES)))
    return [r["xout"] for r in res.results]



POOLC = 56832


class Carve:
    def __init__(self, pool):
        self.pool = pool
        self.off = 0

    def take(self, ncols, dtype=BF16):
        units = ncols * (2 if dtype == F32 else 1)
        sl = self.pool[:, self.off:self.off + units]
        self.off += (units + 15) // 16 * 16
        assert self.off <= POOLC, f"pool overflow {self.off}"
        return sl.bitcast(F32) if dtype == F32 else sl

    def at(self, off_units, ncols, dtype=BF16):
        units = ncols * (2 if dtype == F32 else 1)
        sl = self.pool[:, off_units:off_units + units]
        return sl.bitcast(F32) if dtype == F32 else sl


def _alias_a(c, pool):
    cv = Carve(pool)
    c.wbuf = cv.take(24576)
    c.cbuf = cv.take(1536)
    c.qf = cv.take(1024)
    c.kf = cv.take(1024)
    c.vf = cv.take(1024)
    c.stg = cv.take(2048)
    c.STb = cv.take(512)
    c.ckkd = cv.take(256)
    c.cvt = cv.take(256)
    c.cqd = cv.take(256)
    c.kvs = cv.take(2048, F32)
    c.kvb = cv.take(2048)
    c.kdS = cv.take(32, F32)
    c.bdS = cv.take(16, F32)
    c.ht = cv.take(4096)
    c.stgM = cv.take(2048)
    c.gmS = cv.take(2, F32)
    c.xt2 = cv.take(4096, F32)
    c.h0 = c.regA


def _alias_b(c, pool):
    cv = Carve(pool)
    c.regB = cv.at(0, 32768)
    c.ht = cv.at(0, 4096)
    c.xt2 = cv.at(36864, 4096, F32)
    cv.off = 32768
    c.wb = cv.take(8192)
    c.ft = cv.at(32768, 4096)
    c.abuf = cv.take(2048)
    c.sqb = cv.take(1024)
    c.rstdb = cv.take(1024, F32)
    c.rrow = cv.take(1024, F32)[0:1, :]
    c.rrow2 = cv.take(1024, F32)[0:1, :]
    c.rrowb = cv.take(1024)[0:1, :]
    c.normwS = cv.take(32, F32)
    c.onesB = cv.take(128)


def _alias_cb(c, pool):
    cv = Carve(pool)
    c.u1g = cv.take(8192)
    c.ht = cv.at(0, 4096)
    c.xt2 = cv.at(8192, 4096, F32)
    c.w_fm = cv.take(8192)
    c.wbufC = cv.take(8192)
    c.fb = cv.at(16384, 4096, F32)
    c.wvuS = cv.take(8192)
    c.scores = cv.take(1024, F32)
    c.attnb = cv.take(1024, F32)
    c.pbuf = cv.take(1024, F32)
    c.wtile = c.pbuf[:].bitcast(BF16)[:, 0:1024]
    c.msk = cv.take(1024)
    c.gtmp = cv.take(2048)
    c.aux2S = cv.take(2048)
    c.qr1 = cv.take(1024)
    c.query = c.qr1[0:64, :]
    c.rq = c.qr1[64:112, :]
    c.akr = cv.take(1024)
    c.ak_sb = c.akr[0:64, :]
    c.rk = c.akr[64:112, :]
    c.logrelS = cv.take(1024)
    c.wvdb = cv.take(512)
    c.wr_sb = cv.take(48)[0:64, :]
    c.wq_sb = c.wbufC[:, 0:2048]
    c.lv = cv.take(8, F32)
    c.mstore = cv.take(8, F32)
    c.rstore = cv.take(8, F32)
    c.u2S = cv.take(8, F32)
    c.gateS = cv.take(8, F32)
    c.scget = cv.take(4, F32)
    c.scalsS = cv.take(4, F32)[0:1, :]
    c.prow = cv.take(8, F32)[0:1, :]
    c.t8 = cv.take(8, F32)
    c.bu1S = cv.take(8, F32)
    c.wu2S = cv.take(8)
    c.hsl = c.scr4k[:, 0:512]


def _alias_d(c, pool):
    cv = Carve(pool)
    c.qgS = cv.take(8192)
    c.ht = cv.at(0, 4096)
    c.vS = cv.take(9216)
    c.wbufD = cv.take(16384)
    c.xt2 = cv.at(17408, 4096, F32)
    c.ott = cv.take(4096)
    c.OB = cv.take(1024)
    c.ET = cv.take(1024)
    c.rz = cv.take(4, F32)
    c.ZB = cv.take(128, F32)
    c.rzB = cv.take(128, F32)
    c.bqS = cv.take(8, F32)
    c.bkS = cv.take(8, F32)
    c.bvS = cv.take(8, F32)


def _alias_e(c, pool):
    cv = Carve(pool)
    c.regB = cv.at(0, 32768)
    c.xf = cv.at(0, 8192)
    c.ht = cv.at(8192, 4096)
    cv.off = 32768
    c.wbufE = cv.take(4096)
    c.w2buf = cv.take(11264)
    c.xt2 = cv.at(36864, 4096, F32)
    c.a1S = cv.take(64)
    c.a3S = cv.take(64)
    c.b1S = cv.take(256)[0:2, :]
    c.b3S = cv.take(256)[0:2, :]
    c.a2S = cv.take(4)
    c.b2S = cv.take(4096)[0:2, :]
    c.c1x = cv.take(1024)[0:2, :]
    c.c3x = cv.take(1024)[0:2, :]
    c.c2x = cv.take(1024)[0:2, :]
    c.xftmp = c.scr4k[:, 0:1024]
    c.ybuf = c.scr4k[:, 1024:1280].bitcast(BF16)
    c.gub = c.scr4k[:, 1280:1792].bitcast(BF16)


FUSED_CHANS = (
    "cInit cBc cX cXf cSq cStat cSqrt cHt pT0p pT0f pT1p pT1f pV0p pV0f pV1p pV1f "
    "cH0 cH0f cW cWf cC cCf pS0p pS0f pS1p pS1f pAp pAf pUp pUf cQKV cST cCK cCV "
    "cCQD cKV cStgV cOut cGate cA cAf cSqB cRr cRstd cRb cRbb cProd cOT cFt "
    "cSqH cLv cQev cRKev cRQev cSc cMsk cNegm cP cWt cWfm cU1 cGe1 cGe2 cGe3 "
    "cU2 cA1 cA2 cA3 cVd cVdW cFb cPj cBnc cKQ cKQf cET cOBv cOtt cW2 cW2f "
    "cC13 cXff cGU cY cYf cC2 cPt cAR cAG cBar cStgM pRp pRf cET2 cOBd")


def emit_barrier(P, c):
    for e in ("vector", "scalar", "tensor"):
        def ev_s(t):
            sigd(t, c.cBar)
        P.on(e, ev_s)

    def ev_sync(t):
        wt(t, c.cOut)
        sigi(t, c.cBar)
    P.on("sync", ev_sync)

    def ev_g(t):
        sigi(t, c.cBar)
    P.on("gpsimd", ev_g)
    for e in Prog.ENGINES:
        def ev_w(t):
            wt(t, c.cBar)
        P.on(e, ev_w)


def build_fused():
    import contextlib
    from concourse.aot_env import allow_ici_all_to_all
    nc = bass.Bass("TRN2", target_bir_lowering=False)
    c = types.SimpleNamespace()
    c.fused = True
    dt = nc.dram_tensor
    c.xb = dt("xb", [S, D], F32, kind="ExternalInput").ap()
    c.xown = dt("xown", [TOK, D], F32, kind="ExternalInput").ap()
    c.qkvT = dt("qkvT", [D, HL * 384], F32, kind="ExternalInput").ap()
    c.dcb = dt("dcb", [HL, 128, 768], F32, kind="ExternalInput").ap()
    c.kdall = dt("kdall", [128, 2 * HL], F32, kind="ExternalInput").ap()
    c.bdall = dt("bdall", [128, HL], F32, kind="ExternalInput").ap()
    c.identf = dt("identf", [128, 128], F32, kind="ExternalInput").ap()
    c.consts = dt("consts", [128, NCONST], F32, kind="ExternalInput").ap()
    c.gateTt = dt("gateTt", [32, D, 128], F32, kind="ExternalInput").ap()
    c.outTt = dt("outTt", [32, D, 128], F32, kind="ExternalInput").ap()
    c.normw = dt("normw", [128, 32], F32, kind="ExternalInput").ap()
    c.onesr = dt("onesr", [1, 128], F32, kind="ExternalInput").ap()
    c.wqT = dt("wqT", [D, 64], F32, kind="ExternalInput").ap()
    c.wrT = dt("wrT", [64, 48], F32, kind="ExternalInput").ap()
    c.akT = dt("akT", [64, 1024], F32, kind="ExternalInput").ap()
    c.avt = dt("avt", [32, 1024, 128], F32, kind="ExternalInput").ap()
    c.wvdT = dt("wvdT", [D, 256], F32, kind="ExternalInput").ap()
    c.wvuT = dt("wvuT", [256, D], F32, kind="ExternalInput").ap()
    c.wu1t = dt("wu1t", [8, D, 128], F32, kind="ExternalInput").ap()
    c.bu1c = dt("bu1c", [128, 8], F32, kind="ExternalInput").ap()
    c.wu2c = dt("wu2c", [128, 8], F32, kind="ExternalInput").ap()
    c.scals = dt("scals", [1, 4], F32, kind="ExternalInput").ap()
    c.logrel = dt("logrel", [128, 1024], F32, kind="ExternalInput").ap()
    c.wqt = dt("wqt", [8, 4, D, 128], F32, kind="ExternalInput").ap()
    c.wkt = dt("wkt", [8, D, 128], F32, kind="ExternalInput").ap()
    c.wvt = dt("wvt", [8, D, 128], F32, kind="ExternalInput").ap()
    c.bqj = dt("bqj", [4, 128, 8], F32, kind="ExternalInput").ap()
    c.bkc = dt("bkc", [128, 8], F32, kind="ExternalInput").ap()
    c.bvc = dt("bvc", [128, 8], F32, kind="ExternalInput").ap()
    c.w1pt = dt("w1pt", [22, 4, 1024, 128], F32, kind="ExternalInput").ap()
    c.w3pt = dt("w3pt", [22, 4, 1024, 128], F32, kind="ExternalInput").ap()
    c.b1t = dt("b1t", [22, 4, 2, 128], F32, kind="ExternalInput").ap()
    c.b3t = dt("b3t", [22, 4, 2, 128], F32, kind="ExternalInput").ap()
    c.a2t = dt("a2t", [22, 4, 128, 2], F32, kind="ExternalInput").ap()
    c.a1T = dt("a1T", [D, 2], F32, kind="ExternalInput").ap()
    c.a3T = dt("a3T", [D, 2], F32, kind="ExternalInput").ap()
    c.w2qTp = dt("w2qTp", [WPAD, D], F32, kind="ExternalInput").ap()
    c.b2T = dt("b2T", [2, D], F32, kind="ExternalInput").ap()
    c.xout = dt("xout", [TOK, D], F32, kind="ExternalOutput").ap()
    c.gmask = dt("gmask", [128, 2], F32, kind="ExternalInput").ap()
    ikind = "ExternalOutput" if FUSED_DEBUG else "Internal"
    c.rsin = dt("rsin", [2 * D, 1024], BF16).ap()
    c.afm_rs = dt("afm_rs", [D, 1024], BF16).ap()
    if FUSED_DEBUG:
        c.dbgafm = dt("dbgafm", [D, 1024], BF16, kind="ExternalOutput").ap()
        c.dbg2 = dt("dbg2", [128, 1074], F32, kind="ExternalOutput").ap()
        c.dbg3 = dt("dbg3", [256, 1024], BF16, kind="ExternalOutput").ap()
        c.dbg4 = dt("dbg4", [128, 1024], BF16, kind="ExternalOutput").ap()
    c.afm = c.afm_rs
    c.x1 = dt("x1", [TOK, D], F32, kind=ikind).ap()
    c.x2 = dt("x2", [TOK, D], F32, kind=ikind).ap()
    c.x3 = dt("x3", [TOK, D], F32, kind=ikind).ap()
    c.partA = dt("partA", [1, 1], F32).ap()
    c.partR = dt("partR", [1, 1], F32).ap()
    c.KT = dt("KT", [8, 8, 128, 128], BF16).ap()
    c.QT = dt("QT", [8, 8, 128, 128], BF16).ap()

    with contextlib.ExitStack() as st:
        sb, ps = (lambda *a: _sb(st, nc, *a)), (lambda *a: _ps(st, nc, *a))
        c.cst = sb("cst", [128, NCONST], F32)
        c.identb = sb("identb", [128, 128], BF16)
        c.ones1 = sb("ones1", [1, 128], F32)
        c.regA = sb("regA", [128, 32768], BF16)
        c.pool = sb("pool", [128, POOLC], BF16)
        c.xt = sb("xt", [128, D], F32)
        c.scr4k = sb("scr4k", [128, D], F32)
        c.lnst = sb("lnst", [128, 8], F32)
        c.pS0 = ps("pS0", [128, 512], F32)
        c.pS1 = ps("pS1", [128, 512], F32)
        c.pA = ps("pA", [128, 512], F32)
        c.pU = ps("pU", [128, 512], F32)
        c.pT0 = ps("pT0", [128, 1024], BF16)
        c.pT1 = ps("pT1", [128, 1024], BF16)
        c.pV0 = ps("pV0", [128, 1024], BF16)
        c.pV1 = ps("pV1", [128, 1024], BF16)
        c.pR = c.pV0[:].bitcast(F32)[:, 0:512]
        c.pF0 = c.pT0[:].bitcast(F32)
        c.pF1 = c.pT1[:].bitcast(F32)
        block = st.enter_context(nc.Block())

        P = Prog(nc, st)
        for nm in FUSED_CHANS.split():
            setattr(c, nm, P.ch(nm))

        def ev_ones(t):
            i = t and t.dma_start(out=c.ones1[:], in_=c.onesr)
            sig(t, i, c.cInit, 16)
        P.on("gpsimd", ev_ones)

        cA = types.SimpleNamespace(**vars(c))
        _alias_a(cA, c.pool)
        emit_a(P, cA)
        emit_barrier(P, c)

        def ev_rs(t):
            i = t and t.collective_compute(
                "ReduceScatter", mybir.AluOpType.add,
                replica_groups=[[0, 1], [2, 3], [4, 5], [6, 7]],
                ins=[c.rsin], outs=[c.afm_rs])
            sig(t, i, c.cAG)
        P.on("gpsimd", ev_rs)

        if FUSED_DEBUG:
            def ev_dbgafm(t):
                wt(t, c.cAG)
                i = t and t.dma_start(out=c.dbgafm, in_=c.afm_rs)
                sig(t, i, c.cOut, 16)
                i = t and t.dma_start(out=c.dbg2[0:128, 0:2], in_=cA.gmS[:])
                sig(t, i, c.cOut, 16)
                i = t and t.dma_start(out=c.dbg2[0:128, 2:34], in_=cA.kdS[:])
                sig(t, i, c.cOut, 16)
                i = t and t.dma_start(out=c.dbg2[0:128, 34:50], in_=cA.bdS[:])
                sig(t, i, c.cOut, 16)
                i = t and t.dma_start(out=c.dbg2[0:128, 50:562],
                                      in_=cA.stgM[:, 0:1024].bitcast(F32)[:, 0:512])
                sig(t, i, c.cOut, 16)
                i = t and t.dma_start(out=c.dbg3[0:128, :], in_=c.rsin[0:128, :])
                sig(t, i, c.cOut, 16)
                i = t and t.dma_start(out=c.dbg3[128:256, :], in_=c.rsin[2048:2176, :])
                sig(t, i, c.cOut, 16)
            P.on("sync", ev_dbgafm)

        cB = types.SimpleNamespace(**vars(c))
        _alias_b(cB, c.pool)
        emit_b(P, cB)
        emit_barrier(P, c)
        cC = types.SimpleNamespace(**vars(c))
        _alias_cb(cC, c.pool)
        emit_cb(P, cC)
        emit_barrier(P, c)
        cD = types.SimpleNamespace(**vars(c))
        _alias_d(cD, c.pool)
        emit_d(P, cD)
        emit_barrier(P, c)
        cE = types.SimpleNamespace(**vars(c))
        _alias_e(cE, c.pool)
        emit_e(P, cE)

        with allow_ici_all_to_all():
            P.emit(block)
    return nc


def _prep_fused(inputs):
    consts = _consts_host_c()
    consts[:, C_INVSQ128] = 1.0 / math.sqrt(128.0)
    maps_a = _prep_a(inputs)
    dummy_att = [np.zeros((HL, 128, S), BF) for _ in range(NCORES)]
    maps_b = _prep_b(inputs, dummy_att)
    maps_cb = _prep_cb(inputs, [np.zeros((TOK, D), np.float32)] * NCORES,
                       np.zeros(8, np.float32))
    maps_d = _prep_d(inputs, [np.zeros((TOK, D), np.float32)] * NCORES)
    maps_e = _prep_e(inputs, [np.zeros((TOK, D), np.float32)] * NCORES)
    onesr = np.ones((1, 128), np.float32)
    in_maps = []
    for core in range(NCORES):
        m = {}
        for srcm, drop in ((maps_a[core], ("attn_fm",)),
                           (maps_b[core], ("afm",)),
                           (maps_cb[core], ("x1", "partials")),
                           (maps_d[core], ("x2",)),
                           (maps_e[core], ("x3",))):
            for k, v in srcm.items():
                if k not in drop:
                    m[k] = v
        m["consts"] = consts
        m["onesr"] = onesr
        gm = np.zeros((128, 2), np.float32)
        gm[:, core % 2] = 1.0
        m["gmask"] = gm
        in_maps.append(m)
    return in_maps


def kernel_fused(inputs):
    nc = build_fused()
    in_maps = _prep_fused(inputs)
    res = run_bass_kernel_spmd(nc, in_maps, core_ids=list(range(NCORES)))
    out = np.zeros((B, S, D), np.float32)
    for core in range(NCORES):
        bb, g = core // 2, core % 2
        out[bb, g * TOK:(g + 1) * TOK] = np.asarray(res.results[core]["xout"])
    return out


# version 10
# speedup vs baseline: 1.0362x; 1.0362x over previous
import math
import types
import numpy as np
import ml_dtypes
import concourse.bass as bass
import concourse.mybir as mybir
from concourse.bass_utils import run_bass_kernel_spmd

F32 = mybir.dt.float32
BF16 = mybir.dt.bfloat16
AF = mybir.ActivationFunctionType
ALU = mybir.AluOpType
AX = mybir.AxisListType
BF = ml_dtypes.bfloat16

NCORES = 8
B, S, D = 4, 2048, 4096
HEADS, HD, LBLK = 32, 128, 256
HL = 16
TOK = 1024
NB_SH = 4
DEBUG_DUMP = False
FUSED_DEBUG = False


def _decays():
    base = 1.0 / 2.0 ** (8.0 / HEADS)
    exponent = np.arange(HEADS, dtype=np.float64) + 1.0
    factor = 1.0 - 1 / (32 - 1 + 1e-05) + 1e-05
    slope = (base ** exponent * factor).astype(np.float32)[:, None, None].astype(np.float32)
    r = np.arange(LBLK, dtype=np.float32) + 1.0
    q_decay = np.exp(-slope * r[:, None]).astype(np.float32)[:, :, 0]
    k_decay = np.exp(-slope * (LBLK - r)[:, None]).astype(np.float32)[:, :, 0]
    diff = r[:, None] - r[None, :]
    diag = slope * diff[None]
    diag = np.where(diag >= 0, -diag, -np.inf)
    diag_decay = np.exp(diag).astype(np.float32)
    block_decay = np.exp(-slope * LBLK).astype(np.float32)[:, 0, 0]
    return q_decay, k_decay, diag_decay, block_decay


C_EPS, C_INV_D, C_ONE, C_ZERO = 0, 1, 2, 3
NCONST = 8

def _consts_host():
    c = np.zeros(NCONST, np.float32)
    c[C_EPS] = 1e-5
    c[C_INV_D] = 1.0 / D
    c[C_ONE] = 1.0
    c[C_ZERO] = 0.0
    return np.tile(c[None, :], (128, 1))



class Ch:
    __slots__ = ("sem", "n")

    def __init__(self, sem):
        self.sem = sem
        self.n = 0


class Prog:

    ENGINES = ("gpsimd", "sync", "tensor", "vector", "scalar")

    def __init__(self, nc, ctxstack):
        self.nc = nc
        self.events = []
        self.chans = []
        self._stack = ctxstack

    def ch(self, name):
        sem = self._stack.enter_context(self.nc.semaphore(name))
        c = Ch(sem)
        self.chans.append(c)
        return c

    def on(self, eng_name, fn):
        assert eng_name in self.ENGINES
        self.events.append((eng_name, fn))

    def emit(self, block):
        for name in self.ENGINES:
            deco = getattr(block, name)

            def body(eng, name=name):
                for c in self.chans:
                    c.n = 0
                for (e, fn) in self.events:
                    fn(eng if e == name else None)

            deco(body)


def sig(t, inst, ch, k=1):
    ch.n += k
    if t is not None and inst is not None:
        inst.then_inc(ch.sem, k)


def sigi(t, ch, k=1):
    ch.n += k
    if t is not None:
        t.sem_inc(ch.sem, k)


def sigd(t, ch, k=1):
    ch.n += k
    if t is not None:
        t.drain()
        t.sem_inc(ch.sem, k)


def wt(t, ch, lag=0):
    if t is not None:
        v = ch.n - lag
        if v > 0:
            t.wait_ge(ch.sem, v)


def _sb(stack, nc, name, shape, dtype):
    return stack.enter_context(nc.sbuf_tensor(name, shape, dtype))


def _ps(stack, nc, name, shape, dtype):
    return stack.enter_context(nc.psum_tensor(name, shape, dtype))



def emit_a(P, c):

    def ev_init(t):
        i = t and t.dma_start(out=c.cst[:], in_=c.consts)
        sig(t, i, c.cInit, 16)
        i = t and t.dma_start(out=c.identb[:], in_=c.identf)
        sig(t, i, c.cInit, 16)
        i = t and t.dma_start(out=c.kdS[:], in_=c.kdall)
        sig(t, i, c.cInit, 16)
        i = t and t.dma_start(out=c.bdS[:], in_=c.bdall)
        sig(t, i, c.cInit, 16)
        if getattr(c, "fused", False):
            i = t and t.dma_start(out=c.gmS[:], in_=c.gmask)
            sig(t, i, c.cInit, 16)
    P.on("gpsimd", ev_init)

    def ev_init_v(t):
        wt(t, c.cInit)
        t and t.memset(c.kvs[:], 0.0)
    P.on("vector", ev_init_v)

    def ev_init_s(t):
        wt(t, c.cInit)
    P.on("scalar", ev_init_s)

    def ev_init_t(t):
        wt(t, c.cInit)
    P.on("tensor", ev_init_t)

    for sh in range(2):
        _emit_a_stage1(P, c, sh)
        if DEBUG_DUMP and sh == 0:
            def ev_dbg(t):
                wt(t, c.cH0)
                for j in range(8):
                    i = t and t.dma_start(out=c.dbg[3, :, j * 128:(j + 1) * 128],
                                          in_=c.h0[:, j * 1024: j * 1024 + 128])
                    sig(t, i, c.cOut, 16)
            P.on("sync", ev_dbg)
        _emit_a_heads(P, c, sh)
        if DEBUG_DUMP and sh == 0:
            def ev_dbg2(t):
                wt(t, c.cQKV)
                for j, buf in enumerate((c.qf, c.kf, c.vf)):
                    i = t and t.dma_start(out=c.dbg[j], in_=buf[:])
                    sig(t, i, c.cOut, 16)
            P.on("sync", ev_dbg2)

    def ev_done(t):
        wt(t, c.cOut)
    P.on("sync", ev_done)




def build_phase_a():
    import contextlib
    nc = bass.Bass("TRN2", target_bir_lowering=False)
    c = types.SimpleNamespace()
    c.xb = nc.dram_tensor("xb", [S, D], F32, kind="ExternalInput").ap()
    c.qkvT = nc.dram_tensor("qkvT", [D, HL * 384], F32, kind="ExternalInput").ap()
    c.dcb = nc.dram_tensor("dcb", [HL, 128, 768], F32, kind="ExternalInput").ap()
    c.kdall = nc.dram_tensor("kdall", [128, 2 * HL], F32, kind="ExternalInput").ap()
    c.bdall = nc.dram_tensor("bdall", [128, HL], F32, kind="ExternalInput").ap()
    c.identf = nc.dram_tensor("identf", [128, 128], F32, kind="ExternalInput").ap()
    c.consts = nc.dram_tensor("consts", [128, NCONST], F32, kind="ExternalInput").ap()
    c.attn_out = nc.dram_tensor("attn_fm", [HL, 128, S], BF16, kind="ExternalOutput").ap()
    if DEBUG_DUMP:
        c.dbg = nc.dram_tensor("dbg", [4, 128, 1024], BF16, kind="ExternalOutput").ap()

    with contextlib.ExitStack() as st:
        sb, ps = (lambda *a: _sb(st, nc, *a)), (lambda *a: _ps(st, nc, *a))
        c.cst = sb("cst", [128, NCONST], F32)
        c.identb = sb("identb", [128, 128], BF16)
        c.kdS = sb("kdS", [128, 2 * HL], F32)
        c.bdS = sb("bdS", [128, HL], F32)
        c.h0 = sb("h0", [128, 32 * 1024], BF16)
        c.xt = sb("xt", [128, D], F32)
        c.scr4k = sb("scr4k", [128, D], F32)
        c.lnst = sb("lnst", [128, 8], F32)
        c.ht = sb("ht", [128, D], BF16)
        c.wbuf = sb("wbuf", [128, 2 * 12288], BF16)
        c.cbuf = sb("cbuf", [128, 2 * 768], BF16)
        c.qf = sb("qf", [128, 1024], BF16)
        c.kf = sb("kf", [128, 1024], BF16)
        c.vf = sb("vf", [128, 1024], BF16)
        c.stg = sb("stg", [128, 2 * 1024], BF16)
        c.STb = sb("STb", [128, 2 * 256], BF16)
        c.ckkd = sb("ckkd", [128, 2 * 128], BF16)
        c.cvt = sb("cvt", [128, 2 * 128], BF16)
        c.cqd = sb("cqd", [128, 256], BF16)
        c.kvs = sb("kvs", [128, HL * 128], F32)
        c.kvb = sb("kvb", [128, HL * 128], BF16)
        c.pS0 = ps("pS0", [128, 512], F32)
        c.pS1 = ps("pS1", [128, 512], F32)
        c.pA = ps("pA", [128, 512], F32)
        c.pU = ps("pU", [128, 512], F32)
        c.pT0 = ps("pT0", [128, 1024], BF16)
        c.pT1 = ps("pT1", [128, 1024], BF16)
        c.pV0 = ps("pV0", [128, 1024], BF16)
        c.pV1 = ps("pV1", [128, 1024], BF16)
        block = st.enter_context(nc.Block())

        P = Prog(nc, st)
        for nm in ("cInit cX cXf cSq cStat cSqrt cHt pT0p pT0f pT1p pT1f pV0p pV0f "
                   "pV1p pV1f cH0 cH0f cW cWf cC cCf pS0p pS0f pS1p pS1f pAp pAf "
                   "pUp pUf cQKV cST cCK cCV cCQD cKV cStgV cOut").split():
            setattr(c, nm, P.ch(nm))
        emit_a(P, c)
        P.emit(block)
    return nc


def _emit_a_stage1(P, c, sh):
    for tile in range(8):
        trow = sh * 1024 + tile * 128

        def ev_ldx(t, trow=trow):
            wt(t, c.cXf)
            i = t and t.dma_start(out=c.xt[:], in_=c.xb[trow:trow + 128, :])
            sig(t, i, c.cX, 16)
        P.on("sync", ev_ldx)

        def ev_sq(t):
            wt(t, c.cX)
            i = t and t.activation(c.scr4k[:], c.xt[:], AF.Square,
                                   accum_out=c.lnst[:, 1:2])
            sig(t, i, c.cSq)
        P.on("scalar", ev_sq)

        def ev_stat(t):
            wt(t, c.cSq)
            if t:
                t.reduce_sum(c.lnst[:, 0:1], c.xt[:], axis=AX.X)
                t.drain()
                t.tensor_scalar(c.lnst[:, 2:3], c.lnst[:, 0:1],
                                c.cst[:, C_INV_D:C_INV_D + 1], None, ALU.mult)
                t.tensor_scalar(c.lnst[:, 3:4], c.lnst[:, 1:2],
                                c.cst[:, C_INV_D:C_INV_D + 1], None, ALU.mult)
                t.drain()
                t.tensor_tensor(c.lnst[:, 4:5], c.lnst[:, 2:3], c.lnst[:, 2:3], ALU.mult)
                t.drain()
                i = t.tensor_tensor(c.lnst[:, 4:5], c.lnst[:, 3:4], c.lnst[:, 4:5], ALU.subtract)
            else:
                i = None
            sig(t, i, c.cStat)
        P.on("vector", ev_stat)

        def ev_sqrt(t):
            wt(t, c.cStat)
            i = t and t.activation(c.lnst[:, 5:6], c.lnst[:, 4:5], AF.Sqrt,
                                   bias=c.cst[:, C_EPS:C_EPS + 1])
            sig(t, i, c.cSqrt)
        P.on("scalar", ev_sqrt)

        def ev_apply(t):
            wt(t, c.cSqrt)
            wt(t, c.pT0p)
            wt(t, c.pT1p)
            if t:
                t.reciprocal(c.lnst[:, 6:7], c.lnst[:, 5:6])
                t.drain()
                i = t.tensor_scalar(c.ht[:], c.xt[:], c.lnst[:, 2:3], c.lnst[:, 6:7],
                                    ALU.subtract, ALU.mult)
            else:
                i = None
            sig(t, i, c.cHt)
            sigd(t, c.cXf)
        P.on("vector", ev_apply)

        for cp in range(16):
            def ev_tr(t, cp=cp, sh=sh):
                wt(t, c.cHt)
                wt(t, c.cH0)
                if sh == 1 and cp == 0:
                    wt(t, c.cH0f)
                if t:
                    t.transpose(c.pT0[:, 0:128],
                                c.ht[:, (2 * cp) * 128:(2 * cp + 1) * 128], c.identb[:])
                    t.transpose(c.pT1[:, 0:128],
                                c.ht[:, (2 * cp + 1) * 128:(2 * cp + 2) * 128], c.identb[:])
                    t.drain()
                sigi(t, c.pT0p)
                sigi(t, c.pT1p)
            P.on("tensor", ev_tr)

            eng = "vector" if cp % 2 == 0 else "scalar"

            def ev_ev(t, cp=cp, tile=tile, eng=eng):
                wt(t, c.pT0p)
                wt(t, c.pT1p)
                i = None
                if t:
                    for j, bank in ((2 * cp, c.pT0), (2 * cp + 1, c.pT1)):
                        dst = c.h0[:, j * 1024 + tile * 128: j * 1024 + tile * 128 + 128]
                        i = (t.tensor_copy(dst, bank[:, 0:128]) if eng == "vector"
                             else t.copy(dst, bank[:, 0:128]))
                sig(t, i, c.cH0, 2)
            P.on(eng, ev_ev)


def _emit_a_heads(P, c, sh):
    for h in range(HL):
        par = h % 2
        H = sh * HL + h

        def ev_w(t, h=h, par=par):
            wt(t, c.cWf, lag=1)
            for kg in range(8):
                src = c.qkvT[kg * 512:(kg + 1) * 512, h * 384:(h + 1) * 384]
                src = src.rearrange("(cc p) m -> p cc m", p=128)
                dst = c.wbuf[:, par * 12288 + kg * 1536:par * 12288 + (kg + 1) * 1536]
                dst = dst.rearrange("p (cc m) -> p cc m", cc=4)
                i = t and t.dma_start(out=dst, in_=src)
                sig(t, i, c.cW, 16)
            wt(t, c.cCf, lag=1)
            i = t and t.dma_start(out=c.cbuf[:, par * 768:(par + 1) * 768], in_=c.dcb[h])
            sig(t, i, c.cC, 16)
        P.on("gpsimd", ev_w)

        for m in range(3):
            for n in range(2):
                bank, bp, bf = ((c.pS0, c.pS0p, c.pS0f), (c.pS1, c.pS1p, c.pS1f),
                                (c.pA, c.pAp, c.pAf))[(m * 2 + n) % 3]

                def ev_mm(t, h=h, m=m, n=n, par=par, bank=bank, bp=bp, bf=bf):
                    wt(t, c.cW)
                    wt(t, c.cH0)
                    wt(t, c.cST)
                    wt(t, c.cStgV)
                    wt(t, bf)
                    i = None
                    if t:
                        for kc in range(32):
                            lhs = c.wbuf[:, par * 12288 + (kc // 4) * 1536 + (kc % 4) * 384
                                         + m * 128: par * 12288 + (kc // 4) * 1536
                                         + (kc % 4) * 384 + m * 128 + 128]
                            rhs = c.h0[:, kc * 1024 + n * 512: kc * 1024 + n * 512 + 512]
                            i = t.matmul(bank[:, 0:512], lhs, rhs,
                                         start=(kc == 0), stop=(kc == 31))
                    sig(t, i, bp)
                P.on("tensor", ev_mm)

                def ev_evac(t, m=m, n=n, bank=bank, bp=bp, bf=bf):
                    wt(t, bp)
                    dstf = (c.qf, c.kf, c.vf)[m]
                    i = t and t.activation(dstf[:, n * 512:(n + 1) * 512],
                                           bank[:, 0:512], AF.Silu)
                    sig(t, i, c.cQKV)
                    sigd(t, bf)
                    if m == 2 and n == 1:
                        sigi(t, c.cH0f)
                        sigi(t, c.cWf)
                P.on("scalar", ev_evac)

        for blk in range(NB_SH):
            glob = sh * NB_SH + blk
            ts0 = blk * 256

            def ev_st_mm(t, ts0=ts0):
                wt(t, c.cQKV)
                wt(t, c.cST)
                i0 = t and t.matmul(c.pS0[:, 0:256], c.kf[:, ts0:ts0 + 128],
                                    c.qf[:, ts0:ts0 + 256], start=True, stop=True)
                sig(t, i0, c.pS0p)
                i1 = t and t.matmul(c.pS1[:, 0:256], c.kf[:, ts0 + 128:ts0 + 256],
                                    c.qf[:, ts0:ts0 + 256], start=True, stop=True)
                sig(t, i1, c.pS1p)
            P.on("tensor", ev_st_mm)

            def ev_st_ev(t, par=par):
                wt(t, c.pS0p)
                wt(t, c.pS1p)
                wt(t, c.cC)
                wt(t, c.pAp)
                i0 = t and t.tensor_tensor(c.STb[:, 0:256], c.pS0[:, 0:256],
                                           c.cbuf[:, par * 768:par * 768 + 256], ALU.mult)
                i1 = t and t.tensor_tensor(c.STb[:, 256:512], c.pS1[:, 0:256],
                                           c.cbuf[:, par * 768 + 256:par * 768 + 512], ALU.mult)
                sig(t, i1, c.cST)
            P.on("vector", ev_st_ev)

            def ev_tp(t, ts0=ts0):
                wt(t, c.cCV)
                wt(t, c.cCK)
                if t:
                    t.transpose(c.pT0[:, 0:128], c.kf[:, ts0:ts0 + 128], c.identb[:])
                    t.transpose(c.pT1[:, 0:128], c.kf[:, ts0 + 128:ts0 + 256], c.identb[:])
                    t.transpose(c.pV0[:, 0:128], c.vf[:, ts0:ts0 + 128], c.identb[:])
                    t.transpose(c.pV1[:, 0:128], c.vf[:, ts0 + 128:ts0 + 256], c.identb[:])
                    t.drain()
                sigi(t, c.pT0p)
                sigi(t, c.pT1p)
                sigi(t, c.pV0p)
                sigi(t, c.pV1p)
            P.on("tensor", ev_tp)

            def ev_tp_ev(t, h=h):
                wt(t, c.pT0p)
                wt(t, c.pT1p)
                wt(t, c.pV0p)
                wt(t, c.pV1p)
                wt(t, c.pUp)
                wt(t, c.pAp)
                i0 = t and t.activation(c.ckkd[:, 0:128], c.pT0[:, 0:128], AF.Copy,
                                        scale=c.kdS[:, 2 * h:2 * h + 1])
                i1 = t and t.activation(c.ckkd[:, 128:256], c.pT1[:, 0:128], AF.Copy,
                                        scale=c.kdS[:, 2 * h + 1:2 * h + 2])
                sig(t, i1, c.cCK)
                i2 = t and t.activation(c.cvt[:, 0:128], c.pV0[:, 0:128], AF.Copy)
                i3 = t and t.activation(c.cvt[:, 128:256], c.pV1[:, 0:128], AF.Copy)
                sig(t, i3, c.cCV)
            P.on("scalar", ev_tp_ev)

            if glob > 0:
                def ev_cqd(t, ts0=ts0, par=par, last=(blk == NB_SH - 1)):
                    wt(t, c.cQKV)
                    wt(t, c.cC)
                    wt(t, c.pAp)
                    i = t and t.tensor_tensor(c.cqd[:], c.qf[:, ts0:ts0 + 256],
                                              c.cbuf[:, par * 768 + 512:par * 768 + 768],
                                              ALU.mult)
                    sig(t, i, c.cCQD)
                    if last:
                        sigi(t, c.cCf)
                P.on("vector", ev_cqd)
            elif blk == NB_SH - 1:
                def ev_cf(t):
                    wt(t, c.cST)
                    sig(t, None, c.cCf)
                P.on("vector", ev_cf)

            def ev_attn(t, h=h, glob=glob):
                wt(t, c.cST)
                wt(t, c.cCV)
                if glob > 0:
                    wt(t, c.cCQD)
                    wt(t, c.cKV)
                wt(t, c.pAf)
                i = None
                if t:
                    t.matmul(c.pA[:, 0:256], c.cvt[:, 0:128], c.STb[:, 0:256],
                             start=True, stop=False)
                    i = t.matmul(c.pA[:, 0:256], c.cvt[:, 128:256], c.STb[:, 256:512],
                                 start=False, stop=(glob == 0))
                    if glob > 0:
                        i = t.matmul(c.pA[:, 0:256], c.kvb[:, h * 128:(h + 1) * 128],
                                     c.cqd[:], start=False, stop=True)
                sig(t, i, c.pAp)
            P.on("tensor", ev_attn)

            def ev_attn_ev(t, blk=blk, par=par, H=H):
                wt(t, c.pAp)
                if blk == 0 and H >= 2:
                    wt(t, c.cOut, lag=16)
                i = t and t.tensor_copy(
                    c.stg[:, par * 1024 + blk * 256: par * 1024 + blk * 256 + 256],
                    c.pA[:, 0:256])
                sig(t, i, c.cStgV)
                sigd(t, c.pAf)
            P.on("vector", ev_attn_ev)

            if glob < 7:
                def ev_upd(t):
                    wt(t, c.cCK)
                    wt(t, c.cCV)
                    wt(t, c.pUf)
                    i = None
                    if t:
                        t.matmul(c.pU[:, 0:128], c.ckkd[:, 0:128], c.cvt[:, 0:128],
                                 start=True, stop=False)
                        i = t.matmul(c.pU[:, 0:128], c.ckkd[:, 128:256],
                                     c.cvt[:, 128:256], start=False, stop=True)
                    sig(t, i, c.pUp)
                P.on("tensor", ev_upd)

                def ev_upd_ev(t, h=h):
                    wt(t, c.pUp)
                    sl = c.kvs[:, h * 128:(h + 1) * 128]
                    i = None
                    if t:
                        t.scalar_tensor_tensor(sl, sl, c.bdS[:, h:h + 1],
                                               c.pU[:, 0:128], ALU.mult, ALU.add)
                        i = t.tensor_copy(c.kvb[:, h * 128:(h + 1) * 128], sl)
                    sig(t, i, c.cKV)
                    sigd(t, c.pUf)
                P.on("vector", ev_upd_ev)

        if getattr(c, "fused", False):
            def ev_mask(t, h=h, sh=sh, par=par):
                wt(t, c.cStgV)
                wt(t, c.cOut)
                if t:
                    t.tensor_scalar(c.stgM[:, 0:1024], c.stg[:, par * 1024:(par + 1) * 1024],
                                    c.gmS[:, 0:1], None, ALU.mult)
                    i = t.tensor_scalar(c.stgM[:, 1024:2048],
                                        c.stg[:, par * 1024:(par + 1) * 1024],
                                        c.gmS[:, 1:2], None, ALU.mult)
                else:
                    i = None
                sig(t, i, c.cStgM)
            P.on("vector", ev_mask)

            def ev_stg_out(t, h=h, sh=sh):
                wt(t, c.cStgM)
                base = sh * 4096 + h * 128
                i = t and t.dma_start(out=c.rsin[base:base + 128, :],
                                      in_=c.stgM[:, 0:1024])
                sig(t, i, c.cOut, 16)
                i = t and t.dma_start(out=c.rsin[base + 2048:base + 2048 + 128, :],
                                      in_=c.stgM[:, 1024:2048])
                sig(t, i, c.cOut, 16)
            P.on("sync", ev_stg_out)

            if FUSED_DEBUG and sh == 0 and h == 0:
                def ev_dbg_h0(t):
                    wt(t, c.cOut)
                    i = t and t.dma_start(out=c.dbg4[0:128, :], in_=c.rsin[0:128, :])
                    sig(t, i, c.cOut, 16)
                P.on("sync", ev_dbg_h0)
        else:
            def ev_stg_out(t, h=h, sh=sh, par=par):
                wt(t, c.cStgV)
                dst = c.attn_out[h, :, sh * 1024:(sh + 1) * 1024]
                i = t and t.dma_start(out=dst, in_=c.stg[:, par * 1024:(par + 1) * 1024])
                sig(t, i, c.cOut, 16)
            P.on("sync", ev_stg_out)



def _prep_a(inputs):
    x = inputs["L0_attn_hidden_states"]
    qkv_w = inputs["L0_attn_qkv_w"]
    qd, kd, dd, bd = _decays()
    consts = _consts_host()
    ident = np.eye(128, dtype=np.float32)
    qkvT_full = np.ascontiguousarray(qkv_w.T)
    in_maps = []
    for core in range(NCORES):
        b, g = core // 2, core % 2
        heads = range(16 * g, 16 * g + 16)
        qkvT = np.ascontiguousarray(qkvT_full[:, 384 * 16 * g: 384 * 16 * (g + 1)])
        dcb = np.zeros((HL, 128, 768), np.float32)
        kdall = np.zeros((128, 2 * HL), np.float32)
        bdall = np.zeros((128, HL), np.float32)
        for i, hh in enumerate(heads):
            dcb[i, :, 0:256] = dd[hh][:, 0:128].T
            dcb[i, :, 256:512] = dd[hh][:, 128:256].T
            dcb[i, :, 512:768] = np.tile(qd[hh][None, :], (128, 1))
            kdall[:, 2 * i] = kd[hh][0:128]
            kdall[:, 2 * i + 1] = kd[hh][128:256]
            bdall[:, i] = bd[hh]
        in_maps.append({
            "xb": np.ascontiguousarray(x[b]),
            "qkvT": qkvT,
            "dcb": dcb,
            "kdall": kdall,
            "bdall": bdall,
            "identf": ident,
            "consts": consts,
        })
    return in_maps


def run_phase_a(inputs):
    nc = build_phase_a()
    in_maps = _prep_a(inputs)
    res = run_bass_kernel_spmd(nc, in_maps, core_ids=list(range(NCORES)))
    return [r["attn_fm"] for r in res.results]


def kernel(**inputs):
    return kernel_fused(inputs)


def kernel_unfused(**inputs):
    a = run_phase_a(inputs)
    xb1 = run_phase_b(inputs, a)
    parts = run_phase_ca(xb1)
    xc = run_phase_cb(inputs, xb1, parts)
    xd = run_phase_d(inputs, xc)
    xe = run_phase_e(inputs, xd)
    out = np.zeros((B, S, D), np.float32)
    for core in range(NCORES):
        bb, g = core // 2, core % 2
        out[bb, g * TOK:(g + 1) * TOK] = np.asarray(xe[core])
    return out



def emit_ln_fm(P, c, src_rows_ap, ntiles, dst_h, extras=None):
    TT = ntiles * 128
    xts = (c.xt, getattr(c, "xt2", c.xt))
    dbuf = xts[0] is not xts[1]
    for tile in range(ntiles):
        xb_ = xts[tile % 2]

        def ev_ldx(t, tile=tile, xb_=xb_):
            wt(t, c.cXf, lag=(1 if dbuf else 0))
            i = t and t.dma_start(out=xb_[:, 0:D], in_=src_rows_ap[tile * 128:(tile + 1) * 128, :])
            sig(t, i, c.cX, 16)
        P.on("sync", ev_ldx)

        def ev_sq(t, xb_=xb_):
            wt(t, c.cX)
            i = t and t.activation(c.scr4k[:], xb_[:, 0:D], AF.Square,
                                   accum_out=c.lnst[:, 1:2])
            sig(t, i, c.cSq)
        P.on("scalar", ev_sq)

        def ev_stat(t, xb_=xb_):
            wt(t, c.cSq)
            if t:
                t.reduce_sum(c.lnst[:, 0:1], xb_[:, 0:D], axis=AX.X)
                t.drain()
                t.tensor_scalar(c.lnst[:, 2:3], c.lnst[:, 0:1],
                                c.cst[:, C_INV_D:C_INV_D + 1], None, ALU.mult)
                t.tensor_scalar(c.lnst[:, 3:4], c.lnst[:, 1:2],
                                c.cst[:, C_INV_D:C_INV_D + 1], None, ALU.mult)
                t.drain()
                t.tensor_tensor(c.lnst[:, 4:5], c.lnst[:, 2:3], c.lnst[:, 2:3], ALU.mult)
                t.drain()
                i = t.tensor_tensor(c.lnst[:, 4:5], c.lnst[:, 3:4], c.lnst[:, 4:5], ALU.subtract)
            else:
                i = None
            sig(t, i, c.cStat)
        P.on("vector", ev_stat)

        def ev_sqrt(t):
            wt(t, c.cStat)
            i = t and t.activation(c.lnst[:, 5:6], c.lnst[:, 4:5], AF.Sqrt,
                                   bias=c.cst[:, C_EPS:C_EPS + 1])
            sig(t, i, c.cSqrt)
        P.on("scalar", ev_sqrt)

        def ev_apply(t, xb_=xb_):
            wt(t, c.cSqrt)
            wt(t, c.pT0p)
            wt(t, c.pT1p)
            if t:
                t.reciprocal(c.lnst[:, 6:7], c.lnst[:, 5:6])
                t.drain()
                i = t.tensor_scalar(c.ht[:], xb_[:, 0:D], c.lnst[:, 2:3], c.lnst[:, 6:7],
                                    ALU.subtract, ALU.mult)
            else:
                i = None
            sig(t, i, c.cHt)
            sigd(t, c.cXf)
        P.on("vector", ev_apply)

        if extras is not None:
            extras(tile)

        for cp in range(16):
            def ev_tr(t, cp=cp):
                wt(t, c.cHt)
                wt(t, c.cH0)
                if t:
                    t.transpose(c.pT0[:, 0:128],
                                c.ht[:, (2 * cp) * 128:(2 * cp + 1) * 128], c.identb[:])
                    t.transpose(c.pT1[:, 0:128],
                                c.ht[:, (2 * cp + 1) * 128:(2 * cp + 2) * 128], c.identb[:])
                    t.drain()
                sigi(t, c.pT0p)
                sigi(t, c.pT1p)
            P.on("tensor", ev_tr)

            eng = "vector" if cp % 2 == 0 else "scalar"

            def ev_ev(t, cp=cp, tile=tile, eng=eng):
                wt(t, c.pT0p)
                wt(t, c.pT1p)
                i = None
                if t:
                    for j, bank in ((2 * cp, c.pT0), (2 * cp + 1, c.pT1)):
                        dst = dst_h[:, j * TT + tile * 128: j * TT + tile * 128 + 128]
                        i = (t.tensor_copy(dst, bank[:, 0:128]) if eng == "vector"
                             else t.copy(dst, bank[:, 0:128]))
                sig(t, i, c.cH0, 2)
            P.on(eng, ev_ev)



def emit_b(P, c):

    ht_alias = c.regB
    c.ht = ht_alias[:, 0:4096]

    def ev_init(t):
        i = t and t.dma_start(out=c.cst[:], in_=c.consts)
        sig(t, i, c.cInit, 16)
        i = t and t.dma_start(out=c.identb[:], in_=c.identf)
        sig(t, i, c.cInit, 16)
        i = t and t.dma_start(out=c.normwS[:], in_=c.normw)
        sig(t, i, c.cInit, 16)
    P.on("gpsimd", ev_init)

    def ev_init_v(t):
        wt(t, c.cInit)
        t and t.memset(c.onesB[:], 1.0)
    P.on("vector", ev_init_v)
    for e in ("scalar", "tensor", "sync"):
        def ev_i(t):
            wt(t, c.cInit)
        P.on(e, ev_i)

    emit_ln_fm(P, c, c.xown, 8, c.regA)

    for j in range(32):
        par = j % 2

        def ev_gw(t, j=j, par=par):
            wt(t, c.cWf, lag=1)
            if j < 2 and getattr(c, "fused", False):
                wt(t, c.cH0)
            src = c.gateTt[j].rearrange("(kc p) m -> p kc m", p=128)
            dst = c.wb[:, par * 4096:(par + 1) * 4096].rearrange("p (kc m) -> p kc m", m=128)
            i = t and t.dma_start(out=dst, in_=src)
            sig(t, i, c.cW, 16)
        P.on("gpsimd", ev_gw)

        for n in range(2):
            bank, bp, bf = ((c.pS0, c.pS0p, c.pS0f), (c.pS1, c.pS1p, c.pS1f),
                            (c.pA, c.pAp, c.pAf))[(j * 2 + n) % 3]

            def ev_gmm(t, j=j, n=n, par=par, bank=bank, bp=bp, bf=bf):
                wt(t, c.cW)
                wt(t, c.cH0)
                wt(t, bf)
                i = None
                if t:
                    for kc in range(32):
                        lhs = c.wb[:, par * 4096 + kc * 128:(par * 4096 + kc * 128) + 128]
                        rhs = c.regA[:, kc * TOK + n * 512: kc * TOK + n * 512 + 512]
                        i = t.matmul(bank[:, 0:512], lhs, rhs,
                                     start=(kc == 0), stop=(kc == 31))
                sig(t, i, bp)
            P.on("tensor", ev_gmm)

            def ev_gev(t, j=j, n=n, bank=bank, bp=bp, bf=bf):
                wt(t, bp)
                i = t and t.activation(c.regB[:, j * TOK + n * 512: j * TOK + n * 512 + 512],
                                       bank[:, 0:512], AF.Sigmoid)
                sig(t, i, c.cGate)
                sigd(t, bf)
                if n == 1:
                    sigi(t, c.cWf)
            P.on("scalar", ev_gev)

    for ch_ in range(32):
        par = ch_ % 2

        def ev_la(t, ch_=ch_, par=par):
            wt(t, c.cAf, lag=1)
            if getattr(c, "fused", False):
                wt(t, c.cAG)
                wt(t, c.cH0)
            i = t and t.dma_start(out=c.abuf[:, par * TOK:(par + 1) * TOK],
                                  in_=c.afm[ch_ * 128:(ch_ + 1) * 128, :])
            sig(t, i, c.cA, 16)
        P.on("sync", ev_la)

        def ev_sqa(t, ch_=ch_, par=par):
            wt(t, c.cA)
            wt(t, c.pUp)
            i = t and t.activation(c.sqb[:], c.abuf[:, par * TOK:(par + 1) * TOK],
                                   AF.Square)
            sig(t, i, c.cSqB)
            sigd(t, c.cAf)
        P.on("scalar", ev_sqa)

        def ev_smm(t, ch_=ch_):
            wt(t, c.cSqB)
            if ch_ == 0:
                wt(t, c.pUf)
                wt(t, c.pRf)
            i = None
            if t:
                t.matmul(c.pU[0:1, 0:512], c.onesB[:, 0:1], c.sqb[:, 0:512],
                         start=(ch_ == 0), stop=(ch_ == 31))
                i = t.matmul(c.pR[0:1, 0:512], c.onesB[:, 0:1], c.sqb[:, 512:1024],
                             start=(ch_ == 0), stop=(ch_ == 31))
            sig(t, i, c.pUp)
        P.on("tensor", ev_smm)

    def ev_rstat(t):
        wt(t, c.pUp)
        if t:
            t.tensor_copy(c.rrow[0:1, 0:512], c.pU[0:1, 0:512])
            i = t.tensor_copy(c.rrow[0:1, 512:1024], c.pR[0:1, 0:512])
        else:
            i = None
        sig(t, i, c.cRr)
        sigd(t, c.pUf)
        sigi(t, c.pRf)
    P.on("vector", ev_rstat)

    def ev_rsqrt(t):
        wt(t, c.cRr)
        i = t and t.activation(c.rrow2[:, :], c.rrow[:, :], AF.Sqrt,
                               bias=c.cst[0:1, C_EPS:C_EPS + 1],
                               scale=c.cst[0:1, C_INV_D:C_INV_D + 1])
        sig(t, i, c.cRstd)
    P.on("scalar", ev_rsqrt)

    def ev_rrec(t):
        wt(t, c.cRstd)
        if t:
            t.reciprocal(c.rrow[:, :], c.rrow2[:, :])
            t.drain()
            i = t.tensor_copy(c.rrowb[:, :], c.rrow[:, :])
        else:
            i = None
        sig(t, i, c.cRb)
    P.on("vector", ev_rrec)

    def ev_rbc(t):
        wt(t, c.cRb)
        wt(t, c.pS0f)
        wt(t, c.pS1f)
        i = None
        if t:
            t.matmul(c.pS0[:, 0:512], c.onesB[0:1, 0:128], c.rrowb[:, 0:512],
                     start=True, stop=True)
            i = t.matmul(c.pS1[:, 0:512], c.onesB[0:1, 0:128], c.rrowb[:, 512:1024],
                         start=True, stop=True)
        sig(t, i, c.pS0p)
        sigi(t, c.pS1p)
    P.on("tensor", ev_rbc)

    def ev_rbev(t):
        wt(t, c.pS0p)
        wt(t, c.pS1p)
        if t:
            t.tensor_copy(c.rstdb[:, 0:512], c.pS0[:, 0:512])
            i = t.tensor_copy(c.rstdb[:, 512:1024], c.pS1[:, 0:512])
        else:
            i = None
        sig(t, i, c.cRbb)
        sigd(t, c.pS0f)
        sigi(t, c.pS1f)
    P.on("vector", ev_rbev)

    for ch_ in range(32):
        par = ch_ % 2

        def ev_la2(t, ch_=ch_, par=par):
            wt(t, c.cAf, lag=1)
            if getattr(c, "fused", False):
                wt(t, c.cAG)
            i = t and t.dma_start(out=c.abuf[:, par * TOK:(par + 1) * TOK],
                                  in_=c.afm[ch_ * 128:(ch_ + 1) * 128, :])
            sig(t, i, c.cA, 16)
        P.on("sync", ev_la2)

        def ev_prod(t, ch_=ch_, par=par):
            wt(t, c.cA)
            wt(t, c.cGate)
            wt(t, c.cRbb)
            if t:
                t.scalar_tensor_tensor(c.sqb[:], c.abuf[:, par * TOK:(par + 1) * TOK],
                                       c.normwS[:, ch_:ch_ + 1],
                                       c.regB[:, ch_ * TOK:(ch_ + 1) * TOK],
                                       ALU.mult, ALU.mult)
                i = t.tensor_tensor(c.regA[:, ch_ * TOK:(ch_ + 1) * TOK],
                                    c.sqb[:], c.rstdb[:], ALU.mult)
            else:
                i = None
            sig(t, i, c.cProd)
            sigd(t, c.cAf)
        P.on("vector", ev_prod)

    for j in range(32):
        par = j % 2

        def ev_ow(t, j=j, par=par):
            wt(t, c.cWf, lag=1)
            src = c.outTt[j].rearrange("(kc p) m -> p kc m", p=128)
            dst = c.wb[:, par * 4096:(par + 1) * 4096].rearrange("p (kc m) -> p kc m", m=128)
            i = t and t.dma_start(out=dst, in_=src)
            sig(t, i, c.cW, 16)
        P.on("gpsimd", ev_ow)

        for n in range(2):
            bank, bp, bf = ((c.pS0, c.pS0p, c.pS0f), (c.pS1, c.pS1p, c.pS1f),
                            (c.pA, c.pAp, c.pAf))[(j * 2 + n) % 3]

            def ev_omm(t, j=j, n=n, par=par, bank=bank, bp=bp, bf=bf):
                wt(t, c.cW)
                wt(t, c.cProd)
                wt(t, bf)
                i = None
                if t:
                    for kc in range(32):
                        lhs = c.wb[:, par * 4096 + kc * 128: par * 4096 + kc * 128 + 128]
                        rhs = c.regA[:, kc * TOK + n * 512: kc * TOK + n * 512 + 512]
                        i = t.matmul(bank[:, 0:512], lhs, rhs,
                                     start=(kc == 0), stop=(kc == 31))
                sig(t, i, bp)
            P.on("tensor", ev_omm)

            def ev_oev(t, j=j, n=n, bank=bank, bp=bp, bf=bf):
                wt(t, bp)
                wt(t, c.cProd)
                i = t and t.tensor_copy(
                    c.regB[:, j * TOK + n * 512: j * TOK + n * 512 + 512],
                    bank[:, 0:512])
                sig(t, i, c.cOT)
                sigd(t, bf)
                if n == 1:
                    sigi(t, c.cWf)
            P.on("vector", ev_oev)

    _emit_b5(P, c)

    def ev_done(t):
        wt(t, c.cOut)
    P.on("sync", ev_done)




def build_phase_b():
    import contextlib
    nc = bass.Bass("TRN2", target_bir_lowering=False)
    c = types.SimpleNamespace()
    c.xown = nc.dram_tensor("xown", [TOK, D], F32, kind="ExternalInput").ap()
    c.afm = nc.dram_tensor("afm", [D, TOK], BF16, kind="ExternalInput").ap()
    c.gateTt = nc.dram_tensor("gateTt", [32, D, 128], F32, kind="ExternalInput").ap()
    c.outTt = nc.dram_tensor("outTt", [32, D, 128], F32, kind="ExternalInput").ap()
    c.normw = nc.dram_tensor("normw", [128, 32], F32, kind="ExternalInput").ap()
    c.consts = nc.dram_tensor("consts", [128, NCONST], F32, kind="ExternalInput").ap()
    c.identf = nc.dram_tensor("identf", [128, 128], F32, kind="ExternalInput").ap()
    c.x1 = nc.dram_tensor("x1", [TOK, D], F32, kind="ExternalOutput").ap()

    with contextlib.ExitStack() as st:
        sb, ps = (lambda *a: _sb(st, nc, *a)), (lambda *a: _ps(st, nc, *a))
        c.cst = sb("cst", [128, NCONST], F32)
        c.identb = sb("identb", [128, 128], BF16)
        c.normwS = sb("normwS", [128, 32], F32)
        c.onesB = sb("onesB", [128, 128], BF16)
        c.regA = sb("regA", [128, 32 * TOK], BF16)
        c.regB = sb("regB", [128, 32 * TOK], BF16)
        c.xt = sb("xt", [128, D], F32)
        c.scr4k = sb("scr4k", [128, D], F32)
        c.lnst = sb("lnst", [128, 8], F32)
        c.wb = sb("wb", [128, 2 * 4096], BF16)
        c.abuf = sb("abuf", [128, 2 * TOK], BF16)
        c.sqb = sb("sqb", [128, TOK], BF16)
        c.rrow = sb("rrow", [1, TOK], F32)
        c.rrow2 = sb("rrow2", [1, TOK], F32)
        c.rrowb = sb("rrowb", [1, TOK], BF16)
        c.ft = sb("ft", [128, D], BF16)
        c.rstdb = sb("rstdb", [128, TOK], F32)
        c.pS0 = ps("pS0", [128, 512], F32)
        c.pS1 = ps("pS1", [128, 512], F32)
        c.pA = ps("pA", [128, 512], F32)
        c.pU = ps("pU", [128, 512], F32)
        c.pT0 = ps("pT0", [128, 1024], BF16)
        c.pT1 = ps("pT1", [128, 1024], BF16)
        c.pR = ps("pR", [128, 512], F32)
        block = st.enter_context(nc.Block())

        P = Prog(nc, st)
        for nm in ("cInit cX cXf cSq cStat cSqrt cHt pT0p pT0f pT1p pT1f cH0 "
                   "cW cWf pS0p pS0f pS1p pS1f pAp pAf pUp pUf pRp pRf cGate cA cAf "
                   "cSqB cRr cRstd cRb cRbb cProd cOT cFt cOut").split():
            setattr(c, nm, P.ch(nm))
        emit_b(P, c)
        P.emit(block)
    return nc


def _emit_b5(P, c):
    for tile in range(8):
        for cp in range(16):
            def ev_tr(t, cp=cp, tile=tile):
                wt(t, c.cOT)
                wt(t, c.cFt)
                if cp == 0:
                    wt(t, c.cXf)
                if t:
                    t.transpose(c.pT0[:, 0:128],
                                c.regB[:, (2 * cp) * TOK + tile * 128:
                                       (2 * cp) * TOK + tile * 128 + 128], c.identb[:])
                    t.transpose(c.pT1[:, 0:128],
                                c.regB[:, (2 * cp + 1) * TOK + tile * 128:
                                       (2 * cp + 1) * TOK + tile * 128 + 128], c.identb[:])
                    t.drain()
                sigi(t, c.pT0p)
                sigi(t, c.pT1p)
            P.on("tensor", ev_tr)

            eng = "vector" if cp % 2 == 0 else "scalar"

            def ev_ev(t, cp=cp, eng=eng):
                wt(t, c.pT0p)
                wt(t, c.pT1p)
                i = None
                if t:
                    for j, bank in ((2 * cp, c.pT0), (2 * cp + 1, c.pT1)):
                        dst = c.ft[:, j * 128:(j + 1) * 128]
                        i = (t.tensor_copy(dst, bank[:, 0:128]) if eng == "vector"
                             else t.copy(dst, bank[:, 0:128]))
                sig(t, i, c.cFt)
            P.on(eng, ev_ev)

        xb_ = (c.xt, getattr(c, "xt2", c.xt))[tile % 2]

        def ev_ldx(t, tile=tile, xb_=xb_):
            wt(t, c.cXf, lag=1)
            wt(t, c.cOut, lag=16)
            wt(t, c.cOT)
            i = t and t.dma_start(out=xb_[:, 0:D], in_=c.xown[tile * 128:(tile + 1) * 128, :])
            sig(t, i, c.cX, 16)
        P.on("sync", ev_ldx)

        def ev_sq(t):
            wt(t, c.cFt)
            wt(t, c.cHt)
            i = t and t.activation(c.scr4k[:], c.ft[:], AF.Square,
                                   accum_out=c.lnst[:, 1:2])
            sig(t, i, c.cSq)
        P.on("scalar", ev_sq)

        def ev_stat(t):
            wt(t, c.cSq)
            if t:
                t.reduce_sum(c.lnst[:, 0:1], c.ft[:], axis=AX.X)
                t.drain()
                t.tensor_scalar(c.lnst[:, 2:3], c.lnst[:, 0:1],
                                c.cst[:, C_INV_D:C_INV_D + 1], None, ALU.mult)
                t.tensor_scalar(c.lnst[:, 3:4], c.lnst[:, 1:2],
                                c.cst[:, C_INV_D:C_INV_D + 1], None, ALU.mult)
                t.drain()
                t.tensor_tensor(c.lnst[:, 4:5], c.lnst[:, 2:3], c.lnst[:, 2:3], ALU.mult)
                t.drain()
                i = t.tensor_tensor(c.lnst[:, 4:5], c.lnst[:, 3:4], c.lnst[:, 4:5], ALU.subtract)
            else:
                i = None
            sig(t, i, c.cStat)
        P.on("vector", ev_stat)

        def ev_sqrt(t):
            wt(t, c.cStat)
            i = t and t.activation(c.lnst[:, 5:6], c.lnst[:, 4:5], AF.Sqrt,
                                   bias=c.cst[:, C_EPS:C_EPS + 1])
            sig(t, i, c.cSqrt)
        P.on("scalar", ev_sqrt)

        def ev_apply(t, tile=tile, xb_=xb_):
            wt(t, c.cSqrt)
            wt(t, c.cX)
            if t:
                t.reciprocal(c.lnst[:, 6:7], c.lnst[:, 5:6])
                t.drain()
                t.tensor_scalar(c.scr4k[:], c.ft[:], c.lnst[:, 2:3], c.lnst[:, 6:7],
                                ALU.subtract, ALU.mult)
                i = t.tensor_tensor(xb_[:, 0:D], c.scr4k[:], xb_[:, 0:D], ALU.add)
            else:
                i = None
            sig(t, i, c.cHt)
            sigd(t, c.cXf)
        P.on("vector", ev_apply)

        def ev_st(t, tile=tile, xb_=xb_):
            wt(t, c.cHt)
            i = t and t.dma_start(out=c.x1[tile * 128:(tile + 1) * 128, :], in_=xb_[:, 0:D])
            sig(t, i, c.cOut, 16)
        P.on("sync", ev_st)


def _prep_b(inputs, attn_outs):
    x = inputs["L0_attn_hidden_states"]
    gT = np.ascontiguousarray(inputs["L0_attn_gate_w"].T)
    oT = np.ascontiguousarray(inputs["L0_attn_out_w"].T)
    gateTt = np.ascontiguousarray(gT.reshape(D, 32, 128).transpose(1, 0, 2))
    outTt = np.ascontiguousarray(oT.reshape(D, 32, 128).transpose(1, 0, 2))
    normw = np.ascontiguousarray(inputs["L0_attn_norm_w"].reshape(32, 128).T)
    consts = _consts_host()
    ident = np.eye(128, dtype=np.float32)
    in_maps = []
    for core in range(NCORES):
        b, g = core // 2, core % 2
        att = np.concatenate([np.asarray(attn_outs[2 * b]), np.asarray(attn_outs[2 * b + 1])],
                             axis=0).reshape(D, S)
        afm = np.ascontiguousarray(att[:, g * TOK:(g + 1) * TOK])
        in_maps.append({
            "xown": np.ascontiguousarray(x[b, g * TOK:(g + 1) * TOK, :]),
            "afm": afm,
            "gateTt": gateTt,
            "outTt": outTt,
            "normw": normw,
            "consts": consts,
            "identf": ident,
        })
    return in_maps


def run_phase_b(inputs, attn_outs):
    nc = build_phase_b()
    in_maps = _prep_b(inputs, attn_outs)
    res = run_bass_kernel_spmd(nc, in_maps, core_ids=list(range(NCORES)))
    return [r["x1"] for r in res.results]


C_INVSQ48, C_INV8, C_GELC, C_INVNTOT = 4, 5, 6, 7


def _consts_host_c():
    c = _consts_host()
    c[:, C_INVSQ48] = 1.0 / math.sqrt(48.0)
    c[:, C_INV8] = 1.0 / math.sqrt(64.0)
    c[:, C_GELC] = 0.7978845608028654
    c[:, C_INVNTOT] = 1.0 / 8192.0
    return c


def emit_ca(P, c):

    def ev_init(t):
        i = t and t.dma_start(out=c.cst[:], in_=c.consts)
        sig(t, i, c.cInit, 16)
        i = t and t.dma_start(out=c.ones1[:], in_=c.onesr)
        sig(t, i, c.cInit, 16)
    P.on("gpsimd", ev_init)
    for e in ("scalar", "tensor", "sync", "vector"):
        def ev_i(t):
            wt(t, c.cInit)
        P.on(e, ev_i)

    for tile in range(8):
        def ev_ldx(t, tile=tile):
            wt(t, c.cXf)
            i = t and t.dma_start(out=c.xt[:], in_=c.x1[tile * 128:(tile + 1) * 128, :])
            sig(t, i, c.cX, 16)
        P.on("sync", ev_ldx)

        def ev_sq(t):
            wt(t, c.cX)
            i = t and t.activation(c.scr4k[:], c.xt[:], AF.Square,
                                   accum_out=c.lnst[:, 1:2])
            sig(t, i, c.cSq)
        P.on("scalar", ev_sq)

        def ev_stat(t):
            wt(t, c.cSq)
            if t:
                t.reduce_sum(c.lnst[:, 0:1], c.xt[:], axis=AX.X)
                t.drain()
                t.tensor_scalar(c.lnst[:, 2:3], c.lnst[:, 0:1],
                                c.cst[:, C_INV_D:C_INV_D + 1], None, ALU.mult)
                t.tensor_scalar(c.lnst[:, 3:4], c.lnst[:, 1:2],
                                c.cst[:, C_INV_D:C_INV_D + 1], None, ALU.mult)
                t.drain()
                t.tensor_tensor(c.lnst[:, 4:5], c.lnst[:, 2:3], c.lnst[:, 2:3], ALU.mult)
                t.drain()
                i = t.tensor_tensor(c.lnst[:, 4:5], c.lnst[:, 3:4], c.lnst[:, 4:5], ALU.subtract)
            else:
                i = None
            sig(t, i, c.cStat)
        P.on("vector", ev_stat)

        def ev_sqrt(t):
            wt(t, c.cStat)
            i = t and t.activation(c.lnst[:, 5:6], c.lnst[:, 4:5], AF.Sqrt,
                                   bias=c.cst[:, C_EPS:C_EPS + 1])
            sig(t, i, c.cSqrt)
        P.on("scalar", ev_sqrt)

        def ev_apply(t, tile=tile):
            wt(t, c.cSqrt)
            wt(t, c.cSqH)
            if t:
                t.reciprocal(c.lnst[:, 6:7], c.lnst[:, 5:6])
                t.drain()
                i = t.tensor_scalar(c.hts[:], c.xt[:], c.lnst[:, 2:3], c.lnst[:, 6:7],
                                    ALU.subtract, ALU.mult)
            else:
                i = None
            sig(t, i, c.cHt)
            sigd(t, c.cXf)
        P.on("vector", ev_apply)

        def ev_sqh(t, tile=tile):
            wt(t, c.cHt)
            i = t and t.activation(c.scr4k[:], c.hts[:], AF.Square,
                                   accum_out=c.lnst[:, 7:8])
            sig(t, i, c.cSqH)
        P.on("scalar", ev_sqh)

        def ev_lv(t, tile=tile):
            wt(t, c.cSqH)
            if t:
                t.reduce_sum(c.lnst[:, 0:1], c.hts[:], axis=AX.X)
                t.drain()
                t.tensor_scalar(c.lnst[:, 2:3], c.lnst[:, 0:1],
                                c.cst[:, C_INV_D:C_INV_D + 1], None, ALU.mult)
                t.tensor_scalar(c.lnst[:, 3:4], c.lnst[:, 7:8],
                                c.cst[:, C_INV_D:C_INV_D + 1], None, ALU.mult)
                t.drain()
                t.tensor_tensor(c.lnst[:, 4:5], c.lnst[:, 2:3], c.lnst[:, 2:3], ALU.mult)
                t.drain()
                i = t.tensor_tensor(c.lnst[:, 7:8], c.lnst[:, 3:4],
                                    c.lnst[:, 4:5], ALU.subtract)
            else:
                i = None
            sig(t, i, c.cLv)
        P.on("vector", ev_lv)

        def ev_ln1p(t, tile=tile):
            wt(t, c.cLv)
            i = t and t.activation(c.lv[:, tile:tile + 1], c.lnst[:, 7:8],
                                   AF.Ln, bias=c.cst[:, C_ONE:C_ONE + 1])
            sig(t, i, c.cLv)
        P.on("scalar", ev_ln1p)

    def ev_red(t):
        wt(t, c.cLv)
        if t:
            i = t.reduce_sum(c.lnst[:, 0:1], c.lv[:], axis=AX.X)
        else:
            i = None
        sig(t, i, c.cLv)
    P.on("vector", ev_red)

    def ev_mm(t):
        wt(t, c.cLv)
        i = t and t.matmul(c.pU[0:1, 0:1], c.lnst[:, 0:1], c.cst[:, C_ONE:C_ONE + 1],
                           start=True, stop=True)
        sig(t, i, c.pUp)
    P.on("tensor", ev_mm)

    def ev_out(t):
        wt(t, c.pUp)
        i = t and t.tensor_copy(c.red[0:1, 0:1], c.pU[0:1, 0:1])
        sig(t, i, c.cHt)
    P.on("vector", ev_out)

    def ev_st(t):
        wt(t, c.cHt)
        i = t and t.dma_start(out=c.partial, in_=c.red[0:1, 0:1])
        sig(t, i, c.cOut, 16)
        wt(t, c.cOut)
    P.on("sync", ev_st)




def build_phase_ca():
    import contextlib
    nc = bass.Bass("TRN2", target_bir_lowering=False)
    c = types.SimpleNamespace()
    c.x1 = nc.dram_tensor("x1", [TOK, D], F32, kind="ExternalInput").ap()
    c.consts = nc.dram_tensor("consts", [128, NCONST], F32, kind="ExternalInput").ap()
    c.onesr = nc.dram_tensor("onesr", [1, 128], F32, kind="ExternalInput").ap()
    c.partial = nc.dram_tensor("partial", [1, 1], F32, kind="ExternalOutput").ap()

    with contextlib.ExitStack() as st:
        sb, ps = (lambda *a: _sb(st, nc, *a)), (lambda *a: _ps(st, nc, *a))
        c.cst = sb("cst", [128, NCONST], F32)
        c.ones1 = sb("ones1", [1, 128], F32)
        c.xt = sb("xt", [128, D], F32)
        c.scr4k = sb("scr4k", [128, D], F32)
        c.hts = sb("hts", [128, D], F32)
        c.lnst = sb("lnst", [128, 8], F32)
        c.lv = sb("lv", [128, 8], F32)
        c.red = sb("red", [1, 8], F32)
        c.pU = ps("pU", [128, 512], F32)
        block = st.enter_context(nc.Block())
        P = Prog(nc, st)
        for nm in "cInit cX cXf cSq cStat cSqrt cHt cSqH cLv pUp cOut".split():
            setattr(c, nm, P.ch(nm))
        emit_ca(P, c)
        P.emit(block)
    return nc


def run_phase_ca(x1_outs):
    nc = build_phase_ca()
    consts = _consts_host_c()
    onesr = np.ones((1, 128), np.float32)
    in_maps = [{"x1": np.ascontiguousarray(np.asarray(x)), "consts": consts,
                "onesr": onesr} for x in x1_outs]
    res = run_bass_kernel_spmd(nc, in_maps, core_ids=list(range(NCORES)))
    return np.array([float(r["partial"][0, 0]) for r in res.results], np.float32)


BIGNEG = 30000.0


def emit_cb(P, c):

    def ev_init(t):
        pairs = [(c.cst[:], c.consts), (c.ones1[:], c.onesr),
                 (c.identb[:], c.identf), (c.logrelS[:], c.logrel),
                 (c.scalsS[:], c.scals), (c.bu1S[:], c.bu1c),
                 (c.wu2S[:], c.wu2c), (c.wr_sb[:], c.wrT),
                 (c.ak_sb[:], c.akT)]
        if not getattr(c, "fused", False):
            pairs.append((c.prow[:], c.partials))
        for dst, src in pairs:
            i = t and t.dma_start(out=dst, in_=src)
            sig(t, i, c.cInit, 16)
        src = c.wqT.rearrange("(kc p) m -> p kc m", p=128)
        dst = c.wq_sb[:].rearrange("p (kc m) -> p kc m", m=64)
        i = t and t.dma_start(out=dst, in_=src)
        sig(t, i, c.cInit, 16)
        src = c.wvuT.rearrange("(kc p) m -> p kc m", p=128)
        dst = c.wvuS[:].rearrange("p (kc m) -> p kc m", m=D)
        i = t and t.dma_start(out=dst, in_=src)
        sig(t, i, c.cInit, 16)
    P.on("gpsimd", ev_init)

    def _emit_meanchain():
        def ev_mean_v(t):
            wt(t, c.cInit)
            if getattr(c, "fused", False):
                wt(t, c.cPt)
            if t:
                if not getattr(c, "fused", False):
                    t.reduce_sum(c.prow[0:1, 0:1], c.prow[0:1, 0:8], axis=AX.X)
                t.drain()
                t.tensor_scalar(c.prow[0:1, 1:2], c.prow[0:1, 0:1], 1.0 / 8192.0, 1e-6,
                                ALU.mult, ALU.add)
                t.drain()
                i = t.reciprocal(c.scalsS[0:1, 3:4], c.prow[0:1, 1:2])
            else:
                i = None
            sig(t, i, c.cBc)
        P.on("vector", ev_mean_v)

        def ev_bc_mm(t):
            wt(t, c.cBc)
            i = t and t.matmul(c.pS0[:, 0:4], c.ones1[0:1, :], c.scalsS[0:1, 0:4],
                               start=True, stop=True)
            sig(t, i, c.pS0p)
        P.on("tensor", ev_bc_mm)

        def ev_bc_ev(t):
            wt(t, c.pS0p)
            i = t and t.tensor_copy(c.scget[:, 0:4], c.pS0[:, 0:4])
            sig(t, i, c.cBc)
            sigd(t, c.pS0f)
        P.on("vector", ev_bc_ev)

    if not getattr(c, "fused", False):
        _emit_meanchain()

    def extras(tile):
        def ev_ms(t, tile=tile):
            wt(t, c.cHt)
            if t:
                t.tensor_copy(c.mstore[:, tile:tile + 1], c.lnst[:, 2:3])
                i = t.tensor_copy(c.rstore[:, tile:tile + 1], c.lnst[:, 6:7])
            else:
                i = None
            sig(t, i, c.cLv)
        P.on("vector", ev_ms)

        def ev_sqh(t, tile=tile):
            wt(t, c.cHt)
            i = t and t.activation(c.scr4k[:], c.ht[:], AF.Square,
                                   accum_out=c.lnst[:, 7:8])
            sig(t, i, c.cSqH)
        P.on("scalar", ev_sqh)

        def ev_lv(t, tile=tile):
            wt(t, c.cSqH)
            if t:
                t.reduce_sum(c.lnst[:, 0:1], c.ht[:], axis=AX.X)
                t.drain()
                t.tensor_scalar(c.lnst[:, 2:3], c.lnst[:, 0:1],
                                c.cst[:, C_INV_D:C_INV_D + 1], None, ALU.mult)
                t.tensor_scalar(c.lnst[:, 3:4], c.lnst[:, 7:8],
                                c.cst[:, C_INV_D:C_INV_D + 1], None, ALU.mult)
                t.drain()
                t.tensor_tensor(c.lnst[:, 4:5], c.lnst[:, 2:3], c.lnst[:, 2:3], ALU.mult)
                t.drain()
                i = t.tensor_tensor(c.lnst[:, 7:8], c.lnst[:, 3:4],
                                    c.lnst[:, 4:5], ALU.subtract)
            else:
                i = None
            sig(t, i, c.cLv)
        P.on("vector", ev_lv)

        def ev_ln1p(t, tile=tile):
            wt(t, c.cLv)
            i = t and t.activation(c.lv[:, tile:tile + 1], c.lnst[:, 7:8],
                                   AF.Ln, bias=c.cst[:, C_ONE:C_ONE + 1])
            sig(t, i, c.cLv)
        P.on("scalar", ev_ln1p)

    emit_ln_fm(P, c, c.x1, 8, c.regA, extras=extras)

    if getattr(c, "fused", False):
        def ev_pred(t):
            wt(t, c.cLv)
            i = t and t.reduce_sum(c.lnst[:, 0:1], c.lv[:], axis=AX.X)
            sig(t, i, c.cPt)
        P.on("vector", ev_pred)

        def ev_pmm(t):
            wt(t, c.cPt)
            wt(t, c.pUf)
            i = t and t.matmul(c.pU[0:1, 0:1], c.lnst[:, 0:1],
                               c.cst[:, C_ONE:C_ONE + 1], start=True, stop=True)
            sig(t, i, c.pUp)
        P.on("tensor", ev_pmm)

        def ev_pout(t):
            wt(t, c.pUp)
            i = t and t.tensor_copy(c.prow[0:1, 0:1], c.pU[0:1, 0:1])
            sig(t, i, c.cPt)
            sigd(t, c.pUf)
        P.on("vector", ev_pout)

        def ev_pst(t):
            wt(t, c.cPt)
            i = t and t.dma_start(out=c.partA[0:1, 0:1], in_=c.prow[0:1, 0:1])
            sig(t, i, c.cPt, 16)
        P.on("sync", ev_pst)

        def ev_ar(t):
            wt(t, c.cPt)
            i = t and t.collective_compute(
                "AllReduce", mybir.AluOpType.add,
                replica_groups=[list(range(8))],
                ins=[c.partA], outs=[c.partR])
            sig(t, i, c.cAR)
        P.on("gpsimd", ev_ar)

        def ev_pld(t):
            wt(t, c.cAR)
            i = t and t.dma_start(out=c.prow[0:1, 0:1], in_=c.partR[0:1, 0:1])
            sig(t, i, c.cPt, 16)
        P.on("sync", ev_pld)

        _emit_meanchain()


    for n in range(2):
        bank, bp, bf = ((c.pS0, c.pS0p, c.pS0f), (c.pS1, c.pS1p, c.pS1f))[n]

        def ev_qmm(t, n=n, bank=bank, bp=bp, bf=bf):
            wt(t, c.cInit)
            wt(t, c.cH0)
            wt(t, bf)
            i = None
            if t:
                for kc in range(32):
                    i = t.matmul(bank[0:64, 0:512], c.wq_sb[:, kc * 64:(kc + 1) * 64],
                                 c.regA[:, kc * TOK + n * 512: kc * TOK + n * 512 + 512],
                                 start=(kc == 0), stop=(kc == 31))
            sig(t, i, bp)
        P.on("tensor", ev_qmm)

        def ev_qev(t, n=n, bank=bank, bp=bp, bf=bf):
            wt(t, bp)
            i = t and t.copy(c.query[:, n * 512:(n + 1) * 512], bank[0:64, 0:512])
            sig(t, i, c.cQev)
            sigd(t, bf)
        P.on("scalar", ev_qev)

    def ev_rkmm(t):
        wt(t, c.cInit)
        wt(t, c.pAf)
        wt(t, c.pUf)
        i = None
        if t:
            t.matmul(c.pA[0:48, 0:512], c.wr_sb[:], c.ak_sb[:, 0:512],
                     start=True, stop=True)
            i = t.matmul(c.pU[0:48, 0:512], c.wr_sb[:], c.ak_sb[:, 512:1024],
                         start=True, stop=True)
        sig(t, i, c.pAp)
        sigi(t, c.pUp)
    P.on("tensor", ev_rkmm)

    def ev_rkev(t):
        wt(t, c.pAp)
        wt(t, c.pUp)
        if t:
            t.copy(c.rk[:, 0:512], c.pA[0:48, 0:512])
            i = t.copy(c.rk[:, 512:1024], c.pU[0:48, 0:512])
        else:
            i = None
        sig(t, i, c.cRKev)
        sigd(t, c.pAf)
        sigi(t, c.pUf)
    P.on("scalar", ev_rkev)

    def ev_rqmm(t):
        wt(t, c.cQev)
        wt(t, c.pAf)
        wt(t, c.pUf)
        i = None
        if t:
            t.matmul(c.pA[0:48, 0:512], c.wr_sb[:], c.query[:, 0:512],
                     start=True, stop=True)
            i = t.matmul(c.pU[0:48, 0:512], c.wr_sb[:], c.query[:, 512:1024],
                         start=True, stop=True)
        sig(t, i, c.pAp)
        sigi(t, c.pUp)
    P.on("tensor", ev_rqmm)

    def ev_rqev(t):
        wt(t, c.pAp)
        wt(t, c.pUp)
        if t:
            t.copy(c.rq[:, 0:512], c.pA[0:48, 0:512])
            i = t.copy(c.rq[:, 512:1024], c.pU[0:48, 0:512])
        else:
            i = None
        sig(t, i, c.cRQev)
        sigd(t, c.pAf)
        sigi(t, c.pUf)
    P.on("scalar", ev_rqev)

    def _emit_router(tt):
        def ev_smm(t, tt=tt):
            wt(t, c.cRQev)
            wt(t, c.cRKev)
            wt(t, c.pS0f)
            wt(t, c.pS1f)
            wt(t, c.pAf)
            wt(t, c.pUf)
            i = None
            if t:
                t.matmul(c.pS0[:, 0:512], c.rq[:, tt * 128:(tt + 1) * 128],
                         c.rk[:, 0:512], start=True, stop=True)
                t.matmul(c.pS1[:, 0:512], c.rq[:, tt * 128:(tt + 1) * 128],
                         c.rk[:, 512:1024], start=True, stop=True)
                t.matmul(c.pA[:, 0:512], c.query[:, tt * 128:(tt + 1) * 128],
                         c.ak_sb[:, 0:512], start=True, stop=True)
                i = t.matmul(c.pU[:, 0:512], c.query[:, tt * 128:(tt + 1) * 128],
                             c.ak_sb[:, 512:1024], start=True, stop=True)
            sig(t, i, c.pS0p)
            sigi(t, c.pS1p)
            sigi(t, c.pAp)
            sigi(t, c.pUp)
        P.on("tensor", ev_smm)

        def ev_sev(t, tt=tt):
            wt(t, c.pS0p)
            wt(t, c.pS1p)
            wt(t, c.pAp)
            wt(t, c.pUp)
            wt(t, c.cWt)
            if t:
                t.scalar_tensor_tensor(c.scores[:, 0:512], c.pS0[:, 0:512],
                                       1.0 / math.sqrt(48.0), c.logrelS[:, 0:512],
                                       ALU.mult, ALU.add)
                t.scalar_tensor_tensor(c.scores[:, 512:1024], c.pS1[:, 0:512],
                                       1.0 / math.sqrt(48.0), c.logrelS[:, 512:1024],
                                       ALU.mult, ALU.add)
                t.scalar_tensor_tensor(c.attnb[:, 0:512], c.pA[:, 0:512],
                                       0.125, c.logrelS[:, 0:512],
                                       ALU.mult, ALU.add)
                i = t.scalar_tensor_tensor(c.attnb[:, 512:1024], c.pU[:, 0:512],
                                           0.125, c.logrelS[:, 512:1024],
                                           ALU.mult, ALU.add)
            else:
                i = None
            sig(t, i, c.cSc)
            sigd(t, c.pS0f)
            sigi(t, c.pS1f)
            sigi(t, c.pAf)
            sigi(t, c.pUf)
        P.on("vector", ev_sev)

        def ev_topk(t, tt=tt):
            wt(t, c.cSc)
            if t:
                t.max(out=c.t8[:], in_=c.scores[:])
                t.drain()
                t.tensor_single_scalar(c.msk[:], c.scores[:], c.t8[:, 7:8], ALU.is_ge)
                t.drain()
                t.scalar_tensor_tensor(c.pbuf[:], c.msk[:], BIGNEG, c.attnb[:],
                                       ALU.mult, ALU.add)
                t.drain()
                t.reduce_max(c.lnst[:, 0:1], c.pbuf[:], axis=AX.X, negate=True)
                t.drain()
                i = t.tensor_scalar(c.lnst[:, 1:2], c.lnst[:, 0:1], BIGNEG, None,
                                    ALU.add)
            else:
                i = None
            sig(t, i, c.cNegm)
        P.on("vector", ev_topk)

        def ev_exp(t, tt=tt):
            wt(t, c.cNegm)
            i = t and t.activation(c.pbuf[:], c.attnb[:], AF.Exp,
                                   bias=c.lnst[:, 1:2])
            sig(t, i, c.cP)
        P.on("scalar", ev_exp)

        def ev_wnorm(t, tt=tt):
            wt(t, c.cP)
            if t:
                t.tensor_tensor(c.scores[:], c.pbuf[:], c.msk[:], ALU.mult)
                t.drain()
                t.reduce_sum(c.lnst[:, 2:3], c.scores[:], axis=AX.X)
                t.drain()
                t.reciprocal(c.lnst[:, 3:4], c.lnst[:, 2:3])
                t.drain()
                i = t.tensor_scalar(c.wtile[:], c.scores[:], c.lnst[:, 3:4], None,
                                    ALU.mult)
            else:
                i = None
            sig(t, i, c.cWt)
        P.on("vector", ev_wnorm)

        for cp in range(4):
            def ev_wtr(t, cp=cp, tt=tt):
                wt(t, c.cWt)
                wt(t, c.cWfm)
                if t:
                    t.transpose(c.pT0[:, 0:128],
                                c.wtile[:, (2 * cp) * 128:(2 * cp + 1) * 128],
                                c.identb[:])
                    t.transpose(c.pT1[:, 0:128],
                                c.wtile[:, (2 * cp + 1) * 128:(2 * cp + 2) * 128],
                                c.identb[:])
                    t.drain()
                sigi(t, c.pT0p)
                sigi(t, c.pT1p)
            P.on("tensor", ev_wtr)

            eng = "vector" if cp % 2 == 0 else "scalar"

            def ev_wev(t, cp=cp, tt=tt, eng=eng):
                wt(t, c.pT0p)
                wt(t, c.pT1p)
                i = None
                if t:
                    for j, bank in ((2 * cp, c.pT0), (2 * cp + 1, c.pT1)):
                        dst = c.w_fm[:, j * TOK + tt * 128: j * TOK + tt * 128 + 128]
                        i = (t.tensor_copy(dst, bank[:, 0:128]) if eng == "vector"
                             else t.copy(dst, bank[:, 0:128]))
                sig(t, i, c.cWfm, 2)
            P.on(eng, ev_wev)

    def _emit_uj(j):
        par = j % 2

        def ev_uw(t, j=j, par=par):
            wt(t, c.cWf, lag=1)
            if j == 0:
                wt(t, c.cQev)
            src = c.wu1t[j].rearrange("(kc p) m -> p kc m", p=128)
            dst = c.wbufC[:, par * 4096:(par + 1) * 4096].rearrange(
                "p (kc m) -> p kc m", m=128)
            i = t and t.dma_start(out=dst, in_=src)
            sig(t, i, c.cW, 16)
        P.on("gpsimd", ev_uw)

        for n in range(2):
            bank, bp, bf = ((c.pS0, c.pS0p, c.pS0f), (c.pS1, c.pS1p, c.pS1f),
                            (c.pA, c.pAp, c.pAf))[(j * 2 + n) % 3]

            def ev_umm(t, j=j, n=n, par=par, bank=bank, bp=bp, bf=bf):
                wt(t, c.cW)
                wt(t, c.cH0)
                wt(t, bf)
                i = None
                if t:
                    for kc in range(32):
                        lhs = c.wbufC[:, par * 4096 + kc * 128: par * 4096 + kc * 128 + 128]
                        rhs = c.regA[:, kc * TOK + n * 512: kc * TOK + n * 512 + 512]
                        i = t.matmul(bank[:, 0:512], lhs, rhs,
                                     start=(kc == 0), stop=(kc == 31))
                sig(t, i, bp)
            P.on("tensor", ev_umm)

            def ev_uev(t, j=j, n=n, bank=bank, bp=bp, bf=bf):
                wt(t, bp)
                i = t and t.activation(c.u1g[:, j * TOK + n * 512: j * TOK + n * 512 + 512],
                                       bank[:, 0:512], AF.Identity,
                                       bias=c.bu1S[:, j:j + 1])
                sig(t, i, c.cU1)
                sigd(t, bf)
                if n == 1:
                    sigi(t, c.cWf)
            P.on("scalar", ev_uev)

        def ev_ge1(t, j=j):
            wt(t, c.cU1)
            a = c.u1g[:, j * TOK:(j + 1) * TOK]
            s = c.gtmp[:, 0:TOK]
            s3 = c.gtmp[:, TOK:2 * TOK]
            if t:
                t.tensor_tensor(s, a, a, ALU.mult)
                t.tensor_tensor(s3, s, a, ALU.mult)
                i = t.scalar_tensor_tensor(s, s3, 0.044715, a, ALU.mult, ALU.add)
            else:
                i = None
            sig(t, i, c.cGe1)
        P.on("vector", ev_ge1)

        def ev_ge2(t, j=j):
            wt(t, c.cGe1)
            i = t and t.activation(c.gtmp[:, TOK:2 * TOK], c.gtmp[:, 0:TOK],
                                   AF.Tanh, scale=c.cst[:, C_GELC:C_GELC + 1])
            sig(t, i, c.cGe2)
        P.on("scalar", ev_ge2)

        def ev_ge3(t, j=j):
            wt(t, c.cGe2)
            a = c.u1g[:, j * TOK:(j + 1) * TOK]
            if t:
                t.scalar_tensor_tensor(c.gtmp[:, 0:TOK], c.gtmp[:, TOK:2 * TOK],
                                       1.0, a, ALU.add, ALU.mult)
                t.drain()
                i = t.tensor_scalar(a, c.gtmp[:, 0:TOK], 0.5, None, ALU.mult)
            else:
                i = None
            sig(t, i, c.cGe3)
        P.on("vector", ev_ge3)

    for _k in range(8):
        _emit_router(_k)
        _emit_uj(_k)

    for tt in range(8):
        def ev_u2mm(t, tt=tt):
            wt(t, c.cGe3)
            wt(t, c.pUf)
            i = None
            if t:
                for kc in range(8):
                    i = t.matmul(c.pU[:, 0:1],
                                 c.u1g[:, kc * TOK + tt * 128: kc * TOK + tt * 128 + 128],
                                 c.wu2S[:, kc:kc + 1],
                                 start=(kc == 0), stop=(kc == 7))
            sig(t, i, c.pUp)
        P.on("tensor", ev_u2mm)

        def ev_u2ev(t, tt=tt):
            wt(t, c.pUp)
            i = t and t.activation(c.u2S[:, tt:tt + 1], c.pU[:, 0:1], AF.Sigmoid,
                                   bias=c.scget[:, 2:3])
            sig(t, i, c.cU2)
            sigd(t, c.pUf)
        P.on("scalar", ev_u2ev)

    def ev_gate1(t):
        wt(t, c.cU2)
        wt(t, c.cLv)
        wt(t, c.cBc)
        if t:
            t.tensor_scalar(c.gateS[:], c.lv[:], c.scget[:, 3:4], None, ALU.mult)
            t.tensor_scalar(c.u2S[:], c.u2S[:], 2.5, None, ALU.mult)
            t.drain()
            t.scalar_tensor_tensor(c.gateS[:], c.gateS[:], 0.5, c.u2S[:],
                                   ALU.mult, ALU.add)
            t.drain()
            t.tensor_scalar(c.gateS[:], c.gateS[:], 0.0, 5.0, ALU.max, ALU.min)
            t.drain()
            t.tensor_scalar(c.gateS[:], c.gateS[:], 0.5, 1.0 / 1.5,
                            ALU.subtract, ALU.mult)
            t.drain()
            t.tensor_scalar(c.gateS[:], c.gateS[:], 0.0, 1.0, ALU.max, ALU.min)
            t.drain()
            i = t.tensor_scalar(c.gateS[:], c.gateS[:], c.scget[:, 0:1],
                                c.scget[:, 1:2], ALU.mult, ALU.add)
        else:
            i = None
        sig(t, i, c.cGate)
    P.on("vector", ev_gate1)

    def ev_gate2(t):
        wt(t, c.cGate)
        i = t and t.activation(c.u2S[:], c.gateS[:], AF.Sigmoid)
        sig(t, i, c.cGate)
    P.on("scalar", ev_gate2)

    def ev_gate3(t):
        wt(t, c.cGate)
        if t:
            t.tensor_single_scalar(c.gateS[:], c.u2S[:], 0.05, ALU.is_ge)
            t.drain()
            i = t.tensor_tensor(c.gateS[:], c.gateS[:], c.u2S[:], ALU.mult)
        else:
            i = None
        sig(t, i, c.cGate)
    P.on("vector", ev_gate3)

    for m in range(32):
        par = m % 2

        def ev_avw(t, m=m, par=par):
            wt(t, c.cWf, lag=1)
            src = c.avt[m].rearrange("(sc p) m2 -> p sc m2", p=128)
            dst = c.wbufC[:, par * 4096:par * 4096 + 1024].rearrange(
                "p (sc m2) -> p sc m2", m2=128)
            i = t and t.dma_start(out=dst, in_=src)
            sig(t, i, c.cW, 16)
        P.on("gpsimd", ev_avw)

        for n in range(2):
            bank, bp, bf = ((c.pS0, c.pS0p, c.pS0f), (c.pS1, c.pS1p, c.pS1f),
                            (c.pA, c.pAp, c.pAf))[(m * 2 + n) % 3]

            def ev_a1mm(t, m=m, n=n, par=par, bank=bank, bp=bp, bf=bf):
                wt(t, c.cW)
                wt(t, c.cWfm)
                if m == 0 and n == 0:
                    wt(t, c.cGe3)
                    wt(t, c.cU2)
                wt(t, bf)
                i = None
                if t:
                    for sc in range(8):
                        lhs = c.wbufC[:, par * 4096 + sc * 128: par * 4096 + sc * 128 + 128]
                        rhs = c.w_fm[:, sc * TOK + n * 512: sc * TOK + n * 512 + 512]
                        i = t.matmul(bank[:, 0:512], lhs, rhs,
                                     start=(sc == 0), stop=(sc == 7))
                sig(t, i, bp)
            P.on("tensor", ev_a1mm)

            def ev_a1ev(t, m=m, n=n, bank=bank, bp=bp, bf=bf):
                wt(t, bp)
                i = t and t.tensor_copy(
                    c.regA[:, m * TOK + n * 512: m * TOK + n * 512 + 512],
                    bank[:, 0:512])
                sig(t, i, c.cA1)
                sigd(t, bf)
                if n == 1:
                    sigi(t, c.cWf)
            P.on("vector", ev_a1ev)

    for kc in range(32):
        par = kc % 2

        def ev_vdw(t, kc=kc, par=par):
            wt(t, c.cVd, lag=1)
            i = t and t.dma_start(out=c.wvdb[:, par * 256:(par + 1) * 256],
                                  in_=c.wvdT[kc * 128:(kc + 1) * 128, :])
            sig(t, i, c.cVdW, 16)
        P.on("gpsimd", ev_vdw)

        def ev_a2mm(t, kc=kc, par=par):
            wt(t, c.cVdW)
            wt(t, c.cA1)
            if kc == 0:
                wt(t, c.pS0f)
                wt(t, c.pS1f)
                wt(t, c.pAf)
                wt(t, c.pUf)
            i = None
            if t:
                for m2 in range(2):
                    for n in range(2):
                        bank = ((c.pS0, c.pS1), (c.pA, c.pU))[m2][n]
                        i = t.matmul(
                            bank[:, 0:512],
                            c.wvdb[:, par * 256 + m2 * 128: par * 256 + m2 * 128 + 128],
                            c.regA[:, kc * TOK + n * 512: kc * TOK + n * 512 + 512],
                            start=(kc == 0), stop=(kc == 31))
            sig(t, i, c.cVd)
        P.on("tensor", ev_a2mm)

    def ev_a2ev(t):
        wt(t, c.cVd)
        if t:
            t.tensor_copy(c.aux2S[:, 0:512], c.pS0[:, 0:512])
            t.tensor_copy(c.aux2S[:, 512:1024], c.pS1[:, 0:512])
            t.tensor_copy(c.aux2S[:, 1024:1536], c.pA[:, 0:512])
            i = t.tensor_copy(c.aux2S[:, 1536:2048], c.pU[:, 0:512])
        else:
            i = None
        sig(t, i, c.cA2)
        sigd(t, c.pS0f)
        sigi(t, c.pS1f)
        sigi(t, c.pAf)
        sigi(t, c.pUf)
    P.on("vector", ev_a2ev)

    for tt in range(8):
        xb_ = (c.xt, getattr(c, "xt2", c.xt))[tt % 2]

        def ev_ldx2(t, tt=tt, xb_=xb_):
            wt(t, c.cXf, lag=1)
            wt(t, c.cOut, lag=16)
            wt(t, c.cA1)
            i = t and t.dma_start(out=xb_[:, 0:D], in_=c.x1[tt * 128:(tt + 1) * 128, :])
            sig(t, i, c.cX, 16)
        P.on("sync", ev_ldx2)

        for dg in range(8):
            bank, bp, bf = ((c.pS0, c.pS0p, c.pS0f), (c.pS1, c.pS1p, c.pS1f),
                            (c.pA, c.pAp, c.pAf))[dg % 3]

            def ev_a3mm(t, tt=tt, dg=dg, bank=bank, bp=bp, bf=bf):
                wt(t, c.cA2)
                wt(t, bf)
                i = None
                if t:
                    for kc2 in range(2):
                        i = t.matmul(
                            bank[:, 0:512],
                            c.aux2S[:, kc2 * TOK + tt * 128: kc2 * TOK + tt * 128 + 128],
                            c.wvuS[:, kc2 * D + dg * 512: kc2 * D + dg * 512 + 512],
                            start=(kc2 == 0), stop=(kc2 == 1))
                sig(t, i, bp)
            P.on("tensor", ev_a3mm)

            def ev_a3ev(t, tt=tt, dg=dg, bank=bank, bp=bp, bf=bf, xb_=xb_):
                wt(t, bp)
                wt(t, c.cX)
                wt(t, c.cGate)
                wt(t, c.cHt)
                i = None
                if t:
                    t.tensor_scalar(c.hsl[:], xb_[:, dg * 512:(dg + 1) * 512],
                                    c.mstore[:, tt:tt + 1], c.rstore[:, tt:tt + 1],
                                    ALU.subtract, ALU.mult)
                    t.drain()
                    i = t.scalar_tensor_tensor(c.fb[:, dg * 512:(dg + 1) * 512],
                                               bank[:, 0:512], c.gateS[:, tt:tt + 1],
                                               c.hsl[:], ALU.mult, ALU.add)
                sig(t, i, c.cFb)
                sigd(t, bf)
            P.on("vector", ev_a3ev)

        def ev_sqf(t, tt=tt):
            wt(t, c.cFb)
            wt(t, c.cHt)
            i = t and t.activation(c.scr4k[:], c.fb[:], AF.Square,
                                   accum_out=c.lnst[:, 1:2])
            sig(t, i, c.cSq)
        P.on("scalar", ev_sqf)

        def ev_statf(t):
            wt(t, c.cSq)
            if t:
                t.reduce_sum(c.lnst[:, 0:1], c.fb[:], axis=AX.X)
                t.drain()
                t.tensor_scalar(c.lnst[:, 2:3], c.lnst[:, 0:1],
                                c.cst[:, C_INV_D:C_INV_D + 1], None, ALU.mult)
                t.tensor_scalar(c.lnst[:, 3:4], c.lnst[:, 1:2],
                                c.cst[:, C_INV_D:C_INV_D + 1], None, ALU.mult)
                t.drain()
                t.tensor_tensor(c.lnst[:, 4:5], c.lnst[:, 2:3], c.lnst[:, 2:3], ALU.mult)
                t.drain()
                i = t.tensor_tensor(c.lnst[:, 4:5], c.lnst[:, 3:4], c.lnst[:, 4:5],
                                    ALU.subtract)
            else:
                i = None
            sig(t, i, c.cStat)
        P.on("vector", ev_statf)

        def ev_sqrtf(t):
            wt(t, c.cStat)
            i = t and t.activation(c.lnst[:, 5:6], c.lnst[:, 4:5], AF.Sqrt,
                                   bias=c.cst[:, C_EPS:C_EPS + 1])
            sig(t, i, c.cSqrt)
        P.on("scalar", ev_sqrtf)

        def ev_applyf(t, tt=tt, xb_=xb_):
            wt(t, c.cSqrt)
            if t:
                t.reciprocal(c.lnst[:, 6:7], c.lnst[:, 5:6])
                t.drain()
                t.tensor_scalar(c.scr4k[:], c.fb[:], c.lnst[:, 2:3], c.lnst[:, 6:7],
                                ALU.subtract, ALU.mult)
                i = t.tensor_tensor(xb_[:, 0:D], c.scr4k[:], xb_[:, 0:D], ALU.add)
            else:
                i = None
            sig(t, i, c.cHt)
            sigd(t, c.cXf)
        P.on("vector", ev_applyf)

        def ev_stx2(t, tt=tt, xb_=xb_):
            wt(t, c.cHt)
            i = t and t.dma_start(out=c.x2[tt * 128:(tt + 1) * 128, :], in_=xb_[:, 0:D])
            sig(t, i, c.cOut, 16)
        P.on("sync", ev_stx2)

    def ev_done(t):
        wt(t, c.cOut)
    P.on("sync", ev_done)




def build_phase_cb():
    import contextlib
    nc = bass.Bass("TRN2", target_bir_lowering=False)
    c = types.SimpleNamespace()
    c.x1 = nc.dram_tensor("x1", [TOK, D], F32, kind="ExternalInput").ap()
    c.partials = nc.dram_tensor("partials", [1, 8], F32, kind="ExternalInput").ap()
    c.consts = nc.dram_tensor("consts", [128, NCONST], F32, kind="ExternalInput").ap()
    c.onesr = nc.dram_tensor("onesr", [1, 128], F32, kind="ExternalInput").ap()
    c.identf = nc.dram_tensor("identf", [128, 128], F32, kind="ExternalInput").ap()
    c.wqT = nc.dram_tensor("wqT", [D, 64], F32, kind="ExternalInput").ap()
    c.wrT = nc.dram_tensor("wrT", [64, 48], F32, kind="ExternalInput").ap()
    c.akT = nc.dram_tensor("akT", [64, 1024], F32, kind="ExternalInput").ap()
    c.avt = nc.dram_tensor("avt", [32, 1024, 128], F32, kind="ExternalInput").ap()
    c.wvdT = nc.dram_tensor("wvdT", [D, 256], F32, kind="ExternalInput").ap()
    c.wvuT = nc.dram_tensor("wvuT", [256, D], F32, kind="ExternalInput").ap()
    c.wu1t = nc.dram_tensor("wu1t", [8, D, 128], F32, kind="ExternalInput").ap()
    c.bu1c = nc.dram_tensor("bu1c", [128, 8], F32, kind="ExternalInput").ap()
    c.wu2c = nc.dram_tensor("wu2c", [128, 8], F32, kind="ExternalInput").ap()
    c.scals = nc.dram_tensor("scals", [1, 4], F32, kind="ExternalInput").ap()
    c.logrel = nc.dram_tensor("logrel", [128, 1024], F32, kind="ExternalInput").ap()
    c.x2 = nc.dram_tensor("x2", [TOK, D], F32, kind="ExternalOutput").ap()

    with contextlib.ExitStack() as st:
        sb, ps = (lambda *a: _sb(st, nc, *a)), (lambda *a: _ps(st, nc, *a))
        c.cst = sb("cst", [128, NCONST], F32)
        c.ones1 = sb("ones1", [1, 128], F32)
        c.identb = sb("identb", [128, 128], BF16)
        c.regA = sb("regA", [128, 32 * TOK], BF16)
        c.u1g = sb("u1g", [128, 8 * TOK], BF16)
        c.w_fm = sb("w_fm", [128, 8 * TOK], BF16)
        c.wvuS = sb("wvuS", [128, 2 * D], BF16)
        c.aux2S = sb("aux2S", [128, 2 * TOK], BF16)
        c.qr1 = sb("qr1", [128, TOK], BF16)
        c.query = c.qr1[0:64, :]
        c.rq = c.qr1[64:112, :]
        c.akr = sb("akr", [128, TOK], BF16)
        c.ak_sb = c.akr[0:64, :]
        c.rk = c.akr[64:112, :]
        c.wr_sb = sb("wr_sb", [64, 48], BF16)
        c.xt = sb("xt", [128, D], F32)
        c.scr4k = sb("scr4k", [128, D], F32)
        c.fb = sb("fb", [128, D], F32)
        c.lnst = sb("lnst", [128, 8], F32)
        c.scores = sb("scores", [128, 1024], F32)
        c.attnb = sb("attnb", [128, 1024], F32)
        c.msk = sb("msk", [128, 1024], BF16)
        c.pbuf = sb("pbuf", [128, 1024], F32)
        c.t8 = sb("t8", [128, 8], F32)
        c.wbufC = sb("wbufC", [128, 2 * 4096], BF16)
        c.wvdb = sb("wvdb", [128, 2 * 256], BF16)
        c.logrelS = sb("logrelS", [128, 1024], BF16)
        c.lv = sb("lv", [128, 8], F32)
        c.mstore = sb("mstore", [128, 8], F32)
        c.rstore = sb("rstore", [128, 8], F32)
        c.u2S = sb("u2S", [128, 8], F32)
        c.gateS = sb("gateS", [128, 8], F32)
        c.scget = sb("scget", [128, 4], F32)
        c.scalsS = sb("scalsS", [1, 4], F32)
        c.prow = sb("prow", [1, 8], F32)
        c.gtmp = sb("gtmp", [128, 2 * TOK], BF16)
        c.bu1S = sb("bu1S", [128, 8], F32)
        c.wu2S = sb("wu2S", [128, 8], BF16)
        c.ht = c.u1g[:, 0:4096]
        c.wq_sb = c.wbufC[:, 0:2048]
        c.wtile = c.pbuf[:].bitcast(BF16)[:, 0:1024]
        c.hsl = c.scr4k[:, 0:512]
        c.pS0 = ps("pS0", [128, 512], F32)
        c.pS1 = ps("pS1", [128, 512], F32)
        c.pA = ps("pA", [128, 512], F32)
        c.pU = ps("pU", [128, 512], F32)
        c.pT0 = ps("pT0", [128, 1024], BF16)
        c.pT1 = ps("pT1", [128, 1024], BF16)
        block = st.enter_context(nc.Block())

        P = Prog(nc, st)
        for nm in ("cInit cBc cX cXf cSq cStat cSqrt cHt pT0p pT0f pT1p pT1f cH0 "
                   "cSqH cLv pS0p pS0f pS1p pS1f pAp pAf pUp pUf cW cWf "
                   "cQev cRKev cRQev cSc cMsk cNegm cP cWt cWfm cU1 cGe1 cGe2 cGe3 "
                   "cU2 cGate cA1 cA2 cA3 cVd cVdW cFb cOut").split():
            setattr(c, nm, P.ch(nm))
        emit_cb(P, c)
        P.emit(block)
    return nc


def _prep_cb(inputs, x1_outs, partials):
    consts = _consts_host_c()
    onesr = np.ones((1, 128), np.float32)
    ident = np.eye(128, dtype=np.float32)
    wqT = np.ascontiguousarray(inputs["L0_ffn_Wq"].T)
    wrT = np.ascontiguousarray(inputs["L0_ffn_Wr"].T)
    akT = np.ascontiguousarray(inputs["L0_ffn_aux_keys"].T)
    av = inputs["L0_ffn_aux_values"]
    avt = np.ascontiguousarray(av.reshape(1024, 32, 128).transpose(1, 0, 2))
    wvdT = np.ascontiguousarray(inputs["L0_ffn_Wvd"].T)
    wvuT = np.ascontiguousarray(inputs["L0_ffn_Wvu"].T)
    wu1T = np.ascontiguousarray(inputs["L0_ffn_Wu1"].T)
    wu1t = np.ascontiguousarray(wu1T.reshape(D, 8, 128).transpose(1, 0, 2))
    bu1c = np.ascontiguousarray(inputs["L0_ffn_bu1"].reshape(8, 128).T)
    wu2c = np.ascontiguousarray(inputs["L0_ffn_Wu2"].reshape(8, 128).T)
    scals = np.array([[inputs["L0_ffn_gate_w1"], inputs["L0_ffn_gate_bias"],
                       inputs["L0_ffn_bu2"].reshape(()), 0.0]], np.float32)
    logrel = np.ascontiguousarray(np.tile(inputs["L0_ffn_log_rel"][None, :], (128, 1)))
    parts_row = np.asarray(partials, np.float32).reshape(1, 8)
    in_maps = []
    for core in range(NCORES):
        in_maps.append({
            "x1": np.ascontiguousarray(np.asarray(x1_outs[core])),
            "partials": parts_row,
            "consts": consts, "onesr": onesr, "identf": ident,
            "wqT": wqT, "wrT": wrT, "akT": akT, "avt": avt,
            "wvdT": wvdT, "wvuT": wvuT, "wu1t": wu1t, "bu1c": bu1c,
            "wu2c": wu2c, "scals": scals, "logrel": logrel,
        })
    return in_maps


def run_phase_cb(inputs, x1_outs, partials):
    nc = build_phase_cb()
    in_maps = _prep_cb(inputs, x1_outs, partials)
    res = run_bass_kernel_spmd(nc, in_maps, core_ids=list(range(NCORES)))
    return [r["x2"] for r in res.results]


C_INVSQ128 = 5


def emit_d(P, c):

    KTs = c.regA[0:8, 0:16384]
    QTs = c.regA[0:8, 16384:32768]

    def ev_init(t):
        for dst, src in ((c.cst[:], c.consts), (c.identb[:], c.identf),
                         (c.bkS[:], c.bkc), (c.bvS[:], c.bvc)):
            i = t and t.dma_start(out=dst, in_=src)
            sig(t, i, c.cInit, 16)
        for j in range(4):
            i = t and t.dma_start(out=c.bqS[:], in_=c.bqj[j],
                                  accum_op=(ALU.bypass if j == 0 else ALU.add))
            sig(t, i, c.cInit, 16)
            if t:
                t.wait_ge(c.cInit.sem, c.cInit.n)
    P.on("gpsimd", ev_init)

    def ev_init_v(t):
        wt(t, c.cInit)
        t and t.memset(c.vS[:, 8 * TOK:9 * TOK], 1.0)
    P.on("vector", ev_init_v)
    for e in ("scalar", "tensor", "sync"):
        def ev_i(t):
            wt(t, c.cInit)
        P.on(e, ev_i)

    emit_ln_fm(P, c, c.x2, 8, c.regA)

    for proj in range(3):
        for m in range(8):
            def ev_pw(t, proj=proj, m=m):
                wt(t, c.cWf)
                if proj == 0 and m == 0 and getattr(c, "fused", False):
                    wt(t, c.cH0)
                if proj == 0:
                    for j in range(4):
                        src = c.wqt[m, j].rearrange("(kc p) mm -> p kc mm", p=128)
                        dst = c.wbufD[:, j * D:(j + 1) * D].rearrange(
                            "p (kc mm) -> p kc mm", mm=128)
                        i = t and t.dma_start(out=dst, in_=src)
                        sig(t, i, c.cW, 16)
                else:
                    srcT = (c.wkt if proj == 1 else c.wvt)[m]
                    src = srcT.rearrange("(kc p) mm -> p kc mm", p=128)
                    dst = c.wbufD[:, 0:D].rearrange("p (kc mm) -> p kc mm", mm=128)
                    i = t and t.dma_start(out=dst, in_=src)
                    sig(t, i, c.cW, 16)
            P.on("gpsimd", ev_pw)

            nj = 4 if proj == 0 else 1
            for n in range(2):
                bank, bp, bf = ((c.pS0, c.pS0p, c.pS0f), (c.pS1, c.pS1p, c.pS1f),
                                (c.pA, c.pAp, c.pAf))[(proj * 16 + m * 2 + n) % 3]

                def ev_pmm(t, proj=proj, m=m, n=n, nj=nj, bank=bank, bp=bp, bf=bf):
                    wt(t, c.cW)
                    wt(t, c.cH0)
                    wt(t, bf)
                    i = None
                    if t:
                        first = True
                        for j in range(nj):
                            for kc in range(32):
                                lhs = c.wbufD[:, j * D + kc * 128: j * D + kc * 128 + 128]
                                rhs = c.regA[:, kc * TOK + n * 512: kc * TOK + n * 512 + 512]
                                i = t.matmul(bank[:, 0:512], lhs, rhs,
                                             start=first,
                                             stop=(j == nj - 1 and kc == 31))
                                first = False
                    sig(t, i, bp)
                P.on("tensor", ev_pmm)

                def ev_pev(t, proj=proj, m=m, n=n, bank=bank, bp=bp, bf=bf):
                    wt(t, bp)
                    if proj == 1:
                        wt(t, c.cBnc)
                    bias = (c.bqS, c.bkS, c.bvS)[proj][:, m:m + 1]
                    if proj == 0:
                        dst = c.qgS[:, m * TOK + n * 512: m * TOK + n * 512 + 512]
                    elif proj == 2:
                        dst = c.vS[:, m * TOK + n * 512: m * TOK + n * 512 + 512]
                    else:
                        dst = c.ht[:, n * 512:(n + 1) * 512]
                    i = t and t.activation(dst, bank[:, 0:512], AF.Identity, bias=bias)
                    sig(t, i, c.cPj)
                    sigd(t, bf)
                    if n == 1:
                        sigi(t, c.cWf)
                P.on("scalar", ev_pev)

            if proj == 1:
                def ev_kst(t, m=m):
                    wt(t, c.cPj)
                    src = c.ht[:, 0:1024].rearrange("p (tc dt) -> p tc dt", dt=128)
                    dst = c.KT[m].rearrange("tc e t -> e tc t")
                    i = t and t.dma_start(out=dst, in_=src)
                    sig(t, i, c.cBnc, 16)
                P.on("sync", ev_kst)
            if proj == 0:
                def ev_qst(t, m=m):
                    wt(t, c.cPj)
                    src = c.qgS[:, m * TOK:(m + 1) * TOK].rearrange(
                        "p (tc dt) -> p tc dt", dt=128)
                    dst = c.QT[m].rearrange("tc dd t -> dd tc t")
                    i = t and t.dma_start(out=dst, in_=src)
                    sig(t, i, c.cBnc, 16)
                P.on("sync", ev_qst)

    for tc in range(8):
        par = tc % 2

        def ev_ldkq(t, tc=tc):
            wt(t, c.cBnc)
            if tc == 0:
                wt(t, c.cPj)
            else:
                wt(t, c.pS0f)
                wt(t, c.pS1f)
            i = t and t.dma_start(out=KTs.rearrange(
                "g (e t) -> g e t", t=128), in_=c.KT[:, tc])
            sig(t, i, c.cKQ, 16)
            i = t and t.dma_start(out=QTs.rearrange(
                "g (dd t) -> g dd t", t=128), in_=c.QT[:, tc])
            sig(t, i, c.cKQ, 16)
        P.on("sync", ev_ldkq)

        SBKS = ((c.pS0, c.pS0p), (c.pS1, c.pS1p), (c.pF0, c.pT0p), (c.pF1, c.pT1p))
        for blkj in range(16):
            g3 = blkj // 4
            ub_, up_, uf_ = ((c.pA, c.pAp, c.pAf), (c.pU, c.pUp, c.pUf))[g3 % 2]

            def ev_t1(t, blkj=blkj, tc=tc):
                wt(t, c.cKQ)
                if blkj == 0:
                    wt(t, c.cOtt)
                wt(t, c.cET, lag=1)
                i = None
                if t:
                    for q in range(8):
                        dt = blkj * 8 + q
                        i = t.matmul(SBKS[q // 2][0][:, (q % 2) * 128:(q % 2) * 128 + 128],
                                     KTs[:, dt:16384:128], QTs[:, dt:16384:128],
                                     start=True, stop=True)
                sig(t, i, c.cS2)
            P.on("tensor", ev_t1)

            def ev_exp(t, blkj=blkj):
                wt(t, c.cS2)
                wt(t, c.cET2, lag=1)
                i = None
                if t:
                    for b4 in range(4):
                        i = t.activation(c.ET[:, b4 * 256:(b4 + 1) * 256],
                                         SBKS[b4][0][:, 0:256], AF.Exp,
                                         scale=c.cst[:, C_INVSQ128:C_INVSQ128 + 1])
                sig(t, i, c.cET)
            P.on("scalar", ev_exp)

            def ev_t2(t, blkj=blkj, tc=tc, ub_=ub_, up_=up_, uf_=uf_):
                wt(t, c.cET)
                if blkj % 4 == 0:
                    wt(t, c.cOBv, lag=1)
                i = None
                if t:
                    for q in range(8):
                        dt = blkj * 8 + q
                        tglob = tc * 128 + dt
                        tt_ = dt % 32
                        rhs = c.vS[:, tglob:9 * TOK:TOK]
                        i = t.matmul(ub_[:, 9 * tt_:9 * tt_ + 9],
                                     c.ET[:, q * 128:(q + 1) * 128],
                                     rhs, start=True, stop=True)
                sig(t, i, up_)
                sigi(t, c.cET2)
            P.on("tensor", ev_t2)

            if blkj % 4 == 3:
                def ev_uev(t, g3=g3, ub_=ub_, up_=up_, uf_=uf_):
                    wt(t, up_)
                    i = None
                    if t:
                        uview = ub_[:, 0:288].rearrange("p (tt n) -> p tt n", n=9)
                        oview = c.OB[:, 0:1024].rearrange("p (g tt) -> p tt g", tt=128)
                        t.tensor_copy(oview[:, 32 * g3:32 * g3 + 32, :], uview[:, :, 0:8])
                        i = t.tensor_copy(c.ZB[:, 32 * g3:32 * g3 + 32], uview[:, :, 8:9])
                    sig(t, i, c.cOBv)
                P.on("vector", ev_uev)

        def ev_div(t, tc=tc):
            wt(t, c.cOBv)
            i = None
            if t:
                t.reciprocal(c.rzB[:], c.ZB[:])
                t.drain()
                for gp in range(8):
                    i = t.tensor_tensor(c.OB[:, gp * 128:(gp + 1) * 128],
                                        c.OB[:, gp * 128:(gp + 1) * 128],
                                        c.rzB[:], ALU.mult)
            sig(t, i, c.cOBd)
        P.on("vector", ev_div)

        for gp in range(8):
            bank, bp2, bf2 = ((c.pT0, c.pT0p, c.pT0f), (c.pT1, c.pT1p, c.pT1f))[gp % 2]

            def ev_otr(t, gp=gp, bank=bank, bp2=bp2, bf2=bf2):
                wt(t, c.cOBd)
                wt(t, c.cOtt, lag=1)
                if t:
                    t.transpose(bank[:, 0:128], c.OB[:, gp * 128:(gp + 1) * 128],
                                c.identb[:])
                    t.drain()
                sigi(t, bp2)
            P.on("tensor", ev_otr)

            eng = "vector" if gp % 2 == 0 else "scalar"

            def ev_oev(t, gp=gp, eng=eng, bank=bank, bp2=bp2, bf2=bf2):
                wt(t, bp2)
                wt(t, c.cXf)
                i = None
                if t:
                    for j in range(4):
                        out = c.ott[:, 4 * gp + j:4096:32]
                        i = (t.tensor_copy(out, bank[:, 0:128]) if eng == "vector"
                             else t.copy(out, bank[:, 0:128]))
                sig(t, i, c.cOtt)
            P.on(eng, ev_oev)

        if DEBUG_DUMP and tc == 0:
            def ev_dbgd(t):
                wt(t, c.cOtt)
                i = t and t.dma_start(out=c.dbgD[0, :, 0:1024], in_=c.qgS[:, 0:1024])
                sig(t, i, c.cOut, 16)
                i = t and t.dma_start(out=c.dbgD[1, :, 0:1024], in_=c.vS[:, 0:1024])
                sig(t, i, c.cOut, 16)
                i = t and t.dma_start(out=c.dbgD[2], in_=c.ott[:])
                sig(t, i, c.cOut, 16)
                i = t and t.dma_start(out=c.dbgD[3, :, 0:1024], in_=c.OB[:])
                sig(t, i, c.cOut, 16)
            P.on("sync", ev_dbgd)

        xb_ = (c.xt, getattr(c, "xt2", c.xt))[tc % 2]

        def ev_ldx(t, tc=tc, xb_=xb_):
            wt(t, c.cXf, lag=1)
            wt(t, c.cOut, lag=16)
            wt(t, c.cPj)
            i = t and t.dma_start(out=xb_[:, 0:D], in_=c.x2[tc * 128:(tc + 1) * 128, :])
            sig(t, i, c.cX, 16)
        P.on("sync", ev_ldx)

        def ev_sqf(t):
            wt(t, c.cOtt)
            wt(t, c.cHt)
            i = t and t.activation(c.scr4k[:], c.ott[:], AF.Square,
                                   accum_out=c.lnst[:, 1:2])
            sig(t, i, c.cSq)
        P.on("scalar", ev_sqf)

        def ev_statf(t):
            wt(t, c.cSq)
            if t:
                t.reduce_sum(c.lnst[:, 0:1], c.ott[:], axis=AX.X)
                t.drain()
                t.tensor_scalar(c.lnst[:, 2:3], c.lnst[:, 0:1],
                                c.cst[:, C_INV_D:C_INV_D + 1], None, ALU.mult)
                t.tensor_scalar(c.lnst[:, 3:4], c.lnst[:, 1:2],
                                c.cst[:, C_INV_D:C_INV_D + 1], None, ALU.mult)
                t.drain()
                t.tensor_tensor(c.lnst[:, 4:5], c.lnst[:, 2:3], c.lnst[:, 2:3], ALU.mult)
                t.drain()
                i = t.tensor_tensor(c.lnst[:, 4:5], c.lnst[:, 3:4], c.lnst[:, 4:5],
                                    ALU.subtract)
            else:
                i = None
            sig(t, i, c.cStat)
        P.on("vector", ev_statf)

        def ev_sqrtf(t):
            wt(t, c.cStat)
            i = t and t.activation(c.lnst[:, 5:6], c.lnst[:, 4:5], AF.Sqrt,
                                   bias=c.cst[:, C_EPS:C_EPS + 1])
            sig(t, i, c.cSqrt)
        P.on("scalar", ev_sqrtf)

        def ev_applyf(t, tc=tc, xb_=xb_):
            wt(t, c.cSqrt)
            wt(t, c.cX)
            if t:
                t.reciprocal(c.lnst[:, 6:7], c.lnst[:, 5:6])
                t.drain()
                t.tensor_scalar(c.scr4k[:], c.ott[:], c.lnst[:, 2:3], c.lnst[:, 6:7],
                                ALU.subtract, ALU.mult)
                i = t.tensor_tensor(xb_[:, 0:D], c.scr4k[:], xb_[:, 0:D], ALU.add)
            else:
                i = None
            sig(t, i, c.cHt)
            sigd(t, c.cXf)
        P.on("vector", ev_applyf)

        def ev_stx3(t, tc=tc, xb_=xb_):
            wt(t, c.cHt)
            i = t and t.dma_start(out=c.x3[tc * 128:(tc + 1) * 128, :], in_=xb_[:, 0:D])
            sig(t, i, c.cOut, 16)
        P.on("sync", ev_stx3)

    def ev_done(t):
        wt(t, c.cOut)
    P.on("sync", ev_done)




def build_phase_d():
    import contextlib
    nc = bass.Bass("TRN2", target_bir_lowering=False)
    c = types.SimpleNamespace()
    c.x2 = nc.dram_tensor("x2", [TOK, D], F32, kind="ExternalInput").ap()
    c.wqt = nc.dram_tensor("wqt", [8, 4, D, 128], F32, kind="ExternalInput").ap()
    c.wkt = nc.dram_tensor("wkt", [8, D, 128], F32, kind="ExternalInput").ap()
    c.wvt = nc.dram_tensor("wvt", [8, D, 128], F32, kind="ExternalInput").ap()
    c.bqj = nc.dram_tensor("bqj", [4, 128, 8], F32, kind="ExternalInput").ap()
    c.bkc = nc.dram_tensor("bkc", [128, 8], F32, kind="ExternalInput").ap()
    c.bvc = nc.dram_tensor("bvc", [128, 8], F32, kind="ExternalInput").ap()
    c.consts = nc.dram_tensor("consts", [128, NCONST], F32, kind="ExternalInput").ap()
    c.identf = nc.dram_tensor("identf", [128, 128], F32, kind="ExternalInput").ap()
    c.x3 = nc.dram_tensor("x3", [TOK, D], F32, kind="ExternalOutput").ap()
    if DEBUG_DUMP:
        c.dbgD = nc.dram_tensor("dbgD", [4, 128, 4096], BF16, kind="ExternalOutput").ap()
    c.KT = nc.dram_tensor("KT", [8, 8, 128, 128], BF16).ap()
    c.QT = nc.dram_tensor("QT", [8, 8, 128, 128], BF16).ap()

    with contextlib.ExitStack() as st:
        sb, ps = (lambda *a: _sb(st, nc, *a)), (lambda *a: _ps(st, nc, *a))
        c.cst = sb("cst", [128, NCONST], F32)
        c.identb = sb("identb", [128, 128], BF16)
        c.regA = sb("regA", [128, 32 * TOK], BF16)
        c.qgS = sb("qgS", [128, 8 * TOK], BF16)
        c.vS = sb("vS", [128, 9 * TOK], BF16)
        c.wbufD = sb("wbufD", [128, 4 * D], BF16)
        c.bqS = sb("bqS", [128, 8], F32)
        c.bkS = sb("bkS", [128, 8], F32)
        c.bvS = sb("bvS", [128, 8], F32)
        c.xt = sb("xt", [128, D], F32)
        c.scr4k = sb("scr4k", [128, D], F32)
        c.lnst = sb("lnst", [128, 8], F32)
        c.ht = sb("ht", [128, D], BF16)
        c.ET = sb("ET", [128, 4 * 256], BF16)
        c.OB = sb("OB", [128, 1024], BF16)
        c.ZB = sb("ZB", [128, 128], F32)
        c.rzB = sb("rzB", [128, 128], F32)
        c.ott = sb("ott", [128, D], BF16)
        c.rz = sb("rz", [128, 4], F32)
        c.pS0 = ps("pS0", [128, 512], F32)
        c.pS1 = ps("pS1", [128, 512], F32)
        c.pA = ps("pA", [128, 512], F32)
        c.pU = ps("pU", [128, 512], F32)
        c.pT0 = ps("pT0", [128, 1024], BF16)
        c.pT1 = ps("pT1", [128, 1024], BF16)
        c.pF0 = c.pT0[:].bitcast(F32)
        c.pF1 = c.pT1[:].bitcast(F32)
        block = st.enter_context(nc.Block())

        P = Prog(nc, st)
        for nm in ("cInit cX cXf cSq cStat cSqrt cHt pT0p pT0f pT1p pT1f cH0 "
                   "cW cWf pS0p pS0f pS1p pS1f pAp pAf pUp pUf cPj cBnc cKQ cKQf "
                   "cET cET2 cOBv cOBd cOtt cOut cS2").split():
            setattr(c, nm, P.ch(nm))
        emit_d(P, c)
        P.emit(block)
    return nc


def _prep_d(inputs, x2_outs):
    consts = _consts_host_c()
    consts[:, C_INVSQ128] = 1.0 / math.sqrt(128.0)
    ident = np.eye(128, dtype=np.float32)
    Wq = inputs["L1_attn_Wq"]
    Wk = inputs["L1_attn_Wk"]
    Wv = inputs["L1_attn_Wv"]
    wqt = np.ascontiguousarray(
        Wq.reshape(D, 128, 8, 4).transpose(2, 3, 0, 1))
    wkt = np.ascontiguousarray(Wk.transpose(2, 0, 1))
    wvt = np.ascontiguousarray(Wv.transpose(2, 0, 1))
    bq = inputs["L1_attn_bq"]
    bqj = np.ascontiguousarray(bq.reshape(128, 8, 4).transpose(2, 0, 1))
    bkc = np.ascontiguousarray(inputs["L1_attn_bk"])
    bvc = np.ascontiguousarray(inputs["L1_attn_bv"])
    in_maps = []
    for core in range(NCORES):
        in_maps.append({
            "x2": np.ascontiguousarray(np.asarray(x2_outs[core])),
            "wqt": wqt, "wkt": wkt, "wvt": wvt,
            "bqj": bqj, "bkc": bkc, "bvc": bvc,
            "consts": consts, "identf": ident,
        })
    return in_maps


def run_phase_d(inputs, x2_outs):
    nc = build_phase_d()
    in_maps = _prep_d(inputs, x2_outs)
    res = run_bass_kernel_spmd(nc, in_maps, core_ids=list(range(NCORES)))
    return [r["x3"] for r in res.results]


WPAD = 2816


def emit_e(P, c):

    def ev_init(t):
        for dst, src in ((c.cst[:], c.consts), (c.identb[:], c.identf),
                         (c.b2S[:], c.b2T)):
            i = t and t.dma_start(out=dst, in_=src)
            sig(t, i, c.cInit, 16)
        for dst, src in ((c.a1S, c.a1T), (c.a3S, c.a3T)):
            i = t and t.dma_start(out=dst[:].rearrange("p (kc m) -> p kc m", m=2),
                                  in_=src.rearrange("(kc p) m -> p kc m", p=128))
            sig(t, i, c.cInit, 16)
    P.on("gpsimd", ev_init)
    for e in ("scalar", "tensor", "sync", "vector"):
        def ev_i(t):
            wt(t, c.cInit)
        P.on(e, ev_i)

    emit_ln_fm(P, c, c.x3, 8, c.regA)

    for which in range(2):
        aS = (c.a1S, c.a3S)[which]
        cx = (c.c1x, c.c3x)[which]
        banks = (((c.pS0, c.pS0p, c.pS0f), (c.pS1, c.pS1p, c.pS1f)),
                 ((c.pA, c.pAp, c.pAf), (c.pU, c.pUp, c.pUf)))[which]

        def ev_cmm(t, aS=aS, banks=banks):
            wt(t, c.cH0)
            for _, _, bf in banks:
                wt(t, bf)
            i = None
            if t:
                for n in range(2):
                    bank = banks[n][0]
                    for kc in range(32):
                        i = t.matmul(bank[0:2, 0:512],
                                     aS[:, kc * 2:(kc + 1) * 2],
                                     c.regA[:, kc * TOK + n * 512: kc * TOK + n * 512 + 512],
                                     start=(kc == 0), stop=(kc == 31))
            sig(t, i, banks[1][1])
        P.on("tensor", ev_cmm)

        def ev_cev(t, cx=cx, banks=banks):
            wt(t, banks[1][1])
            i = None
            if t:
                t.tensor_scalar(cx[:, 0:512], banks[0][0][0:2, 0:512], 0.5, None, ALU.mult)
                i = t.tensor_scalar(cx[:, 512:1024], banks[1][0][0:2, 0:512], 0.5, None, ALU.mult)
            sig(t, i, c.cC13)
            for _, _, bf in banks:
                sigd(t, bf)
        P.on("vector", ev_cev)

    for i_ in range(8):
        def ev_xf(t, i_=i_):
            wt(t, c.cH0)
            if t:
                t.tensor_tensor(c.xftmp[:], c.regA[:, i_ * TOK:(i_ + 1) * TOK],
                                c.regA[:, (8 + i_) * TOK:(9 + i_) * TOK], ALU.add)
                t.tensor_tensor(c.xftmp[:], c.xftmp[:],
                                c.regA[:, (16 + i_) * TOK:(17 + i_) * TOK], ALU.add)
                i = t.tensor_tensor(c.xf[:, i_ * TOK:(i_ + 1) * TOK], c.xftmp[:],
                                    c.regA[:, (24 + i_) * TOK:(25 + i_) * TOK], ALU.add)
            else:
                i = None
            sig(t, i, c.cXff)
        P.on("vector", ev_xf)

    for k in range(22):
        for tj in range(4):
            par = (k * 4 + tj) % 2

            def ev_ew(t, k=k, tj=tj, par=par):
                wt(t, c.cYf, lag=2)
                for wi, (wt_, dst0) in enumerate(((c.w1pt, 0), (c.w3pt, 2048))):
                    src = wt_[k, tj].rearrange("(kc p) m -> p kc m", p=128)
                    dst = c.wbufE[:, par * 1024 + dst0: par * 1024 + dst0 + 1024]
                    dst = dst.rearrange("p (kc m) -> p kc m", m=128)
                    i = t and t.dma_start(out=dst, in_=src)
                    sig(t, i, c.cW, 16)
                i = t and t.dma_start(out=c.b1S[:, par * 128:(par + 1) * 128],
                                      in_=c.b1t[k, tj])
                sig(t, i, c.cW, 16)
                i = t and t.dma_start(out=c.b3S[:, par * 128:(par + 1) * 128],
                                      in_=c.b3t[k, tj])
                sig(t, i, c.cW, 16)
                i = t and t.dma_start(out=c.a2S[:, par * 2:(par + 1) * 2],
                                      in_=c.a2t[k, tj])
                sig(t, i, c.cW, 16)
            P.on("gpsimd", ev_ew)

            for n in range(2):
                def ev_bmm(t, k=k, tj=tj, n=n, par=par):
                    wt(t, c.cW)
                    wt(t, c.cXff)
                    wt(t, c.cC13)
                    wt(t, c.pS0f)
                    wt(t, c.pS1f)
                    i = None
                    if t:
                        for kc in range(8):
                            t.matmul(c.pS0[:, 0:512],
                                     c.wbufE[:, par * 1024 + kc * 128: par * 1024 + kc * 128 + 128],
                                     c.xf[:, kc * TOK + n * 512: kc * TOK + n * 512 + 512],
                                     start=(kc == 0), stop=False)
                        t.matmul(c.pS0[:, 0:512], c.b1S[:, par * 128:(par + 1) * 128],
                                 c.c1x[:, n * 512:(n + 1) * 512], start=False, stop=True)
                        for kc in range(8):
                            t.matmul(c.pS1[:, 0:512],
                                     c.wbufE[:, par * 1024 + 2048 + kc * 128:
                                             par * 1024 + 2048 + kc * 128 + 128],
                                     c.xf[:, kc * TOK + n * 512: kc * TOK + n * 512 + 512],
                                     start=(kc == 0), stop=False)
                        i = t.matmul(c.pS1[:, 0:512], c.b3S[:, par * 128:(par + 1) * 128],
                                     c.c3x[:, n * 512:(n + 1) * 512], start=False, stop=True)
                    sig(t, i, c.pS0p)
                    sigi(t, c.pS1p)
                P.on("tensor", ev_bmm)

                def ev_gu(t, n=n):
                    wt(t, c.pS0p)
                    wt(t, c.pS1p)
                    wt(t, c.cY)
                    if t:
                        t.activation(c.gub[:, 0:512], c.pS0[:, 0:512], AF.Silu)
                        i = t.activation(c.gub[:, 512:1024], c.pS1[:, 0:512], AF.Copy)
                    else:
                        i = None
                    sig(t, i, c.cGU)
                    sigd(t, c.pS0f)
                    sigi(t, c.pS1f)
                P.on("scalar", ev_gu)

                def ev_y(t, k=k, tj=tj, n=n):
                    wt(t, c.cGU)
                    wt(t, c.cYf)
                    i = t and t.tensor_tensor(c.ybuf[:], c.gub[:, 0:512],
                                              c.gub[:, 512:1024], ALU.mult)
                    sig(t, i, c.cY)
                P.on("vector", ev_y)

                def ev_fold(t, k=k, tj=tj, n=n, par=par, pF=(c.pF0, c.pF1)[n]):
                    wt(t, c.cY)
                    if tj == 0:
                        wt(t, (c.pT0f, c.pT1f)[n])
                    i = None
                    if t:
                        t.matmul(pF[:, 0:512], c.identb[:], c.ybuf[:],
                                 start=(tj == 0), stop=(tj == 3))
                        i = t.matmul((c.pA, c.pU)[n][0:2, 0:512],
                                     c.a2S[:, par * 2:(par + 1) * 2], c.ybuf[:],
                                     start=(k == 0 and tj == 0),
                                     stop=(k == 21 and tj == 3))
                    sig(t, i, c.cYf)
                P.on("tensor", ev_fold)

                if tj == 3:
                    def ev_yfev(t, k=k, n=n, pF=(c.pF0, c.pF1)[n]):
                        wt(t, c.cYf)
                        i = t and t.tensor_copy(
                            c.regA[:, k * TOK + n * 512: k * TOK + n * 512 + 512],
                            pF[:, 0:512])
                        sig(t, i, c.cOT)
                        sigd(t, (c.pT0f, c.pT1f)[n])
                    P.on("vector", ev_yfev)

    def ev_c2(t):
        wt(t, c.cYf)
        if t:
            t.tensor_scalar(c.c2x[:, 0:512], c.pA[0:2, 0:512], 0.5, None, ALU.mult)
            i = t.tensor_scalar(c.c2x[:, 512:1024], c.pU[0:2, 0:512], 0.5, None, ALU.mult)
        else:
            i = None
        sig(t, i, c.cC2)
        sigd(t, c.pAf)
        sigi(t, c.pUf)
    P.on("vector", ev_c2)

    if DEBUG_DUMP:
        def ev_dbge(t):
            wt(t, c.cOT)
            wt(t, c.cC2)
            i = t and t.dma_start(out=c.dbgE[0], in_=c.xf[:, 0:1024])
            sig(t, i, c.cOut, 16)
            i = t and t.dma_start(out=c.dbgE[1], in_=c.regA[:, 0:1024])
            sig(t, i, c.cOut, 16)
            i = t and t.dma_start(out=c.dbgE[2, 0:2, :], in_=c.c1x[:, :])
            sig(t, i, c.cOut, 16)
            i = t and t.dma_start(out=c.dbgE[3, 0:2, :], in_=c.c3x[:, :])
            sig(t, i, c.cOut, 16)
            i = t and t.dma_start(out=c.dbgE[4, 0:2, :], in_=c.c2x[:, :])
            sig(t, i, c.cOut, 16)
            i = t and t.dma_start(out=c.dbgE[5], in_=c.regA[:, 21 * TOK:22 * TOK])
            sig(t, i, c.cOut, 16)
        P.on("sync", ev_dbge)

    for dgrp in range(8):
        def ev_w2(t, dgrp=dgrp):
            wt(t, c.cW2f)
            if dgrp == 0 and getattr(c, "fused", False):
                wt(t, c.cXff)
            src = c.w2qTp[:, dgrp * 512:(dgrp + 1) * 512]
            src = src.rearrange("(kc p) m -> p kc m", p=128)
            dst = c.w2buf[:].rearrange("p (kc m) -> p kc m", m=512)
            i = t and t.dma_start(out=dst, in_=src)
            sig(t, i, c.cW2, 16)
        P.on("gpsimd", ev_w2)

        for tile in range(8):
            bank, bp, bf = ((c.pS0, c.pS0p, c.pS0f), (c.pS1, c.pS1p, c.pS1f),
                            (c.pA, c.pAp, c.pAf))[(dgrp * 8 + tile) % 3]

            def ev_omm(t, dgrp=dgrp, tile=tile, bank=bank, bp=bp, bf=bf):
                wt(t, c.cW2)
                wt(t, c.cOT)
                wt(t, c.cC2)
                wt(t, bf)
                i = None
                if t:
                    for kc in range(22):
                        t.matmul(bank[:, 0:512],
                                 c.regA[:, kc * TOK + tile * 128: kc * TOK + tile * 128 + 128],
                                 c.w2buf[:, kc * 512:(kc + 1) * 512],
                                 start=(kc == 0), stop=False)
                    i = t.matmul(bank[:, 0:512], c.c2x[:, tile * 128:(tile + 1) * 128],
                                 c.b2S[:, dgrp * 512:(dgrp + 1) * 512],
                                 start=False, stop=True)
                sig(t, i, bp)
            P.on("tensor", ev_omm)

            def ev_oev(t, dgrp=dgrp, tile=tile, bank=bank, bp=bp, bf=bf):
                wt(t, bp)
                if dgrp == 0 and tile == 0:
                    wt(t, c.cYf)
                i = t and t.tensor_copy(
                    c.regB[:, tile * 4096 + dgrp * 512: tile * 4096 + dgrp * 512 + 512],
                    bank[:, 0:512])
                sig(t, i, c.cGU)
                sigd(t, bf)
                if tile == 7:
                    sigi(t, c.cW2f)
            P.on("vector", ev_oev)

    for tile in range(8):
        xb_ = (c.xt, getattr(c, "xt2", c.xt))[tile % 2]

        def ev_ldx(t, tile=tile, xb_=xb_):
            wt(t, c.cXf, lag=1)
            wt(t, c.cOut, lag=16)
            wt(t, c.cGU)
            i = t and t.dma_start(out=xb_[:, 0:D], in_=c.x3[tile * 128:(tile + 1) * 128, :])
            sig(t, i, c.cX, 16)
        P.on("sync", ev_ldx)

        def ev_sqf(t, tile=tile):
            wt(t, c.cGU)
            wt(t, c.cHt)
            ot = c.regB[:, tile * 4096:(tile + 1) * 4096]
            i = t and t.activation(c.scr4k[:], ot, AF.Square,
                                   accum_out=c.lnst[:, 1:2])
            sig(t, i, c.cSq)
        P.on("scalar", ev_sqf)

        def ev_statf(t, tile=tile):
            wt(t, c.cSq)
            ot = c.regB[:, tile * 4096:(tile + 1) * 4096]
            if t:
                t.reduce_sum(c.lnst[:, 0:1], ot, axis=AX.X)
                t.drain()
                t.tensor_scalar(c.lnst[:, 2:3], c.lnst[:, 0:1],
                                c.cst[:, C_INV_D:C_INV_D + 1], None, ALU.mult)
                t.tensor_scalar(c.lnst[:, 3:4], c.lnst[:, 1:2],
                                c.cst[:, C_INV_D:C_INV_D + 1], None, ALU.mult)
                t.drain()
                t.tensor_tensor(c.lnst[:, 4:5], c.lnst[:, 2:3], c.lnst[:, 2:3], ALU.mult)
                t.drain()
                i = t.tensor_tensor(c.lnst[:, 4:5], c.lnst[:, 3:4], c.lnst[:, 4:5],
                                    ALU.subtract)
            else:
                i = None
            sig(t, i, c.cStat)
        P.on("vector", ev_statf)

        def ev_sqrtf(t):
            wt(t, c.cStat)
            i = t and t.activation(c.lnst[:, 5:6], c.lnst[:, 4:5], AF.Sqrt,
                                   bias=c.cst[:, C_EPS:C_EPS + 1])
            sig(t, i, c.cSqrt)
        P.on("scalar", ev_sqrtf)

        def ev_applyf(t, tile=tile, xb_=xb_):
            wt(t, c.cSqrt)
            wt(t, c.cX)
            ot = c.regB[:, tile * 4096:(tile + 1) * 4096]
            if t:
                t.reciprocal(c.lnst[:, 6:7], c.lnst[:, 5:6])
                t.drain()
                t.tensor_scalar(c.scr4k[:], ot, c.lnst[:, 2:3], c.lnst[:, 6:7],
                                ALU.subtract, ALU.mult)
                i = t.tensor_tensor(xb_[:, 0:D], c.scr4k[:], xb_[:, 0:D], ALU.add)
            else:
                i = None
            sig(t, i, c.cHt)
            sigd(t, c.cXf)
        P.on("vector", ev_applyf)

        def ev_st(t, tile=tile, xb_=xb_):
            wt(t, c.cHt)
            i = t and t.dma_start(out=c.xout[tile * 128:(tile + 1) * 128, :],
                                  in_=xb_[:, 0:D])
            sig(t, i, c.cOut, 16)
        P.on("sync", ev_st)

    def ev_done(t):
        wt(t, c.cOut)
    P.on("sync", ev_done)




def build_phase_e():
    import contextlib
    nc = bass.Bass("TRN2", target_bir_lowering=False)
    c = types.SimpleNamespace()
    c.x3 = nc.dram_tensor("x3", [TOK, D], F32, kind="ExternalInput").ap()
    c.w1pt = nc.dram_tensor("w1pt", [22, 4, 1024, 128], F32, kind="ExternalInput").ap()
    c.w3pt = nc.dram_tensor("w3pt", [22, 4, 1024, 128], F32, kind="ExternalInput").ap()
    c.b1t = nc.dram_tensor("b1t", [22, 4, 2, 128], F32, kind="ExternalInput").ap()
    c.b3t = nc.dram_tensor("b3t", [22, 4, 2, 128], F32, kind="ExternalInput").ap()
    c.a2t = nc.dram_tensor("a2t", [22, 4, 128, 2], F32, kind="ExternalInput").ap()
    c.a1T = nc.dram_tensor("a1T", [D, 2], F32, kind="ExternalInput").ap()
    c.a3T = nc.dram_tensor("a3T", [D, 2], F32, kind="ExternalInput").ap()
    c.w2qTp = nc.dram_tensor("w2qTp", [WPAD, D], F32, kind="ExternalInput").ap()
    c.b2T = nc.dram_tensor("b2T", [2, D], F32, kind="ExternalInput").ap()
    c.consts = nc.dram_tensor("consts", [128, NCONST], F32, kind="ExternalInput").ap()
    c.identf = nc.dram_tensor("identf", [128, 128], F32, kind="ExternalInput").ap()
    c.xout = nc.dram_tensor("xout", [TOK, D], F32, kind="ExternalOutput").ap()
    if DEBUG_DUMP:
        c.dbgE = nc.dram_tensor("dbgE", [6, 128, 1024], BF16, kind="ExternalOutput").ap()

    with contextlib.ExitStack() as st:
        sb, ps = (lambda *a: _sb(st, nc, *a)), (lambda *a: _ps(st, nc, *a))
        c.cst = sb("cst", [128, NCONST], F32)
        c.identb = sb("identb", [128, 128], BF16)
        c.regA = sb("regA", [128, 32 * TOK], BF16)
        c.regB = sb("regB", [128, 32 * TOK], BF16)
        c.xt = sb("xt", [128, D], F32)
        c.scr4k = sb("scr4k", [128, D], F32)
        c.lnst = sb("lnst", [128, 8], F32)
        c.wbufE = sb("wbufE", [128, 4 * 1024], BF16)
        c.w2buf = sb("w2buf", [128, 22 * 512], BF16)
        c.a1S = sb("a1S", [128, 64], BF16)
        c.a3S = sb("a3S", [128, 64], BF16)
        c.b1S = sb("b1S", [2, 2 * 128], BF16)
        c.b3S = sb("b3S", [2, 2 * 128], BF16)
        c.a2S = sb("a2S", [128, 2 * 2], BF16)
        c.b2S = sb("b2S", [2, D], BF16)
        c.c1x = sb("c1x", [2, TOK], BF16)
        c.c3x = sb("c3x", [2, TOK], BF16)
        c.c2x = sb("c2x", [2, TOK], BF16)
        c.pS0 = ps("pS0", [128, 512], F32)
        c.pS1 = ps("pS1", [128, 512], F32)
        c.pA = ps("pA", [128, 512], F32)
        c.pU = ps("pU", [128, 512], F32)
        c.pT0 = ps("pT0", [128, 1024], BF16)
        c.pT1 = ps("pT1", [128, 1024], BF16)
        c.xf = c.regB[:, 0:8192]
        c.ht = c.regB[:, 8192:12288]
        c.xftmp = c.scr4k[:, 0:1024]
        c.ybuf = c.scr4k[:, 1024:1280].bitcast(BF16)
        c.gub = c.scr4k[:, 1280:1792].bitcast(BF16)
        c.pF0 = c.pT0.ap().bitcast(F32) if False else None
        block = st.enter_context(nc.Block())
        c.pF0 = c.pT0[:].bitcast(F32)
        c.pF1 = c.pT1[:].bitcast(F32)

        P = Prog(nc, st)
        for nm in ("cInit cX cXf cSq cStat cSqrt cHt pT0p pT0f pT1p pT1f cH0 "
                   "cW cWf cW2 cW2f pS0p pS0f pS1p pS1f pAp pAf pUp pUf "
                   "cC13 cXff cGU cY cYf cC2 cOT cOut").split():
            setattr(c, nm, P.ch(nm))
        emit_e(P, c)
        P.emit(block)
    return nc


def _prep_e(inputs, x3_outs):
    consts = _consts_host_c()
    ident = np.eye(128, dtype=np.float32)

    def padw(w):
        w4 = w.reshape(4, 2752, -1)
        out = np.zeros((4, WPAD, w4.shape[2]), np.float32)
        out[:, :2752] = w4
        return out

    w1 = padw(inputs["L1_ffn_w1q"])
    w3 = padw(inputs["L1_ffn_w3q"])
    w1pt = np.ascontiguousarray(
        w1.reshape(4, 22, 128, 1024).transpose(1, 0, 3, 2))
    w3pt = np.ascontiguousarray(
        w3.reshape(4, 22, 128, 1024).transpose(1, 0, 3, 2))
    b1 = padw(inputs["L1_ffn_b1"])
    b3 = padw(inputs["L1_ffn_b3"])
    b1t = np.ascontiguousarray(b1.reshape(4, 22, 128, 2).transpose(1, 0, 3, 2))
    b3t = np.ascontiguousarray(b3.reshape(4, 22, 128, 2).transpose(1, 0, 3, 2))
    a2 = padw(np.ascontiguousarray(inputs["L1_ffn_a2"].T))
    a2t = np.ascontiguousarray(a2.reshape(4, 22, 128, 2).transpose(1, 0, 2, 3))
    a1T = np.ascontiguousarray(inputs["L1_ffn_a1"].T)
    a3T = np.ascontiguousarray(inputs["L1_ffn_a3"].T)
    w2qT = np.ascontiguousarray(inputs["L1_ffn_w2q"].T)
    w2qTp = np.zeros((WPAD, D), np.float32)
    w2qTp[:2752] = w2qT
    b2T = np.ascontiguousarray(inputs["L1_ffn_b2"].T)
    in_maps = []
    for core in range(NCORES):
        in_maps.append({
            "x3": np.ascontiguousarray(np.asarray(x3_outs[core])),
            "w1pt": w1pt, "w3pt": w3pt, "b1t": b1t, "b3t": b3t, "a2t": a2t,
            "a1T": a1T, "a3T": a3T, "w2qTp": w2qTp, "b2T": b2T,
            "consts": consts, "identf": ident,
        })
    return in_maps


def run_phase_e(inputs, x3_outs):
    nc = build_phase_e()
    in_maps = _prep_e(inputs, x3_outs)
    res = run_bass_kernel_spmd(nc, in_maps, core_ids=list(range(NCORES)))
    return [r["xout"] for r in res.results]



POOLC = 56832


class Carve:
    def __init__(self, pool):
        self.pool = pool
        self.off = 0

    def take(self, ncols, dtype=BF16):
        units = ncols * (2 if dtype == F32 else 1)
        sl = self.pool[:, self.off:self.off + units]
        self.off += (units + 15) // 16 * 16
        assert self.off <= POOLC, f"pool overflow {self.off}"
        return sl.bitcast(F32) if dtype == F32 else sl

    def at(self, off_units, ncols, dtype=BF16):
        units = ncols * (2 if dtype == F32 else 1)
        sl = self.pool[:, off_units:off_units + units]
        return sl.bitcast(F32) if dtype == F32 else sl


def _alias_a(c, pool):
    cv = Carve(pool)
    c.wbuf = cv.take(24576)
    c.cbuf = cv.take(1536)
    c.qf = cv.take(1024)
    c.kf = cv.take(1024)
    c.vf = cv.take(1024)
    c.stg = cv.take(2048)
    c.STb = cv.take(512)
    c.ckkd = cv.take(256)
    c.cvt = cv.take(256)
    c.cqd = cv.take(256)
    c.kvs = cv.take(2048, F32)
    c.kvb = cv.take(2048)
    c.kdS = cv.take(32, F32)
    c.bdS = cv.take(16, F32)
    c.ht = cv.take(4096)
    c.stgM = cv.take(2048)
    c.gmS = cv.take(2, F32)
    c.xt2 = cv.take(4096, F32)
    c.h0 = c.regA


def _alias_b(c, pool):
    cv = Carve(pool)
    c.regB = cv.at(0, 32768)
    c.ht = cv.at(0, 4096)
    c.xt2 = cv.at(36864, 4096, F32)
    cv.off = 32768
    c.wb = cv.take(8192)
    c.ft = cv.at(32768, 4096)
    c.abuf = cv.take(2048)
    c.sqb = cv.take(1024)
    c.rstdb = cv.take(1024, F32)
    c.rrow = cv.take(1024, F32)[0:1, :]
    c.rrow2 = cv.take(1024, F32)[0:1, :]
    c.rrowb = cv.take(1024)[0:1, :]
    c.normwS = cv.take(32, F32)
    c.onesB = cv.take(128)


def _alias_cb(c, pool):
    cv = Carve(pool)
    c.u1g = cv.take(8192)
    c.ht = cv.at(0, 4096)
    c.xt2 = cv.at(8192, 4096, F32)
    c.w_fm = cv.take(8192)
    c.wbufC = cv.take(8192)
    c.fb = cv.at(16384, 4096, F32)
    c.wvuS = cv.take(8192)
    c.scores = cv.take(1024, F32)
    c.attnb = cv.take(1024, F32)
    c.pbuf = cv.take(1024, F32)
    c.wtile = c.pbuf[:].bitcast(BF16)[:, 0:1024]
    c.msk = cv.take(1024)
    c.gtmp = cv.take(2048)
    c.aux2S = cv.take(2048)
    c.qr1 = cv.take(1024)
    c.query = c.qr1[0:64, :]
    c.rq = c.qr1[64:112, :]
    c.akr = cv.take(1024)
    c.ak_sb = c.akr[0:64, :]
    c.rk = c.akr[64:112, :]
    c.logrelS = cv.take(1024)
    c.wvdb = cv.take(512)
    c.wr_sb = cv.take(48)[0:64, :]
    c.wq_sb = c.wbufC[:, 0:2048]
    c.lv = cv.take(8, F32)
    c.mstore = cv.take(8, F32)
    c.rstore = cv.take(8, F32)
    c.u2S = cv.take(8, F32)
    c.gateS = cv.take(8, F32)
    c.scget = cv.take(4, F32)
    c.scalsS = cv.take(4, F32)[0:1, :]
    c.prow = cv.take(8, F32)[0:1, :]
    c.t8 = cv.take(8, F32)
    c.bu1S = cv.take(8, F32)
    c.wu2S = cv.take(8)
    c.hsl = c.scr4k[:, 0:512]


def _alias_d(c, pool):
    cv = Carve(pool)
    c.qgS = cv.take(8192)
    c.ht = cv.at(0, 4096)
    c.vS = cv.take(9216)
    c.wbufD = cv.take(16384)
    c.xt2 = cv.at(17408, 4096, F32)
    c.ott = cv.take(4096)
    c.OB = cv.take(1024)
    c.ET = cv.take(1024)
    c.rz = cv.take(4, F32)
    c.ZB = cv.take(128, F32)
    c.rzB = cv.take(128, F32)
    c.bqS = cv.take(8, F32)
    c.bkS = cv.take(8, F32)
    c.bvS = cv.take(8, F32)


def _alias_e(c, pool):
    cv = Carve(pool)
    c.regB = cv.at(0, 32768)
    c.xf = cv.at(0, 8192)
    c.ht = cv.at(8192, 4096)
    cv.off = 32768
    c.wbufE = cv.take(4096)
    c.w2buf = cv.take(11264)
    c.xt2 = cv.at(36864, 4096, F32)
    c.a1S = cv.take(64)
    c.a3S = cv.take(64)
    c.b1S = cv.take(256)[0:2, :]
    c.b3S = cv.take(256)[0:2, :]
    c.a2S = cv.take(4)
    c.b2S = cv.take(4096)[0:2, :]
    c.c1x = cv.take(1024)[0:2, :]
    c.c3x = cv.take(1024)[0:2, :]
    c.c2x = cv.take(1024)[0:2, :]
    c.xftmp = c.scr4k[:, 0:1024]
    c.ybuf = c.scr4k[:, 1024:1280].bitcast(BF16)
    c.gub = c.scr4k[:, 1280:1792].bitcast(BF16)


FUSED_CHANS = (
    "cInit cBc cX cXf cSq cStat cSqrt cHt pT0p pT0f pT1p pT1f pV0p pV0f pV1p pV1f "
    "cH0 cH0f cW cWf cC cCf pS0p pS0f pS1p pS1f pAp pAf pUp pUf cQKV cST cCK cCV "
    "cCQD cKV cStgV cOut cGate cA cAf cSqB cRr cRstd cRb cRbb cProd cOT cFt "
    "cSqH cLv cQev cRKev cRQev cSc cMsk cNegm cP cWt cWfm cU1 cGe1 cGe2 cGe3 "
    "cU2 cA1 cA2 cA3 cVd cVdW cFb cPj cBnc cKQ cKQf cET cOBv cOtt cW2 cW2f "
    "cC13 cXff cGU cY cYf cC2 cPt cAR cAG cBar cStgM pRp pRf cET2 cOBd cS2")


def emit_barrier(P, c):
    for e in ("vector", "scalar", "tensor"):
        def ev_s(t):
            sigd(t, c.cBar)
        P.on(e, ev_s)

    def ev_sync(t):
        wt(t, c.cOut)
        sigi(t, c.cBar)
    P.on("sync", ev_sync)

    def ev_g(t):
        sigi(t, c.cBar)
    P.on("gpsimd", ev_g)
    for e in Prog.ENGINES:
        def ev_w(t):
            wt(t, c.cBar)
        P.on(e, ev_w)


def build_fused():
    import contextlib
    from concourse.aot_env import allow_ici_all_to_all
    nc = bass.Bass("TRN2", target_bir_lowering=False)
    c = types.SimpleNamespace()
    c.fused = True
    dt = nc.dram_tensor
    c.xb = dt("xb", [S, D], F32, kind="ExternalInput").ap()
    c.xown = dt("xown", [TOK, D], F32, kind="ExternalInput").ap()
    c.qkvT = dt("qkvT", [D, HL * 384], F32, kind="ExternalInput").ap()
    c.dcb = dt("dcb", [HL, 128, 768], F32, kind="ExternalInput").ap()
    c.kdall = dt("kdall", [128, 2 * HL], F32, kind="ExternalInput").ap()
    c.bdall = dt("bdall", [128, HL], F32, kind="ExternalInput").ap()
    c.identf = dt("identf", [128, 128], F32, kind="ExternalInput").ap()
    c.consts = dt("consts", [128, NCONST], F32, kind="ExternalInput").ap()
    c.gateTt = dt("gateTt", [32, D, 128], F32, kind="ExternalInput").ap()
    c.outTt = dt("outTt", [32, D, 128], F32, kind="ExternalInput").ap()
    c.normw = dt("normw", [128, 32], F32, kind="ExternalInput").ap()
    c.onesr = dt("onesr", [1, 128], F32, kind="ExternalInput").ap()
    c.wqT = dt("wqT", [D, 64], F32, kind="ExternalInput").ap()
    c.wrT = dt("wrT", [64, 48], F32, kind="ExternalInput").ap()
    c.akT = dt("akT", [64, 1024], F32, kind="ExternalInput").ap()
    c.avt = dt("avt", [32, 1024, 128], F32, kind="ExternalInput").ap()
    c.wvdT = dt("wvdT", [D, 256], F32, kind="ExternalInput").ap()
    c.wvuT = dt("wvuT", [256, D], F32, kind="ExternalInput").ap()
    c.wu1t = dt("wu1t", [8, D, 128], F32, kind="ExternalInput").ap()
    c.bu1c = dt("bu1c", [128, 8], F32, kind="ExternalInput").ap()
    c.wu2c = dt("wu2c", [128, 8], F32, kind="ExternalInput").ap()
    c.scals = dt("scals", [1, 4], F32, kind="ExternalInput").ap()
    c.logrel = dt("logrel", [128, 1024], F32, kind="ExternalInput").ap()
    c.wqt = dt("wqt", [8, 4, D, 128], F32, kind="ExternalInput").ap()
    c.wkt = dt("wkt", [8, D, 128], F32, kind="ExternalInput").ap()
    c.wvt = dt("wvt", [8, D, 128], F32, kind="ExternalInput").ap()
    c.bqj = dt("bqj", [4, 128, 8], F32, kind="ExternalInput").ap()
    c.bkc = dt("bkc", [128, 8], F32, kind="ExternalInput").ap()
    c.bvc = dt("bvc", [128, 8], F32, kind="ExternalInput").ap()
    c.w1pt = dt("w1pt", [22, 4, 1024, 128], F32, kind="ExternalInput").ap()
    c.w3pt = dt("w3pt", [22, 4, 1024, 128], F32, kind="ExternalInput").ap()
    c.b1t = dt("b1t", [22, 4, 2, 128], F32, kind="ExternalInput").ap()
    c.b3t = dt("b3t", [22, 4, 2, 128], F32, kind="ExternalInput").ap()
    c.a2t = dt("a2t", [22, 4, 128, 2], F32, kind="ExternalInput").ap()
    c.a1T = dt("a1T", [D, 2], F32, kind="ExternalInput").ap()
    c.a3T = dt("a3T", [D, 2], F32, kind="ExternalInput").ap()
    c.w2qTp = dt("w2qTp", [WPAD, D], F32, kind="ExternalInput").ap()
    c.b2T = dt("b2T", [2, D], F32, kind="ExternalInput").ap()
    c.xout = dt("xout", [TOK, D], F32, kind="ExternalOutput").ap()
    c.gmask = dt("gmask", [128, 2], F32, kind="ExternalInput").ap()
    ikind = "ExternalOutput" if FUSED_DEBUG else "Internal"
    c.rsin = dt("rsin", [2 * D, 1024], BF16).ap()
    c.afm_rs = dt("afm_rs", [D, 1024], BF16).ap()
    if FUSED_DEBUG:
        c.dbgafm = dt("dbgafm", [D, 1024], BF16, kind="ExternalOutput").ap()
        c.dbg2 = dt("dbg2", [128, 1074], F32, kind="ExternalOutput").ap()
        c.dbg3 = dt("dbg3", [256, 1024], BF16, kind="ExternalOutput").ap()
        c.dbg4 = dt("dbg4", [128, 1024], BF16, kind="ExternalOutput").ap()
    c.afm = c.afm_rs
    c.x1 = dt("x1", [TOK, D], F32, kind=ikind).ap()
    c.x2 = dt("x2", [TOK, D], F32, kind=ikind).ap()
    c.x3 = dt("x3", [TOK, D], F32, kind=ikind).ap()
    c.partA = dt("partA", [1, 1], F32).ap()
    c.partR = dt("partR", [1, 1], F32).ap()
    c.KT = dt("KT", [8, 8, 128, 128], BF16).ap()
    c.QT = dt("QT", [8, 8, 128, 128], BF16).ap()

    with contextlib.ExitStack() as st:
        sb, ps = (lambda *a: _sb(st, nc, *a)), (lambda *a: _ps(st, nc, *a))
        c.cst = sb("cst", [128, NCONST], F32)
        c.identb = sb("identb", [128, 128], BF16)
        c.ones1 = sb("ones1", [1, 128], F32)
        c.regA = sb("regA", [128, 32768], BF16)
        c.pool = sb("pool", [128, POOLC], BF16)
        c.xt = sb("xt", [128, D], F32)
        c.scr4k = sb("scr4k", [128, D], F32)
        c.lnst = sb("lnst", [128, 8], F32)
        c.pS0 = ps("pS0", [128, 512], F32)
        c.pS1 = ps("pS1", [128, 512], F32)
        c.pA = ps("pA", [128, 512], F32)
        c.pU = ps("pU", [128, 512], F32)
        c.pT0 = ps("pT0", [128, 1024], BF16)
        c.pT1 = ps("pT1", [128, 1024], BF16)
        c.pV0 = ps("pV0", [128, 1024], BF16)
        c.pV1 = ps("pV1", [128, 1024], BF16)
        c.pR = c.pV0[:].bitcast(F32)[:, 0:512]
        c.pF0 = c.pT0[:].bitcast(F32)
        c.pF1 = c.pT1[:].bitcast(F32)
        block = st.enter_context(nc.Block())

        P = Prog(nc, st)
        for nm in FUSED_CHANS.split():
            setattr(c, nm, P.ch(nm))

        def ev_ones(t):
            i = t and t.dma_start(out=c.ones1[:], in_=c.onesr)
            sig(t, i, c.cInit, 16)
        P.on("gpsimd", ev_ones)

        cA = types.SimpleNamespace(**vars(c))
        _alias_a(cA, c.pool)
        emit_a(P, cA)
        emit_barrier(P, c)

        def ev_rs(t):
            i = t and t.collective_compute(
                "ReduceScatter", mybir.AluOpType.add,
                replica_groups=[[0, 1], [2, 3], [4, 5], [6, 7]],
                ins=[c.rsin], outs=[c.afm_rs])
            sig(t, i, c.cAG)
        P.on("gpsimd", ev_rs)

        if FUSED_DEBUG:
            def ev_dbgafm(t):
                wt(t, c.cAG)
                i = t and t.dma_start(out=c.dbgafm, in_=c.afm_rs)
                sig(t, i, c.cOut, 16)
                i = t and t.dma_start(out=c.dbg2[0:128, 0:2], in_=cA.gmS[:])
                sig(t, i, c.cOut, 16)
                i = t and t.dma_start(out=c.dbg2[0:128, 2:34], in_=cA.kdS[:])
                sig(t, i, c.cOut, 16)
                i = t and t.dma_start(out=c.dbg2[0:128, 34:50], in_=cA.bdS[:])
                sig(t, i, c.cOut, 16)
                i = t and t.dma_start(out=c.dbg2[0:128, 50:562],
                                      in_=cA.stgM[:, 0:1024].bitcast(F32)[:, 0:512])
                sig(t, i, c.cOut, 16)
                i = t and t.dma_start(out=c.dbg3[0:128, :], in_=c.rsin[0:128, :])
                sig(t, i, c.cOut, 16)
                i = t and t.dma_start(out=c.dbg3[128:256, :], in_=c.rsin[2048:2176, :])
                sig(t, i, c.cOut, 16)
            P.on("sync", ev_dbgafm)

        cB = types.SimpleNamespace(**vars(c))
        _alias_b(cB, c.pool)
        emit_b(P, cB)
        emit_barrier(P, c)
        cC = types.SimpleNamespace(**vars(c))
        _alias_cb(cC, c.pool)
        emit_cb(P, cC)
        emit_barrier(P, c)
        cD = types.SimpleNamespace(**vars(c))
        _alias_d(cD, c.pool)
        emit_d(P, cD)
        emit_barrier(P, c)
        cE = types.SimpleNamespace(**vars(c))
        _alias_e(cE, c.pool)
        emit_e(P, cE)

        with allow_ici_all_to_all():
            P.emit(block)
    return nc


def _prep_fused(inputs):
    consts = _consts_host_c()
    consts[:, C_INVSQ128] = 1.0 / math.sqrt(128.0)
    maps_a = _prep_a(inputs)
    dummy_att = [np.zeros((HL, 128, S), BF) for _ in range(NCORES)]
    maps_b = _prep_b(inputs, dummy_att)
    maps_cb = _prep_cb(inputs, [np.zeros((TOK, D), np.float32)] * NCORES,
                       np.zeros(8, np.float32))
    maps_d = _prep_d(inputs, [np.zeros((TOK, D), np.float32)] * NCORES)
    maps_e = _prep_e(inputs, [np.zeros((TOK, D), np.float32)] * NCORES)
    onesr = np.ones((1, 128), np.float32)
    in_maps = []
    for core in range(NCORES):
        m = {}
        for srcm, drop in ((maps_a[core], ("attn_fm",)),
                           (maps_b[core], ("afm",)),
                           (maps_cb[core], ("x1", "partials")),
                           (maps_d[core], ("x2",)),
                           (maps_e[core], ("x3",))):
            for k, v in srcm.items():
                if k not in drop:
                    m[k] = v
        m["consts"] = consts
        m["onesr"] = onesr
        gm = np.zeros((128, 2), np.float32)
        gm[:, core % 2] = 1.0
        m["gmask"] = gm
        in_maps.append(m)
    return in_maps


def kernel_fused(inputs):
    nc = build_fused()
    in_maps = _prep_fused(inputs)
    res = run_bass_kernel_spmd(nc, in_maps, core_ids=list(range(NCORES)))
    out = np.zeros((B, S, D), np.float32)
    for core in range(NCORES):
        bb, g = core // 2, core % 2
        out[bb, g * TOK:(g + 1) * TOK] = np.asarray(res.results[core]["xout"])
    return out


# version 13
# speedup vs baseline: 1.0561x; 1.0192x over previous
import math
import types
import numpy as np
import ml_dtypes
import concourse.bass as bass
import concourse.mybir as mybir
from concourse.bass_utils import run_bass_kernel_spmd

F32 = mybir.dt.float32
BF16 = mybir.dt.bfloat16
AF = mybir.ActivationFunctionType
ALU = mybir.AluOpType
AX = mybir.AxisListType
BF = ml_dtypes.bfloat16

NCORES = 8
B, S, D = 4, 2048, 4096
HEADS, HD, LBLK = 32, 128, 256
HL = 16
TOK = 1024
NB_SH = 4
DEBUG_DUMP = False
D_DBUF = True
FUSED_DEBUG = False


def _decays():
    base = 1.0 / 2.0 ** (8.0 / HEADS)
    exponent = np.arange(HEADS, dtype=np.float64) + 1.0
    factor = 1.0 - 1 / (32 - 1 + 1e-05) + 1e-05
    slope = (base ** exponent * factor).astype(np.float32)[:, None, None].astype(np.float32)
    r = np.arange(LBLK, dtype=np.float32) + 1.0
    q_decay = np.exp(-slope * r[:, None]).astype(np.float32)[:, :, 0]
    k_decay = np.exp(-slope * (LBLK - r)[:, None]).astype(np.float32)[:, :, 0]
    diff = r[:, None] - r[None, :]
    diag = slope * diff[None]
    diag = np.where(diag >= 0, -diag, -np.inf)
    diag_decay = np.exp(diag).astype(np.float32)
    block_decay = np.exp(-slope * LBLK).astype(np.float32)[:, 0, 0]
    return q_decay, k_decay, diag_decay, block_decay


C_EPS, C_INV_D, C_ONE, C_ZERO = 0, 1, 2, 3
NCONST = 8

def _consts_host():
    c = np.zeros(NCONST, np.float32)
    c[C_EPS] = 1e-5
    c[C_INV_D] = 1.0 / D
    c[C_ONE] = 1.0
    c[C_ZERO] = 0.0
    return np.tile(c[None, :], (128, 1))



class Ch:
    __slots__ = ("sem", "n")

    def __init__(self, sem):
        self.sem = sem
        self.n = 0


class Prog:

    ENGINES = ("gpsimd", "sync", "tensor", "vector", "scalar")

    def __init__(self, nc, ctxstack):
        self.nc = nc
        self.events = []
        self.chans = []
        self._stack = ctxstack

    def ch(self, name):
        sem = self._stack.enter_context(self.nc.semaphore(name))
        c = Ch(sem)
        self.chans.append(c)
        return c

    def on(self, eng_name, fn):
        assert eng_name in self.ENGINES
        self.events.append((eng_name, fn))

    def emit(self, block):
        for name in self.ENGINES:
            deco = getattr(block, name)

            def body(eng, name=name):
                for c in self.chans:
                    c.n = 0
                for (e, fn) in self.events:
                    fn(eng if e == name else None)

            deco(body)


def sig(t, inst, ch, k=1):
    ch.n += k
    if t is not None and inst is not None:
        inst.then_inc(ch.sem, k)


def sigi(t, ch, k=1):
    ch.n += k
    if t is not None:
        t.sem_inc(ch.sem, k)


def sigd(t, ch, k=1):
    ch.n += k
    if t is not None:
        t.drain()
        t.sem_inc(ch.sem, k)


def wt(t, ch, lag=0):
    if t is not None:
        v = ch.n - lag
        if v > 0:
            t.wait_ge(ch.sem, v)


def _sb(stack, nc, name, shape, dtype):
    return stack.enter_context(nc.sbuf_tensor(name, shape, dtype))


def _ps(stack, nc, name, shape, dtype):
    return stack.enter_context(nc.psum_tensor(name, shape, dtype))



def emit_a(P, c):

    def ev_init(t):
        i = t and t.dma_start(out=c.cst[:], in_=c.consts)
        sig(t, i, c.cInit, 16)
        i = t and t.dma_start(out=c.identb[:], in_=c.identf)
        sig(t, i, c.cInit, 16)
        i = t and t.dma_start(out=c.kdS[:], in_=c.kdall)
        sig(t, i, c.cInit, 16)
        i = t and t.dma_start(out=c.bdS[:], in_=c.bdall)
        sig(t, i, c.cInit, 16)
        if getattr(c, "fused", False):
            i = t and t.dma_start(out=c.gmS[:], in_=c.gmask)
            sig(t, i, c.cInit, 16)
    P.on("gpsimd", ev_init)

    def ev_init_v(t):
        wt(t, c.cInit)
        t and t.memset(c.kvs[:], 0.0)
    P.on("vector", ev_init_v)

    def ev_init_s(t):
        wt(t, c.cInit)
    P.on("scalar", ev_init_s)

    def ev_init_t(t):
        wt(t, c.cInit)
    P.on("tensor", ev_init_t)

    for sh in range(2):
        _emit_a_stage1(P, c, sh)
        if DEBUG_DUMP and sh == 0:
            def ev_dbg(t):
                wt(t, c.cH0)
                for j in range(8):
                    i = t and t.dma_start(out=c.dbg[3, :, j * 128:(j + 1) * 128],
                                          in_=c.h0[:, j * 1024: j * 1024 + 128])
                    sig(t, i, c.cOut, 16)
            P.on("sync", ev_dbg)
        _emit_a_heads(P, c, sh)
        if DEBUG_DUMP and sh == 0:
            def ev_dbg2(t):
                wt(t, c.cQKV)
                for j, buf in enumerate((c.qf, c.kf, c.vf)):
                    i = t and t.dma_start(out=c.dbg[j], in_=buf[:])
                    sig(t, i, c.cOut, 16)
            P.on("sync", ev_dbg2)

    def ev_done(t):
        wt(t, c.cOut)
    P.on("sync", ev_done)




def build_phase_a():
    import contextlib
    nc = bass.Bass("TRN2", target_bir_lowering=False)
    c = types.SimpleNamespace()
    c.xb = nc.dram_tensor("xb", [S, D], F32, kind="ExternalInput").ap()
    c.qkvT = nc.dram_tensor("qkvT", [D, HL * 384], BF16, kind="ExternalInput").ap()
    c.dcb = nc.dram_tensor("dcb", [HL, 128, 768], F32, kind="ExternalInput").ap()
    c.kdall = nc.dram_tensor("kdall", [128, 2 * HL], F32, kind="ExternalInput").ap()
    c.bdall = nc.dram_tensor("bdall", [128, HL], F32, kind="ExternalInput").ap()
    c.identf = nc.dram_tensor("identf", [128, 128], F32, kind="ExternalInput").ap()
    c.consts = nc.dram_tensor("consts", [128, NCONST], F32, kind="ExternalInput").ap()
    c.attn_out = nc.dram_tensor("attn_fm", [HL, 128, S], BF16, kind="ExternalOutput").ap()
    if DEBUG_DUMP:
        c.dbg = nc.dram_tensor("dbg", [4, 128, 1024], BF16, kind="ExternalOutput").ap()

    with contextlib.ExitStack() as st:
        sb, ps = (lambda *a: _sb(st, nc, *a)), (lambda *a: _ps(st, nc, *a))
        c.cst = sb("cst", [128, NCONST], F32)
        c.identb = sb("identb", [128, 128], BF16)
        c.kdS = sb("kdS", [128, 2 * HL], F32)
        c.bdS = sb("bdS", [128, HL], F32)
        c.h0 = sb("h0", [128, 32 * 1024], BF16)
        c.xt = sb("xt", [128, D], F32)
        c.scr4k = sb("scr4k", [128, D], F32)
        c.lnst = sb("lnst", [128, 8], F32)
        c.ht = sb("ht", [128, D], BF16)
        c.wbuf = sb("wbuf", [128, 2 * 12288], BF16)
        c.cbuf = sb("cbuf", [128, 2 * 768], BF16)
        c.qf = sb("qf", [128, 1024], BF16)
        c.kf = sb("kf", [128, 1024], BF16)
        c.vf = sb("vf", [128, 1024], BF16)
        c.stg = sb("stg", [128, 2 * 1024], BF16)
        c.STb = sb("STb", [128, 2 * 256], BF16)
        c.ckkd = sb("ckkd", [128, 2 * 128], BF16)
        c.cvt = sb("cvt", [128, 2 * 128], BF16)
        c.cqd = sb("cqd", [128, 256], BF16)
        c.kvs = sb("kvs", [128, HL * 128], F32)
        c.kvb = sb("kvb", [128, HL * 128], BF16)
        c.pS0 = ps("pS0", [128, 512], F32)
        c.pS1 = ps("pS1", [128, 512], F32)
        c.pA = ps("pA", [128, 512], F32)
        c.pU = ps("pU", [128, 512], F32)
        c.pT0 = ps("pT0", [128, 1024], BF16)
        c.pT1 = ps("pT1", [128, 1024], BF16)
        c.pV0 = ps("pV0", [128, 1024], BF16)
        c.pV1 = ps("pV1", [128, 1024], BF16)
        block = st.enter_context(nc.Block())

        P = Prog(nc, st)
        for nm in ("cInit cX cXf cSq cStat cSqrt cHt pT0p pT0f pT1p pT1f pV0p pV0f "
                   "pV1p pV1f cH0 cH0f cW cWf cC cCf pS0p pS0f pS1p pS1f pAp pAf "
                   "pUp pUf cQKV cST cCK cCV cCQD cKV cStgV cOut").split():
            setattr(c, nm, P.ch(nm))
        emit_a(P, c)
        P.emit(block)
    return nc


def _emit_a_stage1(P, c, sh):
    for tile in range(8):
        trow = sh * 1024 + tile * 128

        def ev_ldx(t, trow=trow):
            wt(t, c.cXf)
            i = t and t.dma_start(out=c.xt[:], in_=c.xb[trow:trow + 128, :])
            sig(t, i, c.cX, 16)
        P.on("sync", ev_ldx)

        def ev_sq(t):
            wt(t, c.cX)
            i = t and t.activation(c.scr4k[:], c.xt[:], AF.Square,
                                   accum_out=c.lnst[:, 1:2])
            sig(t, i, c.cSq)
        P.on("scalar", ev_sq)

        def ev_stat(t):
            wt(t, c.cSq)
            if t:
                t.reduce_sum(c.lnst[:, 0:1], c.xt[:], axis=AX.X)
                t.drain()
                t.tensor_scalar(c.lnst[:, 2:3], c.lnst[:, 0:1],
                                c.cst[:, C_INV_D:C_INV_D + 1], None, ALU.mult)
                t.tensor_scalar(c.lnst[:, 3:4], c.lnst[:, 1:2],
                                c.cst[:, C_INV_D:C_INV_D + 1], None, ALU.mult)
                t.drain()
                t.tensor_tensor(c.lnst[:, 4:5], c.lnst[:, 2:3], c.lnst[:, 2:3], ALU.mult)
                t.drain()
                i = t.tensor_tensor(c.lnst[:, 4:5], c.lnst[:, 3:4], c.lnst[:, 4:5], ALU.subtract)
            else:
                i = None
            sig(t, i, c.cStat)
        P.on("vector", ev_stat)

        def ev_sqrt(t):
            wt(t, c.cStat)
            i = t and t.activation(c.lnst[:, 5:6], c.lnst[:, 4:5], AF.Sqrt,
                                   bias=c.cst[:, C_EPS:C_EPS + 1])
            sig(t, i, c.cSqrt)
        P.on("scalar", ev_sqrt)

        def ev_apply(t):
            wt(t, c.cSqrt)
            wt(t, c.pT0p)
            wt(t, c.pT1p)
            if t:
                t.reciprocal(c.lnst[:, 6:7], c.lnst[:, 5:6])
                t.drain()
                i = t.tensor_scalar(c.ht[:], c.xt[:], c.lnst[:, 2:3], c.lnst[:, 6:7],
                                    ALU.subtract, ALU.mult)
            else:
                i = None
            sig(t, i, c.cHt)
            sigd(t, c.cXf)
        P.on("vector", ev_apply)

        for cp in range(16):
            def ev_tr(t, cp=cp, sh=sh):
                wt(t, c.cHt)
                wt(t, c.cH0)
                if sh == 1 and cp == 0:
                    wt(t, c.cH0f)
                if t:
                    t.transpose(c.pT0[:, 0:128],
                                c.ht[:, (2 * cp) * 128:(2 * cp + 1) * 128], c.identb[:])
                    t.transpose(c.pT1[:, 0:128],
                                c.ht[:, (2 * cp + 1) * 128:(2 * cp + 2) * 128], c.identb[:])
                    t.drain()
                sigi(t, c.pT0p)
                sigi(t, c.pT1p)
            P.on("tensor", ev_tr)

            eng = "vector" if cp % 2 == 0 else "scalar"

            def ev_ev(t, cp=cp, tile=tile, eng=eng):
                wt(t, c.pT0p)
                wt(t, c.pT1p)
                i = None
                if t:
                    for j, bank in ((2 * cp, c.pT0), (2 * cp + 1, c.pT1)):
                        dst = c.h0[:, j * 1024 + tile * 128: j * 1024 + tile * 128 + 128]
                        i = (t.tensor_copy(dst, bank[:, 0:128]) if eng == "vector"
                             else t.copy(dst, bank[:, 0:128]))
                sig(t, i, c.cH0, 2)
            P.on(eng, ev_ev)


def _emit_a_heads(P, c, sh):
    for h in range(HL):
        par = h % 2
        H = sh * HL + h

        def ev_w(t, h=h, par=par):
            wt(t, c.cWf, lag=1)
            for kg in range(8):
                src = c.qkvT[kg * 512:(kg + 1) * 512, h * 384:(h + 1) * 384]
                src = src.rearrange("(cc p) m -> p cc m", p=128)
                dst = c.wbuf[:, par * 12288 + kg * 1536:par * 12288 + (kg + 1) * 1536]
                dst = dst.rearrange("p (cc m) -> p cc m", cc=4)
                i = t and t.dma_start(out=dst, in_=src)
                sig(t, i, c.cW, 16)
            wt(t, c.cCf, lag=1)
            i = t and t.dma_start(out=c.cbuf[:, par * 768:(par + 1) * 768], in_=c.dcb[h])
            sig(t, i, c.cC, 16)
        P.on("gpsimd", ev_w)

        for m in range(3):
            for n in range(2):
                bank, bp, bf = ((c.pS0, c.pS0p, c.pS0f), (c.pS1, c.pS1p, c.pS1f),
                                (c.pA, c.pAp, c.pAf))[(m * 2 + n) % 3]

                def ev_mm(t, h=h, m=m, n=n, par=par, bank=bank, bp=bp, bf=bf):
                    wt(t, c.cW)
                    wt(t, c.cH0)
                    wt(t, c.cST)
                    wt(t, c.cStgV)
                    wt(t, bf)
                    i = None
                    if t:
                        for kc in range(32):
                            lhs = c.wbuf[:, par * 12288 + (kc // 4) * 1536 + (kc % 4) * 384
                                         + m * 128: par * 12288 + (kc // 4) * 1536
                                         + (kc % 4) * 384 + m * 128 + 128]
                            rhs = c.h0[:, kc * 1024 + n * 512: kc * 1024 + n * 512 + 512]
                            i = t.matmul(bank[:, 0:512], lhs, rhs,
                                         start=(kc == 0), stop=(kc == 31))
                    sig(t, i, bp)
                P.on("tensor", ev_mm)

                def ev_evac(t, m=m, n=n, bank=bank, bp=bp, bf=bf):
                    wt(t, bp)
                    dstf = (c.qf, c.kf, c.vf)[m]
                    i = t and t.activation(dstf[:, n * 512:(n + 1) * 512],
                                           bank[:, 0:512], AF.Silu)
                    sig(t, i, c.cQKV)
                    sigd(t, bf)
                    if m == 2 and n == 1:
                        sigi(t, c.cH0f)
                        sigi(t, c.cWf)
                P.on("scalar", ev_evac)

        for blk in range(NB_SH):
            glob = sh * NB_SH + blk
            ts0 = blk * 256

            def ev_st_mm(t, ts0=ts0):
                wt(t, c.cQKV)
                wt(t, c.cST)
                i0 = t and t.matmul(c.pS0[:, 0:256], c.kf[:, ts0:ts0 + 128],
                                    c.qf[:, ts0:ts0 + 256], start=True, stop=True)
                sig(t, i0, c.pS0p)
                i1 = t and t.matmul(c.pS1[:, 0:256], c.kf[:, ts0 + 128:ts0 + 256],
                                    c.qf[:, ts0:ts0 + 256], start=True, stop=True)
                sig(t, i1, c.pS1p)
            P.on("tensor", ev_st_mm)

            def ev_st_ev(t, par=par):
                wt(t, c.pS0p)
                wt(t, c.pS1p)
                wt(t, c.cC)
                wt(t, c.pAp)
                i0 = t and t.tensor_tensor(c.STb[:, 0:256], c.pS0[:, 0:256],
                                           c.cbuf[:, par * 768:par * 768 + 256], ALU.mult)
                i1 = t and t.tensor_tensor(c.STb[:, 256:512], c.pS1[:, 0:256],
                                           c.cbuf[:, par * 768 + 256:par * 768 + 512], ALU.mult)
                sig(t, i1, c.cST)
            P.on("vector", ev_st_ev)

            def ev_tp(t, ts0=ts0):
                wt(t, c.cCV)
                wt(t, c.cCK)
                if t:
                    t.transpose(c.pT0[:, 0:128], c.kf[:, ts0:ts0 + 128], c.identb[:])
                    t.transpose(c.pT1[:, 0:128], c.kf[:, ts0 + 128:ts0 + 256], c.identb[:])
                    t.transpose(c.pV0[:, 0:128], c.vf[:, ts0:ts0 + 128], c.identb[:])
                    t.transpose(c.pV1[:, 0:128], c.vf[:, ts0 + 128:ts0 + 256], c.identb[:])
                    t.drain()
                sigi(t, c.pT0p)
                sigi(t, c.pT1p)
                sigi(t, c.pV0p)
                sigi(t, c.pV1p)
            P.on("tensor", ev_tp)

            def ev_tp_ev(t, h=h):
                wt(t, c.pT0p)
                wt(t, c.pT1p)
                wt(t, c.pV0p)
                wt(t, c.pV1p)
                wt(t, c.pUp)
                wt(t, c.pAp)
                i0 = t and t.activation(c.ckkd[:, 0:128], c.pT0[:, 0:128], AF.Copy,
                                        scale=c.kdS[:, 2 * h:2 * h + 1])
                i1 = t and t.activation(c.ckkd[:, 128:256], c.pT1[:, 0:128], AF.Copy,
                                        scale=c.kdS[:, 2 * h + 1:2 * h + 2])
                sig(t, i1, c.cCK)
                i2 = t and t.activation(c.cvt[:, 0:128], c.pV0[:, 0:128], AF.Copy)
                i3 = t and t.activation(c.cvt[:, 128:256], c.pV1[:, 0:128], AF.Copy)
                sig(t, i3, c.cCV)
            P.on("scalar", ev_tp_ev)

            if glob > 0:
                def ev_cqd(t, ts0=ts0, par=par, last=(blk == NB_SH - 1)):
                    wt(t, c.cQKV)
                    wt(t, c.cC)
                    wt(t, c.pAp)
                    i = t and t.tensor_tensor(c.cqd[:], c.qf[:, ts0:ts0 + 256],
                                              c.cbuf[:, par * 768 + 512:par * 768 + 768],
                                              ALU.mult)
                    sig(t, i, c.cCQD)
                    if last:
                        sigi(t, c.cCf)
                P.on("vector", ev_cqd)
            elif blk == NB_SH - 1:
                def ev_cf(t):
                    wt(t, c.cST)
                    sig(t, None, c.cCf)
                P.on("vector", ev_cf)

            def ev_attn(t, h=h, glob=glob):
                wt(t, c.cST)
                wt(t, c.cCV)
                if glob > 0:
                    wt(t, c.cCQD)
                    wt(t, c.cKV)
                wt(t, c.pAf)
                i = None
                if t:
                    t.matmul(c.pA[:, 0:256], c.cvt[:, 0:128], c.STb[:, 0:256],
                             start=True, stop=False)
                    i = t.matmul(c.pA[:, 0:256], c.cvt[:, 128:256], c.STb[:, 256:512],
                                 start=False, stop=(glob == 0))
                    if glob > 0:
                        i = t.matmul(c.pA[:, 0:256], c.kvb[:, h * 128:(h + 1) * 128],
                                     c.cqd[:], start=False, stop=True)
                sig(t, i, c.pAp)
            P.on("tensor", ev_attn)

            def ev_attn_ev(t, blk=blk, par=par, H=H):
                wt(t, c.pAp)
                if blk == 0 and H >= 2:
                    wt(t, c.cOut, lag=16)
                i = t and t.tensor_copy(
                    c.stg[:, par * 1024 + blk * 256: par * 1024 + blk * 256 + 256],
                    c.pA[:, 0:256])
                sig(t, i, c.cStgV)
                sigd(t, c.pAf)
            P.on("vector", ev_attn_ev)

            if glob < 7:
                def ev_upd(t):
                    wt(t, c.cCK)
                    wt(t, c.cCV)
                    wt(t, c.pUf)
                    i = None
                    if t:
                        t.matmul(c.pU[:, 0:128], c.ckkd[:, 0:128], c.cvt[:, 0:128],
                                 start=True, stop=False)
                        i = t.matmul(c.pU[:, 0:128], c.ckkd[:, 128:256],
                                     c.cvt[:, 128:256], start=False, stop=True)
                    sig(t, i, c.pUp)
                P.on("tensor", ev_upd)

                def ev_upd_ev(t, h=h):
                    wt(t, c.pUp)
                    sl = c.kvs[:, h * 128:(h + 1) * 128]
                    i = None
                    if t:
                        t.scalar_tensor_tensor(sl, sl, c.bdS[:, h:h + 1],
                                               c.pU[:, 0:128], ALU.mult, ALU.add)
                        i = t.tensor_copy(c.kvb[:, h * 128:(h + 1) * 128], sl)
                    sig(t, i, c.cKV)
                    sigd(t, c.pUf)
                P.on("vector", ev_upd_ev)

        if getattr(c, "fused", False):
            def ev_mask(t, h=h, sh=sh, par=par):
                wt(t, c.cStgV)
                wt(t, c.cOut)
                if t:
                    t.tensor_scalar(c.stgM[:, 0:1024], c.stg[:, par * 1024:(par + 1) * 1024],
                                    c.gmS[:, 0:1], None, ALU.mult)
                    i = t.tensor_scalar(c.stgM[:, 1024:2048],
                                        c.stg[:, par * 1024:(par + 1) * 1024],
                                        c.gmS[:, 1:2], None, ALU.mult)
                else:
                    i = None
                sig(t, i, c.cStgM)
            P.on("vector", ev_mask)

            def ev_stg_out(t, h=h, sh=sh):
                wt(t, c.cStgM)
                base = sh * 4096 + h * 128
                i = t and t.dma_start(out=c.rsin[base:base + 128, :],
                                      in_=c.stgM[:, 0:1024])
                sig(t, i, c.cOut, 16)
                i = t and t.dma_start(out=c.rsin[base + 2048:base + 2048 + 128, :],
                                      in_=c.stgM[:, 1024:2048])
                sig(t, i, c.cOut, 16)
            P.on("sync", ev_stg_out)

            if FUSED_DEBUG and sh == 0 and h == 0:
                def ev_dbg_h0(t):
                    wt(t, c.cOut)
                    i = t and t.dma_start(out=c.dbg4[0:128, :], in_=c.rsin[0:128, :])
                    sig(t, i, c.cOut, 16)
                P.on("sync", ev_dbg_h0)
        else:
            def ev_stg_out(t, h=h, sh=sh, par=par):
                wt(t, c.cStgV)
                dst = c.attn_out[h, :, sh * 1024:(sh + 1) * 1024]
                i = t and t.dma_start(out=dst, in_=c.stg[:, par * 1024:(par + 1) * 1024])
                sig(t, i, c.cOut, 16)
            P.on("sync", ev_stg_out)



def _prep_a(inputs):
    x = inputs["L0_attn_hidden_states"]
    qkv_w = inputs["L0_attn_qkv_w"]
    qd, kd, dd, bd = _decays()
    consts = _consts_host()
    ident = np.eye(128, dtype=np.float32)
    qkvT_full = np.ascontiguousarray(qkv_w.T)
    in_maps = []
    for core in range(NCORES):
        b, g = core // 2, core % 2
        heads = range(16 * g, 16 * g + 16)
        qkvT = np.ascontiguousarray(qkvT_full[:, 384 * 16 * g: 384 * 16 * (g + 1)]).astype(BF)
        dcb = np.zeros((HL, 128, 768), np.float32)
        kdall = np.zeros((128, 2 * HL), np.float32)
        bdall = np.zeros((128, HL), np.float32)
        for i, hh in enumerate(heads):
            dcb[i, :, 0:256] = dd[hh][:, 0:128].T
            dcb[i, :, 256:512] = dd[hh][:, 128:256].T
            dcb[i, :, 512:768] = np.tile(qd[hh][None, :], (128, 1))
            kdall[:, 2 * i] = kd[hh][0:128]
            kdall[:, 2 * i + 1] = kd[hh][128:256]
            bdall[:, i] = bd[hh]
        in_maps.append({
            "xb": np.ascontiguousarray(x[b]),
            "qkvT": qkvT,
            "dcb": dcb,
            "kdall": kdall,
            "bdall": bdall,
            "identf": ident,
            "consts": consts,
        })
    return in_maps


def run_phase_a(inputs):
    nc = build_phase_a()
    in_maps = _prep_a(inputs)
    res = run_bass_kernel_spmd(nc, in_maps, core_ids=list(range(NCORES)))
    return [r["attn_fm"] for r in res.results]


def kernel(**inputs):
    return kernel_fused(inputs)


def kernel_unfused(**inputs):
    a = run_phase_a(inputs)
    xb1 = run_phase_b(inputs, a)
    parts = run_phase_ca(xb1)
    xc = run_phase_cb(inputs, xb1, parts)
    xd = run_phase_d(inputs, xc)
    xe = run_phase_e(inputs, xd)
    out = np.zeros((B, S, D), np.float32)
    for core in range(NCORES):
        bb, g = core // 2, core % 2
        out[bb, g * TOK:(g + 1) * TOK] = np.asarray(xe[core])
    return out



def emit_ln_fm(P, c, src_rows_ap, ntiles, dst_h, extras=None):
    TT = ntiles * 128
    xts = (c.xt, getattr(c, "xt2", c.xt))
    dbuf = xts[0] is not xts[1]
    for tile in range(ntiles):
        xb_ = xts[tile % 2]

        def ev_ldx(t, tile=tile, xb_=xb_):
            wt(t, c.cXf, lag=(1 if dbuf else 0))
            i = t and t.dma_start(out=xb_[:, 0:D], in_=src_rows_ap[tile * 128:(tile + 1) * 128, :])
            sig(t, i, c.cX, 16)
        P.on("sync", ev_ldx)

        def ev_sq(t, xb_=xb_):
            wt(t, c.cX)
            i = t and t.activation(c.scr4k[:], xb_[:, 0:D], AF.Square,
                                   accum_out=c.lnst[:, 1:2])
            sig(t, i, c.cSq)
        P.on("scalar", ev_sq)

        def ev_stat(t, xb_=xb_):
            wt(t, c.cSq)
            if t:
                t.reduce_sum(c.lnst[:, 0:1], xb_[:, 0:D], axis=AX.X)
                t.drain()
                t.tensor_scalar(c.lnst[:, 2:3], c.lnst[:, 0:1],
                                c.cst[:, C_INV_D:C_INV_D + 1], None, ALU.mult)
                t.tensor_scalar(c.lnst[:, 3:4], c.lnst[:, 1:2],
                                c.cst[:, C_INV_D:C_INV_D + 1], None, ALU.mult)
                t.drain()
                t.tensor_tensor(c.lnst[:, 4:5], c.lnst[:, 2:3], c.lnst[:, 2:3], ALU.mult)
                t.drain()
                i = t.tensor_tensor(c.lnst[:, 4:5], c.lnst[:, 3:4], c.lnst[:, 4:5], ALU.subtract)
            else:
                i = None
            sig(t, i, c.cStat)
        P.on("vector", ev_stat)

        def ev_sqrt(t):
            wt(t, c.cStat)
            i = t and t.activation(c.lnst[:, 5:6], c.lnst[:, 4:5], AF.Sqrt,
                                   bias=c.cst[:, C_EPS:C_EPS + 1])
            sig(t, i, c.cSqrt)
        P.on("scalar", ev_sqrt)

        def ev_apply(t, xb_=xb_):
            wt(t, c.cSqrt)
            wt(t, c.pT0p)
            wt(t, c.pT1p)
            if t:
                t.reciprocal(c.lnst[:, 6:7], c.lnst[:, 5:6])
                t.drain()
                i = t.tensor_scalar(c.ht[:], xb_[:, 0:D], c.lnst[:, 2:3], c.lnst[:, 6:7],
                                    ALU.subtract, ALU.mult)
            else:
                i = None
            sig(t, i, c.cHt)
            sigd(t, c.cXf)
        P.on("vector", ev_apply)

        if extras is not None:
            extras(tile)

        for cp in range(16):
            def ev_tr(t, cp=cp):
                wt(t, c.cHt)
                wt(t, c.cH0)
                if t:
                    t.transpose(c.pT0[:, 0:128],
                                c.ht[:, (2 * cp) * 128:(2 * cp + 1) * 128], c.identb[:])
                    t.transpose(c.pT1[:, 0:128],
                                c.ht[:, (2 * cp + 1) * 128:(2 * cp + 2) * 128], c.identb[:])
                    t.drain()
                sigi(t, c.pT0p)
                sigi(t, c.pT1p)
            P.on("tensor", ev_tr)

            eng = "vector" if cp % 2 == 0 else "scalar"

            def ev_ev(t, cp=cp, tile=tile, eng=eng):
                wt(t, c.pT0p)
                wt(t, c.pT1p)
                i = None
                if t:
                    for j, bank in ((2 * cp, c.pT0), (2 * cp + 1, c.pT1)):
                        dst = dst_h[:, j * TT + tile * 128: j * TT + tile * 128 + 128]
                        i = (t.tensor_copy(dst, bank[:, 0:128]) if eng == "vector"
                             else t.copy(dst, bank[:, 0:128]))
                sig(t, i, c.cH0, 2)
            P.on(eng, ev_ev)



def emit_b(P, c):

    ht_alias = c.regB
    c.ht = ht_alias[:, 0:4096]

    def ev_init(t):
        i = t and t.dma_start(out=c.cst[:], in_=c.consts)
        sig(t, i, c.cInit, 16)
        i = t and t.dma_start(out=c.identb[:], in_=c.identf)
        sig(t, i, c.cInit, 16)
        i = t and t.dma_start(out=c.normwS[:], in_=c.normw)
        sig(t, i, c.cInit, 16)
    P.on("gpsimd", ev_init)

    def ev_init_v(t):
        wt(t, c.cInit)
        t and t.memset(c.onesB[:], 1.0)
    P.on("vector", ev_init_v)
    for e in ("scalar", "tensor", "sync"):
        def ev_i(t):
            wt(t, c.cInit)
        P.on(e, ev_i)

    emit_ln_fm(P, c, c.xown, 8, c.regA)

    for j in range(32):
        par = j % 2

        def ev_gw(t, j=j, par=par):
            wt(t, c.cWf, lag=1)
            if j < 2 and getattr(c, "fused", False):
                wt(t, c.cH0)
            src = c.gateTt[j].rearrange("(kc p) m -> p kc m", p=128)
            dst = c.wb[:, par * 4096:(par + 1) * 4096].rearrange("p (kc m) -> p kc m", m=128)
            i = t and t.dma_start(out=dst, in_=src)
            sig(t, i, c.cW, 16)
        P.on("gpsimd", ev_gw)

        for n in range(2):
            bank, bp, bf = ((c.pS0, c.pS0p, c.pS0f), (c.pS1, c.pS1p, c.pS1f),
                            (c.pA, c.pAp, c.pAf))[(j * 2 + n) % 3]

            def ev_gmm(t, j=j, n=n, par=par, bank=bank, bp=bp, bf=bf):
                wt(t, c.cW)
                wt(t, c.cH0)
                wt(t, bf)
                i = None
                if t:
                    for kc in range(32):
                        lhs = c.wb[:, par * 4096 + kc * 128:(par * 4096 + kc * 128) + 128]
                        rhs = c.regA[:, kc * TOK + n * 512: kc * TOK + n * 512 + 512]
                        i = t.matmul(bank[:, 0:512], lhs, rhs,
                                     start=(kc == 0), stop=(kc == 31))
                sig(t, i, bp)
            P.on("tensor", ev_gmm)

            def ev_gev(t, j=j, n=n, bank=bank, bp=bp, bf=bf):
                wt(t, bp)
                i = t and t.activation(c.regB[:, j * TOK + n * 512: j * TOK + n * 512 + 512],
                                       bank[:, 0:512], AF.Sigmoid)
                sig(t, i, c.cGate)
                sigd(t, bf)
                if n == 1:
                    sigi(t, c.cWf)
            P.on("scalar", ev_gev)

    for ch_ in range(32):
        par = ch_ % 2

        def ev_la(t, ch_=ch_, par=par):
            wt(t, c.cAf, lag=1)
            if getattr(c, "fused", False):
                wt(t, c.cAG)
                wt(t, c.cH0)
            i = t and t.dma_start(out=c.abuf[:, par * TOK:(par + 1) * TOK],
                                  in_=c.afm[ch_ * 128:(ch_ + 1) * 128, :])
            sig(t, i, c.cA, 16)
        P.on("sync", ev_la)

        def ev_sqa(t, ch_=ch_, par=par):
            wt(t, c.cA)
            wt(t, c.pUp)
            i = t and t.activation(c.sqb[:], c.abuf[:, par * TOK:(par + 1) * TOK],
                                   AF.Square)
            sig(t, i, c.cSqB)
            sigd(t, c.cAf)
        P.on("scalar", ev_sqa)

        def ev_smm(t, ch_=ch_):
            wt(t, c.cSqB)
            if ch_ == 0:
                wt(t, c.pUf)
                wt(t, c.pRf)
            i = None
            if t:
                t.matmul(c.pU[0:1, 0:512], c.onesB[:, 0:1], c.sqb[:, 0:512],
                         start=(ch_ == 0), stop=(ch_ == 31))
                i = t.matmul(c.pR[0:1, 0:512], c.onesB[:, 0:1], c.sqb[:, 512:1024],
                             start=(ch_ == 0), stop=(ch_ == 31))
            sig(t, i, c.pUp)
        P.on("tensor", ev_smm)

    def ev_rstat(t):
        wt(t, c.pUp)
        if t:
            t.tensor_copy(c.rrow[0:1, 0:512], c.pU[0:1, 0:512])
            i = t.tensor_copy(c.rrow[0:1, 512:1024], c.pR[0:1, 0:512])
        else:
            i = None
        sig(t, i, c.cRr)
        sigd(t, c.pUf)
        sigi(t, c.pRf)
    P.on("vector", ev_rstat)

    def ev_rsqrt(t):
        wt(t, c.cRr)
        i = t and t.activation(c.rrow2[:, :], c.rrow[:, :], AF.Sqrt,
                               bias=c.cst[0:1, C_EPS:C_EPS + 1],
                               scale=c.cst[0:1, C_INV_D:C_INV_D + 1])
        sig(t, i, c.cRstd)
    P.on("scalar", ev_rsqrt)

    def ev_rrec(t):
        wt(t, c.cRstd)
        if t:
            t.reciprocal(c.rrow[:, :], c.rrow2[:, :])
            t.drain()
            i = t.tensor_copy(c.rrowb[:, :], c.rrow[:, :])
        else:
            i = None
        sig(t, i, c.cRb)
    P.on("vector", ev_rrec)

    def ev_rbc(t):
        wt(t, c.cRb)
        wt(t, c.pS0f)
        wt(t, c.pS1f)
        i = None
        if t:
            t.matmul(c.pS0[:, 0:512], c.onesB[0:1, 0:128], c.rrowb[:, 0:512],
                     start=True, stop=True)
            i = t.matmul(c.pS1[:, 0:512], c.onesB[0:1, 0:128], c.rrowb[:, 512:1024],
                         start=True, stop=True)
        sig(t, i, c.pS0p)
        sigi(t, c.pS1p)
    P.on("tensor", ev_rbc)

    def ev_rbev(t):
        wt(t, c.pS0p)
        wt(t, c.pS1p)
        if t:
            t.tensor_copy(c.rstdb[:, 0:512], c.pS0[:, 0:512])
            i = t.tensor_copy(c.rstdb[:, 512:1024], c.pS1[:, 0:512])
        else:
            i = None
        sig(t, i, c.cRbb)
        sigd(t, c.pS0f)
        sigi(t, c.pS1f)
    P.on("vector", ev_rbev)

    for ch_ in range(32):
        par = ch_ % 2

        def ev_la2(t, ch_=ch_, par=par):
            wt(t, c.cAf, lag=1)
            if getattr(c, "fused", False):
                wt(t, c.cAG)
            i = t and t.dma_start(out=c.abuf[:, par * TOK:(par + 1) * TOK],
                                  in_=c.afm[ch_ * 128:(ch_ + 1) * 128, :])
            sig(t, i, c.cA, 16)
        P.on("sync", ev_la2)

        def ev_prod(t, ch_=ch_, par=par):
            wt(t, c.cA)
            wt(t, c.cGate)
            wt(t, c.cRbb)
            if t:
                t.scalar_tensor_tensor(c.sqb[:], c.abuf[:, par * TOK:(par + 1) * TOK],
                                       c.normwS[:, ch_:ch_ + 1],
                                       c.regB[:, ch_ * TOK:(ch_ + 1) * TOK],
                                       ALU.mult, ALU.mult)
                i = t.tensor_tensor(c.regA[:, ch_ * TOK:(ch_ + 1) * TOK],
                                    c.sqb[:], c.rstdb[:], ALU.mult)
            else:
                i = None
            sig(t, i, c.cProd)
            sigd(t, c.cAf)
        P.on("vector", ev_prod)

    for j in range(32):
        par = j % 2

        def ev_ow(t, j=j, par=par):
            wt(t, c.cWf, lag=1)
            src = c.outTt[j].rearrange("(kc p) m -> p kc m", p=128)
            dst = c.wb[:, par * 4096:(par + 1) * 4096].rearrange("p (kc m) -> p kc m", m=128)
            i = t and t.dma_start(out=dst, in_=src)
            sig(t, i, c.cW, 16)
        P.on("gpsimd", ev_ow)

        for n in range(2):
            bank, bp, bf = ((c.pS0, c.pS0p, c.pS0f), (c.pS1, c.pS1p, c.pS1f),
                            (c.pA, c.pAp, c.pAf))[(j * 2 + n) % 3]

            def ev_omm(t, j=j, n=n, par=par, bank=bank, bp=bp, bf=bf):
                wt(t, c.cW)
                wt(t, c.cProd)
                wt(t, bf)
                i = None
                if t:
                    for kc in range(32):
                        lhs = c.wb[:, par * 4096 + kc * 128: par * 4096 + kc * 128 + 128]
                        rhs = c.regA[:, kc * TOK + n * 512: kc * TOK + n * 512 + 512]
                        i = t.matmul(bank[:, 0:512], lhs, rhs,
                                     start=(kc == 0), stop=(kc == 31))
                sig(t, i, bp)
            P.on("tensor", ev_omm)

            def ev_oev(t, j=j, n=n, bank=bank, bp=bp, bf=bf):
                wt(t, bp)
                wt(t, c.cProd)
                i = t and t.tensor_copy(
                    c.regB[:, j * TOK + n * 512: j * TOK + n * 512 + 512],
                    bank[:, 0:512])
                sig(t, i, c.cOT)
                sigd(t, bf)
                if n == 1:
                    sigi(t, c.cWf)
            P.on("vector", ev_oev)

    _emit_b5(P, c)

    def ev_done(t):
        wt(t, c.cOut)
    P.on("sync", ev_done)




def build_phase_b():
    import contextlib
    nc = bass.Bass("TRN2", target_bir_lowering=False)
    c = types.SimpleNamespace()
    c.xown = nc.dram_tensor("xown", [TOK, D], F32, kind="ExternalInput").ap()
    c.afm = nc.dram_tensor("afm", [D, TOK], BF16, kind="ExternalInput").ap()
    c.gateTt = nc.dram_tensor("gateTt", [32, D, 128], BF16, kind="ExternalInput").ap()
    c.outTt = nc.dram_tensor("outTt", [32, D, 128], BF16, kind="ExternalInput").ap()
    c.normw = nc.dram_tensor("normw", [128, 32], F32, kind="ExternalInput").ap()
    c.consts = nc.dram_tensor("consts", [128, NCONST], F32, kind="ExternalInput").ap()
    c.identf = nc.dram_tensor("identf", [128, 128], F32, kind="ExternalInput").ap()
    c.x1 = nc.dram_tensor("x1", [TOK, D], F32, kind="ExternalOutput").ap()

    with contextlib.ExitStack() as st:
        sb, ps = (lambda *a: _sb(st, nc, *a)), (lambda *a: _ps(st, nc, *a))
        c.cst = sb("cst", [128, NCONST], F32)
        c.identb = sb("identb", [128, 128], BF16)
        c.normwS = sb("normwS", [128, 32], F32)
        c.onesB = sb("onesB", [128, 128], BF16)
        c.regA = sb("regA", [128, 32 * TOK], BF16)
        c.regB = sb("regB", [128, 32 * TOK], BF16)
        c.xt = sb("xt", [128, D], F32)
        c.scr4k = sb("scr4k", [128, D], F32)
        c.lnst = sb("lnst", [128, 8], F32)
        c.wb = sb("wb", [128, 2 * 4096], BF16)
        c.abuf = sb("abuf", [128, 2 * TOK], BF16)
        c.sqb = sb("sqb", [128, TOK], BF16)
        c.rrow = sb("rrow", [1, TOK], F32)
        c.rrow2 = sb("rrow2", [1, TOK], F32)
        c.rrowb = sb("rrowb", [1, TOK], BF16)
        c.ft = sb("ft", [128, D], BF16)
        c.rstdb = sb("rstdb", [128, TOK], F32)
        c.pS0 = ps("pS0", [128, 512], F32)
        c.pS1 = ps("pS1", [128, 512], F32)
        c.pA = ps("pA", [128, 512], F32)
        c.pU = ps("pU", [128, 512], F32)
        c.pT0 = ps("pT0", [128, 1024], BF16)
        c.pT1 = ps("pT1", [128, 1024], BF16)
        c.pR = ps("pR", [128, 512], F32)
        block = st.enter_context(nc.Block())

        P = Prog(nc, st)
        for nm in ("cInit cX cXf cSq cStat cSqrt cHt pT0p pT0f pT1p pT1f cH0 "
                   "cW cWf pS0p pS0f pS1p pS1f pAp pAf pUp pUf pRp pRf cGate cA cAf "
                   "cSqB cRr cRstd cRb cRbb cProd cOT cFt cOut").split():
            setattr(c, nm, P.ch(nm))
        emit_b(P, c)
        P.emit(block)
    return nc


def _emit_b5(P, c):
    for tile in range(8):
        for cp in range(16):
            def ev_tr(t, cp=cp, tile=tile):
                wt(t, c.cOT)
                wt(t, c.cFt)
                if cp == 0:
                    wt(t, c.cXf)
                if t:
                    t.transpose(c.pT0[:, 0:128],
                                c.regB[:, (2 * cp) * TOK + tile * 128:
                                       (2 * cp) * TOK + tile * 128 + 128], c.identb[:])
                    t.transpose(c.pT1[:, 0:128],
                                c.regB[:, (2 * cp + 1) * TOK + tile * 128:
                                       (2 * cp + 1) * TOK + tile * 128 + 128], c.identb[:])
                    t.drain()
                sigi(t, c.pT0p)
                sigi(t, c.pT1p)
            P.on("tensor", ev_tr)

            eng = "vector" if cp % 2 == 0 else "scalar"

            def ev_ev(t, cp=cp, eng=eng):
                wt(t, c.pT0p)
                wt(t, c.pT1p)
                i = None
                if t:
                    for j, bank in ((2 * cp, c.pT0), (2 * cp + 1, c.pT1)):
                        dst = c.ft[:, j * 128:(j + 1) * 128]
                        i = (t.tensor_copy(dst, bank[:, 0:128]) if eng == "vector"
                             else t.copy(dst, bank[:, 0:128]))
                sig(t, i, c.cFt)
            P.on(eng, ev_ev)

        xb_ = (c.xt, getattr(c, "xt2", c.xt))[tile % 2]

        def ev_ldx(t, tile=tile, xb_=xb_):
            wt(t, c.cXf, lag=1)
            wt(t, c.cOut, lag=16)
            wt(t, c.cOT)
            i = t and t.dma_start(out=xb_[:, 0:D], in_=c.xown[tile * 128:(tile + 1) * 128, :])
            sig(t, i, c.cX, 16)
        P.on("sync", ev_ldx)

        def ev_sq(t):
            wt(t, c.cFt)
            wt(t, c.cHt)
            i = t and t.activation(c.scr4k[:], c.ft[:], AF.Square,
                                   accum_out=c.lnst[:, 1:2])
            sig(t, i, c.cSq)
        P.on("scalar", ev_sq)

        def ev_stat(t):
            wt(t, c.cSq)
            if t:
                t.reduce_sum(c.lnst[:, 0:1], c.ft[:], axis=AX.X)
                t.drain()
                t.tensor_scalar(c.lnst[:, 2:3], c.lnst[:, 0:1],
                                c.cst[:, C_INV_D:C_INV_D + 1], None, ALU.mult)
                t.tensor_scalar(c.lnst[:, 3:4], c.lnst[:, 1:2],
                                c.cst[:, C_INV_D:C_INV_D + 1], None, ALU.mult)
                t.drain()
                t.tensor_tensor(c.lnst[:, 4:5], c.lnst[:, 2:3], c.lnst[:, 2:3], ALU.mult)
                t.drain()
                i = t.tensor_tensor(c.lnst[:, 4:5], c.lnst[:, 3:4], c.lnst[:, 4:5], ALU.subtract)
            else:
                i = None
            sig(t, i, c.cStat)
        P.on("vector", ev_stat)

        def ev_sqrt(t):
            wt(t, c.cStat)
            i = t and t.activation(c.lnst[:, 5:6], c.lnst[:, 4:5], AF.Sqrt,
                                   bias=c.cst[:, C_EPS:C_EPS + 1])
            sig(t, i, c.cSqrt)
        P.on("scalar", ev_sqrt)

        def ev_apply(t, tile=tile, xb_=xb_):
            wt(t, c.cSqrt)
            wt(t, c.cX)
            if t:
                t.reciprocal(c.lnst[:, 6:7], c.lnst[:, 5:6])
                t.drain()
                t.tensor_scalar(c.scr4k[:], c.ft[:], c.lnst[:, 2:3], c.lnst[:, 6:7],
                                ALU.subtract, ALU.mult)
                i = t.tensor_tensor(xb_[:, 0:D], c.scr4k[:], xb_[:, 0:D], ALU.add)
            else:
                i = None
            sig(t, i, c.cHt)
            sigd(t, c.cXf)
        P.on("vector", ev_apply)

        def ev_st(t, tile=tile, xb_=xb_):
            wt(t, c.cHt)
            i = t and t.dma_start(out=c.x1[tile * 128:(tile + 1) * 128, :], in_=xb_[:, 0:D])
            sig(t, i, c.cOut, 16)
        P.on("sync", ev_st)


def _prep_b(inputs, attn_outs):
    x = inputs["L0_attn_hidden_states"]
    gT = np.ascontiguousarray(inputs["L0_attn_gate_w"].T)
    oT = np.ascontiguousarray(inputs["L0_attn_out_w"].T)
    gateTt = np.ascontiguousarray(gT.reshape(D, 32, 128).transpose(1, 0, 2)).astype(BF)
    outTt = np.ascontiguousarray(oT.reshape(D, 32, 128).transpose(1, 0, 2)).astype(BF)
    normw = np.ascontiguousarray(inputs["L0_attn_norm_w"].reshape(32, 128).T)
    consts = _consts_host()
    ident = np.eye(128, dtype=np.float32)
    in_maps = []
    for core in range(NCORES):
        b, g = core // 2, core % 2
        att = np.concatenate([np.asarray(attn_outs[2 * b]), np.asarray(attn_outs[2 * b + 1])],
                             axis=0).reshape(D, S)
        afm = np.ascontiguousarray(att[:, g * TOK:(g + 1) * TOK])
        in_maps.append({
            "xown": np.ascontiguousarray(x[b, g * TOK:(g + 1) * TOK, :]),
            "afm": afm,
            "gateTt": gateTt,
            "outTt": outTt,
            "normw": normw,
            "consts": consts,
            "identf": ident,
        })
    return in_maps


def run_phase_b(inputs, attn_outs):
    nc = build_phase_b()
    in_maps = _prep_b(inputs, attn_outs)
    res = run_bass_kernel_spmd(nc, in_maps, core_ids=list(range(NCORES)))
    return [r["x1"] for r in res.results]


C_INVSQ48, C_INV8, C_GELC, C_INVNTOT = 4, 5, 6, 7


def _consts_host_c():
    c = _consts_host()
    c[:, C_INVSQ48] = 1.0 / math.sqrt(48.0)
    c[:, C_INV8] = 1.0 / math.sqrt(64.0)
    c[:, C_GELC] = 0.7978845608028654
    c[:, C_INVNTOT] = 1.0 / 8192.0
    return c


def emit_ca(P, c):

    def ev_init(t):
        i = t and t.dma_start(out=c.cst[:], in_=c.consts)
        sig(t, i, c.cInit, 16)
        i = t and t.dma_start(out=c.ones1[:], in_=c.onesr)
        sig(t, i, c.cInit, 16)
    P.on("gpsimd", ev_init)
    for e in ("scalar", "tensor", "sync", "vector"):
        def ev_i(t):
            wt(t, c.cInit)
        P.on(e, ev_i)

    for tile in range(8):
        def ev_ldx(t, tile=tile):
            wt(t, c.cXf)
            i = t and t.dma_start(out=c.xt[:], in_=c.x1[tile * 128:(tile + 1) * 128, :])
            sig(t, i, c.cX, 16)
        P.on("sync", ev_ldx)

        def ev_sq(t):
            wt(t, c.cX)
            i = t and t.activation(c.scr4k[:], c.xt[:], AF.Square,
                                   accum_out=c.lnst[:, 1:2])
            sig(t, i, c.cSq)
        P.on("scalar", ev_sq)

        def ev_stat(t):
            wt(t, c.cSq)
            if t:
                t.reduce_sum(c.lnst[:, 0:1], c.xt[:], axis=AX.X)
                t.drain()
                t.tensor_scalar(c.lnst[:, 2:3], c.lnst[:, 0:1],
                                c.cst[:, C_INV_D:C_INV_D + 1], None, ALU.mult)
                t.tensor_scalar(c.lnst[:, 3:4], c.lnst[:, 1:2],
                                c.cst[:, C_INV_D:C_INV_D + 1], None, ALU.mult)
                t.drain()
                t.tensor_tensor(c.lnst[:, 4:5], c.lnst[:, 2:3], c.lnst[:, 2:3], ALU.mult)
                t.drain()
                i = t.tensor_tensor(c.lnst[:, 4:5], c.lnst[:, 3:4], c.lnst[:, 4:5], ALU.subtract)
            else:
                i = None
            sig(t, i, c.cStat)
        P.on("vector", ev_stat)

        def ev_sqrt(t):
            wt(t, c.cStat)
            i = t and t.activation(c.lnst[:, 5:6], c.lnst[:, 4:5], AF.Sqrt,
                                   bias=c.cst[:, C_EPS:C_EPS + 1])
            sig(t, i, c.cSqrt)
        P.on("scalar", ev_sqrt)

        def ev_apply(t, tile=tile):
            wt(t, c.cSqrt)
            wt(t, c.cSqH)
            if t:
                t.reciprocal(c.lnst[:, 6:7], c.lnst[:, 5:6])
                t.drain()
                i = t.tensor_scalar(c.hts[:], c.xt[:], c.lnst[:, 2:3], c.lnst[:, 6:7],
                                    ALU.subtract, ALU.mult)
            else:
                i = None
            sig(t, i, c.cHt)
            sigd(t, c.cXf)
        P.on("vector", ev_apply)

        def ev_sqh(t, tile=tile):
            wt(t, c.cHt)
            i = t and t.activation(c.scr4k[:], c.hts[:], AF.Square,
                                   accum_out=c.lnst[:, 7:8])
            sig(t, i, c.cSqH)
        P.on("scalar", ev_sqh)

        def ev_lv(t, tile=tile):
            wt(t, c.cSqH)
            if t:
                t.reduce_sum(c.lnst[:, 0:1], c.hts[:], axis=AX.X)
                t.drain()
                t.tensor_scalar(c.lnst[:, 2:3], c.lnst[:, 0:1],
                                c.cst[:, C_INV_D:C_INV_D + 1], None, ALU.mult)
                t.tensor_scalar(c.lnst[:, 3:4], c.lnst[:, 7:8],
                                c.cst[:, C_INV_D:C_INV_D + 1], None, ALU.mult)
                t.drain()
                t.tensor_tensor(c.lnst[:, 4:5], c.lnst[:, 2:3], c.lnst[:, 2:3], ALU.mult)
                t.drain()
                i = t.tensor_tensor(c.lnst[:, 7:8], c.lnst[:, 3:4],
                                    c.lnst[:, 4:5], ALU.subtract)
            else:
                i = None
            sig(t, i, c.cLv)
        P.on("vector", ev_lv)

        def ev_ln1p(t, tile=tile):
            wt(t, c.cLv)
            i = t and t.activation(c.lv[:, tile:tile + 1], c.lnst[:, 7:8],
                                   AF.Ln, bias=c.cst[:, C_ONE:C_ONE + 1])
            sig(t, i, c.cLv)
        P.on("scalar", ev_ln1p)

    def ev_red(t):
        wt(t, c.cLv)
        if t:
            i = t.reduce_sum(c.lnst[:, 0:1], c.lv[:], axis=AX.X)
        else:
            i = None
        sig(t, i, c.cLv)
    P.on("vector", ev_red)

    def ev_mm(t):
        wt(t, c.cLv)
        i = t and t.matmul(c.pU[0:1, 0:1], c.lnst[:, 0:1], c.cst[:, C_ONE:C_ONE + 1],
                           start=True, stop=True)
        sig(t, i, c.pUp)
    P.on("tensor", ev_mm)

    def ev_out(t):
        wt(t, c.pUp)
        i = t and t.tensor_copy(c.red[0:1, 0:1], c.pU[0:1, 0:1])
        sig(t, i, c.cHt)
    P.on("vector", ev_out)

    def ev_st(t):
        wt(t, c.cHt)
        i = t and t.dma_start(out=c.partial, in_=c.red[0:1, 0:1])
        sig(t, i, c.cOut, 16)
        wt(t, c.cOut)
    P.on("sync", ev_st)




def build_phase_ca():
    import contextlib
    nc = bass.Bass("TRN2", target_bir_lowering=False)
    c = types.SimpleNamespace()
    c.x1 = nc.dram_tensor("x1", [TOK, D], F32, kind="ExternalInput").ap()
    c.consts = nc.dram_tensor("consts", [128, NCONST], F32, kind="ExternalInput").ap()
    c.onesr = nc.dram_tensor("onesr", [1, 128], F32, kind="ExternalInput").ap()
    c.partial = nc.dram_tensor("partial", [1, 1], F32, kind="ExternalOutput").ap()

    with contextlib.ExitStack() as st:
        sb, ps = (lambda *a: _sb(st, nc, *a)), (lambda *a: _ps(st, nc, *a))
        c.cst = sb("cst", [128, NCONST], F32)
        c.ones1 = sb("ones1", [1, 128], F32)
        c.xt = sb("xt", [128, D], F32)
        c.scr4k = sb("scr4k", [128, D], F32)
        c.hts = sb("hts", [128, D], F32)
        c.lnst = sb("lnst", [128, 8], F32)
        c.lv = sb("lv", [128, 8], F32)
        c.red = sb("red", [1, 8], F32)
        c.pU = ps("pU", [128, 512], F32)
        block = st.enter_context(nc.Block())
        P = Prog(nc, st)
        for nm in "cInit cX cXf cSq cStat cSqrt cHt cSqH cLv pUp cOut".split():
            setattr(c, nm, P.ch(nm))
        emit_ca(P, c)
        P.emit(block)
    return nc


def run_phase_ca(x1_outs):
    nc = build_phase_ca()
    consts = _consts_host_c()
    onesr = np.ones((1, 128), np.float32)
    in_maps = [{"x1": np.ascontiguousarray(np.asarray(x)), "consts": consts,
                "onesr": onesr} for x in x1_outs]
    res = run_bass_kernel_spmd(nc, in_maps, core_ids=list(range(NCORES)))
    return np.array([float(r["partial"][0, 0]) for r in res.results], np.float32)


BIGNEG = 30000.0


def emit_cb(P, c):

    def ev_init(t):
        pairs = [(c.cst[:], c.consts), (c.ones1[:], c.onesr),
                 (c.identb[:], c.identf), (c.logrelS[:], c.logrel),
                 (c.scalsS[:], c.scals), (c.bu1S[:], c.bu1c),
                 (c.wu2S[:], c.wu2c), (c.wr_sb[:], c.wrT),
                 (c.ak_sb[:], c.akT)]
        if not getattr(c, "fused", False):
            pairs.append((c.prow[:], c.partials))
        for dst, src in pairs:
            i = t and t.dma_start(out=dst, in_=src)
            sig(t, i, c.cInit, 16)
        src = c.wqT.rearrange("(kc p) m -> p kc m", p=128)
        dst = c.wq_sb[:].rearrange("p (kc m) -> p kc m", m=64)
        i = t and t.dma_start(out=dst, in_=src)
        sig(t, i, c.cInit, 16)
        src = c.wvuT.rearrange("(kc p) m -> p kc m", p=128)
        dst = c.wvuS[:].rearrange("p (kc m) -> p kc m", m=D)
        i = t and t.dma_start(out=dst, in_=src)
        sig(t, i, c.cInit, 16)
    P.on("gpsimd", ev_init)

    def _emit_meanchain():
        def ev_mean_v(t):
            wt(t, c.cInit)
            if getattr(c, "fused", False):
                wt(t, c.cPt)
            if t:
                if not getattr(c, "fused", False):
                    t.reduce_sum(c.prow[0:1, 0:1], c.prow[0:1, 0:8], axis=AX.X)
                t.drain()
                t.tensor_scalar(c.prow[0:1, 1:2], c.prow[0:1, 0:1], 1.0 / 8192.0, 1e-6,
                                ALU.mult, ALU.add)
                t.drain()
                i = t.reciprocal(c.scalsS[0:1, 3:4], c.prow[0:1, 1:2])
            else:
                i = None
            sig(t, i, c.cBc)
        P.on("vector", ev_mean_v)

        def ev_bc_mm(t):
            wt(t, c.cBc)
            i = t and t.matmul(c.pS0[:, 0:4], c.ones1[0:1, :], c.scalsS[0:1, 0:4],
                               start=True, stop=True)
            sig(t, i, c.pS0p)
        P.on("tensor", ev_bc_mm)

        def ev_bc_ev(t):
            wt(t, c.pS0p)
            i = t and t.tensor_copy(c.scget[:, 0:4], c.pS0[:, 0:4])
            sig(t, i, c.cBc)
            sigd(t, c.pS0f)
        P.on("vector", ev_bc_ev)

    if not getattr(c, "fused", False):
        _emit_meanchain()

    def extras(tile):
        def ev_ms(t, tile=tile):
            wt(t, c.cHt)
            if t:
                t.tensor_copy(c.mstore[:, tile:tile + 1], c.lnst[:, 2:3])
                i = t.tensor_copy(c.rstore[:, tile:tile + 1], c.lnst[:, 6:7])
            else:
                i = None
            sig(t, i, c.cLv)
        P.on("vector", ev_ms)

        def ev_sqh(t, tile=tile):
            wt(t, c.cHt)
            i = t and t.activation(c.scr4k[:], c.ht[:], AF.Square,
                                   accum_out=c.lnst[:, 7:8])
            sig(t, i, c.cSqH)
        P.on("scalar", ev_sqh)

        def ev_lv(t, tile=tile):
            wt(t, c.cSqH)
            if t:
                t.reduce_sum(c.lnst[:, 0:1], c.ht[:], axis=AX.X)
                t.drain()
                t.tensor_scalar(c.lnst[:, 2:3], c.lnst[:, 0:1],
                                c.cst[:, C_INV_D:C_INV_D + 1], None, ALU.mult)
                t.tensor_scalar(c.lnst[:, 3:4], c.lnst[:, 7:8],
                                c.cst[:, C_INV_D:C_INV_D + 1], None, ALU.mult)
                t.drain()
                t.tensor_tensor(c.lnst[:, 4:5], c.lnst[:, 2:3], c.lnst[:, 2:3], ALU.mult)
                t.drain()
                i = t.tensor_tensor(c.lnst[:, 7:8], c.lnst[:, 3:4],
                                    c.lnst[:, 4:5], ALU.subtract)
            else:
                i = None
            sig(t, i, c.cLv)
        P.on("vector", ev_lv)

        def ev_ln1p(t, tile=tile):
            wt(t, c.cLv)
            i = t and t.activation(c.lv[:, tile:tile + 1], c.lnst[:, 7:8],
                                   AF.Ln, bias=c.cst[:, C_ONE:C_ONE + 1])
            sig(t, i, c.cLv)
        P.on("scalar", ev_ln1p)

    emit_ln_fm(P, c, c.x1, 8, c.regA, extras=extras)

    if getattr(c, "fused", False):
        def ev_pred(t):
            wt(t, c.cLv)
            i = t and t.reduce_sum(c.lnst[:, 0:1], c.lv[:], axis=AX.X)
            sig(t, i, c.cPt)
        P.on("vector", ev_pred)

        def ev_pmm(t):
            wt(t, c.cPt)
            wt(t, c.pUf)
            i = t and t.matmul(c.pU[0:1, 0:1], c.lnst[:, 0:1],
                               c.cst[:, C_ONE:C_ONE + 1], start=True, stop=True)
            sig(t, i, c.pUp)
        P.on("tensor", ev_pmm)

        def ev_pout(t):
            wt(t, c.pUp)
            i = t and t.tensor_copy(c.prow[0:1, 0:1], c.pU[0:1, 0:1])
            sig(t, i, c.cPt)
            sigd(t, c.pUf)
        P.on("vector", ev_pout)

        def ev_pst(t):
            wt(t, c.cPt)
            i = t and t.dma_start(out=c.partA[0:1, 0:1], in_=c.prow[0:1, 0:1])
            sig(t, i, c.cPt, 16)
        P.on("sync", ev_pst)

        def ev_ar(t):
            wt(t, c.cPt)
            i = t and t.collective_compute(
                "AllReduce", mybir.AluOpType.add,
                replica_groups=[list(range(8))],
                ins=[c.partA], outs=[c.partR])
            sig(t, i, c.cAR)
        P.on("gpsimd", ev_ar)

        def ev_pld(t):
            wt(t, c.cAR)
            i = t and t.dma_start(out=c.prow[0:1, 0:1], in_=c.partR[0:1, 0:1])
            sig(t, i, c.cPt, 16)
        P.on("sync", ev_pld)

        _emit_meanchain()


    for n in range(2):
        bank, bp, bf = ((c.pS0, c.pS0p, c.pS0f), (c.pS1, c.pS1p, c.pS1f))[n]

        def ev_qmm(t, n=n, bank=bank, bp=bp, bf=bf):
            wt(t, c.cInit)
            wt(t, c.cH0)
            wt(t, bf)
            i = None
            if t:
                for kc in range(32):
                    i = t.matmul(bank[0:64, 0:512], c.wq_sb[:, kc * 64:(kc + 1) * 64],
                                 c.regA[:, kc * TOK + n * 512: kc * TOK + n * 512 + 512],
                                 start=(kc == 0), stop=(kc == 31))
            sig(t, i, bp)
        P.on("tensor", ev_qmm)

        def ev_qev(t, n=n, bank=bank, bp=bp, bf=bf):
            wt(t, bp)
            i = t and t.copy(c.query[:, n * 512:(n + 1) * 512], bank[0:64, 0:512])
            sig(t, i, c.cQev)
            sigd(t, bf)
        P.on("scalar", ev_qev)

    def ev_rkmm(t):
        wt(t, c.cInit)
        wt(t, c.pAf)
        wt(t, c.pUf)
        i = None
        if t:
            t.matmul(c.pA[0:48, 0:512], c.wr_sb[:], c.ak_sb[:, 0:512],
                     start=True, stop=True)
            i = t.matmul(c.pU[0:48, 0:512], c.wr_sb[:], c.ak_sb[:, 512:1024],
                         start=True, stop=True)
        sig(t, i, c.pAp)
        sigi(t, c.pUp)
    P.on("tensor", ev_rkmm)

    def ev_rkev(t):
        wt(t, c.pAp)
        wt(t, c.pUp)
        if t:
            t.copy(c.rk[:, 0:512], c.pA[0:48, 0:512])
            i = t.copy(c.rk[:, 512:1024], c.pU[0:48, 0:512])
        else:
            i = None
        sig(t, i, c.cRKev)
        sigd(t, c.pAf)
        sigi(t, c.pUf)
    P.on("scalar", ev_rkev)

    def ev_rqmm(t):
        wt(t, c.cQev)
        wt(t, c.pAf)
        wt(t, c.pUf)
        i = None
        if t:
            t.matmul(c.pA[0:48, 0:512], c.wr_sb[:], c.query[:, 0:512],
                     start=True, stop=True)
            i = t.matmul(c.pU[0:48, 0:512], c.wr_sb[:], c.query[:, 512:1024],
                         start=True, stop=True)
        sig(t, i, c.pAp)
        sigi(t, c.pUp)
    P.on("tensor", ev_rqmm)

    def ev_rqev(t):
        wt(t, c.pAp)
        wt(t, c.pUp)
        if t:
            t.copy(c.rq[:, 0:512], c.pA[0:48, 0:512])
            i = t.copy(c.rq[:, 512:1024], c.pU[0:48, 0:512])
        else:
            i = None
        sig(t, i, c.cRQev)
        sigd(t, c.pAf)
        sigi(t, c.pUf)
    P.on("scalar", ev_rqev)

    def _emit_router(tt):
        def ev_smm(t, tt=tt):
            wt(t, c.cRQev)
            wt(t, c.cRKev)
            wt(t, c.pS0f)
            wt(t, c.pS1f)
            wt(t, c.pAf)
            wt(t, c.pUf)
            i = None
            if t:
                t.matmul(c.pS0[:, 0:512], c.rq[:, tt * 128:(tt + 1) * 128],
                         c.rk[:, 0:512], start=True, stop=True)
                t.matmul(c.pS1[:, 0:512], c.rq[:, tt * 128:(tt + 1) * 128],
                         c.rk[:, 512:1024], start=True, stop=True)
                t.matmul(c.pA[:, 0:512], c.query[:, tt * 128:(tt + 1) * 128],
                         c.ak_sb[:, 0:512], start=True, stop=True)
                i = t.matmul(c.pU[:, 0:512], c.query[:, tt * 128:(tt + 1) * 128],
                             c.ak_sb[:, 512:1024], start=True, stop=True)
            sig(t, i, c.pS0p)
            sigi(t, c.pS1p)
            sigi(t, c.pAp)
            sigi(t, c.pUp)
        P.on("tensor", ev_smm)

        def ev_sev(t, tt=tt):
            wt(t, c.pS0p)
            wt(t, c.pS1p)
            wt(t, c.pAp)
            wt(t, c.pUp)
            wt(t, c.cWt)
            if t:
                t.scalar_tensor_tensor(c.scores[:, 0:512], c.pS0[:, 0:512],
                                       1.0 / math.sqrt(48.0), c.logrelS[:, 0:512],
                                       ALU.mult, ALU.add)
                t.scalar_tensor_tensor(c.scores[:, 512:1024], c.pS1[:, 0:512],
                                       1.0 / math.sqrt(48.0), c.logrelS[:, 512:1024],
                                       ALU.mult, ALU.add)
                t.scalar_tensor_tensor(c.attnb[:, 0:512], c.pA[:, 0:512],
                                       0.125, c.logrelS[:, 0:512],
                                       ALU.mult, ALU.add)
                i = t.scalar_tensor_tensor(c.attnb[:, 512:1024], c.pU[:, 0:512],
                                           0.125, c.logrelS[:, 512:1024],
                                           ALU.mult, ALU.add)
            else:
                i = None
            sig(t, i, c.cSc)
            sigd(t, c.pS0f)
            sigi(t, c.pS1f)
            sigi(t, c.pAf)
            sigi(t, c.pUf)
        P.on("vector", ev_sev)

        def ev_topk(t, tt=tt):
            wt(t, c.cSc)
            if t:
                t.max(out=c.t8[:], in_=c.scores[:])
                t.drain()
                t.tensor_single_scalar(c.msk[:], c.scores[:], c.t8[:, 7:8], ALU.is_ge)
                t.drain()
                t.scalar_tensor_tensor(c.pbuf[:], c.msk[:], BIGNEG, c.attnb[:],
                                       ALU.mult, ALU.add)
                t.drain()
                t.reduce_max(c.lnst[:, 0:1], c.pbuf[:], axis=AX.X, negate=True)
                t.drain()
                i = t.tensor_scalar(c.lnst[:, 1:2], c.lnst[:, 0:1], BIGNEG, None,
                                    ALU.add)
            else:
                i = None
            sig(t, i, c.cNegm)
        P.on("vector", ev_topk)

        def ev_exp(t, tt=tt):
            wt(t, c.cNegm)
            i = t and t.activation(c.pbuf[:], c.attnb[:], AF.Exp,
                                   bias=c.lnst[:, 1:2])
            sig(t, i, c.cP)
        P.on("scalar", ev_exp)

        def ev_wnorm(t, tt=tt):
            wt(t, c.cP)
            if t:
                t.tensor_tensor(c.scores[:], c.pbuf[:], c.msk[:], ALU.mult)
                t.drain()
                t.reduce_sum(c.lnst[:, 2:3], c.scores[:], axis=AX.X)
                t.drain()
                t.reciprocal(c.lnst[:, 3:4], c.lnst[:, 2:3])
                t.drain()
                i = t.tensor_scalar(c.wtile[:], c.scores[:], c.lnst[:, 3:4], None,
                                    ALU.mult)
            else:
                i = None
            sig(t, i, c.cWt)
        P.on("vector", ev_wnorm)

        for cp in range(4):
            def ev_wtr(t, cp=cp, tt=tt):
                wt(t, c.cWt)
                wt(t, c.cWfm)
                if t:
                    t.transpose(c.pT0[:, 0:128],
                                c.wtile[:, (2 * cp) * 128:(2 * cp + 1) * 128],
                                c.identb[:])
                    t.transpose(c.pT1[:, 0:128],
                                c.wtile[:, (2 * cp + 1) * 128:(2 * cp + 2) * 128],
                                c.identb[:])
                    t.drain()
                sigi(t, c.pT0p)
                sigi(t, c.pT1p)
            P.on("tensor", ev_wtr)

            eng = "vector" if cp % 2 == 0 else "scalar"

            def ev_wev(t, cp=cp, tt=tt, eng=eng):
                wt(t, c.pT0p)
                wt(t, c.pT1p)
                i = None
                if t:
                    for j, bank in ((2 * cp, c.pT0), (2 * cp + 1, c.pT1)):
                        dst = c.w_fm[:, j * TOK + tt * 128: j * TOK + tt * 128 + 128]
                        i = (t.tensor_copy(dst, bank[:, 0:128]) if eng == "vector"
                             else t.copy(dst, bank[:, 0:128]))
                sig(t, i, c.cWfm, 2)
            P.on(eng, ev_wev)

    def _emit_uj(j):
        par = j % 2

        def ev_uw(t, j=j, par=par):
            wt(t, c.cWf, lag=1)
            if j == 0:
                wt(t, c.cQev)
            src = c.wu1t[j].rearrange("(kc p) m -> p kc m", p=128)
            dst = c.wbufC[:, par * 4096:(par + 1) * 4096].rearrange(
                "p (kc m) -> p kc m", m=128)
            i = t and t.dma_start(out=dst, in_=src)
            sig(t, i, c.cW, 16)
        P.on("gpsimd", ev_uw)

        for n in range(2):
            bank, bp, bf = ((c.pS0, c.pS0p, c.pS0f), (c.pS1, c.pS1p, c.pS1f),
                            (c.pA, c.pAp, c.pAf))[(j * 2 + n) % 3]

            def ev_umm(t, j=j, n=n, par=par, bank=bank, bp=bp, bf=bf):
                wt(t, c.cW)
                wt(t, c.cH0)
                wt(t, bf)
                i = None
                if t:
                    for kc in range(32):
                        lhs = c.wbufC[:, par * 4096 + kc * 128: par * 4096 + kc * 128 + 128]
                        rhs = c.regA[:, kc * TOK + n * 512: kc * TOK + n * 512 + 512]
                        i = t.matmul(bank[:, 0:512], lhs, rhs,
                                     start=(kc == 0), stop=(kc == 31))
                sig(t, i, bp)
            P.on("tensor", ev_umm)

            def ev_uev(t, j=j, n=n, bank=bank, bp=bp, bf=bf):
                wt(t, bp)
                i = t and t.activation(c.u1g[:, j * TOK + n * 512: j * TOK + n * 512 + 512],
                                       bank[:, 0:512], AF.Identity,
                                       bias=c.bu1S[:, j:j + 1])
                sig(t, i, c.cU1)
                sigd(t, bf)
                if n == 1:
                    sigi(t, c.cWf)
            P.on("scalar", ev_uev)

        def ev_ge1(t, j=j):
            wt(t, c.cU1)
            a = c.u1g[:, j * TOK:(j + 1) * TOK]
            s = c.gtmp[:, 0:TOK]
            s3 = c.gtmp[:, TOK:2 * TOK]
            if t:
                t.tensor_tensor(s, a, a, ALU.mult)
                t.tensor_tensor(s3, s, a, ALU.mult)
                i = t.scalar_tensor_tensor(s, s3, 0.044715, a, ALU.mult, ALU.add)
            else:
                i = None
            sig(t, i, c.cGe1)
        P.on("vector", ev_ge1)

        def ev_ge2(t, j=j):
            wt(t, c.cGe1)
            i = t and t.activation(c.gtmp[:, TOK:2 * TOK], c.gtmp[:, 0:TOK],
                                   AF.Tanh, scale=c.cst[:, C_GELC:C_GELC + 1])
            sig(t, i, c.cGe2)
        P.on("scalar", ev_ge2)

        def ev_ge3(t, j=j):
            wt(t, c.cGe2)
            a = c.u1g[:, j * TOK:(j + 1) * TOK]
            if t:
                t.scalar_tensor_tensor(c.gtmp[:, 0:TOK], c.gtmp[:, TOK:2 * TOK],
                                       1.0, a, ALU.add, ALU.mult)
                t.drain()
                i = t.tensor_scalar(a, c.gtmp[:, 0:TOK], 0.5, None, ALU.mult)
            else:
                i = None
            sig(t, i, c.cGe3)
        P.on("vector", ev_ge3)

    for _k in range(8):
        _emit_router(_k)
        _emit_uj(_k)

    for tt in range(8):
        def ev_u2mm(t, tt=tt):
            wt(t, c.cGe3)
            wt(t, c.pUf)
            i = None
            if t:
                for kc in range(8):
                    i = t.matmul(c.pU[:, 0:1],
                                 c.u1g[:, kc * TOK + tt * 128: kc * TOK + tt * 128 + 128],
                                 c.wu2S[:, kc:kc + 1],
                                 start=(kc == 0), stop=(kc == 7))
            sig(t, i, c.pUp)
        P.on("tensor", ev_u2mm)

        def ev_u2ev(t, tt=tt):
            wt(t, c.pUp)
            i = t and t.activation(c.u2S[:, tt:tt + 1], c.pU[:, 0:1], AF.Sigmoid,
                                   bias=c.scget[:, 2:3])
            sig(t, i, c.cU2)
            sigd(t, c.pUf)
        P.on("scalar", ev_u2ev)

    def ev_gate1(t):
        wt(t, c.cU2)
        wt(t, c.cLv)
        wt(t, c.cBc)
        if t:
            t.tensor_scalar(c.gateS[:], c.lv[:], c.scget[:, 3:4], None, ALU.mult)
            t.tensor_scalar(c.u2S[:], c.u2S[:], 2.5, None, ALU.mult)
            t.drain()
            t.scalar_tensor_tensor(c.gateS[:], c.gateS[:], 0.5, c.u2S[:],
                                   ALU.mult, ALU.add)
            t.drain()
            t.tensor_scalar(c.gateS[:], c.gateS[:], 0.0, 5.0, ALU.max, ALU.min)
            t.drain()
            t.tensor_scalar(c.gateS[:], c.gateS[:], 0.5, 1.0 / 1.5,
                            ALU.subtract, ALU.mult)
            t.drain()
            t.tensor_scalar(c.gateS[:], c.gateS[:], 0.0, 1.0, ALU.max, ALU.min)
            t.drain()
            i = t.tensor_scalar(c.gateS[:], c.gateS[:], c.scget[:, 0:1],
                                c.scget[:, 1:2], ALU.mult, ALU.add)
        else:
            i = None
        sig(t, i, c.cGate)
    P.on("vector", ev_gate1)

    def ev_gate2(t):
        wt(t, c.cGate)
        i = t and t.activation(c.u2S[:], c.gateS[:], AF.Sigmoid)
        sig(t, i, c.cGate)
    P.on("scalar", ev_gate2)

    def ev_gate3(t):
        wt(t, c.cGate)
        if t:
            t.tensor_single_scalar(c.gateS[:], c.u2S[:], 0.05, ALU.is_ge)
            t.drain()
            i = t.tensor_tensor(c.gateS[:], c.gateS[:], c.u2S[:], ALU.mult)
        else:
            i = None
        sig(t, i, c.cGate)
    P.on("vector", ev_gate3)

    for m in range(32):
        par = m % 2

        def ev_avw(t, m=m, par=par):
            wt(t, c.cWf, lag=1)
            src = c.avt[m].rearrange("(sc p) m2 -> p sc m2", p=128)
            dst = c.wbufC[:, par * 4096:par * 4096 + 1024].rearrange(
                "p (sc m2) -> p sc m2", m2=128)
            i = t and t.dma_start(out=dst, in_=src)
            sig(t, i, c.cW, 16)
        P.on("gpsimd", ev_avw)

        for n in range(2):
            bank, bp, bf = ((c.pS0, c.pS0p, c.pS0f), (c.pS1, c.pS1p, c.pS1f),
                            (c.pA, c.pAp, c.pAf))[(m * 2 + n) % 3]

            def ev_a1mm(t, m=m, n=n, par=par, bank=bank, bp=bp, bf=bf):
                wt(t, c.cW)
                wt(t, c.cWfm)
                if m == 0 and n == 0:
                    wt(t, c.cGe3)
                    wt(t, c.cU2)
                wt(t, bf)
                i = None
                if t:
                    for sc in range(8):
                        lhs = c.wbufC[:, par * 4096 + sc * 128: par * 4096 + sc * 128 + 128]
                        rhs = c.w_fm[:, sc * TOK + n * 512: sc * TOK + n * 512 + 512]
                        i = t.matmul(bank[:, 0:512], lhs, rhs,
                                     start=(sc == 0), stop=(sc == 7))
                sig(t, i, bp)
            P.on("tensor", ev_a1mm)

            def ev_a1ev(t, m=m, n=n, bank=bank, bp=bp, bf=bf):
                wt(t, bp)
                i = t and t.tensor_copy(
                    c.regA[:, m * TOK + n * 512: m * TOK + n * 512 + 512],
                    bank[:, 0:512])
                sig(t, i, c.cA1)
                sigd(t, bf)
                if n == 1:
                    sigi(t, c.cWf)
            P.on("vector", ev_a1ev)

    for kc in range(32):
        par = kc % 2

        def ev_vdw(t, kc=kc, par=par):
            wt(t, c.cVd, lag=1)
            i = t and t.dma_start(out=c.wvdb[:, par * 256:(par + 1) * 256],
                                  in_=c.wvdT[kc * 128:(kc + 1) * 128, :])
            sig(t, i, c.cVdW, 16)
        P.on("gpsimd", ev_vdw)

        def ev_a2mm(t, kc=kc, par=par):
            wt(t, c.cVdW)
            wt(t, c.cA1)
            if kc == 0:
                wt(t, c.pS0f)
                wt(t, c.pS1f)
                wt(t, c.pAf)
                wt(t, c.pUf)
            i = None
            if t:
                for m2 in range(2):
                    for n in range(2):
                        bank = ((c.pS0, c.pS1), (c.pA, c.pU))[m2][n]
                        i = t.matmul(
                            bank[:, 0:512],
                            c.wvdb[:, par * 256 + m2 * 128: par * 256 + m2 * 128 + 128],
                            c.regA[:, kc * TOK + n * 512: kc * TOK + n * 512 + 512],
                            start=(kc == 0), stop=(kc == 31))
            sig(t, i, c.cVd)
        P.on("tensor", ev_a2mm)

    def ev_a2ev(t):
        wt(t, c.cVd)
        if t:
            t.tensor_copy(c.aux2S[:, 0:512], c.pS0[:, 0:512])
            t.tensor_copy(c.aux2S[:, 512:1024], c.pS1[:, 0:512])
            t.tensor_copy(c.aux2S[:, 1024:1536], c.pA[:, 0:512])
            i = t.tensor_copy(c.aux2S[:, 1536:2048], c.pU[:, 0:512])
        else:
            i = None
        sig(t, i, c.cA2)
        sigd(t, c.pS0f)
        sigi(t, c.pS1f)
        sigi(t, c.pAf)
        sigi(t, c.pUf)
    P.on("vector", ev_a2ev)

    for tt in range(8):
        xb_ = (c.xt, getattr(c, "xt2", c.xt))[tt % 2]

        def ev_ldx2(t, tt=tt, xb_=xb_):
            wt(t, c.cXf, lag=1)
            wt(t, c.cOut, lag=16)
            wt(t, c.cA1)
            i = t and t.dma_start(out=xb_[:, 0:D], in_=c.x1[tt * 128:(tt + 1) * 128, :])
            sig(t, i, c.cX, 16)
        P.on("sync", ev_ldx2)

        for dg in range(8):
            bank, bp, bf = ((c.pS0, c.pS0p, c.pS0f), (c.pS1, c.pS1p, c.pS1f),
                            (c.pA, c.pAp, c.pAf))[dg % 3]

            def ev_a3mm(t, tt=tt, dg=dg, bank=bank, bp=bp, bf=bf):
                wt(t, c.cA2)
                wt(t, bf)
                i = None
                if t:
                    for kc2 in range(2):
                        i = t.matmul(
                            bank[:, 0:512],
                            c.aux2S[:, kc2 * TOK + tt * 128: kc2 * TOK + tt * 128 + 128],
                            c.wvuS[:, kc2 * D + dg * 512: kc2 * D + dg * 512 + 512],
                            start=(kc2 == 0), stop=(kc2 == 1))
                sig(t, i, bp)
            P.on("tensor", ev_a3mm)

            def ev_a3ev(t, tt=tt, dg=dg, bank=bank, bp=bp, bf=bf, xb_=xb_):
                wt(t, bp)
                wt(t, c.cX)
                wt(t, c.cGate)
                wt(t, c.cHt)
                i = None
                if t:
                    t.tensor_scalar(c.hsl[:], xb_[:, dg * 512:(dg + 1) * 512],
                                    c.mstore[:, tt:tt + 1], c.rstore[:, tt:tt + 1],
                                    ALU.subtract, ALU.mult)
                    t.drain()
                    i = t.scalar_tensor_tensor(c.fb[:, dg * 512:(dg + 1) * 512],
                                               bank[:, 0:512], c.gateS[:, tt:tt + 1],
                                               c.hsl[:], ALU.mult, ALU.add)
                sig(t, i, c.cFb)
                sigd(t, bf)
            P.on("vector", ev_a3ev)

        def ev_sqf(t, tt=tt):
            wt(t, c.cFb)
            wt(t, c.cHt)
            i = t and t.activation(c.scr4k[:], c.fb[:], AF.Square,
                                   accum_out=c.lnst[:, 1:2])
            sig(t, i, c.cSq)
        P.on("scalar", ev_sqf)

        def ev_statf(t):
            wt(t, c.cSq)
            if t:
                t.reduce_sum(c.lnst[:, 0:1], c.fb[:], axis=AX.X)
                t.drain()
                t.tensor_scalar(c.lnst[:, 2:3], c.lnst[:, 0:1],
                                c.cst[:, C_INV_D:C_INV_D + 1], None, ALU.mult)
                t.tensor_scalar(c.lnst[:, 3:4], c.lnst[:, 1:2],
                                c.cst[:, C_INV_D:C_INV_D + 1], None, ALU.mult)
                t.drain()
                t.tensor_tensor(c.lnst[:, 4:5], c.lnst[:, 2:3], c.lnst[:, 2:3], ALU.mult)
                t.drain()
                i = t.tensor_tensor(c.lnst[:, 4:5], c.lnst[:, 3:4], c.lnst[:, 4:5],
                                    ALU.subtract)
            else:
                i = None
            sig(t, i, c.cStat)
        P.on("vector", ev_statf)

        def ev_sqrtf(t):
            wt(t, c.cStat)
            i = t and t.activation(c.lnst[:, 5:6], c.lnst[:, 4:5], AF.Sqrt,
                                   bias=c.cst[:, C_EPS:C_EPS + 1])
            sig(t, i, c.cSqrt)
        P.on("scalar", ev_sqrtf)

        def ev_applyf(t, tt=tt, xb_=xb_):
            wt(t, c.cSqrt)
            if t:
                t.reciprocal(c.lnst[:, 6:7], c.lnst[:, 5:6])
                t.drain()
                t.tensor_scalar(c.scr4k[:], c.fb[:], c.lnst[:, 2:3], c.lnst[:, 6:7],
                                ALU.subtract, ALU.mult)
                i = t.tensor_tensor(xb_[:, 0:D], c.scr4k[:], xb_[:, 0:D], ALU.add)
            else:
                i = None
            sig(t, i, c.cHt)
            sigd(t, c.cXf)
        P.on("vector", ev_applyf)

        def ev_stx2(t, tt=tt, xb_=xb_):
            wt(t, c.cHt)
            i = t and t.dma_start(out=c.x2[tt * 128:(tt + 1) * 128, :], in_=xb_[:, 0:D])
            sig(t, i, c.cOut, 16)
        P.on("sync", ev_stx2)

    def ev_done(t):
        wt(t, c.cOut)
    P.on("sync", ev_done)




def build_phase_cb():
    import contextlib
    nc = bass.Bass("TRN2", target_bir_lowering=False)
    c = types.SimpleNamespace()
    c.x1 = nc.dram_tensor("x1", [TOK, D], F32, kind="ExternalInput").ap()
    c.partials = nc.dram_tensor("partials", [1, 8], F32, kind="ExternalInput").ap()
    c.consts = nc.dram_tensor("consts", [128, NCONST], F32, kind="ExternalInput").ap()
    c.onesr = nc.dram_tensor("onesr", [1, 128], F32, kind="ExternalInput").ap()
    c.identf = nc.dram_tensor("identf", [128, 128], F32, kind="ExternalInput").ap()
    c.wqT = nc.dram_tensor("wqT", [D, 64], F32, kind="ExternalInput").ap()
    c.wrT = nc.dram_tensor("wrT", [64, 48], F32, kind="ExternalInput").ap()
    c.akT = nc.dram_tensor("akT", [64, 1024], F32, kind="ExternalInput").ap()
    c.avt = nc.dram_tensor("avt", [32, 1024, 128], BF16, kind="ExternalInput").ap()
    c.wvdT = nc.dram_tensor("wvdT", [D, 256], BF16, kind="ExternalInput").ap()
    c.wvuT = nc.dram_tensor("wvuT", [256, D], BF16, kind="ExternalInput").ap()
    c.wu1t = nc.dram_tensor("wu1t", [8, D, 128], BF16, kind="ExternalInput").ap()
    c.bu1c = nc.dram_tensor("bu1c", [128, 8], F32, kind="ExternalInput").ap()
    c.wu2c = nc.dram_tensor("wu2c", [128, 8], F32, kind="ExternalInput").ap()
    c.scals = nc.dram_tensor("scals", [1, 4], F32, kind="ExternalInput").ap()
    c.logrel = nc.dram_tensor("logrel", [128, 1024], F32, kind="ExternalInput").ap()
    c.x2 = nc.dram_tensor("x2", [TOK, D], F32, kind="ExternalOutput").ap()

    with contextlib.ExitStack() as st:
        sb, ps = (lambda *a: _sb(st, nc, *a)), (lambda *a: _ps(st, nc, *a))
        c.cst = sb("cst", [128, NCONST], F32)
        c.ones1 = sb("ones1", [1, 128], F32)
        c.identb = sb("identb", [128, 128], BF16)
        c.regA = sb("regA", [128, 32 * TOK], BF16)
        c.u1g = sb("u1g", [128, 8 * TOK], BF16)
        c.w_fm = sb("w_fm", [128, 8 * TOK], BF16)
        c.wvuS = sb("wvuS", [128, 2 * D], BF16)
        c.aux2S = sb("aux2S", [128, 2 * TOK], BF16)
        c.qr1 = sb("qr1", [128, TOK], BF16)
        c.query = c.qr1[0:64, :]
        c.rq = c.qr1[64:112, :]
        c.akr = sb("akr", [128, TOK], BF16)
        c.ak_sb = c.akr[0:64, :]
        c.rk = c.akr[64:112, :]
        c.wr_sb = sb("wr_sb", [64, 48], BF16)
        c.xt = sb("xt", [128, D], F32)
        c.scr4k = sb("scr4k", [128, D], F32)
        c.fb = sb("fb", [128, D], F32)
        c.lnst = sb("lnst", [128, 8], F32)
        c.scores = sb("scores", [128, 1024], F32)
        c.attnb = sb("attnb", [128, 1024], F32)
        c.msk = sb("msk", [128, 1024], BF16)
        c.pbuf = sb("pbuf", [128, 1024], F32)
        c.t8 = sb("t8", [128, 8], F32)
        c.wbufC = sb("wbufC", [128, 2 * 4096], BF16)
        c.wvdb = sb("wvdb", [128, 2 * 256], BF16)
        c.logrelS = sb("logrelS", [128, 1024], BF16)
        c.lv = sb("lv", [128, 8], F32)
        c.mstore = sb("mstore", [128, 8], F32)
        c.rstore = sb("rstore", [128, 8], F32)
        c.u2S = sb("u2S", [128, 8], F32)
        c.gateS = sb("gateS", [128, 8], F32)
        c.scget = sb("scget", [128, 4], F32)
        c.scalsS = sb("scalsS", [1, 4], F32)
        c.prow = sb("prow", [1, 8], F32)
        c.gtmp = sb("gtmp", [128, 2 * TOK], BF16)
        c.bu1S = sb("bu1S", [128, 8], F32)
        c.wu2S = sb("wu2S", [128, 8], BF16)
        c.ht = c.u1g[:, 0:4096]
        c.wq_sb = c.wbufC[:, 0:2048]
        c.wtile = c.pbuf[:].bitcast(BF16)[:, 0:1024]
        c.hsl = c.scr4k[:, 0:512]
        c.pS0 = ps("pS0", [128, 512], F32)
        c.pS1 = ps("pS1", [128, 512], F32)
        c.pA = ps("pA", [128, 512], F32)
        c.pU = ps("pU", [128, 512], F32)
        c.pT0 = ps("pT0", [128, 1024], BF16)
        c.pT1 = ps("pT1", [128, 1024], BF16)
        block = st.enter_context(nc.Block())

        P = Prog(nc, st)
        for nm in ("cInit cBc cX cXf cSq cStat cSqrt cHt pT0p pT0f pT1p pT1f cH0 "
                   "cSqH cLv pS0p pS0f pS1p pS1f pAp pAf pUp pUf cW cWf "
                   "cQev cRKev cRQev cSc cMsk cNegm cP cWt cWfm cU1 cGe1 cGe2 cGe3 "
                   "cU2 cGate cA1 cA2 cA3 cVd cVdW cFb cOut").split():
            setattr(c, nm, P.ch(nm))
        emit_cb(P, c)
        P.emit(block)
    return nc


def _prep_cb(inputs, x1_outs, partials):
    consts = _consts_host_c()
    onesr = np.ones((1, 128), np.float32)
    ident = np.eye(128, dtype=np.float32)
    wqT = np.ascontiguousarray(inputs["L0_ffn_Wq"].T)
    wrT = np.ascontiguousarray(inputs["L0_ffn_Wr"].T)
    akT = np.ascontiguousarray(inputs["L0_ffn_aux_keys"].T)
    av = inputs["L0_ffn_aux_values"]
    avt = np.ascontiguousarray(av.reshape(1024, 32, 128).transpose(1, 0, 2)).astype(BF)
    wvdT = np.ascontiguousarray(inputs["L0_ffn_Wvd"].T).astype(BF)
    wvuT = np.ascontiguousarray(inputs["L0_ffn_Wvu"].T).astype(BF)
    wu1T = np.ascontiguousarray(inputs["L0_ffn_Wu1"].T)
    wu1t = np.ascontiguousarray(wu1T.reshape(D, 8, 128).transpose(1, 0, 2)).astype(BF)
    bu1c = np.ascontiguousarray(inputs["L0_ffn_bu1"].reshape(8, 128).T)
    wu2c = np.ascontiguousarray(inputs["L0_ffn_Wu2"].reshape(8, 128).T)
    scals = np.array([[inputs["L0_ffn_gate_w1"], inputs["L0_ffn_gate_bias"],
                       inputs["L0_ffn_bu2"].reshape(()), 0.0]], np.float32)
    logrel = np.ascontiguousarray(np.tile(inputs["L0_ffn_log_rel"][None, :], (128, 1)))
    parts_row = np.asarray(partials, np.float32).reshape(1, 8)
    in_maps = []
    for core in range(NCORES):
        in_maps.append({
            "x1": np.ascontiguousarray(np.asarray(x1_outs[core])),
            "partials": parts_row,
            "consts": consts, "onesr": onesr, "identf": ident,
            "wqT": wqT, "wrT": wrT, "akT": akT, "avt": avt,
            "wvdT": wvdT, "wvuT": wvuT, "wu1t": wu1t, "bu1c": bu1c,
            "wu2c": wu2c, "scals": scals, "logrel": logrel,
        })
    return in_maps


def run_phase_cb(inputs, x1_outs, partials):
    nc = build_phase_cb()
    in_maps = _prep_cb(inputs, x1_outs, partials)
    res = run_bass_kernel_spmd(nc, in_maps, core_ids=list(range(NCORES)))
    return [r["x2"] for r in res.results]


C_INVSQ128 = 5


def emit_d(P, c):

    KTs = c.regA[0:8, 0:16384]
    QTs = c.regA[0:8, 16384:32768]

    def ev_init(t):
        for dst, src in ((c.cst[:], c.consts), (c.identb[:], c.identf),
                         (c.bkS[:], c.bkc), (c.bvS[:], c.bvc)):
            i = t and t.dma_start(out=dst, in_=src)
            sig(t, i, c.cInit, 16)
        for j in range(4):
            i = t and t.dma_start(out=c.bqS[:], in_=c.bqj[j],
                                  accum_op=(ALU.bypass if j == 0 else ALU.add))
            sig(t, i, c.cInit, 16)
            if t:
                t.wait_ge(c.cInit.sem, c.cInit.n)
    P.on("gpsimd", ev_init)

    def ev_init_v(t):
        wt(t, c.cInit)
        t and t.memset(c.vS[:, 8 * TOK:9 * TOK], 1.0)
    P.on("vector", ev_init_v)
    for e in ("scalar", "tensor", "sync"):
        def ev_i(t):
            wt(t, c.cInit)
        P.on(e, ev_i)

    emit_ln_fm(P, c, c.x2, 8, c.regA)

    for proj in range(3):
        for m in range(8):
            def ev_pw(t, proj=proj, m=m):
                wt(t, c.cWf)
                if proj == 0 and m == 0 and getattr(c, "fused", False):
                    wt(t, c.cH0)
                if proj == 0:
                    for j in range(4):
                        src = c.wqt[m, j].rearrange("(kc p) mm -> p kc mm", p=128)
                        dst = c.wbufD[:, j * D:(j + 1) * D].rearrange(
                            "p (kc mm) -> p kc mm", mm=128)
                        i = t and t.dma_start(out=dst, in_=src)
                        sig(t, i, c.cW, 16)
                else:
                    srcT = (c.wkt if proj == 1 else c.wvt)[m]
                    src = srcT.rearrange("(kc p) mm -> p kc mm", p=128)
                    dst = c.wbufD[:, 0:D].rearrange("p (kc mm) -> p kc mm", mm=128)
                    i = t and t.dma_start(out=dst, in_=src)
                    sig(t, i, c.cW, 16)
            P.on("gpsimd", ev_pw)

            nj = 4 if proj == 0 else 1
            for n in range(2):
                bank, bp, bf = ((c.pS0, c.pS0p, c.pS0f), (c.pS1, c.pS1p, c.pS1f),
                                (c.pA, c.pAp, c.pAf))[(proj * 16 + m * 2 + n) % 3]

                def ev_pmm(t, proj=proj, m=m, n=n, nj=nj, bank=bank, bp=bp, bf=bf):
                    wt(t, c.cW)
                    wt(t, c.cH0)
                    wt(t, bf)
                    i = None
                    if t:
                        first = True
                        for j in range(nj):
                            for kc in range(32):
                                lhs = c.wbufD[:, j * D + kc * 128: j * D + kc * 128 + 128]
                                rhs = c.regA[:, kc * TOK + n * 512: kc * TOK + n * 512 + 512]
                                i = t.matmul(bank[:, 0:512], lhs, rhs,
                                             start=first,
                                             stop=(j == nj - 1 and kc == 31))
                                first = False
                    sig(t, i, bp)
                P.on("tensor", ev_pmm)

                def ev_pev(t, proj=proj, m=m, n=n, bank=bank, bp=bp, bf=bf):
                    wt(t, bp)
                    if proj == 1:
                        wt(t, c.cBnc)
                    bias = (c.bqS, c.bkS, c.bvS)[proj][:, m:m + 1]
                    if proj == 0:
                        dst = c.qgS[:, m * TOK + n * 512: m * TOK + n * 512 + 512]
                    elif proj == 2:
                        dst = c.vS[:, m * TOK + n * 512: m * TOK + n * 512 + 512]
                    else:
                        dst = c.ht[:, n * 512:(n + 1) * 512]
                    i = t and t.activation(dst, bank[:, 0:512], AF.Identity, bias=bias)
                    sig(t, i, c.cPj)
                    sigd(t, bf)
                    if n == 1:
                        sigi(t, c.cWf)
                P.on("scalar", ev_pev)

            if proj == 1:
                def ev_kst(t, m=m):
                    wt(t, c.cPj)
                    src = c.ht[:, 0:1024].rearrange("p (tc dt) -> p tc dt", dt=128)
                    dst = c.KT[m].rearrange("tc e t -> e tc t")
                    i = t and t.dma_start(out=dst, in_=src)
                    sig(t, i, c.cBnc, 16)
                P.on("sync", ev_kst)
            if proj == 0:
                def ev_qst(t, m=m):
                    wt(t, c.cPj)
                    src = c.qgS[:, m * TOK:(m + 1) * TOK].rearrange(
                        "p (tc dt) -> p tc dt", dt=128)
                    dst = c.QT[m].rearrange("tc dd t -> dd tc t")
                    i = t and t.dma_start(out=dst, in_=src)
                    sig(t, i, c.cBnc, 16)
                P.on("sync", ev_qst)

    for tc in range(8):
        par = tc % 2

        def ev_ldkq(t, tc=tc):
            wt(t, c.cBnc)
            if tc == 0:
                wt(t, c.cPj)
            else:
                wt(t, c.pS0f)
                wt(t, c.pS1f)
            i = t and t.dma_start(out=KTs.rearrange(
                "g (e t) -> g e t", t=128), in_=c.KT[:, tc])
            sig(t, i, c.cKQ, 16)
            i = t and t.dma_start(out=QTs.rearrange(
                "g (dd t) -> g dd t", t=128), in_=c.QT[:, tc])
            sig(t, i, c.cKQ, 16)
        P.on("sync", ev_ldkq)

        SBKS = ((c.pS0, c.pS0p), (c.pS1, c.pS1p), (c.pF0, c.pT0p), (c.pF1, c.pT1p))
        for blkj in range(16):
            g3 = blkj // 4
            ub_, up_, uf_ = ((c.pA, c.pAp, c.pAf), (c.pU, c.pUp, c.pUf))[g3 % 2]

            def ev_t1(t, blkj=blkj, tc=tc):
                wt(t, c.cKQ)
                if blkj == 0:
                    wt(t, c.cOtt)
                wt(t, c.cET, lag=1)
                i = None
                if t:
                    for q in range(8):
                        dt = blkj * 8 + q
                        i = t.matmul(SBKS[q // 2][0][:, (q % 2) * 128:(q % 2) * 128 + 128],
                                     KTs[:, dt:16384:128], QTs[:, dt:16384:128],
                                     start=True, stop=True)
                sig(t, i, c.cS2)
            P.on("tensor", ev_t1)

            def ev_exp(t, blkj=blkj):
                wt(t, c.cS2)
                wt(t, c.cET2, lag=1)
                i = None
                if t:
                    for b4 in range(4):
                        i = t.activation(c.ET[:, b4 * 256:(b4 + 1) * 256],
                                         SBKS[b4][0][:, 0:256], AF.Exp,
                                         scale=c.cst[:, C_INVSQ128:C_INVSQ128 + 1])
                sig(t, i, c.cET)
            P.on("scalar", ev_exp)

            def ev_t2(t, blkj=blkj, tc=tc, ub_=ub_, up_=up_, uf_=uf_):
                wt(t, c.cET)
                if blkj % 4 == 0:
                    wt(t, c.cOBv, lag=1)
                i = None
                if t:
                    for q in range(8):
                        dt = blkj * 8 + q
                        tglob = tc * 128 + dt
                        tt_ = dt % 32
                        rhs = c.vS[:, tglob:9 * TOK:TOK]
                        i = t.matmul(ub_[:, 9 * tt_:9 * tt_ + 9],
                                     c.ET[:, q * 128:(q + 1) * 128],
                                     rhs, start=True, stop=True)
                sig(t, i, up_)
                sigi(t, c.cET2)
            P.on("tensor", ev_t2)

            if blkj % 4 == 3:
                def ev_uev(t, g3=g3, ub_=ub_, up_=up_, uf_=uf_):
                    wt(t, up_)
                    i = None
                    if t:
                        uview = ub_[:, 0:288].rearrange("p (tt n) -> p tt n", n=9)
                        oview = c.OB[:, 0:1024].rearrange("p (g tt) -> p tt g", tt=128)
                        t.tensor_copy(oview[:, 32 * g3:32 * g3 + 32, :], uview[:, :, 0:8])
                        i = t.tensor_copy(c.ZB[:, 32 * g3:32 * g3 + 32], uview[:, :, 8:9])
                    sig(t, i, c.cOBv)
                P.on("vector", ev_uev)

        def ev_div(t, tc=tc):
            wt(t, c.cOBv)
            i = None
            if t:
                t.reciprocal(c.rzB[:], c.ZB[:])
                t.drain()
                for gp in range(8):
                    i = t.tensor_tensor(c.OB[:, gp * 128:(gp + 1) * 128],
                                        c.OB[:, gp * 128:(gp + 1) * 128],
                                        c.rzB[:], ALU.mult)
            sig(t, i, c.cOBd)
        P.on("vector", ev_div)

        for gp in range(8):
            bank, bp2, bf2 = ((c.pT0, c.pT0p, c.pT0f), (c.pT1, c.pT1p, c.pT1f))[gp % 2]

            def ev_otr(t, gp=gp, bank=bank, bp2=bp2, bf2=bf2):
                wt(t, c.cOBd)
                wt(t, c.cOtt, lag=1)
                if t:
                    t.transpose(bank[:, 0:128], c.OB[:, gp * 128:(gp + 1) * 128],
                                c.identb[:])
                    t.drain()
                sigi(t, bp2)
            P.on("tensor", ev_otr)

            eng = "vector" if gp % 2 == 0 else "scalar"

            def ev_oev(t, gp=gp, eng=eng, bank=bank, bp2=bp2, bf2=bf2):
                wt(t, bp2)
                wt(t, c.cXf)
                i = None
                if t:
                    for j in range(4):
                        out = c.ott[:, 4 * gp + j:4096:32]
                        i = (t.tensor_copy(out, bank[:, 0:128]) if eng == "vector"
                             else t.copy(out, bank[:, 0:128]))
                sig(t, i, c.cOtt)
            P.on(eng, ev_oev)

        if DEBUG_DUMP and tc == 0:
            def ev_dbgd(t):
                wt(t, c.cOtt)
                i = t and t.dma_start(out=c.dbgD[0, :, 0:1024], in_=c.qgS[:, 0:1024])
                sig(t, i, c.cOut, 16)
                i = t and t.dma_start(out=c.dbgD[1, :, 0:1024], in_=c.vS[:, 0:1024])
                sig(t, i, c.cOut, 16)
                i = t and t.dma_start(out=c.dbgD[2], in_=c.ott[:])
                sig(t, i, c.cOut, 16)
                i = t and t.dma_start(out=c.dbgD[3, :, 0:1024], in_=c.OB[:])
                sig(t, i, c.cOut, 16)
            P.on("sync", ev_dbgd)

        xb_ = (c.xt, getattr(c, "xt2", c.xt))[tc % 2]

        def ev_ldx(t, tc=tc, xb_=xb_):
            wt(t, c.cXf, lag=1)
            wt(t, c.cOut, lag=16)
            wt(t, c.cPj)
            i = t and t.dma_start(out=xb_[:, 0:D], in_=c.x2[tc * 128:(tc + 1) * 128, :])
            sig(t, i, c.cX, 16)
        P.on("sync", ev_ldx)

        def ev_sqf(t):
            wt(t, c.cOtt)
            wt(t, c.cHt)
            i = t and t.activation(c.scr4k[:], c.ott[:], AF.Square,
                                   accum_out=c.lnst[:, 1:2])
            sig(t, i, c.cSq)
        P.on("scalar", ev_sqf)

        def ev_statf(t):
            wt(t, c.cSq)
            if t:
                t.reduce_sum(c.lnst[:, 0:1], c.ott[:], axis=AX.X)
                t.drain()
                t.tensor_scalar(c.lnst[:, 2:3], c.lnst[:, 0:1],
                                c.cst[:, C_INV_D:C_INV_D + 1], None, ALU.mult)
                t.tensor_scalar(c.lnst[:, 3:4], c.lnst[:, 1:2],
                                c.cst[:, C_INV_D:C_INV_D + 1], None, ALU.mult)
                t.drain()
                t.tensor_tensor(c.lnst[:, 4:5], c.lnst[:, 2:3], c.lnst[:, 2:3], ALU.mult)
                t.drain()
                i = t.tensor_tensor(c.lnst[:, 4:5], c.lnst[:, 3:4], c.lnst[:, 4:5],
                                    ALU.subtract)
            else:
                i = None
            sig(t, i, c.cStat)
        P.on("vector", ev_statf)

        def ev_sqrtf(t):
            wt(t, c.cStat)
            i = t and t.activation(c.lnst[:, 5:6], c.lnst[:, 4:5], AF.Sqrt,
                                   bias=c.cst[:, C_EPS:C_EPS + 1])
            sig(t, i, c.cSqrt)
        P.on("scalar", ev_sqrtf)

        def ev_applyf(t, tc=tc, xb_=xb_):
            wt(t, c.cSqrt)
            wt(t, c.cX)
            if t:
                t.reciprocal(c.lnst[:, 6:7], c.lnst[:, 5:6])
                t.drain()
                t.tensor_scalar(c.scr4k[:], c.ott[:], c.lnst[:, 2:3], c.lnst[:, 6:7],
                                ALU.subtract, ALU.mult)
                i = t.tensor_tensor(xb_[:, 0:D], c.scr4k[:], xb_[:, 0:D], ALU.add)
            else:
                i = None
            sig(t, i, c.cHt)
            sigd(t, c.cXf)
        P.on("vector", ev_applyf)

        def ev_stx3(t, tc=tc, xb_=xb_):
            wt(t, c.cHt)
            i = t and t.dma_start(out=c.x3[tc * 128:(tc + 1) * 128, :], in_=xb_[:, 0:D])
            sig(t, i, c.cOut, 16)
        P.on("sync", ev_stx3)

    def ev_done(t):
        wt(t, c.cOut)
    P.on("sync", ev_done)




def build_phase_d():
    import contextlib
    nc = bass.Bass("TRN2", target_bir_lowering=False)
    c = types.SimpleNamespace()
    c.x2 = nc.dram_tensor("x2", [TOK, D], F32, kind="ExternalInput").ap()
    c.wqt = nc.dram_tensor("wqt", [8, 4, D, 128], BF16, kind="ExternalInput").ap()
    c.wkt = nc.dram_tensor("wkt", [8, D, 128], BF16, kind="ExternalInput").ap()
    c.wvt = nc.dram_tensor("wvt", [8, D, 128], BF16, kind="ExternalInput").ap()
    c.bqj = nc.dram_tensor("bqj", [4, 128, 8], F32, kind="ExternalInput").ap()
    c.bkc = nc.dram_tensor("bkc", [128, 8], F32, kind="ExternalInput").ap()
    c.bvc = nc.dram_tensor("bvc", [128, 8], F32, kind="ExternalInput").ap()
    c.consts = nc.dram_tensor("consts", [128, NCONST], F32, kind="ExternalInput").ap()
    c.identf = nc.dram_tensor("identf", [128, 128], F32, kind="ExternalInput").ap()
    c.x3 = nc.dram_tensor("x3", [TOK, D], F32, kind="ExternalOutput").ap()
    if DEBUG_DUMP:
        c.dbgD = nc.dram_tensor("dbgD", [4, 128, 4096], BF16, kind="ExternalOutput").ap()
    bkind = "ExternalOutput" if DEBUG_DUMP else "Internal"
    c.KT = nc.dram_tensor("KT", [8, 8, 128, 128], BF16, kind=bkind).ap()
    c.QT = nc.dram_tensor("QT", [8, 8, 128, 128], BF16, kind=bkind).ap()

    with contextlib.ExitStack() as st:
        sb, ps = (lambda *a: _sb(st, nc, *a)), (lambda *a: _ps(st, nc, *a))
        c.cst = sb("cst", [128, NCONST], F32)
        c.identb = sb("identb", [128, 128], BF16)
        c.regA = sb("regA", [128, 32 * TOK], BF16)
        c.qgS = sb("qgS", [128, 8 * TOK], BF16)
        c.vS = sb("vS", [128, 9 * TOK], BF16)
        c.wbufD = sb("wbufD", [128, 4 * D], BF16)
        c.bqS = sb("bqS", [128, 8], F32)
        c.bkS = sb("bkS", [128, 8], F32)
        c.bvS = sb("bvS", [128, 8], F32)
        c.xt = sb("xt", [128, D], F32)
        c.scr4k = sb("scr4k", [128, D], F32)
        c.lnst = sb("lnst", [128, 8], F32)
        c.ht = sb("ht", [128, D], BF16)
        c.ET = sb("ET", [128, 4 * 256], BF16)
        c.OB = sb("OB", [128, 1024], BF16)
        c.ZB = sb("ZB", [128, 128], F32)
        c.rzB = sb("rzB", [128, 128], F32)
        c.ott = sb("ott", [128, D], BF16)
        c.rz = sb("rz", [128, 4], F32)
        c.pS0 = ps("pS0", [128, 512], F32)
        c.pS1 = ps("pS1", [128, 512], F32)
        c.pA = ps("pA", [128, 512], F32)
        c.pU = ps("pU", [128, 512], F32)
        c.pT0 = ps("pT0", [128, 1024], BF16)
        c.pT1 = ps("pT1", [128, 1024], BF16)
        c.pF0 = c.pT0[:].bitcast(F32)
        c.pF1 = c.pT1[:].bitcast(F32)
        block = st.enter_context(nc.Block())

        P = Prog(nc, st)
        for nm in ("cInit cX cXf cSq cStat cSqrt cHt pT0p pT0f pT1p pT1f cH0 "
                   "cW cWf pS0p pS0f pS1p pS1f pAp pAf pUp pUf cPj cBnc cKQ cKQf "
                   "cET cET2 cOBv cOBd cOtt cOut cS2 cWu").split():
            setattr(c, nm, P.ch(nm))
        emit_d(P, c)
        P.emit(block)
    return nc


def _prep_d(inputs, x2_outs):
    consts = _consts_host_c()
    consts[:, C_INVSQ128] = 1.0 / math.sqrt(128.0)
    ident = np.eye(128, dtype=np.float32)
    Wq = inputs["L1_attn_Wq"]
    Wk = inputs["L1_attn_Wk"]
    Wv = inputs["L1_attn_Wv"]
    wqt = np.ascontiguousarray(
        Wq.reshape(D, 128, 8, 4).transpose(2, 3, 0, 1)).astype(BF)
    wkt = np.ascontiguousarray(Wk.transpose(2, 0, 1)).astype(BF)
    wvt = np.ascontiguousarray(Wv.transpose(2, 0, 1)).astype(BF)
    bq = inputs["L1_attn_bq"]
    bqj = np.ascontiguousarray(bq.reshape(128, 8, 4).transpose(2, 0, 1))
    bkc = np.ascontiguousarray(inputs["L1_attn_bk"])
    bvc = np.ascontiguousarray(inputs["L1_attn_bv"])
    in_maps = []
    for core in range(NCORES):
        in_maps.append({
            "x2": np.ascontiguousarray(np.asarray(x2_outs[core])),
            "wqt": wqt, "wkt": wkt, "wvt": wvt,
            "bqj": bqj, "bkc": bkc, "bvc": bvc,
            "consts": consts, "identf": ident,
        })
    return in_maps


def run_phase_d(inputs, x2_outs):
    nc = build_phase_d()
    in_maps = _prep_d(inputs, x2_outs)
    res = run_bass_kernel_spmd(nc, in_maps, core_ids=list(range(NCORES)))
    return [r["x3"] for r in res.results]


WPAD = 2816


def emit_e(P, c):

    def ev_init(t):
        for dst, src in ((c.cst[:], c.consts), (c.identb[:], c.identf),
                         (c.b2S[:], c.b2T)):
            i = t and t.dma_start(out=dst, in_=src)
            sig(t, i, c.cInit, 16)
        for dst, src in ((c.a1S, c.a1T), (c.a3S, c.a3T)):
            i = t and t.dma_start(out=dst[:].rearrange("p (kc m) -> p kc m", m=2),
                                  in_=src.rearrange("(kc p) m -> p kc m", p=128))
            sig(t, i, c.cInit, 16)
    P.on("gpsimd", ev_init)
    for e in ("scalar", "tensor", "sync", "vector"):
        def ev_i(t):
            wt(t, c.cInit)
        P.on(e, ev_i)

    emit_ln_fm(P, c, c.x3, 8, c.regA)

    for which in range(2):
        aS = (c.a1S, c.a3S)[which]
        cx = (c.c1x, c.c3x)[which]
        banks = (((c.pS0, c.pS0p, c.pS0f), (c.pS1, c.pS1p, c.pS1f)),
                 ((c.pA, c.pAp, c.pAf), (c.pU, c.pUp, c.pUf)))[which]

        def ev_cmm(t, aS=aS, banks=banks):
            wt(t, c.cH0)
            for _, _, bf in banks:
                wt(t, bf)
            i = None
            if t:
                for n in range(2):
                    bank = banks[n][0]
                    for kc in range(32):
                        i = t.matmul(bank[0:2, 0:512],
                                     aS[:, kc * 2:(kc + 1) * 2],
                                     c.regA[:, kc * TOK + n * 512: kc * TOK + n * 512 + 512],
                                     start=(kc == 0), stop=(kc == 31))
            sig(t, i, banks[1][1])
        P.on("tensor", ev_cmm)

        def ev_cev(t, cx=cx, banks=banks):
            wt(t, banks[1][1])
            i = None
            if t:
                t.tensor_scalar(cx[:, 0:512], banks[0][0][0:2, 0:512], 0.5, None, ALU.mult)
                i = t.tensor_scalar(cx[:, 512:1024], banks[1][0][0:2, 0:512], 0.5, None, ALU.mult)
            sig(t, i, c.cC13)
            for _, _, bf in banks:
                sigd(t, bf)
        P.on("vector", ev_cev)

    for i_ in range(8):
        def ev_xf(t, i_=i_):
            wt(t, c.cH0)
            if t:
                t.tensor_tensor(c.xftmp[:], c.regA[:, i_ * TOK:(i_ + 1) * TOK],
                                c.regA[:, (8 + i_) * TOK:(9 + i_) * TOK], ALU.add)
                t.tensor_tensor(c.xftmp[:], c.xftmp[:],
                                c.regA[:, (16 + i_) * TOK:(17 + i_) * TOK], ALU.add)
                i = t.tensor_tensor(c.xf[:, i_ * TOK:(i_ + 1) * TOK], c.xftmp[:],
                                    c.regA[:, (24 + i_) * TOK:(25 + i_) * TOK], ALU.add)
            else:
                i = None
            sig(t, i, c.cXff)
        P.on("vector", ev_xf)

    for k in range(22):
        for tj in range(4):
            par = (k * 4 + tj) % 2

            def ev_ew(t, k=k, tj=tj, par=par):
                wt(t, c.cYf, lag=2)
                for wi, (wt_, dst0) in enumerate(((c.w1pt, 0), (c.w3pt, 2048))):
                    src = wt_[k, tj].rearrange("(kc p) m -> p kc m", p=128)
                    dst = c.wbufE[:, par * 1024 + dst0: par * 1024 + dst0 + 1024]
                    dst = dst.rearrange("p (kc m) -> p kc m", m=128)
                    i = t and t.dma_start(out=dst, in_=src)
                    sig(t, i, c.cW, 16)
                i = t and t.dma_start(out=c.b1S[:, par * 128:(par + 1) * 128],
                                      in_=c.b1t[k, tj])
                sig(t, i, c.cW, 16)
                i = t and t.dma_start(out=c.b3S[:, par * 128:(par + 1) * 128],
                                      in_=c.b3t[k, tj])
                sig(t, i, c.cW, 16)
                i = t and t.dma_start(out=c.a2S[:, par * 2:(par + 1) * 2],
                                      in_=c.a2t[k, tj])
                sig(t, i, c.cW, 16)
            P.on("gpsimd", ev_ew)

            for n in range(2):
                def ev_bmm(t, k=k, tj=tj, n=n, par=par):
                    wt(t, c.cW)
                    wt(t, c.cXff)
                    wt(t, c.cC13)
                    wt(t, c.pS0f)
                    wt(t, c.pS1f)
                    i = None
                    if t:
                        for kc in range(8):
                            t.matmul(c.pS0[:, 0:512],
                                     c.wbufE[:, par * 1024 + kc * 128: par * 1024 + kc * 128 + 128],
                                     c.xf[:, kc * TOK + n * 512: kc * TOK + n * 512 + 512],
                                     start=(kc == 0), stop=False)
                        t.matmul(c.pS0[:, 0:512], c.b1S[:, par * 128:(par + 1) * 128],
                                 c.c1x[:, n * 512:(n + 1) * 512], start=False, stop=True)
                        for kc in range(8):
                            t.matmul(c.pS1[:, 0:512],
                                     c.wbufE[:, par * 1024 + 2048 + kc * 128:
                                             par * 1024 + 2048 + kc * 128 + 128],
                                     c.xf[:, kc * TOK + n * 512: kc * TOK + n * 512 + 512],
                                     start=(kc == 0), stop=False)
                        i = t.matmul(c.pS1[:, 0:512], c.b3S[:, par * 128:(par + 1) * 128],
                                     c.c3x[:, n * 512:(n + 1) * 512], start=False, stop=True)
                    sig(t, i, c.pS0p)
                    sigi(t, c.pS1p)
                P.on("tensor", ev_bmm)

                def ev_gu(t, n=n):
                    wt(t, c.pS0p)
                    wt(t, c.pS1p)
                    wt(t, c.cY)
                    if t:
                        t.activation(c.gub[:, 0:512], c.pS0[:, 0:512], AF.Silu)
                        i = t.activation(c.gub[:, 512:1024], c.pS1[:, 0:512], AF.Copy)
                    else:
                        i = None
                    sig(t, i, c.cGU)
                    sigd(t, c.pS0f)
                    sigi(t, c.pS1f)
                P.on("scalar", ev_gu)

                def ev_y(t, k=k, tj=tj, n=n):
                    wt(t, c.cGU)
                    wt(t, c.cYf)
                    i = t and t.tensor_tensor(c.ybuf[:], c.gub[:, 0:512],
                                              c.gub[:, 512:1024], ALU.mult)
                    sig(t, i, c.cY)
                P.on("vector", ev_y)

                def ev_fold(t, k=k, tj=tj, n=n, par=par, pF=(c.pF0, c.pF1)[n]):
                    wt(t, c.cY)
                    if tj == 0:
                        wt(t, (c.pT0f, c.pT1f)[n])
                    i = None
                    if t:
                        t.matmul(pF[:, 0:512], c.identb[:], c.ybuf[:],
                                 start=(tj == 0), stop=(tj == 3))
                        i = t.matmul((c.pA, c.pU)[n][0:2, 0:512],
                                     c.a2S[:, par * 2:(par + 1) * 2], c.ybuf[:],
                                     start=(k == 0 and tj == 0),
                                     stop=(k == 21 and tj == 3))
                    sig(t, i, c.cYf)
                P.on("tensor", ev_fold)

                if tj == 3:
                    def ev_yfev(t, k=k, n=n, pF=(c.pF0, c.pF1)[n]):
                        wt(t, c.cYf)
                        i = t and t.tensor_copy(
                            c.regA[:, k * TOK + n * 512: k * TOK + n * 512 + 512],
                            pF[:, 0:512])
                        sig(t, i, c.cOT)
                        sigd(t, (c.pT0f, c.pT1f)[n])
                    P.on("vector", ev_yfev)

    def ev_c2(t):
        wt(t, c.cYf)
        if t:
            t.tensor_scalar(c.c2x[:, 0:512], c.pA[0:2, 0:512], 0.5, None, ALU.mult)
            i = t.tensor_scalar(c.c2x[:, 512:1024], c.pU[0:2, 0:512], 0.5, None, ALU.mult)
        else:
            i = None
        sig(t, i, c.cC2)
        sigd(t, c.pAf)
        sigi(t, c.pUf)
    P.on("vector", ev_c2)

    if DEBUG_DUMP:
        def ev_dbge(t):
            wt(t, c.cOT)
            wt(t, c.cC2)
            i = t and t.dma_start(out=c.dbgE[0], in_=c.xf[:, 0:1024])
            sig(t, i, c.cOut, 16)
            i = t and t.dma_start(out=c.dbgE[1], in_=c.regA[:, 0:1024])
            sig(t, i, c.cOut, 16)
            i = t and t.dma_start(out=c.dbgE[2, 0:2, :], in_=c.c1x[:, :])
            sig(t, i, c.cOut, 16)
            i = t and t.dma_start(out=c.dbgE[3, 0:2, :], in_=c.c3x[:, :])
            sig(t, i, c.cOut, 16)
            i = t and t.dma_start(out=c.dbgE[4, 0:2, :], in_=c.c2x[:, :])
            sig(t, i, c.cOut, 16)
            i = t and t.dma_start(out=c.dbgE[5], in_=c.regA[:, 21 * TOK:22 * TOK])
            sig(t, i, c.cOut, 16)
        P.on("sync", ev_dbge)

    for dgrp in range(8):
        def ev_w2(t, dgrp=dgrp):
            wt(t, c.cW2f)
            if dgrp == 0 and getattr(c, "fused", False):
                wt(t, c.cXff)
            src = c.w2qTp[:, dgrp * 512:(dgrp + 1) * 512]
            src = src.rearrange("(kc p) m -> p kc m", p=128)
            dst = c.w2buf[:].rearrange("p (kc m) -> p kc m", m=512)
            i = t and t.dma_start(out=dst, in_=src)
            sig(t, i, c.cW2, 16)
        P.on("gpsimd", ev_w2)

        for tile in range(8):
            bank, bp, bf = ((c.pS0, c.pS0p, c.pS0f), (c.pS1, c.pS1p, c.pS1f),
                            (c.pA, c.pAp, c.pAf))[(dgrp * 8 + tile) % 3]

            def ev_omm(t, dgrp=dgrp, tile=tile, bank=bank, bp=bp, bf=bf):
                wt(t, c.cW2)
                wt(t, c.cOT)
                wt(t, c.cC2)
                wt(t, bf)
                i = None
                if t:
                    for kc in range(22):
                        t.matmul(bank[:, 0:512],
                                 c.regA[:, kc * TOK + tile * 128: kc * TOK + tile * 128 + 128],
                                 c.w2buf[:, kc * 512:(kc + 1) * 512],
                                 start=(kc == 0), stop=False)
                    i = t.matmul(bank[:, 0:512], c.c2x[:, tile * 128:(tile + 1) * 128],
                                 c.b2S[:, dgrp * 512:(dgrp + 1) * 512],
                                 start=False, stop=True)
                sig(t, i, bp)
            P.on("tensor", ev_omm)

            def ev_oev(t, dgrp=dgrp, tile=tile, bank=bank, bp=bp, bf=bf):
                wt(t, bp)
                if dgrp == 0 and tile == 0:
                    wt(t, c.cYf)
                i = t and t.tensor_copy(
                    c.regB[:, tile * 4096 + dgrp * 512: tile * 4096 + dgrp * 512 + 512],
                    bank[:, 0:512])
                sig(t, i, c.cGU)
                sigd(t, bf)
                if tile == 7:
                    sigi(t, c.cW2f)
            P.on("vector", ev_oev)

    for tile in range(8):
        xb_ = (c.xt, getattr(c, "xt2", c.xt))[tile % 2]

        def ev_ldx(t, tile=tile, xb_=xb_):
            wt(t, c.cXf, lag=1)
            wt(t, c.cOut, lag=16)
            wt(t, c.cGU)
            i = t and t.dma_start(out=xb_[:, 0:D], in_=c.x3[tile * 128:(tile + 1) * 128, :])
            sig(t, i, c.cX, 16)
        P.on("sync", ev_ldx)

        def ev_sqf(t, tile=tile):
            wt(t, c.cGU)
            wt(t, c.cHt)
            ot = c.regB[:, tile * 4096:(tile + 1) * 4096]
            i = t and t.activation(c.scr4k[:], ot, AF.Square,
                                   accum_out=c.lnst[:, 1:2])
            sig(t, i, c.cSq)
        P.on("scalar", ev_sqf)

        def ev_statf(t, tile=tile):
            wt(t, c.cSq)
            ot = c.regB[:, tile * 4096:(tile + 1) * 4096]
            if t:
                t.reduce_sum(c.lnst[:, 0:1], ot, axis=AX.X)
                t.drain()
                t.tensor_scalar(c.lnst[:, 2:3], c.lnst[:, 0:1],
                                c.cst[:, C_INV_D:C_INV_D + 1], None, ALU.mult)
                t.tensor_scalar(c.lnst[:, 3:4], c.lnst[:, 1:2],
                                c.cst[:, C_INV_D:C_INV_D + 1], None, ALU.mult)
                t.drain()
                t.tensor_tensor(c.lnst[:, 4:5], c.lnst[:, 2:3], c.lnst[:, 2:3], ALU.mult)
                t.drain()
                i = t.tensor_tensor(c.lnst[:, 4:5], c.lnst[:, 3:4], c.lnst[:, 4:5],
                                    ALU.subtract)
            else:
                i = None
            sig(t, i, c.cStat)
        P.on("vector", ev_statf)

        def ev_sqrtf(t):
            wt(t, c.cStat)
            i = t and t.activation(c.lnst[:, 5:6], c.lnst[:, 4:5], AF.Sqrt,
                                   bias=c.cst[:, C_EPS:C_EPS + 1])
            sig(t, i, c.cSqrt)
        P.on("scalar", ev_sqrtf)

        def ev_applyf(t, tile=tile, xb_=xb_):
            wt(t, c.cSqrt)
            wt(t, c.cX)
            ot = c.regB[:, tile * 4096:(tile + 1) * 4096]
            if t:
                t.reciprocal(c.lnst[:, 6:7], c.lnst[:, 5:6])
                t.drain()
                t.tensor_scalar(c.scr4k[:], ot, c.lnst[:, 2:3], c.lnst[:, 6:7],
                                ALU.subtract, ALU.mult)
                i = t.tensor_tensor(xb_[:, 0:D], c.scr4k[:], xb_[:, 0:D], ALU.add)
            else:
                i = None
            sig(t, i, c.cHt)
            sigd(t, c.cXf)
        P.on("vector", ev_applyf)

        def ev_st(t, tile=tile, xb_=xb_):
            wt(t, c.cHt)
            i = t and t.dma_start(out=c.xout[tile * 128:(tile + 1) * 128, :],
                                  in_=xb_[:, 0:D])
            sig(t, i, c.cOut, 16)
        P.on("sync", ev_st)

    def ev_done(t):
        wt(t, c.cOut)
    P.on("sync", ev_done)




def build_phase_e():
    import contextlib
    nc = bass.Bass("TRN2", target_bir_lowering=False)
    c = types.SimpleNamespace()
    c.x3 = nc.dram_tensor("x3", [TOK, D], F32, kind="ExternalInput").ap()
    c.w1pt = nc.dram_tensor("w1pt", [22, 4, 1024, 128], BF16, kind="ExternalInput").ap()
    c.w3pt = nc.dram_tensor("w3pt", [22, 4, 1024, 128], BF16, kind="ExternalInput").ap()
    c.b1t = nc.dram_tensor("b1t", [22, 4, 2, 128], F32, kind="ExternalInput").ap()
    c.b3t = nc.dram_tensor("b3t", [22, 4, 2, 128], F32, kind="ExternalInput").ap()
    c.a2t = nc.dram_tensor("a2t", [22, 4, 128, 2], F32, kind="ExternalInput").ap()
    c.a1T = nc.dram_tensor("a1T", [D, 2], F32, kind="ExternalInput").ap()
    c.a3T = nc.dram_tensor("a3T", [D, 2], F32, kind="ExternalInput").ap()
    c.w2qTp = nc.dram_tensor("w2qTp", [WPAD, D], BF16, kind="ExternalInput").ap()
    c.b2T = nc.dram_tensor("b2T", [2, D], F32, kind="ExternalInput").ap()
    c.consts = nc.dram_tensor("consts", [128, NCONST], F32, kind="ExternalInput").ap()
    c.identf = nc.dram_tensor("identf", [128, 128], F32, kind="ExternalInput").ap()
    c.xout = nc.dram_tensor("xout", [TOK, D], F32, kind="ExternalOutput").ap()
    if DEBUG_DUMP:
        c.dbgE = nc.dram_tensor("dbgE", [6, 128, 1024], BF16, kind="ExternalOutput").ap()

    with contextlib.ExitStack() as st:
        sb, ps = (lambda *a: _sb(st, nc, *a)), (lambda *a: _ps(st, nc, *a))
        c.cst = sb("cst", [128, NCONST], F32)
        c.identb = sb("identb", [128, 128], BF16)
        c.regA = sb("regA", [128, 32 * TOK], BF16)
        c.regB = sb("regB", [128, 32 * TOK], BF16)
        c.xt = sb("xt", [128, D], F32)
        c.scr4k = sb("scr4k", [128, D], F32)
        c.lnst = sb("lnst", [128, 8], F32)
        c.wbufE = sb("wbufE", [128, 4 * 1024], BF16)
        c.w2buf = sb("w2buf", [128, 22 * 512], BF16)
        c.a1S = sb("a1S", [128, 64], BF16)
        c.a3S = sb("a3S", [128, 64], BF16)
        c.b1S = sb("b1S", [2, 2 * 128], BF16)
        c.b3S = sb("b3S", [2, 2 * 128], BF16)
        c.a2S = sb("a2S", [128, 2 * 2], BF16)
        c.b2S = sb("b2S", [2, D], BF16)
        c.c1x = sb("c1x", [2, TOK], BF16)
        c.c3x = sb("c3x", [2, TOK], BF16)
        c.c2x = sb("c2x", [2, TOK], BF16)
        c.pS0 = ps("pS0", [128, 512], F32)
        c.pS1 = ps("pS1", [128, 512], F32)
        c.pA = ps("pA", [128, 512], F32)
        c.pU = ps("pU", [128, 512], F32)
        c.pT0 = ps("pT0", [128, 1024], BF16)
        c.pT1 = ps("pT1", [128, 1024], BF16)
        c.xf = c.regB[:, 0:8192]
        c.ht = c.regB[:, 8192:12288]
        c.xftmp = c.scr4k[:, 0:1024]
        c.ybuf = c.scr4k[:, 1024:1280].bitcast(BF16)
        c.gub = c.scr4k[:, 1280:1792].bitcast(BF16)
        c.pF0 = c.pT0.ap().bitcast(F32) if False else None
        block = st.enter_context(nc.Block())
        c.pF0 = c.pT0[:].bitcast(F32)
        c.pF1 = c.pT1[:].bitcast(F32)

        P = Prog(nc, st)
        for nm in ("cInit cX cXf cSq cStat cSqrt cHt pT0p pT0f pT1p pT1f cH0 "
                   "cW cWf cW2 cW2f pS0p pS0f pS1p pS1f pAp pAf pUp pUf "
                   "cC13 cXff cGU cY cYf cC2 cOT cOut").split():
            setattr(c, nm, P.ch(nm))
        emit_e(P, c)
        P.emit(block)
    return nc


def _prep_e(inputs, x3_outs):
    consts = _consts_host_c()
    ident = np.eye(128, dtype=np.float32)

    def padw(w):
        w4 = w.reshape(4, 2752, -1)
        out = np.zeros((4, WPAD, w4.shape[2]), np.float32)
        out[:, :2752] = w4
        return out

    w1 = padw(inputs["L1_ffn_w1q"])
    w3 = padw(inputs["L1_ffn_w3q"])
    w1pt = np.ascontiguousarray(
        w1.reshape(4, 22, 128, 1024).transpose(1, 0, 3, 2)).astype(BF)
    w3pt = np.ascontiguousarray(
        w3.reshape(4, 22, 128, 1024).transpose(1, 0, 3, 2)).astype(BF)
    b1 = padw(inputs["L1_ffn_b1"])
    b3 = padw(inputs["L1_ffn_b3"])
    b1t = np.ascontiguousarray(b1.reshape(4, 22, 128, 2).transpose(1, 0, 3, 2))
    b3t = np.ascontiguousarray(b3.reshape(4, 22, 128, 2).transpose(1, 0, 3, 2))
    a2 = padw(np.ascontiguousarray(inputs["L1_ffn_a2"].T))
    a2t = np.ascontiguousarray(a2.reshape(4, 22, 128, 2).transpose(1, 0, 2, 3))
    a1T = np.ascontiguousarray(inputs["L1_ffn_a1"].T)
    a3T = np.ascontiguousarray(inputs["L1_ffn_a3"].T)
    w2qT = np.ascontiguousarray(inputs["L1_ffn_w2q"].T)
    w2qTp = np.zeros((WPAD, D), BF)
    w2qTp[:2752] = w2qT
    b2T = np.ascontiguousarray(inputs["L1_ffn_b2"].T)
    in_maps = []
    for core in range(NCORES):
        in_maps.append({
            "x3": np.ascontiguousarray(np.asarray(x3_outs[core])),
            "w1pt": w1pt, "w3pt": w3pt, "b1t": b1t, "b3t": b3t, "a2t": a2t,
            "a1T": a1T, "a3T": a3T, "w2qTp": w2qTp, "b2T": b2T,
            "consts": consts, "identf": ident,
        })
    return in_maps


def run_phase_e(inputs, x3_outs):
    nc = build_phase_e()
    in_maps = _prep_e(inputs, x3_outs)
    res = run_bass_kernel_spmd(nc, in_maps, core_ids=list(range(NCORES)))
    return [r["xout"] for r in res.results]



POOLC = 56832


class Carve:
    def __init__(self, pool):
        self.pool = pool
        self.off = 0

    def take(self, ncols, dtype=BF16):
        units = ncols * (2 if dtype == F32 else 1)
        sl = self.pool[:, self.off:self.off + units]
        self.off += (units + 15) // 16 * 16
        assert self.off <= POOLC, f"pool overflow {self.off}"
        return sl.bitcast(F32) if dtype == F32 else sl

    def at(self, off_units, ncols, dtype=BF16):
        units = ncols * (2 if dtype == F32 else 1)
        sl = self.pool[:, off_units:off_units + units]
        return sl.bitcast(F32) if dtype == F32 else sl


def _alias_a(c, pool):
    cv = Carve(pool)
    c.wbuf = cv.take(24576)
    c.cbuf = cv.take(1536)
    c.qf = cv.take(1024)
    c.kf = cv.take(1024)
    c.vf = cv.take(1024)
    c.stg = cv.take(2048)
    c.STb = cv.take(512)
    c.ckkd = cv.take(256)
    c.cvt = cv.take(256)
    c.cqd = cv.take(256)
    c.kvs = cv.take(2048, F32)
    c.kvb = cv.take(2048)
    c.kdS = cv.take(32, F32)
    c.bdS = cv.take(16, F32)
    c.ht = cv.take(4096)
    c.stgM = cv.take(2048)
    c.gmS = cv.take(2, F32)
    c.xt2 = cv.take(4096, F32)
    c.h0 = c.regA


def _alias_b(c, pool):
    cv = Carve(pool)
    c.regB = cv.at(0, 32768)
    c.ht = cv.at(0, 4096)
    c.xt2 = cv.at(36864, 4096, F32)
    cv.off = 32768
    c.wb = cv.take(8192)
    c.ft = cv.at(32768, 4096)
    c.abuf = cv.take(2048)
    c.sqb = cv.take(1024)
    c.rstdb = cv.take(1024, F32)
    c.rrow = cv.take(1024, F32)[0:1, :]
    c.rrow2 = cv.take(1024, F32)[0:1, :]
    c.rrowb = cv.take(1024)[0:1, :]
    c.normwS = cv.take(32, F32)
    c.onesB = cv.take(128)


def _alias_cb(c, pool):
    cv = Carve(pool)
    c.u1g = cv.take(8192)
    c.ht = cv.at(0, 4096)
    c.xt2 = cv.at(8192, 4096, F32)
    c.w_fm = cv.take(8192)
    c.wbufC = cv.take(8192)
    c.fb = cv.at(16384, 4096, F32)
    c.wvuS = cv.take(8192)
    c.scores = cv.take(1024, F32)
    c.attnb = cv.take(1024, F32)
    c.pbuf = cv.take(1024, F32)
    c.wtile = c.pbuf[:].bitcast(BF16)[:, 0:1024]
    c.msk = cv.take(1024)
    c.gtmp = cv.take(2048)
    c.aux2S = cv.take(2048)
    c.qr1 = cv.take(1024)
    c.query = c.qr1[0:64, :]
    c.rq = c.qr1[64:112, :]
    c.akr = cv.take(1024)
    c.ak_sb = c.akr[0:64, :]
    c.rk = c.akr[64:112, :]
    c.logrelS = cv.take(1024)
    c.wvdb = cv.take(512)
    c.wr_sb = cv.take(48)[0:64, :]
    c.wq_sb = c.wbufC[:, 0:2048]
    c.lv = cv.take(8, F32)
    c.mstore = cv.take(8, F32)
    c.rstore = cv.take(8, F32)
    c.u2S = cv.take(8, F32)
    c.gateS = cv.take(8, F32)
    c.scget = cv.take(4, F32)
    c.scalsS = cv.take(4, F32)[0:1, :]
    c.prow = cv.take(8, F32)[0:1, :]
    c.t8 = cv.take(8, F32)
    c.bu1S = cv.take(8, F32)
    c.wu2S = cv.take(8)
    c.hsl = c.scr4k[:, 0:512]


def _alias_d(c, pool):
    cv = Carve(pool)
    c.qgS = cv.take(8192)
    c.ht = cv.at(0, 4096)
    c.vS = cv.take(9216)
    c.wbufD = cv.take(16384)
    c.xt2 = cv.at(17408, 4096, F32)
    c.ott = cv.take(4096)
    c.OB = cv.take(1024)
    c.ET = cv.take(1024)
    c.rz = cv.take(4, F32)
    c.ZB = cv.take(128, F32)
    c.rzB = cv.take(128, F32)
    c.bqS = cv.take(8, F32)
    c.bkS = cv.take(8, F32)
    c.bvS = cv.take(8, F32)


def _alias_e(c, pool):
    cv = Carve(pool)
    c.regB = cv.at(0, 32768)
    c.xf = cv.at(0, 8192)
    c.ht = cv.at(8192, 4096)
    cv.off = 32768
    c.wbufE = cv.take(4096)
    c.w2buf = cv.take(11264)
    c.xt2 = cv.at(36864, 4096, F32)
    c.a1S = cv.take(64)
    c.a3S = cv.take(64)
    c.b1S = cv.take(256)[0:2, :]
    c.b3S = cv.take(256)[0:2, :]
    c.a2S = cv.take(4)
    c.b2S = cv.take(4096)[0:2, :]
    c.c1x = cv.take(1024)[0:2, :]
    c.c3x = cv.take(1024)[0:2, :]
    c.c2x = cv.take(1024)[0:2, :]
    c.xftmp = c.scr4k[:, 0:1024]
    c.ybuf = c.scr4k[:, 1024:1280].bitcast(BF16)
    c.gub = c.scr4k[:, 1280:1792].bitcast(BF16)


FUSED_CHANS = (
    "cInit cBc cX cXf cSq cStat cSqrt cHt pT0p pT0f pT1p pT1f pV0p pV0f pV1p pV1f "
    "cH0 cH0f cW cWf cC cCf pS0p pS0f pS1p pS1f pAp pAf pUp pUf cQKV cST cCK cCV "
    "cCQD cKV cStgV cOut cGate cA cAf cSqB cRr cRstd cRb cRbb cProd cOT cFt "
    "cSqH cLv cQev cRKev cRQev cSc cMsk cNegm cP cWt cWfm cU1 cGe1 cGe2 cGe3 "
    "cU2 cA1 cA2 cA3 cVd cVdW cFb cPj cBnc cKQ cKQf cET cOBv cOtt cW2 cW2f "
    "cC13 cXff cGU cY cYf cC2 cPt cAR cAG cBar cStgM pRp pRf cET2 cOBd cS2 cWu")


def emit_barrier(P, c):
    for e in ("vector", "scalar", "tensor"):
        def ev_s(t):
            sigd(t, c.cBar)
        P.on(e, ev_s)

    def ev_sync(t):
        wt(t, c.cOut)
        sigi(t, c.cBar)
    P.on("sync", ev_sync)

    def ev_g(t):
        sigi(t, c.cBar)
    P.on("gpsimd", ev_g)
    for e in Prog.ENGINES:
        def ev_w(t):
            wt(t, c.cBar)
        P.on(e, ev_w)


def build_fused():
    import contextlib
    from concourse.aot_env import allow_ici_all_to_all
    nc = bass.Bass("TRN2", target_bir_lowering=False)
    c = types.SimpleNamespace()
    c.fused = True
    dt = nc.dram_tensor
    c.xb = dt("xb", [S, D], F32, kind="ExternalInput").ap()
    c.xown = dt("xown", [TOK, D], F32, kind="ExternalInput").ap()
    c.qkvT = dt("qkvT", [D, HL * 384], BF16, kind="ExternalInput").ap()
    c.dcb = dt("dcb", [HL, 128, 768], F32, kind="ExternalInput").ap()
    c.kdall = dt("kdall", [128, 2 * HL], F32, kind="ExternalInput").ap()
    c.bdall = dt("bdall", [128, HL], F32, kind="ExternalInput").ap()
    c.identf = dt("identf", [128, 128], F32, kind="ExternalInput").ap()
    c.consts = dt("consts", [128, NCONST], F32, kind="ExternalInput").ap()
    c.gateTt = dt("gateTt", [32, D, 128], BF16, kind="ExternalInput").ap()
    c.outTt = dt("outTt", [32, D, 128], BF16, kind="ExternalInput").ap()
    c.normw = dt("normw", [128, 32], F32, kind="ExternalInput").ap()
    c.onesr = dt("onesr", [1, 128], F32, kind="ExternalInput").ap()
    c.wqT = dt("wqT", [D, 64], F32, kind="ExternalInput").ap()
    c.wrT = dt("wrT", [64, 48], F32, kind="ExternalInput").ap()
    c.akT = dt("akT", [64, 1024], F32, kind="ExternalInput").ap()
    c.avt = dt("avt", [32, 1024, 128], BF16, kind="ExternalInput").ap()
    c.wvdT = dt("wvdT", [D, 256], BF16, kind="ExternalInput").ap()
    c.wvuT = dt("wvuT", [256, D], BF16, kind="ExternalInput").ap()
    c.wu1t = dt("wu1t", [8, D, 128], BF16, kind="ExternalInput").ap()
    c.bu1c = dt("bu1c", [128, 8], F32, kind="ExternalInput").ap()
    c.wu2c = dt("wu2c", [128, 8], F32, kind="ExternalInput").ap()
    c.scals = dt("scals", [1, 4], F32, kind="ExternalInput").ap()
    c.logrel = dt("logrel", [128, 1024], F32, kind="ExternalInput").ap()
    c.wqt = dt("wqt", [8, 4, D, 128], BF16, kind="ExternalInput").ap()
    c.wkt = dt("wkt", [8, D, 128], BF16, kind="ExternalInput").ap()
    c.wvt = dt("wvt", [8, D, 128], BF16, kind="ExternalInput").ap()
    c.bqj = dt("bqj", [4, 128, 8], F32, kind="ExternalInput").ap()
    c.bkc = dt("bkc", [128, 8], F32, kind="ExternalInput").ap()
    c.bvc = dt("bvc", [128, 8], F32, kind="ExternalInput").ap()
    c.w1pt = dt("w1pt", [22, 4, 1024, 128], BF16, kind="ExternalInput").ap()
    c.w3pt = dt("w3pt", [22, 4, 1024, 128], BF16, kind="ExternalInput").ap()
    c.b1t = dt("b1t", [22, 4, 2, 128], F32, kind="ExternalInput").ap()
    c.b3t = dt("b3t", [22, 4, 2, 128], F32, kind="ExternalInput").ap()
    c.a2t = dt("a2t", [22, 4, 128, 2], F32, kind="ExternalInput").ap()
    c.a1T = dt("a1T", [D, 2], F32, kind="ExternalInput").ap()
    c.a3T = dt("a3T", [D, 2], F32, kind="ExternalInput").ap()
    c.w2qTp = dt("w2qTp", [WPAD, D], BF16, kind="ExternalInput").ap()
    c.b2T = dt("b2T", [2, D], F32, kind="ExternalInput").ap()
    c.xout = dt("xout", [TOK, D], F32, kind="ExternalOutput").ap()
    c.gmask = dt("gmask", [128, 2], F32, kind="ExternalInput").ap()
    ikind = "ExternalOutput" if FUSED_DEBUG else "Internal"
    c.rsin = dt("rsin", [2 * D, 1024], BF16).ap()
    c.afm_rs = dt("afm_rs", [D, 1024], BF16).ap()
    if FUSED_DEBUG:
        c.dbgafm = dt("dbgafm", [D, 1024], BF16, kind="ExternalOutput").ap()
        c.dbg2 = dt("dbg2", [128, 1074], F32, kind="ExternalOutput").ap()
        c.dbg3 = dt("dbg3", [256, 1024], BF16, kind="ExternalOutput").ap()
        c.dbg4 = dt("dbg4", [128, 1024], BF16, kind="ExternalOutput").ap()
    c.afm = c.afm_rs
    c.x1 = dt("x1", [TOK, D], F32, kind=ikind).ap()
    c.x2 = dt("x2", [TOK, D], F32, kind=ikind).ap()
    c.x3 = dt("x3", [TOK, D], F32, kind=ikind).ap()
    c.partA = dt("partA", [1, 1], F32).ap()
    c.partR = dt("partR", [1, 1], F32).ap()
    c.KT = dt("KT", [8, 8, 128, 128], BF16).ap()
    c.QT = dt("QT", [8, 8, 128, 128], BF16).ap()

    with contextlib.ExitStack() as st:
        sb, ps = (lambda *a: _sb(st, nc, *a)), (lambda *a: _ps(st, nc, *a))
        c.cst = sb("cst", [128, NCONST], F32)
        c.identb = sb("identb", [128, 128], BF16)
        c.ones1 = sb("ones1", [1, 128], F32)
        c.regA = sb("regA", [128, 32768], BF16)
        c.pool = sb("pool", [128, POOLC], BF16)
        c.xt = sb("xt", [128, D], F32)
        c.scr4k = sb("scr4k", [128, D], F32)
        c.lnst = sb("lnst", [128, 8], F32)
        c.pS0 = ps("pS0", [128, 512], F32)
        c.pS1 = ps("pS1", [128, 512], F32)
        c.pA = ps("pA", [128, 512], F32)
        c.pU = ps("pU", [128, 512], F32)
        c.pT0 = ps("pT0", [128, 1024], BF16)
        c.pT1 = ps("pT1", [128, 1024], BF16)
        c.pV0 = ps("pV0", [128, 1024], BF16)
        c.pV1 = ps("pV1", [128, 1024], BF16)
        c.pR = c.pV0[:].bitcast(F32)[:, 0:512]
        c.pF0 = c.pT0[:].bitcast(F32)
        c.pF1 = c.pT1[:].bitcast(F32)
        block = st.enter_context(nc.Block())

        P = Prog(nc, st)
        for nm in FUSED_CHANS.split():
            setattr(c, nm, P.ch(nm))

        def ev_ones(t):
            i = t and t.dma_start(out=c.ones1[:], in_=c.onesr)
            sig(t, i, c.cInit, 16)
        P.on("gpsimd", ev_ones)

        cA = types.SimpleNamespace(**vars(c))
        _alias_a(cA, c.pool)
        emit_a(P, cA)
        emit_barrier(P, c)

        def ev_rs(t):
            i = t and t.collective_compute(
                "ReduceScatter", mybir.AluOpType.add,
                replica_groups=[[0, 1], [2, 3], [4, 5], [6, 7]],
                ins=[c.rsin], outs=[c.afm_rs])
            sig(t, i, c.cAG)
        P.on("gpsimd", ev_rs)

        if FUSED_DEBUG:
            def ev_dbgafm(t):
                wt(t, c.cAG)
                i = t and t.dma_start(out=c.dbgafm, in_=c.afm_rs)
                sig(t, i, c.cOut, 16)
                i = t and t.dma_start(out=c.dbg2[0:128, 0:2], in_=cA.gmS[:])
                sig(t, i, c.cOut, 16)
                i = t and t.dma_start(out=c.dbg2[0:128, 2:34], in_=cA.kdS[:])
                sig(t, i, c.cOut, 16)
                i = t and t.dma_start(out=c.dbg2[0:128, 34:50], in_=cA.bdS[:])
                sig(t, i, c.cOut, 16)
                i = t and t.dma_start(out=c.dbg2[0:128, 50:562],
                                      in_=cA.stgM[:, 0:1024].bitcast(F32)[:, 0:512])
                sig(t, i, c.cOut, 16)
                i = t and t.dma_start(out=c.dbg3[0:128, :], in_=c.rsin[0:128, :])
                sig(t, i, c.cOut, 16)
                i = t and t.dma_start(out=c.dbg3[128:256, :], in_=c.rsin[2048:2176, :])
                sig(t, i, c.cOut, 16)
            P.on("sync", ev_dbgafm)

        cB = types.SimpleNamespace(**vars(c))
        _alias_b(cB, c.pool)
        emit_b(P, cB)
        emit_barrier(P, c)
        cC = types.SimpleNamespace(**vars(c))
        _alias_cb(cC, c.pool)
        emit_cb(P, cC)
        emit_barrier(P, c)
        cD = types.SimpleNamespace(**vars(c))
        _alias_d(cD, c.pool)
        emit_d(P, cD)
        emit_barrier(P, c)
        cE = types.SimpleNamespace(**vars(c))
        _alias_e(cE, c.pool)
        emit_e(P, cE)

        with allow_ici_all_to_all():
            P.emit(block)
    return nc


def _prep_fused(inputs):
    consts = _consts_host_c()
    consts[:, C_INVSQ128] = 1.0 / math.sqrt(128.0)
    maps_a = _prep_a(inputs)
    dummy_att = [np.zeros((HL, 128, S), BF) for _ in range(NCORES)]
    maps_b = _prep_b(inputs, dummy_att)
    maps_cb = _prep_cb(inputs, [np.zeros((TOK, D), np.float32)] * NCORES,
                       np.zeros(8, np.float32))
    maps_d = _prep_d(inputs, [np.zeros((TOK, D), np.float32)] * NCORES)
    maps_e = _prep_e(inputs, [np.zeros((TOK, D), np.float32)] * NCORES)
    onesr = np.ones((1, 128), np.float32)
    in_maps = []
    for core in range(NCORES):
        m = {}
        for srcm, drop in ((maps_a[core], ("attn_fm",)),
                           (maps_b[core], ("afm",)),
                           (maps_cb[core], ("x1", "partials")),
                           (maps_d[core], ("x2",)),
                           (maps_e[core], ("x3",))):
            for k, v in srcm.items():
                if k not in drop:
                    m[k] = v
        m["consts"] = consts
        m["onesr"] = onesr
        gm = np.zeros((128, 2), np.float32)
        gm[:, core % 2] = 1.0
        m["gmask"] = gm
        in_maps.append(m)
    return in_maps


def kernel_fused(inputs):
    nc = build_fused()
    in_maps = _prep_fused(inputs)
    res = run_bass_kernel_spmd(nc, in_maps, core_ids=list(range(NCORES)))
    out = np.zeros((B, S, D), np.float32)
    for core in range(NCORES):
        bb, g = core // 2, core % 2
        out[bb, g * TOK:(g + 1) * TOK] = np.asarray(res.results[core]["xout"])
    return out
